# version 03
# baseline (speedup 1.0000x reference)
import numpy as np
import contextlib
import concourse.bass as bass
import concourse.mybir as mybir
from concourse.bass_utils import run_bass_kernel_spmd

F32 = mybir.dt.float32
BF16 = mybir.dt.bfloat16
I32 = mybir.dt.int32
AF = mybir.ActivationFunctionType
ALU = mybir.AluOpType
AX = mybir.AxisListType

D = 1024
NCORE = 8
B_PER_CORE = 2
T_SEQ = 2048
T_CORE = B_PER_CORE * T_SEQ
NCH = D // 128
LN_EPS = 1e-5
BRANCHES_L0B = (0, 1)
DBG_L1F = 3
DBG_GATE = 3
DBG_FIN = 1
DBG_NEXP = 15
DBG_DUMP = 0
DBG_SHBIAS = 1
DBG_SHPH = 2
DBG_EXPPH = 3

ENGINES = ("tensor", "vector", "scalar", "gpsimd", "sync")


class Step:
    __slots__ = ("engine", "emit", "deps", "is_dma", "idx", "inc_needed",
                 "sem_val_after", "tag")

    def __init__(self, engine, emit, deps, is_dma, idx, tag=""):
        self.engine = engine
        self.emit = emit
        self.deps = deps
        self.is_dma = is_dma
        self.idx = idx
        self.inc_needed = False
        self.sem_val_after = None
        self.tag = tag


def _flat(deps):
    out = []
    for d in deps:
        if d is None:
            continue
        if isinstance(d, (list, tuple)):
            out.extend(_flat(d))
        else:
            out.append(d)
    return out


class Prog:
    def __init__(self, nc):
        self.nc = nc
        self.steps = []

    def step(self, engine, emit, deps=(), tag=""):
        s = Step(engine, emit, tuple(_flat(deps)), False, len(self.steps), tag)
        self.steps.append(s)
        return s

    def dma(self, queue, emit, deps=(), tag=""):
        s = Step(queue, emit, tuple(_flat(deps)), True, len(self.steps), tag)
        self.steps.append(s)
        return s

    def finalize(self, stack):
        nc = self.nc
        for s in self.steps:
            if s.is_dma:
                s.inc_needed = True
            for d in s.deps:
                d.inc_needed = True
        chan_of = {}
        for s in self.steps:
            chan_of[s.idx] = ("dma_" + s.engine) if s.is_dma else ("eng_" + s.engine)
        chans = sorted(set(chan_of.values()))
        sems = {c: stack.enter_context(nc.semaphore("sem_" + c)) for c in chans}
        counts = {c: 0 for c in chans}
        for s in self.steps:
            c = chan_of[s.idx]
            if s.inc_needed:
                counts[c] += 16 if s.is_dma else 1
                s.sem_val_after = counts[c]
        per_engine = {e: [] for e in ENGINES}
        for s in self.steps:
            per_engine[s.engine].append(s)
        block = stack.enter_context(nc.Block())

        prev_dma_target = {}

        def mk_body(steps):
            def body(eng):
                for s in steps:
                    needs = {}
                    for d in s.deps:
                        c = chan_of[d.idx]
                        needs[c] = max(needs.get(c, 0), d.sem_val_after)
                    if s.is_dma:
                        c = chan_of[s.idx]
                        t = prev_dma_target.get(c)
                        if t is not None:
                            needs[c] = max(needs.get(c, 0), t)
                    for c, v in sorted(needs.items()):
                        eng.wait_ge(sems[c], v)
                    last = s.emit(eng)
                    if s.inc_needed:
                        assert last is not None, f"step '{s.tag}' must return last instruction"
                        last.then_inc(sems[chan_of[s.idx]], 16 if s.is_dma else 1)
                        if s.is_dma:
                            prev_dma_target[chan_of[s.idx]] = s.sem_val_after
            return body

        for e in ENGINES:
            if per_engine[e]:
                getattr(block, e)(mk_body(per_engine[e]))
        return sems


class MK:
    def __init__(self, nc, stack):
        self.nc = nc
        self.stack = stack
        self.P = Prog(nc)
        self.dram = {}
        sb = lambda n, sh, dt: stack.enter_context(nc.sbuf_tensor(n, list(sh), dt))
        self.XN = sb("XN", [128, 8 * T_CORE], BF16)
        self.PA = sb("PA", [128, 16384], BF16)
        self.PB = sb("PB", [128, 16384], BF16)
        self.PC = sb("PC", [128, 8192], F32)
        self.PD = sb("PD", [128, 8192], F32)
        self.ROWS = sb("ROWS", [1, 2048], F32)
        self.ROWB = sb("ROWB", [1, 1024], BF16)
        self.COLS = sb("COLS", [128, 640], F32)
        self.CONST = sb("CONST", [128, 16], F32)
        self.ONEB = sb("ONEB", [1, 512], BF16)
        self.ONEF = sb("ONEF", [1, 128], F32)
        self.ONECB = sb("ONECB", [128, 1], BF16)
        self.EH = sb("EH", [1, 256], F32)
        self.IDB = sb("IDB", [128, 128], BF16)
        self.IDF = sb("IDF", [128, 128], F32)
        self.pb = [stack.enter_context(nc.psum_tensor(f"pb{i}", [128, 512], F32))
                   for i in range(8)]
        self._col_off = 0
        self.cols = {}
        self._const_off = 0
        self.consts = {}
        self.init_deps = []

    def dram_in(self, name, shape, dtype=F32):
        ap = self.nc.dram_tensor(name, list(shape), dtype, kind="ExternalInput").ap()
        self.dram[name] = ap
        return ap

    def dram_out(self, name, shape, dtype=F32):
        ap = self.nc.dram_tensor(name, list(shape), dtype, kind="ExternalOutput").ap()
        self.dram[name] = ap
        return ap

    def dram_tmp(self, name, shape, dtype=F32):
        ap = self.nc.dram_tensor(name, list(shape), dtype).ap()
        self.dram[name] = ap
        return ap

    def const_col(self, value):
        if value not in self.consts:
            i = self._const_off
            self._const_off += 1
            assert i < 15
            ap = self.CONST[:, i:i + 1]
            st = self.P.step("vector", (lambda ap=ap, v=value: lambda e: e.memset(ap, v))(),
                             tag=f"const{value}")
            self.init_deps.append(st)
            self.consts[value] = (ap, st)
        return self.consts[value][0]

    def col(self, name):
        return self.cols[name][0]

    def cdep(self, value):
        return self.consts[value][1]

    def load_cols(self, name, n):
        src = self.dram_in("c_" + name, [128, n])
        o = self._col_off
        self._col_off += n
        assert self._col_off <= 640, "COLS overflow"
        dst = self.COLS[:, o:o + n]
        st = self.P.dma("sync", (lambda dst=dst, src=src: lambda e: e.dma_start(out=dst, in_=src))(),
                        tag=f"cols_{name}")
        self.cols[name] = (dst, st)
        self.init_deps.append(st)
        return dst

    def derive_col(self, name, src_name, fn):
        srcap, srcstep = self.cols[src_name]
        n = srcap.shape[-1]
        o = self._col_off
        self._col_off += n
        assert self._col_off <= 640
        dst = self.COLS[:, o:o + n]
        st = self.P.step("vector", (lambda dst=dst, srcap=srcap: lambda e: fn(e, dst, srcap))(),
                         deps=(srcstep,), tag=f"dcol_{name}")
        self.cols[name] = (dst, st)
        self.init_deps.append(st)
        return dst

    def init_consts(self):
        P = self.P
        s = []
        s.append(P.step("vector", lambda e: e.memset(self.ONEB[:], 1.0), tag="i_oneb"))
        s.append(P.step("vector", lambda e: e.memset(self.ONEF[:], 1.0), tag="i_onef"))
        s.append(P.step("vector", lambda e: e.memset(self.ONECB[:], 1.0), tag="i_onecb"))
        s.append(P.step("vector", lambda e: e.memset(self.EH[0:1, 0:64], 1.0), tag="i_eh0"))
        s.append(P.step("vector", lambda e: e.memset(self.EH[0:1, 64:192], 0.0), tag="i_eh1"))
        s.append(P.step("vector", lambda e: e.memset(self.EH[0:1, 192:256], 1.0), tag="i_eh2"))
        io = self.PC[:, 0:128].bitcast(I32)
        s0 = P.step("gpsimd", lambda e: e.iota(io, [[1, 128]], base=0, channel_multiplier=-1),
                    tag="i_iota")
        s.append(P.step("vector", lambda e: e.tensor_single_scalar(self.IDB[:], io, 0, ALU.is_equal),
                        deps=(s0,), tag="i_idb"))
        s.append(P.step("vector", lambda e: e.tensor_single_scalar(self.IDF[:], io, 0, ALU.is_equal),
                        deps=(s0,), tag="i_idf"))
        self.init_deps.extend(s)
        return s

    def load_wT(self, dram_name, dst, dout, deps=(), din=D, col0=0, rowstride=None,
                base=0, tag=""):
        src = self.dram[dram_name]
        if rowstride is None:
            rowstride = dout
        nchunk = din // 128
        view = dst.rearrange("p (c n) -> p c n", c=nchunk)
        return self.P.dma("gpsimd",
                          (lambda view=view, src=src: lambda e: e.dma_start(
                               out=view,
                               in_=bass.AP(src.tensor, base + col0,
                                           [[rowstride, 128], [128 * rowstride, nchunk], [1, dout]])))(),
                          deps=deps, tag=f"w_{dram_name}{tag}")



def emit_ln(mk, x_in, deps=()):
    P = mk.P
    pb = mk.pb
    r_m = mk.ROWS[0:1, 0:512]
    r_t = mk.ROWS[0:1, 512:1024]
    r_r = mk.ROWS[0:1, 1024:1536]
    r_mr = mk.ROWS[0:1, 1536:2048]
    epscol = mk.const_col(LN_EPS)
    done = []
    prev_x = {0: [deps, mk.init_deps], 1: [deps, mk.init_deps]}
    prev_pa = mk.init_deps
    prev_rows = None
    prev_b = {4: None, 5: None, 6: None, 7: None}
    for tt in range(8):
        xt = mk.PC[:, (tt % 2) * 4096:(tt % 2) * 4096 + 4096]
        ld = P.dma("sync",
                   (lambda xt=xt, tt=tt: lambda e: e.dma_start(
                       out=xt.rearrange("p (c t) -> p c t", t=512),
                       in_=bass.AP(x_in.tensor, tt * 512, [[T_CORE, 128], [128 * T_CORE, NCH], [1, 512]])))(),
                   deps=(prev_x[tt % 2],), tag=f"ln_ld{tt}")
        xbf = mk.PA[:, 0:4096]
        x2b = mk.PA[:, 4096:8192]
        c1 = P.step("vector", (lambda xt=xt, xbf=xbf: lambda e: e.tensor_copy(xbf, xt))(),
                    deps=(ld, prev_pa), tag=f"ln_cast{tt}")
        c2 = P.step("scalar", (lambda xbf=xbf, x2b=x2b: lambda e: e.activation(x2b, xbf, AF.Square))(),
                    deps=(c1,), tag=f"ln_sq{tt}")

        def mk_stats(src, bank):
            def f(e):
                mm = None
                for c in range(NCH):
                    mm = e.matmul(bank[0:1, 0:512], mk.ONECB[:, 0:1],
                                  src[:, c * 512:(c + 1) * 512],
                                  start=(c == 0), stop=(c == NCH - 1))
                return mm
            return f
        m1 = P.step("tensor", mk_stats(xbf, pb[6]), deps=(c1, prev_b[6]), tag=f"ln_s1_{tt}")
        m2 = P.step("tensor", mk_stats(x2b, pb[7]), deps=(c2, prev_b[7]), tag=f"ln_s2_{tt}")
        q1 = P.step("vector", lambda e: e.tensor_single_scalar(r_m, pb[6][0:1, 0:512], 1.0 / D, ALU.mult),
                    deps=(m1, prev_rows), tag=f"ln_m{tt}")
        q2 = P.step("vector", lambda e: e.tensor_single_scalar(r_t, pb[7][0:1, 0:512], 1.0 / D, ALU.mult),
                    deps=(m2, prev_rows), tag=f"ln_e2{tt}")
        q3a = P.step("vector", lambda e: e.tensor_mul(r_r, r_m, r_m),
                     deps=(q1, prev_rows), tag=f"ln_msq{tt}")
        q3 = P.step("vector", lambda e: e.tensor_sub(r_t, r_t, r_r),
                    deps=(q2, q3a), tag=f"ln_var{tt}")
        q5 = P.step("scalar", lambda e: e.activation(r_t, r_t, AF.Ln, bias=epscol[0:1, 0:1], scale=1.0),
                    deps=(q3, mk.cdep(LN_EPS)), tag=f"ln_ln{tt}")
        q6 = P.step("scalar", lambda e: e.activation(r_r, r_t, AF.Exp, scale=-0.5),
                    deps=(q5,), tag=f"ln_rstd{tt}")
        q7 = P.step("vector", lambda e: e.tensor_mul(r_mr, r_m, r_r), deps=(q6, q1), tag=f"ln_mr{tt}")
        b1 = P.step("tensor", lambda e: e.matmul(pb[4][:, 0:512], mk.ONEF[0:1, 0:128], r_r,
                                                 start=True, stop=True),
                    deps=(q6, prev_b[4]), tag=f"ln_rb{tt}")
        b2 = P.step("tensor", lambda e: e.matmul(pb[5][:, 0:512], mk.ONEF[0:1, 0:128], r_mr,
                                                 start=True, stop=True),
                    deps=(q7, prev_b[5]), tag=f"ln_mrb{tt}")
        prev_rows = [b1, b2]
        xn_steps = []
        x_cons = []
        for c in range(NCH):
            xc = xt[:, c * 512:(c + 1) * 512]
            t1 = mk.PD[:, (c % 2) * 512:(c % 2) * 512 + 512]
            dep_prev_t1 = xn_steps[c - 2] if c >= 2 else None
            u1 = P.step("vector",
                        (lambda xc=xc, t1=t1: lambda e: e.tensor_mul(t1, xc, pb[4][:, 0:512]))(),
                        deps=(ld, b1, dep_prev_t1), tag=f"ln_t1_{tt}_{c}")
            xn_sl = mk.XN[:, c * T_CORE + tt * 512: c * T_CORE + tt * 512 + 512]
            u2 = P.step("vector",
                        (lambda t1=t1, xn_sl=xn_sl: lambda e: e.tensor_sub(xn_sl, t1, pb[5][:, 0:512]))(),
                        deps=(u1, b2), tag=f"ln_xn_{tt}_{c}")
            xn_steps.append(u2)
            x_cons.append(u1)
        prev_x[tt % 2] = x_cons
        prev_pa = [m1, m2]
        prev_b[6] = q1
        prev_b[7] = q2
        prev_b[4] = xn_steps[-1]
        prev_b[5] = xn_steps[-1]
        done.append(xn_steps)
    return done



def emit_l0_attn_a(mk, x_in, x_out, prev=()):
    P = mk.P
    pb = mk.pb
    XN, PA, PB, PC, PD = mk.XN, mk.PA, mk.PB, mk.PC, mk.PD
    WB = PC[:, 0:4096].bitcast(BF16)
    ln_done = emit_ln(mk, x_in, deps=prev)
    lnbar = ln_done[-1]
    bq = mk.col("l0a_bq"); bq1 = mk.col("l0a_bq1"); bo = mk.col("l0a_bo")

    rrow = mk.dram["r_l0a_bkv"]
    rowf = PD[0:1, 6656:7680]
    out_final = []
    prev_batch_done = None

    wb_free = None
    for b in range(B_PER_CORE):
        tb = b * T_SEQ
        ev_all = {}
        rowb_cons = prev_batch_done
        for wname, dstpool, kind in (("l0a_wkT", PA, "k"), ("l0a_wvT", PB, "v")):
            src_off = 0 if kind == "k" else 1024
            ldr = P.dma("sync", (lambda src_off=src_off: lambda e: e.dma_start(
                out=rowf[0:1, 0:1024], in_=rrow[0:1, src_off:src_off + 1024]))(),
                deps=(rowb_cons, prev_batch_done), tag=f"l0a_ldr_{kind}{b}")
            rcast = P.step("vector", lambda e: e.tensor_copy(mk.ROWB[0:1, 0:1024], rowf[0:1, 0:1024]),
                           deps=(ldr, rowb_cons), tag=f"l0a_rc_{kind}{b}")
            wld = mk.load_wT(wname, WB[:, :], 1024, deps=(wb_free, lnbar, prev_batch_done))
            ev_steps = []
            for j in range(16):
                for h in range(2):
                    bank = pb[(j * 2 + h) % 4]
                    tsl = tb + j * 128

                    def mm(e, tsl=tsl, h=h, bank=bank):
                        for c in range(NCH):
                            e.matmul(bank[:, 0:512],
                                     XN[:, c * T_CORE + tsl: c * T_CORE + tsl + 128],
                                     WB[:, c * 1024 + h * 512: c * 1024 + h * 512 + 512],
                                     start=(c == 0), stop=False)
                        return e.matmul(bank[:, 0:512], mk.ONEB[0:1, 0:128],
                                        mk.ROWB[0:1, h * 512: h * 512 + 512],
                                        start=False, stop=True)
                    prev_bank = ev_steps[-4] if len(ev_steps) >= 4 else None
                    s_mm = P.step("tensor", mm,
                                  deps=(wld, rcast, ln_done[tsl // 512], prev_bank),
                                  tag=f"l0a_p{kind}{b}_{j}_{h}")
                    dst = dstpool[:, j * 1024 + h * 512: j * 1024 + h * 512 + 512]
                    if kind == "k":
                        t_zm = PD[:, 2048 + (h % 2) * 512: 2560 + (h % 2) * 512]
                        t_e = PD[:, 3072 + (h % 2) * 512: 3584 + (h % 2) * 512]
                        t_z1 = PD[:, 4096 + (h % 2) * 512: 4608 + (h % 2) * 512]
                        prev_scr = ev_steps[-2] if len(ev_steps) >= 2 else None
                        e1 = P.step("vector", (lambda bank=bank, t_zm=t_zm: lambda e:
                                               e.tensor_single_scalar(t_zm, bank[:, 0:512], 0.0, ALU.min))(),
                                    deps=(s_mm, prev_scr), tag=f"l0a_kzm{b}_{j}_{h}")
                        e2 = P.step("scalar", (lambda t_zm=t_zm, t_e=t_e: lambda e:
                                               e.activation(t_e, t_zm, AF.Exp, scale=1.0))(),
                                    deps=(e1, prev_scr), tag=f"l0a_kexp{b}_{j}_{h}")
                        e3 = P.step("vector", (lambda bank=bank, t_z1=t_z1: lambda e:
                                               e.tensor_single_scalar(t_z1, bank[:, 0:512], 1.0, ALU.add))(),
                                    deps=(s_mm, prev_scr), tag=f"l0a_kz1{b}_{j}_{h}")
                        e4 = P.step("vector", (lambda t_e=t_e, t_z1=t_z1, dst=dst: lambda e:
                                               e.tensor_max(dst, t_e, t_z1))(),
                                    deps=(e2, e3), tag=f"l0a_kf{b}_{j}_{h}")
                        ev_steps.append(e4)
                    else:
                        e1 = P.step("vector", (lambda bank=bank, dst=dst: lambda e:
                                               e.tensor_copy(dst, bank[:, 0:512]))(),
                                    deps=(s_mm,), tag=f"l0a_vev{b}_{j}_{h}")
                        ev_steps.append(e1)
            ev_all[kind] = ev_steps
            rowb_cons = ev_steps[-1]
            wb_free = ev_steps[-1]

        k_done = ev_all["k"][-1]
        v_done = ev_all["v"][-1]

        KV = PD[:, 5120:5376].bitcast(BF16)
        ksum = PD[:, 5376:5384].bitcast(BF16)
        kv_ev = []
        for h in range(16):
            bank = pb[4 + (h % 2)]

            def mmkv(e, h=h, bank=bank):
                mmop = None
                for j in range(16):
                    mmop = e.matmul(bank[0:64, 0:64],
                                    PA[:, j * 1024 + h * 64: j * 1024 + h * 64 + 64],
                                    PB[:, j * 1024 + h * 64: j * 1024 + h * 64 + 64],
                                    start=(j == 0), stop=(j == 15))
                return mmop
            prev_kv = kv_ev[h - 2] if h >= 2 else None
            s_kv = P.step("tensor", mmkv, deps=(k_done, v_done, prev_kv), tag=f"l0a_kv{b}_{h}")
            dst = KV[(h % 2) * 64:(h % 2) * 64 + 64, (h // 2) * 64:(h // 2) * 64 + 64]
            e_kv = P.step("vector", (lambda bank=bank, dst=dst: lambda e:
                                     e.tensor_copy(dst, bank[0:64, 0:64]))(),
                          deps=(s_kv,), tag=f"l0a_kvev{b}_{h}")
            kv_ev.append(e_kv)

        def mmksum(e):
            mmop = None
            for c in range(NCH):
                for j in range(16):
                    mmop = e.matmul(pb[6][:, c:c + 1],
                                    PA[:, j * 1024 + c * 128: j * 1024 + c * 128 + 128],
                                    mk.ONECB[:, 0:1],
                                    start=(j == 0), stop=(j == 15))
            return mmop
        s_ks = P.step("tensor", mmksum, deps=(k_done,), tag=f"l0a_ksum{b}")
        e_ks = P.step("vector", lambda e: e.tensor_copy(ksum[:, 0:8], pb[6][:, 0:8]),
                      deps=(s_ks,), tag=f"l0a_ksumev{b}")

        wq_ld = mk.load_wT("l0a_wqT", PB[:, 0:8192], 1024, deps=(kv_ev[-1], e_ks))
        wo_ld = mk.load_wT("l0a_woT", PB[:, 8192:16384], 1024, deps=(kv_ev[-1], e_ks))
        q_tile = PA[:, 0:4096]
        a_tile = PA[:, 4096:8192]
        prev_nt = {}
        for nt in range(4):
            t0 = tb + nt * 512
            ttile = t0 // 512
            ldx = P.dma("sync",
                        (lambda t0=t0: lambda e: e.dma_start(
                            out=PC[:, 0:4096].rearrange("p (c t) -> p c t", t=512),
                            in_=bass.AP(x_in.tensor, t0, [[T_CORE, 128], [128 * T_CORE, NCH], [1, 512]])))(),
                        deps=(prev_nt.get("x_cons"), lnbar, k_done, v_done),
                        tag=f"l0a_ldx{b}_{nt}")
            q_ev = []
            for o in range(NCH):
                bank = pb[o % 4]

                def mmq(e, o=o, t0=t0, bank=bank):
                    mmop = None
                    for c in range(NCH):
                        mmop = e.matmul(bank[:, 0:512],
                                        PB[:, c * 1024 + o * 128: c * 1024 + o * 128 + 128],
                                        XN[:, c * T_CORE + t0: c * T_CORE + t0 + 512],
                                        start=(c == 0), stop=(c == NCH - 1))
                    return mmop
                prev_bank = q_ev[o - 4] if o >= 4 else prev_nt.get("qbank")
                s_q = P.step("tensor", mmq, deps=(wq_ld, e_ks, prev_bank),
                             tag=f"l0a_q{b}_{nt}_{o}")
                t_zm = PD[:, 4096:4608]
                t_e = PD[:, 6656:7168]
                t_z1 = PD[:, 7168:7680]
                prev_scr = q_ev[o - 1] if o >= 1 else prev_nt.get("qscr")
                e1 = P.step("vector", (lambda bank=bank, t_zm=t_zm, o=o: lambda e:
                                       e.tensor_scalar(t_zm, bank[:, 0:512], bq[:, o:o + 1], 0.0,
                                                       ALU.add, ALU.min))(),
                            deps=(s_q, prev_scr), tag=f"l0a_qzm{b}_{nt}_{o}")
                e2 = P.step("scalar", (lambda t_zm=t_zm, t_e=t_e: lambda e:
                                       e.activation(t_e, t_zm, AF.Exp, scale=1.0))(),
                            deps=(e1, prev_scr), tag=f"l0a_qexp{b}_{nt}_{o}")
                e3 = P.step("vector", (lambda bank=bank, t_z1=t_z1, o=o: lambda e:
                                       e.tensor_single_scalar(t_z1, bank[:, 0:512], bq1[:, o:o + 1],
                                                              ALU.add))(),
                            deps=(s_q, prev_scr), tag=f"l0a_qz1{b}_{nt}_{o}")
                dst = q_tile[:, o * 512:(o + 1) * 512]
                e4 = P.step("vector", (lambda t_e=t_e, t_z1=t_z1, dst=dst: lambda e:
                                       e.tensor_max(dst, t_e, t_z1))(),
                            deps=(e2, e3), tag=f"l0a_qf{b}_{nt}_{o}")
                q_ev.append(e4)
            at_ev = []
            for p in range(8):
                h0, h1 = 2 * p, 2 * p + 1
                z_rows = []
                for hh, h in enumerate((h0, h1)):
                    def mmden(e, h=h):
                        return e.matmul(pb[6][0:1, 0:512],
                                        ksum[(h % 2) * 64:(h % 2) * 64 + 64, (h // 2):(h // 2) + 1],
                                        q_tile[(h % 2) * 64:(h % 2) * 64 + 64,
                                               (h // 2) * 512:(h // 2) * 512 + 512],
                                        start=True, stop=True)
                    dep_prev_den = z_rows[-1] if z_rows else (at_ev[-1] if at_ev else prev_nt.get("den"))
                    s_d = P.step("tensor", mmden, deps=(e_ks, q_ev[h // 2], dep_prev_den),
                                 tag=f"l0a_den{b}_{nt}_{h}")
                    r_d = mk.ROWS[0:1, hh * 512: hh * 512 + 512]
                    d1 = P.step("vector", (lambda r_d=r_d: lambda e:
                                           e.tensor_single_scalar(r_d, pb[6][0:1, 0:512], 1e-6, ALU.add))(),
                                deps=(s_d, prev_nt.get("zrows"), lnbar), tag=f"l0a_zadd{b}_{nt}_{h}")
                    d2 = P.step("scalar", (lambda r_d=r_d: lambda e:
                                           e.activation(r_d, r_d, AF.Ln, scale=1.0))(),
                                deps=(d1,), tag=f"l0a_zln{b}_{nt}_{h}")
                    d3 = P.step("scalar", (lambda r_d=r_d: lambda e:
                                           e.activation(r_d, r_d, AF.Exp, scale=-1.0))(),
                                deps=(d2,), tag=f"l0a_zexp{b}_{nt}_{h}")
                    z_rows.append(d3)
                def mmU0(e, h0=h0):
                    return e.matmul(pb[5][0:64, 0:512],
                                    KV[0:64, (h0 // 2) * 64:(h0 // 2) * 64 + 64],
                                    q_tile[0:64, (h0 // 2) * 512:(h0 // 2) * 512 + 512],
                                    start=True, stop=True)

                def mmU1(e, h1=h1):
                    return e.matmul(pb[4][0:64, 0:512],
                                    KV[64:128, (h1 // 2) * 64:(h1 // 2) * 64 + 64],
                                    q_tile[64:128, (h1 // 2) * 512:(h1 // 2) * 512 + 512],
                                    start=True, stop=True)
                u_dep = at_ev[-1] if at_ev else prev_nt.get("u")
                s_u0 = P.step("tensor", mmU0, deps=(kv_ev[-1], q_ev[p], u_dep), tag=f"l0a_U0{b}_{nt}_{p}")
                s_u1 = P.step("tensor", mmU1, deps=(kv_ev[-1], q_ev[p], u_dep), tag=f"l0a_U1{b}_{nt}_{p}")
                def mmrbc(e):
                    e.matmul(pb[7][:, 0:512], mk.EH[0:1, 0:128], mk.ROWS[0:1, 0:512],
                             start=True, stop=False)
                    return e.matmul(pb[7][:, 0:512], mk.EH[0:1, 128:256], mk.ROWS[0:1, 512:1024],
                                    start=False, stop=True)
                s_r = P.step("tensor", mmrbc,
                             deps=(z_rows, at_ev[-1] if at_ev else prev_nt.get("rbc")),
                             tag=f"l0a_rbc{b}_{nt}_{p}")
                rbc_sb = PD[:, 5632 + (p % 2) * 512: 6144 + (p % 2) * 512]
                c_r = P.step("vector", (lambda rbc_sb=rbc_sb: lambda e:
                                        e.tensor_copy(rbc_sb, pb[7][:, 0:512]))(),
                             deps=(s_r, at_ev[-2] if len(at_ev) >= 2 else prev_nt.get("rbc2")),
                             tag=f"l0a_rbcev{b}_{nt}_{p}")
                dst = a_tile[:, p * 512:(p + 1) * 512]
                ea0 = P.step("vector", (lambda rbc_sb=rbc_sb, dst=dst: lambda e:
                                        e.tensor_mul(dst[0:64, :], pb[5][0:64, 0:512], rbc_sb[0:64, :]))(),
                             deps=(s_u0, c_r), tag=f"l0a_at0{b}_{nt}_{p}")
                ea1 = P.step("vector", (lambda rbc_sb=rbc_sb, dst=dst: lambda e:
                                        e.tensor_mul(dst[64:128, :], pb[4][0:64, 0:512], rbc_sb[64:128, :]))(),
                             deps=(s_u1, c_r, ea0), tag=f"l0a_at1{b}_{nt}_{p}")
                at_ev.append(ea1)
            o_ev = []
            for o in range(NCH):
                bank = pb[o % 4]

                def mmo(e, o=o, bank=bank):
                    mmop = None
                    for c in range(NCH):
                        mmop = e.matmul(bank[:, 0:512],
                                        PB[:, 8192 + c * 1024 + o * 128: 8192 + c * 1024 + o * 128 + 128],
                                        a_tile[:, c * 512:(c + 1) * 512],
                                        start=(c == 0), stop=(c == NCH - 1))
                    return mmop
                prev_bank = o_ev[o - 4] if o >= 4 else q_ev[-1]
                s_o = P.step("tensor", mmo, deps=(wo_ld, at_ev[-1], prev_bank),
                             tag=f"l0a_o{b}_{nt}_{o}")
                dst = PD[:, o * 512:(o + 1) * 512]
                e_o = P.step("vector",
                             (lambda bank=bank, dst=dst, o=o: lambda e:
                              e.scalar_tensor_tensor(dst, bank[:, 0:512], bo[:, o:o + 1],
                                                     PC[:, o * 512:(o + 1) * 512],
                                                     ALU.add, ALU.add))(),
                             deps=(s_o, ldx, prev_nt.get("pd_out")), tag=f"l0a_oev{b}_{nt}_{o}")
                o_ev.append(e_o)
            st = P.dma("sync",
                       (lambda t0=t0: lambda e: e.dma_start(
                           out=bass.AP(x_out.tensor, t0, [[T_CORE, 128], [128 * T_CORE, NCH], [1, 512]]),
                           in_=PD[:, 0:4096].rearrange("p (c t) -> p c t", t=512)))(),
                       deps=(o_ev,), tag=f"l0a_st{b}_{nt}")
            out_final.append(st)
            prev_nt = {"x_cons": o_ev[-1], "qbank": q_ev[-4:], "qscr": q_ev[-1],
                       "den": at_ev[-1], "u": at_ev[-1], "rbc": at_ev[-1],
                       "rbc2": at_ev[-2] if len(at_ev) >= 2 else None, "pd_out": st,
                       "zrows": at_ev[-1]}
        prev_batch_done = out_final[-1]
        wb_free = prev_batch_done
    return out_final



def quant_scratch(mk):
    PD = mk.PD
    PAf = mk.PA[:, :].bitcast(F32)
    scr = {}
    names = ["vt", "y", "y2", "s1", "s2", "m", "im", "u", "ri", "rf", "d", "p"]
    for i, nm in enumerate(names):
        scr[nm] = PD[:, 1024 + i * 512: 1024 + (i + 1) * 512]
        scr[nm + "_off"] = 1024 + i * 512
    for i, nm in enumerate(["red", "sc", "lnsc", "rs"]):
        scr[nm] = PD[:, 7168 + i * 16: 7168 + (i + 1) * 16]
        scr[nm + "_off"] = 7168 + i * 16
    for i, nm in enumerate(["rs_fm", "s_fm", "cs", "bc"]):
        scr[nm] = PAf[:, 4096 + i * 512: 4096 + (i + 1) * 512]
        scr[nm + "_off"] = 4096 + i * 512
    return scr


def _quant_core(mk, tag, scr, xin, rs_fm, s_fm, out_bf, deps_y):
    P = mk.P
    c_m2 = mk.const_col(-2.0)
    c_q = mk.const_col(0.25)
    s10 = P.step("vector", (lambda: lambda e: e.tensor_mul(scr["y"], xin, rs_fm))(),
                 deps=deps_y, tag=tag + "_y")
    s11 = P.step("scalar", lambda e: e.activation(scr["y2"], scr["y"], AF.Square),
                 deps=(s10,), tag=tag + "_y2")
    s12 = P.step("gpsimd", lambda e: e.tensor_single_scalar(scr["s1"], scr["y2"], 4.0, ALU.is_ge),
                 deps=(s11,), tag=tag + "_s1")
    s13 = P.step("gpsimd", lambda e: e.tensor_single_scalar(scr["s2"], scr["y2"], 16.0, ALU.is_ge),
                 deps=(s11,), tag=tag + "_s2")
    s14 = P.step("gpsimd", lambda e: e.tensor_scalar(scr["m"], scr["s1"], -4.0, 8.0, ALU.mult, ALU.add),
                 deps=(s12,), tag=tag + "_m8")
    s15 = P.step("vector", lambda e: e.scalar_tensor_tensor(scr["m"], scr["s2"], c_m2[:, 0:1], scr["m"],
                                                            ALU.mult, ALU.add),
                 deps=(s13, s14, mk.cdep(-2.0)), tag=tag + "_m")
    s16 = P.step("gpsimd", lambda e: e.tensor_scalar(scr["im"], scr["s1"], 0.125, 0.125, ALU.mult, ALU.add),
                 deps=(s12,), tag=tag + "_im8")
    s17 = P.step("vector", lambda e: e.scalar_tensor_tensor(scr["im"], scr["s2"], c_q[:, 0:1], scr["im"],
                                                            ALU.mult, ALU.add),
                 deps=(s13, s16, mk.cdep(0.25)), tag=tag + "_im")
    s18 = P.step("gpsimd", lambda e: e.tensor_mul(scr["u"], scr["y"], scr["m"]),
                 deps=(s10, s15), tag=tag + "_u")
    ri = scr["ri"].bitcast(I32)[:, 0:512]
    s19 = P.step("vector", (lambda ri=ri: lambda e: e.tensor_single_scalar(ri, scr["u"], 0.0, ALU.add))(),
                 deps=(s18,), tag=tag + "_conv")
    s20 = P.step("vector", (lambda ri=ri: lambda e: e.tensor_scalar(scr["rf"], ri, 1.0, None, ALU.mult))(),
                 deps=(s19,), tag=tag + "_back")
    s21 = P.step("vector", lambda e: e.tensor_sub(scr["d"], scr["u"], scr["rf"]),
                 deps=(s20, s18), tag=tag + "_d")
    s22 = P.step("vector", lambda e: e.tensor_single_scalar(scr["p"], scr["d"], 0.5, ALU.is_ge),
                 deps=(s21,), tag=tag + "_p1")
    s23 = P.step("vector", lambda e: e.tensor_add(scr["rf"], scr["rf"], scr["p"]),
                 deps=(s22,), tag=tag + "_cor1")
    s24 = P.step("vector", lambda e: e.tensor_single_scalar(scr["p"], scr["d"], -0.5, ALU.is_le),
                 deps=(s23, s21), tag=tag + "_p2")
    s25 = P.step("vector", lambda e: e.tensor_sub(scr["rf"], scr["rf"], scr["p"]),
                 deps=(s24,), tag=tag + "_cor2")
    s26 = P.step("gpsimd", (lambda s_fm=s_fm: lambda e: e.tensor_mul(scr["cs"], scr["im"], s_fm))(),
                 deps=(s17, deps_y), tag=tag + "_cs")
    s27 = P.step("vector", (lambda out_bf=out_bf: lambda e: e.tensor_mul(out_bf, scr["rf"], scr["cs"]))(),
                 deps=(s25, s26), tag=tag + "_xd")
    return s27


def _scales_steps(mk, tag, scr, red_dep):
    P = mk.P
    s3 = P.step("vector", lambda e: e.tensor_scalar(scr["sc"], scr["red"], 1.0 / 7.5, 1e-30,
                                                    ALU.mult, ALU.max),
                deps=(red_dep,), tag=tag + "_sc")
    s4 = P.step("scalar", lambda e: e.activation(scr["lnsc"], scr["sc"], AF.Ln, scale=1.0),
                deps=(s3,), tag=tag + "_ln")
    s5 = P.step("scalar", lambda e: e.activation(scr["rs"], scr["lnsc"], AF.Exp, scale=-1.0),
                deps=(s4,), tag=tag + "_exp")
    return s3, s5


def _bcast32(mk, tag, scr, small_off, dst, deps):
    P = mk.P
    srcap = bass.AP(mk.PD, small_off, [[8192, 128], [1, 16], [0, 32]])
    return P.step("vector", (lambda dst=dst, srcap=srcap: lambda e: e.tensor_copy(
        dst.rearrange("p (b k) -> p b k", k=32), srcap))(), deps=deps, tag=tag)


def emit_quant_chain(mk, tag, xn32, scr, out_bf, deps):
    P = mk.P
    s1 = P.step("vector", (lambda xn32=xn32: lambda e: e.transpose(scr["vt"], xn32))(),
                deps=deps, tag=tag + "_vt")
    s2 = P.step("vector", lambda e: e.tensor_reduce(scr["red"], scr["vt"].rearrange("p (b k) -> p b k", k=32),
                                                    AX.X, ALU.max, apply_absolute_value=True),
                deps=(s1,), tag=tag + "_red")
    s3, s5 = _scales_steps(mk, tag, scr, s2)
    b1 = _bcast32(mk, tag + "_bc1", scr, scr["rs_off"], scr["bc"], (s5,))
    t1 = P.step("vector", lambda e: e.transpose(scr["rs_fm"], scr["bc"]), deps=(b1,), tag=tag + "_vt2")
    b2 = _bcast32(mk, tag + "_bc2", scr, scr["sc_off"], scr["bc"], (t1, s3))
    t2 = P.step("vector", lambda e: e.transpose(scr["s_fm"], scr["bc"]), deps=(b2,), tag=tag + "_vt3")
    return _quant_core(mk, tag, scr, xn32, scr["rs_fm"], scr["s_fm"], out_bf,
                       deps_y=(t1, t2, deps))


def emit_quant_free(mk, tag, xin, scr, out_bf, deps):
    P = mk.P
    s2_ = P.step("vector", (lambda xin=xin: lambda e: e.tensor_reduce(
        scr["red"], xin.rearrange("p (b k) -> p b k", k=32),
        AX.X, ALU.max, apply_absolute_value=True))(),
        deps=deps, tag=tag + "_red")
    s3, s5 = _scales_steps(mk, tag, scr, s2_)
    b1 = _bcast32(mk, tag + "_bc1", scr, scr["rs_off"], scr["rs_fm"], (s5,))
    b2 = _bcast32(mk, tag + "_bc2", scr, scr["sc_off"], scr["s_fm"], (s3,))
    return _quant_core(mk, tag, scr, xin, scr["rs_fm"], scr["s_fm"], out_bf,
                       deps_y=(b1, b2, deps))


def emit_ln_quant(mk, x_in, deps=()):
    P = mk.P
    pb = mk.pb
    r_m = mk.ROWS[0:1, 0:512]
    r_t = mk.ROWS[0:1, 512:1024]
    r_r = mk.ROWS[0:1, 1024:1536]
    r_mr = mk.ROWS[0:1, 1536:2048]
    epscol = mk.const_col(LN_EPS)
    scr = quant_scratch(mk)
    done = []
    prev_x = {0: [deps, mk.init_deps], 1: [deps, mk.init_deps]}
    prev_pa = mk.init_deps
    prev_rows = None
    prev_b = {4: None, 5: None, 6: None, 7: None}
    prev_q = None
    for tt in range(8):
        xt = mk.PC[:, (tt % 2) * 4096:(tt % 2) * 4096 + 4096]
        ld = P.dma("sync",
                   (lambda xt=xt, tt=tt: lambda e: e.dma_start(
                       out=xt.rearrange("p (c t) -> p c t", t=512),
                       in_=bass.AP(x_in.tensor, tt * 512, [[T_CORE, 128], [128 * T_CORE, NCH], [1, 512]])))(),
                   deps=(prev_x[tt % 2],), tag=f"lq_ld{tt}")
        xbf = mk.PA[:, 0:4096]
        x2b = mk.PA[:, 4096:8192]
        c1 = P.step("vector", (lambda xt=xt, xbf=xbf: lambda e: e.tensor_copy(xbf, xt))(),
                    deps=(ld, prev_pa), tag=f"lq_cast{tt}")
        c2 = P.step("scalar", (lambda xbf=xbf, x2b=x2b: lambda e: e.activation(x2b, xbf, AF.Square))(),
                    deps=(c1,), tag=f"lq_sq{tt}")

        def mk_stats(src, bank):
            def f(e):
                mm = None
                for c in range(NCH):
                    mm = e.matmul(bank[0:1, 0:512], mk.ONECB[:, 0:1],
                                  src[:, c * 512:(c + 1) * 512],
                                  start=(c == 0), stop=(c == NCH - 1))
                return mm
            return f
        m1 = P.step("tensor", mk_stats(xbf, pb[6]), deps=(c1, prev_b[6]), tag=f"lq_s1_{tt}")
        m2 = P.step("tensor", mk_stats(x2b, pb[7]), deps=(c2, prev_b[7]), tag=f"lq_s2_{tt}")
        q1 = P.step("vector", lambda e: e.tensor_single_scalar(r_m, pb[6][0:1, 0:512], 1.0 / D, ALU.mult),
                    deps=(m1, prev_rows), tag=f"lq_m{tt}")
        q2 = P.step("vector", lambda e: e.tensor_single_scalar(r_t, pb[7][0:1, 0:512], 1.0 / D, ALU.mult),
                    deps=(m2, prev_rows), tag=f"lq_e2{tt}")
        q3a = P.step("vector", lambda e: e.tensor_mul(r_r, r_m, r_m),
                     deps=(q1, prev_rows), tag=f"lq_msq{tt}")
        q3 = P.step("vector", lambda e: e.tensor_sub(r_t, r_t, r_r),
                    deps=(q2, q3a), tag=f"lq_var{tt}")
        q5 = P.step("scalar", lambda e: e.activation(r_t, r_t, AF.Ln, bias=epscol[0:1, 0:1], scale=1.0),
                    deps=(q3, mk.cdep(LN_EPS)), tag=f"lq_ln{tt}")
        q6 = P.step("scalar", lambda e: e.activation(r_r, r_t, AF.Exp, scale=-0.5),
                    deps=(q5,), tag=f"lq_rstd{tt}")
        q7 = P.step("vector", lambda e: e.tensor_mul(r_mr, r_m, r_r), deps=(q6, q1), tag=f"lq_mr{tt}")
        b1 = P.step("tensor", lambda e: e.matmul(pb[4][:, 0:512], mk.ONEF[0:1, 0:128], r_r,
                                                 start=True, stop=True),
                    deps=(q6, prev_b[4]), tag=f"lq_rb{tt}")
        b2 = P.step("tensor", lambda e: e.matmul(pb[5][:, 0:512], mk.ONEF[0:1, 0:128], r_mr,
                                                 start=True, stop=True),
                    deps=(q7, prev_b[5]), tag=f"lq_mrb{tt}")
        prev_rows = [b1, b2]
        xd_steps = []
        x_cons = []
        for c in range(NCH):
            xc = xt[:, c * 512:(c + 1) * 512]
            t1 = mk.PD[:, 0:512]
            xn32 = mk.PD[:, 512:1024]
            u1 = P.step("vector",
                        (lambda xc=xc, t1=t1: lambda e: e.tensor_mul(t1, xc, pb[4][:, 0:512]))(),
                        deps=(ld, b1, prev_q), tag=f"lq_t1_{tt}_{c}")
            u2 = P.step("vector",
                        (lambda t1=t1, xn32=xn32: lambda e: e.tensor_sub(xn32, t1, pb[5][:, 0:512]))(),
                        deps=(u1, b2), tag=f"lq_xn_{tt}_{c}")
            xn_sl = mk.XN[:, c * T_CORE + tt * 512: c * T_CORE + tt * 512 + 512]
            qd = emit_quant_chain(mk, f"lq_{tt}_{c}", xn32, scr, xn_sl, deps=(u2,))
            prev_q = qd
            xd_steps.append(qd)
            x_cons.append(u1)
        prev_x[tt % 2] = x_cons
        prev_pa = [m1, m2]
        prev_b[6] = q1
        prev_b[7] = q2
        prev_b[4] = xd_steps[-1]
        prev_b[5] = xd_steps[-1]
        done.append(xd_steps)
    return done


def emit_l0_ffn(mk, x_in, x_out, prev=(), first_call=True):
    P = mk.P
    pb = mk.pb
    XN, PA, PB, PC, PD = mk.XN, mk.PA, mk.PB, mk.PC, mk.PD
    bias = mk.col("l0f_bias")
    wdT = PB[:, 0:8192]
    wdT_hbm = mk.dram.get("wdT_bf")
    if wdT_hbm is None:
        wdT_hbm = mk.dram_tmp("wdT_bf", [128, 8192], BF16)

    if first_call:
        PBf = PB[:, :].bitcast(F32)
        wsrc = mk.dram["l0f_w"]
        wq_scr = quant_scratch(mk)
        wtile = PBf[:, 5120:6144]
        wd_bf = PB[:, 12288:13312]
        pbB = pb[3][:, 0:64].bitcast(BF16)
        wd_done = []
        prev_c = prev
        for r in range(NCH):
            ldw = P.dma("sync", (lambda r=r: lambda e: e.dma_start(
                out=wtile, in_=wsrc[r * 128:(r + 1) * 128, :]))(),
                deps=(prev_c, mk.init_deps), tag=f"l0f_wld{r}")
            halves = []
            for hh in range(2):
                xin = wtile[:, hh * 512:(hh + 1) * 512]
                outb = wd_bf[:, hh * 512:(hh + 1) * 512]
                qs = emit_quant_free(mk, f"l0f_w{r}_{hh}", xin, wq_scr, outb,
                                     deps=(ldw, halves[-1] if halves else prev_c))
                halves.append(qs)
            tr_steps = []
            for c in range(NCH):
                def mtr(e, r=r, c=c):
                    return e.transpose(pbB[:, 0:128], wd_bf[:, c * 128:(c + 1) * 128], mk.IDB[:])
                t1 = P.step("tensor", mtr, deps=(halves, tr_steps[-1] if tr_steps else None,
                                                 mk.init_deps),
                            tag=f"l0f_tr{r}_{c}")
                def etr(e, r=r, c=c):
                    return e.tensor_copy(wdT[:, c * 1024 + r * 128: c * 1024 + r * 128 + 128],
                                         pbB[:, 0:128])
                t2 = P.step("vector", etr, deps=(t1,), tag=f"l0f_trev{r}_{c}")
                tr_steps.append(t2)
            wd_done.extend(tr_steps)
            prev_c = tr_steps[-1]
        wst = P.dma("sync", lambda e: e.dma_start(out=wdT_hbm, in_=wdT), deps=(wd_done,),
                    tag="l0f_wst")
        wd_ready = wd_done + [wst]
    else:
        wld2 = P.dma("sync", lambda e: e.dma_start(out=wdT, in_=wdT_hbm), deps=(prev,),
                     tag="l0f_wld2")
        wd_ready = [wld2]

    ln_done = emit_ln_quant(mk, x_in, deps=(prev, wd_ready))
    lnbar = ln_done[-1]

    out_final = []
    prev_nt = {}
    for tt in range(8):
        t0 = tt * 512
        ldx = P.dma("sync",
                    (lambda t0=t0: lambda e: e.dma_start(
                        out=PC[:, 0:4096].rearrange("p (c t) -> p c t", t=512),
                        in_=bass.AP(x_in.tensor, t0, [[T_CORE, 128], [128 * T_CORE, NCH], [1, 512]])))(),
                    deps=(prev_nt.get("x_cons"), lnbar), tag=f"l0f_ldx{tt}")
        o_ev = []
        for o in range(NCH):
            bank = pb[o % 4]

            def mmo(e, o=o, t0=t0, bank=bank):
                mmop = None
                for c in range(NCH):
                    mmop = e.matmul(bank[:, 0:512],
                                    wdT[:, c * 1024 + o * 128: c * 1024 + o * 128 + 128],
                                    XN[:, c * T_CORE + t0: c * T_CORE + t0 + 512],
                                    start=(c == 0), stop=(c == NCH - 1))
                return mmop
            prev_bank = o_ev[o - 4] if o >= 4 else prev_nt.get("banks")
            s_o = P.step("tensor", mmo, deps=(wd_ready, ln_done[tt], prev_bank),
                         tag=f"l0f_mm{tt}_{o}")
            dst = PD[:, o * 512:(o + 1) * 512]
            e_o = P.step("vector",
                         (lambda bank=bank, dst=dst, o=o: lambda e:
                          e.scalar_tensor_tensor(dst, bank[:, 0:512], bias[:, o:o + 1],
                                                 PC[:, o * 512:(o + 1) * 512],
                                                 ALU.add, ALU.add))(),
                         deps=(s_o, ldx, prev_nt.get("pd_out"), lnbar), tag=f"l0f_oev{tt}_{o}")
            o_ev.append(e_o)
        st = P.dma("sync",
                   (lambda t0=t0: lambda e: e.dma_start(
                       out=bass.AP(x_out.tensor, t0, [[T_CORE, 128], [128 * T_CORE, NCH], [1, 512]]),
                       in_=PD[:, 0:4096].rearrange("p (c t) -> p c t", t=512)))(),
                   deps=(o_ev,), tag=f"l0f_st{tt}")
        out_final.append(st)
        prev_nt = {"x_cons": o_ev[-1], "banks": o_ev[-4:], "pd_out": st}
    return out_final



def emit_l0_attn_b(mk, x_in, x_out, prev=(), branches=(0, 1)):
    P = mk.P
    pb = mk.pb
    XN, PA, PB, PC, PD = mk.XN, mk.PA, mk.PB, mk.PC, mk.PD
    XNs = 8 * T_CORE
    ln_done = emit_ln(mk, x_in, deps=prev)
    lnbar = ln_done[-1]
    bqk = mk.col("l0b_bqk")
    bsum3 = mk.col("l0b_bsum3")
    third = mk.const_col(1.0 / 3.0)
    vrow_src = mk.dram["r_l0b_bv"]
    out_final = []
    prev_half = {}
    for b in range(B_PER_CORE):
        for hf in range(2):
            H0 = b * T_SEQ + hf * 1024
            ldx = P.dma("sync",
                        (lambda H0=H0: lambda e: e.dma_start(
                            out=PC[:, :].rearrange("p (c t) -> p c t", t=1024),
                            in_=bass.AP(x_in.tensor, H0, [[T_CORE, 128], [128 * T_CORE, NCH], [1, 1024]])))(),
                        deps=(prev_half.get("x_cons"), lnbar), tag=f"l0b_ldx{b}{hf}")
            st = None
            for br in branches:
                if br == 0:
                    tsel = [(H0, 1), (H0 + 512, 1)]
                else:
                    tsel = [(H0, 2), (H0 + 1, 2)]
                last_kind = prev_half.get("wcons") if br == 0 else p1_done
                proj_ev = {}
                for wi, kind in enumerate(("q", "k", "v")):
                    wld = mk.load_wT("l0b_wqkvT", PB[:, 8192:16384], 1024,
                                     deps=(last_kind, lnbar), col0=wi * 1024, rowstride=3072,
                                     base=br * D * 3072, tag=f"_{b}{hf}{br}{kind}")
                    if kind == "v":
                        ldr = P.dma("sync", (lambda br=br: lambda e: e.dma_start(
                            out=PD[0:1, 5120:6144], in_=vrow_src[br:br + 1, :]))(),
                            deps=(prev_half.get("rowb"), proj_ev.get("k")), tag=f"l0b_vr{b}{hf}{br}")
                        rcast = P.step("vector", lambda e: e.tensor_copy(mk.ROWB[0:1, 0:1024],
                                                                        PD[0:1, 5120:6144]),
                                       deps=(ldr, prev_half.get("rowb")), tag=f"l0b_vrc{b}{hf}{br}")
                    ev = []
                    if kind in ("q", "k"):
                        dstbuf = PA
                        dst0 = 0 if kind == "q" else 8192
                        bcol = bqk[:, br * 16 + (0 if kind == "q" else 8):
                                   br * 16 + (0 if kind == "q" else 8) + 8]
                        for o in range(NCH):
                            for s2 in range(2):
                                off, stride = tsel[s2]
                                bank = pb[(o * 2 + s2) % 4]

                                def mmp(e, o=o, off=off, stride=stride, bank=bank):
                                    mmop = None
                                    for c in range(NCH):
                                        rhs = bass.AP(XN, c * T_CORE + off, [[XNs, 128], [stride, 512]])
                                        mmop = e.matmul(bank[:, 0:512],
                                                        PB[:, 8192 + c * 1024 + o * 128:
                                                           8192 + c * 1024 + o * 128 + 128],
                                                        rhs, start=(c == 0), stop=(c == NCH - 1))
                                    return mmop
                                pv = ev[-4] if len(ev) >= 4 else last_kind
                                s_mm = P.step("tensor", mmp, deps=(wld, lnbar, pv),
                                              tag=f"l0b_{kind}{b}{hf}{br}_{o}_{s2}")
                                dst = dstbuf[:, dst0 + o * 1024 + s2 * 512: dst0 + o * 1024 + s2 * 512 + 512]
                                if kind == "q":
                                    e1 = P.step("vector",
                                                (lambda bank=bank, dst=dst, bcol=bcol, o=o: lambda e:
                                                 e.tensor_scalar(dst, bank[:, 0:512], bcol[:, o:o + 1],
                                                                 0.125, ALU.add, ALU.mult))(),
                                                deps=(s_mm,), tag=f"l0b_{kind}ev{b}{hf}{br}_{o}_{s2}")
                                else:
                                    e1 = P.step("vector",
                                                (lambda bank=bank, dst=dst, bcol=bcol, o=o: lambda e:
                                                 e.tensor_scalar(dst, bank[:, 0:512], bcol[:, o:o + 1], None,
                                                                 ALU.add))(),
                                                deps=(s_mm,), tag=f"l0b_{kind}ev{b}{hf}{br}_{o}_{s2}")
                                ev.append(e1)
                    else:
                        for j in range(8):
                            for h2 in range(2):
                                s2, jj = divmod(j, 4)
                                off, stride = tsel[s2]
                                bank = pb[(j * 2 + h2) % 4]

                                def mmv(e, off=off, stride=stride, jj=jj, h2=h2, bank=bank):
                                    for c in range(NCH):
                                        lhs = bass.AP(XN, c * T_CORE + off + stride * jj * 128,
                                                      [[XNs, 128], [stride, 128]])
                                        e.matmul(bank[:, 0:512], lhs,
                                                 PB[:, 8192 + c * 1024 + h2 * 512:
                                                    8192 + c * 1024 + h2 * 512 + 512],
                                                 start=(c == 0), stop=False)
                                    return e.matmul(bank[:, 0:512], mk.ONEB[0:1, 0:128],
                                                    mk.ROWB[0:1, h2 * 512: h2 * 512 + 512],
                                                    start=False, stop=True)
                                pv = ev[-4] if len(ev) >= 4 else last_kind
                                s_mm = P.step("tensor", mmv, deps=(wld, rcast, lnbar, pv),
                                              tag=f"l0b_v{b}{hf}{br}_{j}_{h2}")
                                dst = PB[:, j * 1024 + h2 * 512: j * 1024 + h2 * 512 + 512]
                                e1 = P.step("vector", (lambda bank=bank, dst=dst: lambda e:
                                                       e.tensor_copy(dst, bank[:, 0:512]))(),
                                            deps=(s_mm,), tag=f"l0b_vev{b}{hf}{br}_{j}_{h2}")
                                ev.append(e1)
                    proj_ev[kind] = ev
                    last_kind = ev[-1]
                q_done = proj_ev["q"][-1]
                k_done = proj_ev["k"][-1]
                v_done = proj_ev["v"][-1]
                E = PD[:, 0:1024].bitcast(BF16)
                mx = PD[:, 1024:1028]
                negB = PD[:, 1032:1033]
                rbc_sb = [PD[:, 1544:2056], PD[:, 2056:2568]]
                r_neg = PD[0:1, 4096:4100]
                r_nb = PD[0:1, 4104:4105]
                r_s = [mk.ROWS[0:1, 0:512], mk.ROWS[0:1, 512:1024]]
                at_done = []
                prev_p2 = {"E": v_done, "srow": None}
                for s2 in range(2):
                    for p in range(8):
                        pair_rows = []
                        for hh in range(2):
                            h = 2 * p + hh
                            hb = (h % 2) * 64
                            hc = h // 2
                            def mmS(e, hb=hb, hc=hc, s2=s2):
                                last = None
                                for i in range(4):
                                    last = e.matmul(pb[i][0:128, 0:512],
                                                    PA[hb:hb + 64, 8192 + hc * 1024 + s2 * 512 + i * 128:
                                                       8192 + hc * 1024 + s2 * 512 + i * 128 + 128],
                                                    PA[hb:hb + 64, hc * 1024 + s2 * 512:
                                                       hc * 1024 + s2 * 512 + 512],
                                                    start=True, stop=True)
                                return last
                            s_S = P.step("tensor", mmS, deps=(q_done, k_done, prev_p2["E"]),
                                         tag=f"l0b_S{b}{hf}{br}_{s2}_{h}")
                            mxs = []
                            for i in range(4):
                                mxs.append(P.step("vector",
                                                  (lambda i=i: lambda e: e.reduce_max(
                                                      mx[:, i:i + 1], pb[i][:, 0:512], AX.X))(),
                                                  deps=(s_S, prev_p2.get("mx")),
                                                  tag=f"l0b_mx{b}{hf}{br}_{s2}_{h}_{i}"))
                            g1 = P.step("gpsimd", lambda e: e.reduce_max(r_neg, mx, AX.C),
                                        deps=(mxs, prev_p2.get("rneg")), tag=f"l0b_gr{b}{hf}{br}_{s2}_{h}")
                            g2 = P.step("vector", lambda e: e.reduce_max(r_nb, r_neg, AX.X, negate=True),
                                        deps=(g1, prev_p2.get("nbread")), tag=f"l0b_nb{b}{hf}{br}_{s2}_{h}")
                            g3 = P.step("tensor", lambda e: e.matmul(pb[6][:, 8:9], mk.ONEF[0:1, 0:128],
                                                                     r_nb, start=True, stop=True),
                                        deps=(g2, prev_p2.get("nbmm")), tag=f"l0b_nbb{b}{hf}{br}_{s2}_{h}")
                            g4 = P.step("vector", lambda e: e.tensor_copy(negB, pb[6][:, 8:9]),
                                        deps=(g3, prev_p2.get("negB")), tag=f"l0b_nbc{b}{hf}{br}_{s2}_{h}")
                            exps = []
                            for i in range(4):
                                exps.append(P.step("scalar",
                                                   (lambda i=i: lambda e: e.activation(
                                                       E[:, i * 512:(i + 1) * 512], pb[i][:, 0:512],
                                                       AF.Exp, bias=negB[:, 0:1], scale=1.0))(),
                                                   deps=(g4, s_S, prev_p2.get("Econs"),),
                                                   tag=f"l0b_E{b}{hf}{br}_{s2}_{h}_{i}"))
                            def mms(e):
                                last = None
                                for i in range(4):
                                    last = e.matmul(pb[6][0:1, 0:512], mk.ONECB[:, 0:1],
                                                    E[:, i * 512:(i + 1) * 512],
                                                    start=(i == 0), stop=(i == 3))
                                return last
                            s_sum = P.step("tensor", mms, deps=(exps, prev_p2.get("srow_mm")),
                                           tag=f"l0b_sum{b}{hf}{br}_{s2}_{h}")
                            rr = r_s[hh]
                            z1 = P.step("vector", (lambda rr=rr: lambda e: e.tensor_copy(rr, pb[6][0:1, 0:512]))(),
                                        deps=(s_sum, prev_p2.get("rrow")), tag=f"l0b_sc{b}{hf}{br}_{s2}_{h}")
                            z2 = P.step("scalar", (lambda rr=rr: lambda e: e.activation(rr, rr, AF.Ln, scale=1.0))(),
                                        deps=(z1,), tag=f"l0b_zl{b}{hf}{br}_{s2}_{h}")
                            z3 = P.step("scalar", (lambda rr=rr: lambda e: e.activation(rr, rr, AF.Exp, scale=-1.0))(),
                                        deps=(z2,), tag=f"l0b_ze{b}{hf}{br}_{s2}_{h}")
                            def mmu(e, hb=hb, hc=hc, s2=s2, hh=hh):
                                bank = pb[5] if hh == 0 else pb[4]
                                last = None
                                for i in range(4):
                                    last = e.matmul(bank[0:64, 0:512],
                                                    PB[:, (s2 * 4 + i) * 1024 + hc * 128 + hb:
                                                       (s2 * 4 + i) * 1024 + hc * 128 + hb + 64],
                                                    E[:, i * 512:(i + 1) * 512],
                                                    start=(i == 0), stop=(i == 3))
                                return last
                            s_u = P.step("tensor", mmu, deps=(exps, v_done, prev_p2.get("u" + str(hh))),
                                         tag=f"l0b_U{b}{hf}{br}_{s2}_{h}")
                            pair_rows.append((z3, s_u))
                            prev_p2["E"] = s_sum
                            prev_p2["Econs"] = [s_sum, s_u]
                            prev_p2["mx"] = g1
                            prev_p2["rneg"] = g2
                            prev_p2["nbmm"] = z1
                            prev_p2["nbread"] = g3
                            prev_p2["negB"] = exps
                            prev_p2["srow_mm"] = z1
                            prev_p2["rrow"] = None
                        def mmr(e):
                            e.matmul(pb[7][:, 0:512], mk.EH[0:1, 0:128], r_s[0],
                                     start=True, stop=False)
                            return e.matmul(pb[7][:, 0:512], mk.EH[0:1, 128:256], r_s[1],
                                            start=False, stop=True)
                        s_r = P.step("tensor", mmr, deps=(pair_rows[0][0], pair_rows[1][0],
                                                          prev_p2.get("rbcmm")),
                                     tag=f"l0b_rbc{b}{hf}{br}_{s2}_{p}")
                        c_r = P.step("vector", (lambda p=p: lambda e: e.tensor_copy(
                            rbc_sb[p % 2], pb[7][:, 0:512]))(),
                            deps=(s_r, prev_p2.get("rbccons" + str(p % 2))),
                            tag=f"l0b_rbcev{b}{hf}{br}_{s2}_{p}")
                        prev_p2["rbcmm"] = c_r
                        prev_p2["rrow"] = s_r
                        dsts = PB[:, 8192 + s2 * 4096 + p * 512: 8192 + s2 * 4096 + p * 512 + 512]
                        ea0 = P.step("vector", (lambda dsts=dsts, p=p: lambda e: e.tensor_mul(
                            dsts[0:64, :], pb[5][0:64, 0:512], rbc_sb[p % 2][0:64, :]))(),
                            deps=(pair_rows[0][1], c_r), tag=f"l0b_a0{b}{hf}{br}_{s2}_{p}")
                        ea1 = P.step("vector", (lambda dsts=dsts, p=p: lambda e: e.tensor_mul(
                            dsts[64:128, :], pb[4][0:64, 0:512], rbc_sb[p % 2][64:128, :]))(),
                            deps=(pair_rows[1][1], c_r, ea0), tag=f"l0b_a1{b}{hf}{br}_{s2}_{p}")
                        prev_p2["u0"] = ea0
                        prev_p2["u1"] = ea1
                        prev_p2["rbccons" + str(p % 2)] = [ea0, ea1]
                        at_done.append(ea1)
                wldo = mk.load_wT("l0b_woutT", PB[:, 0:8192], 1024,
                                  deps=(at_done, prev_p2.get("u0"), prev_p2.get("u1")),
                                  base=br * D * D, tag=f"_{b}{hf}{br}o")
                o_evs = []
                for o in range(NCH):
                    for s2 in range(2):
                        bank = pb[(o * 2 + s2) % 4]

                        def mmo(e, o=o, s2=s2, bank=bank):
                            mmop = None
                            for c in range(NCH):
                                mmop = e.matmul(bank[:, 0:512],
                                                PB[:, c * 1024 + o * 128: c * 1024 + o * 128 + 128],
                                                PB[:, 8192 + s2 * 4096 + c * 512:
                                                   8192 + s2 * 4096 + c * 512 + 512],
                                                start=(c == 0), stop=(c == NCH - 1))
                            return mmop
                        pv = o_evs[-4] if len(o_evs) >= 4 else at_done[-1]
                        s_o = P.step("tensor", mmo, deps=(wldo, at_done[-1], pv),
                                     tag=f"l0b_o{b}{hf}{br}_{o}_{s2}")
                        if br == 0:
                            pcsl = PC[:, o * 1024 + s2 * 512: o * 1024 + s2 * 512 + 512]
                        else:
                            pcsl = bass.AP(PC, o * 1024 + s2, [[8192, 128], [2, 512]])
                        e_o = P.step("vector",
                                     (lambda bank=bank, pcsl=pcsl, o=o: lambda e:
                                      e.scalar_tensor_tensor(pcsl, bank[:, 0:512], third[:, 0:1],
                                                             pcsl, ALU.mult, ALU.add))(),
                                     deps=(s_o, ldx, prev_half.get("pc_cons"), mk.cdep(1.0 / 3.0)),
                                     tag=f"l0b_ov{b}{hf}{br}_{o}_{s2}")
                        if br == 0:
                            e_b = P.step("vector",
                                         (lambda pcsl=pcsl, o=o: lambda e:
                                          e.tensor_scalar(pcsl, pcsl, bsum3[:, o:o + 1], None, ALU.add))(),
                                         deps=(e_o,), tag=f"l0b_ob{b}{hf}{br}_{o}_{s2}")
                            o_evs.append(e_b)
                        else:
                            o_evs.append(e_o)
                p1_done = o_evs[-1]
                if br == branches[-1]:
                    st = P.dma("sync",
                               (lambda H0=H0: lambda e: e.dma_start(
                                   out=bass.AP(x_out.tensor, H0, [[T_CORE, 128], [128 * T_CORE, NCH], [1, 1024]]),
                                   in_=PC[:, :].rearrange("p (c t) -> p c t", t=1024)))(),
                               deps=(o_evs,), tag=f"l0b_st{b}{hf}")
                    out_final.append(st)
            prev_half = {"x_cons": st, "wcons": o_evs[-1], "banks": o_evs[-4:],
                         "rowb": p1_done, "pc_cons": st}
    return out_final



def emit_l1_attn_b(mk, x_in, x_out, prev=()):
    P = mk.P
    pb = mk.pb
    XN, PA, PB, PC, PD = mk.XN, mk.PA, mk.PB, mk.PC, mk.PD
    XNs = 8 * T_CORE
    ln_done = emit_ln(mk, x_in, deps=prev)
    lnbar = ln_done[-1]
    SCALE = 1.0 / np.sqrt(128.0)
    out_final = []

    MB = PD[:, 4096:5120].bitcast(BF16)
    mio = PD[:, 3072:3584].bitcast(I32)
    mask_steps = []
    for mi in range(4):
        g1 = P.step("gpsimd", (lambda mi=mi: lambda e: e.iota(mio[:, 0:512], [[1, 512]],
                                                              base=-mi * 128,
                                                              channel_multiplier=-1))(),
                    deps=(prev, mask_steps), tag=f"l1b_mio{mi}")
        g2 = P.step("vector", (lambda mi=mi: lambda e: e.tensor_single_scalar(
            MB[:, mi * 512:(mi + 1) * 512], mio[:, 0:512], 0, ALU.is_ge))(),
            deps=(g1,), tag=f"l1b_msk{mi}")
        mask_steps.append(g2)

    prev_b = {}
    for b in range(B_PER_CORE):
        tb = b * T_SEQ
        wqld = mk.load_wT("l1b_wq", PB[:, 8192:16384], 1024,
                          deps=(prev_b.get("E_cons"), lnbar, prev, mask_steps),
                          tag=f"_{b}")
        q_ev = []
        for h in range(8):
            for jt in range(4):
                bank = pb[(h * 4 + jt) % 4]

                def mmq(e, h=h, jt=jt, bank=bank, tb=tb):
                    mmop = None
                    for c in range(NCH):
                        mmop = e.matmul(bank[:, 0:512],
                                        PB[:, 8192 + c * 1024 + h * 128: 8192 + c * 1024 + h * 128 + 128],
                                        XN[:, c * T_CORE + tb + jt * 512: c * T_CORE + tb + jt * 512 + 512],
                                        start=(c == 0), stop=(c == NCH - 1))
                    return mmop
                pv = q_ev[-4] if len(q_ev) >= 4 else prev_b.get("banks")
                s_mm = P.step("tensor", mmq, deps=(wqld, ln_done[(tb + jt * 512) // 512], pv),
                              tag=f"l1b_q{b}_{h}_{jt}")
                dst = PA[:, h * 2048 + jt * 512: h * 2048 + jt * 512 + 512]
                e1 = P.step("vector", (lambda bank=bank, dst=dst: lambda e:
                                       e.tensor_scalar(dst, bank[:, 0:512], SCALE, None, ALU.mult))(),
                            deps=(s_mm,), tag=f"l1b_qe{b}_{h}_{jt}")
                q_ev.append(e1)
        klat_g = [PD[:, 5120:6144].bitcast(BF16), PD[:, 6144:7168].bitcast(BF16)]
        vlat_g = [PD[:, 2048:3072].bitcast(BF16), PD[:, 3072:4096].bitcast(BF16)]
        wkd = mk.load_wT("l1b_wk_down", PB[:, 8192:8704], 64, deps=(q_ev,), tag=f"_{b}")
        wvd = mk.load_wT("l1b_wv_down", PB[:, 8704:9216], 64, deps=(q_ev,), tag=f"_{b}")
        lat_ev = []
        for (wofs, dstl, wld) in ((8192, klat_g, wkd), (8704, vlat_g, wvd)):
            for g in range(2):
                for jt in range(4):
                    bank = pb[4 + (len(lat_ev) % 2)]

                    def mml(e, wofs=wofs, g=g, jt=jt, bank=bank, tb=tb):
                        mmop = None
                        for c in range(NCH):
                            mmop = e.matmul(bank[0:32, 0:512],
                                            PB[:, wofs + c * 64 + g * 32: wofs + c * 64 + g * 32 + 32],
                                            XN[:, c * T_CORE + tb + jt * 512: c * T_CORE + tb + jt * 512 + 512],
                                            start=(c == 0), stop=(c == NCH - 1))
                        return mmop
                    pv = lat_ev[-2] if len(lat_ev) >= 2 else prev_b.get("latb")
                    s_mm = P.step("tensor", mml, deps=(wld, ln_done[(tb + jt * 512) // 512], pv),
                                  tag=f"l1b_lat{b}_{wofs}_{g}_{jt}")
                    dstb = dstl[g]
                    e1 = P.step("vector", (lambda dstb=dstb, jt=jt, bank=bank: lambda e:
                                           e.tensor_copy(dstb[0:32, jt * 512:(jt + 1) * 512],
                                                         bank[0:32, 0:512]))(),
                                deps=(s_mm,), tag=f"l1b_late{b}_{wofs}_{g}_{jt}")
                    lat_ev.append(e1)
        wkup = PD[:, 7168:7232].bitcast(BF16)
        wvup = PD[:, 7232:7296].bitcast(BF16)
        lku = P.dma("gpsimd", lambda e: e.dma_start(out=wkup[0:32, :], in_=mk.dram["l1b_wk_up"]),
                    deps=(prev_b.get("upcons"), prev), tag=f"l1b_wku{b}")
        lvu = P.dma("gpsimd", lambda e: e.dma_start(out=wvup[0:32, :], in_=mk.dram["l1b_wv_up"]),
                    deps=(prev_b.get("upcons"), prev), tag=f"l1b_wvu{b}")
        up_ev = []
        for g in range(2):
            for jt in range(4):
                bank = pb[(len(up_ev)) % 4]

                def mmk(e, g=g, jt=jt, bank=bank):
                    return e.matmul(bank[:, 0:512], wkup[0:32, :],
                                    klat_g[g][0:32, jt * 512:(jt + 1) * 512],
                                    start=True, stop=True)
                pv = up_ev[-4] if len(up_ev) >= 4 else lat_ev[-1]
                s_mm = P.step("tensor", mmk, deps=(lku, lat_ev, pv), tag=f"l1b_kup{b}_{g}_{jt}")
                e1 = P.step("vector", (lambda g=g, jt=jt, bank=bank: lambda e:
                                       e.tensor_copy(PB[:, g * 2048 + jt * 512: g * 2048 + jt * 512 + 512],
                                                     bank[:, 0:512]))(),
                            deps=(s_mm, prev_b.get("kv_cons")), tag=f"l1b_kupe{b}_{g}_{jt}")
                up_ev.append(e1)
        for g in range(2):
            for j in range(16):
                bank = pb[(len(up_ev)) % 4]

                def mmv(e, g=g, j=j, bank=bank):
                    return e.matmul(bank[:, 0:128],
                                    vlat_g[g][0:32, j * 128:(j + 1) * 128],
                                    wvup[0:32, :], start=True, stop=True)
                pv = up_ev[-4] if len(up_ev) >= 4 else lat_ev[-1]
                s_mm = P.step("tensor", mmv, deps=(lvu, lat_ev, pv), tag=f"l1b_vup{b}_{g}_{j}")
                e1 = P.step("vector", (lambda g=g, j=j, bank=bank: lambda e:
                                       e.tensor_copy(PB[:, 4096 + j * 256 + g * 128:
                                                        4096 + j * 256 + g * 128 + 128],
                                                     bank[:, 0:128]))(),
                            deps=(s_mm, prev_b.get("kv_cons")), tag=f"l1b_vupe{b}_{g}_{j}")
                up_ev.append(e1)
        kv_done = up_ev[-1]

        E = PB[:, 8192:16384]
        mx = PD[:, 0:16]
        negB = PD[:, 16:17]
        r_ng = PD[0:1, 7296:7312]
        r_nb = PD[0:1, 7320:7321]
        r_s = mk.ROWS[0:1, 0:512]
        AT = PD[:, 2048:4096].bitcast(BF16)
        wo_bufs = [PD[:, 5120:5632].bitcast(BF16), PD[:, 5632:6144].bitcast(BF16)]
        p2 = {"qcons": kv_done, "wo": lat_ev}
        for J in range(4):
            nI = 4 * J + 4
            tq0 = J * 512
            ldx = P.dma("sync",
                        (lambda tb=tb, tq0=tq0, J=J: lambda e: e.dma_start(
                            out=PC[:, (J % 2) * 4096:(J % 2) * 4096 + 4096].rearrange("p (c t) -> p c t", t=512),
                            in_=bass.AP(x_in.tensor, tb + tq0,
                                        [[T_CORE, 128], [128 * T_CORE, NCH], [1, 512]])))(),
                        deps=(prev_b.get("pc" + str(J % 2)), lnbar, p2.get("pc" + str(J % 2))),
                        tag=f"l1b_ldx{b}_{J}")
            at_ev = []
            for h in range(8):
                g = h // 4
                def mmS1(e, g=g, h=h, tq0=tq0, nI=nI):
                    last = None
                    for I in range(nI):
                        last = e.matmul(pb[I % 2][:, 0:512],
                                        PB[:, g * 2048 + I * 128: g * 2048 + I * 128 + 128],
                                        PA[:, h * 2048 + tq0: h * 2048 + tq0 + 512],
                                        start=True, stop=True)
                    return last
                sS1 = []
                mx_ev = []
                for I0 in range(0, nI, 2):
                    def mmS1p(e, g=g, h=h, tq0=tq0, I0=I0, nI=nI):
                        last = None
                        for I in range(I0, min(I0 + 2, nI)):
                            last = e.matmul(pb[I % 2][:, 0:512],
                                            PB[:, g * 2048 + I * 128: g * 2048 + I * 128 + 128],
                                            PA[:, h * 2048 + tq0: h * 2048 + tq0 + 512],
                                            start=True, stop=True)
                        return last
                    dep_pb = mx_ev[-2:] if len(mx_ev) >= 2 else (p2.get("s1b"),)
                    s1 = P.step("tensor", mmS1p, deps=(kv_done, q_ev, dep_pb),
                                tag=f"l1b_S1{b}_{J}_{h}_{I0}")
                    for I in range(I0, min(I0 + 2, nI)):
                        mI = P.step("vector", (lambda I=I: lambda e: e.reduce_max(
                            mx[:, I:I + 1], pb[I % 2][:, 0:512], AX.X))(),
                            deps=(s1, p2.get("mx")), tag=f"l1b_mx{b}_{J}_{h}_{I}")
                        mx_ev.append(mI)
                g1 = P.step("gpsimd", (lambda nI=nI: lambda e: e.reduce_max(
                    r_ng[0:1, 0:nI], mx[:, 0:nI], AX.C))(),
                    deps=(mx_ev, p2.get("rng")), tag=f"l1b_g1{b}_{J}_{h}")
                g2 = P.step("vector", (lambda nI=nI: lambda e: e.reduce_max(
                    r_nb, r_ng[0:1, 0:nI], AX.X, negate=True))(),
                    deps=(g1, p2.get("nbread")), tag=f"l1b_g2{b}_{J}_{h}")
                g3 = P.step("tensor", lambda e: e.matmul(pb[6][:, 8:9], mk.ONEF[0:1, 0:128],
                                                         r_nb, start=True, stop=True),
                            deps=(g2, p2.get("nbmm")), tag=f"l1b_g3{b}_{J}_{h}")
                g4 = P.step("vector", lambda e: e.tensor_copy(negB, pb[6][:, 8:9]),
                            deps=(g3, p2.get("negBc")), tag=f"l1b_g4{b}_{J}_{h}")
                e_ev = []
                for I in range(nI):
                    bank = pb[I % 4]

                    def mmS2(e, g=g, h=h, tq0=tq0, I=I, bank=bank):
                        return e.matmul(bank[:, 0:512],
                                        PB[:, g * 2048 + I * 128: g * 2048 + I * 128 + 128],
                                        PA[:, h * 2048 + tq0: h * 2048 + tq0 + 512],
                                        start=True, stop=True)
                    dep_pb = e_ev[I - 4] if I >= 4 else (mx_ev, p2.get("s2b"))
                    s2 = P.step("tensor", mmS2, deps=(kv_done, q_ev, dep_pb, g4),
                                tag=f"l1b_S2{b}_{J}_{h}_{I}")
                    dstE = E[:, I * 512:(I + 1) * 512]
                    ex = P.step("scalar", (lambda bank=bank, dstE=dstE: lambda e:
                                           e.activation(dstE, bank[:, 0:512], AF.Exp,
                                                        bias=negB[:, 0:1], scale=1.0))(),
                                deps=(s2, g4, p2.get("Econs" + str(I))), tag=f"l1b_E{b}_{J}_{h}_{I}")
                    if I >= 4 * J:
                        mi = I - 4 * J
                        ex = P.step("vector", (lambda dstE=dstE, mi=mi: lambda e:
                                               e.tensor_mul(dstE, dstE, MB[:, mi * 512:(mi + 1) * 512]))(),
                                    deps=(ex,), tag=f"l1b_Em{b}_{J}_{h}_{I}")
                    e_ev.append(ex)
                    p2["Econs" + str(I)] = None
                def mms(e, nI=nI):
                    last = None
                    for I in range(nI):
                        last = e.matmul(pb[6][0:1, 0:512], mk.ONECB[:, 0:1],
                                        E[:, I * 512:(I + 1) * 512],
                                        start=(I == 0), stop=(I == nI - 1))
                    return last
                s_sum = P.step("tensor", mms, deps=(e_ev, p2.get("srmm")), tag=f"l1b_sum{b}_{J}_{h}")
                z1 = P.step("vector", lambda e: e.tensor_copy(r_s, pb[6][0:1, 0:512]),
                            deps=(s_sum, p2.get("rrow")), tag=f"l1b_z1{b}_{J}_{h}")
                z2 = P.step("scalar", lambda e: e.activation(r_s, r_s, AF.Ln, scale=1.0),
                            deps=(z1,), tag=f"l1b_z2{b}_{J}_{h}")
                z3 = P.step("scalar", lambda e: e.activation(r_s, r_s, AF.Exp, scale=-1.0),
                            deps=(z2,), tag=f"l1b_z3{b}_{J}_{h}")
                def mmu(e, g=g, nI=nI):
                    last = None
                    for I in range(nI):
                        last = e.matmul(pb[5][:, 0:512],
                                        PB[:, 4096 + I * 256 + g * 128: 4096 + I * 256 + g * 128 + 128],
                                        E[:, I * 512:(I + 1) * 512],
                                        start=(I == 0), stop=(I == nI - 1))
                    return last
                s_u = P.step("tensor", mmu, deps=(e_ev, kv_done, p2.get("ucons")),
                             tag=f"l1b_U{b}_{J}_{h}")
                s_r = P.step("tensor", lambda e: e.matmul(pb[7][:, 0:512], mk.ONEF[0:1, 0:128],
                                                          r_s, start=True, stop=True),
                             deps=(z3, p2.get("rbcmm")), tag=f"l1b_rbc{b}_{J}_{h}")
                rbc_sb = PD[:, 1024 + (h % 2) * 512: 1536 + (h % 2) * 512]
                c_r = P.step("vector", (lambda rbc_sb=rbc_sb: lambda e:
                                        e.tensor_copy(rbc_sb, pb[7][:, 0:512]))(),
                             deps=(s_r, p2.get("rbccons" + str(h % 2))),
                             tag=f"l1b_rbce{b}_{J}_{h}")
                dstA = AT[:, h * 512:(h + 1) * 512]
                ea = P.step("vector", (lambda dstA=dstA, rbc_sb=rbc_sb: lambda e:
                                       e.tensor_mul(dstA, pb[5][:, 0:512], rbc_sb))(),
                            deps=(s_u, c_r, p2.get("atcons"), kv_done), tag=f"l1b_at{b}_{J}_{h}")
                at_ev.append(ea)
                p2["mx"] = g1
                p2["rng"] = g2
                p2["nbread"] = g3
                p2["nbmm"] = z1
                p2["negBc"] = e_ev
                p2["s1b"] = e_ev[-4:]
                p2["s2b"] = e_ev[-1]
                p2["srmm"] = z1
                p2["rrow"] = s_r
                p2["ucons"] = ea
                p2["rbcmm"] = c_r
                p2["rbccons" + str(h % 2)] = ea
                p2["atcons"] = None
                for I in range(nI):
                    p2["Econs" + str(I)] = [s_sum, s_u]
            o_ev = []
            for o in range(NCH):
                wob = wo_bufs[o % 2]
                wld = mk.load_wT("l1b_wo", wob, 128,
                                 deps=((o_ev[o - 2],) if o >= 2 else (kv_done, p2.get("wo"))),
                                 col0=o * 128, rowstride=1024, tag=f"_{b}_{J}_{o}")
                bank = pb[o % 4]

                def mmo(e, wob=wob, bank=bank):
                    mmop = None
                    for c in range(NCH):
                        mmop = e.matmul(bank[:, 0:512],
                                        wob[:, c * 128:(c + 1) * 128],
                                        AT[:, c * 512:(c + 1) * 512],
                                        start=(c == 0), stop=(c == NCH - 1))
                    return mmop
                pv = o_ev[o - 4] if o >= 4 else (at_ev, p2.get("s2b"))
                s_o = P.step("tensor", mmo, deps=(wld, at_ev, pv), tag=f"l1b_o{b}_{J}_{o}")
                pcsl = PC[:, (J % 2) * 4096 + o * 512: (J % 2) * 4096 + o * 512 + 512]
                e_o = P.step("vector",
                             (lambda bank=bank, pcsl=pcsl: lambda e:
                              e.tensor_add(pcsl, bank[:, 0:512], pcsl))(),
                             deps=(s_o, ldx), tag=f"l1b_oe{b}_{J}_{o}")
                o_ev.append(e_o)
            st = P.dma("sync",
                       (lambda tb=tb, tq0=tq0, J=J: lambda e: e.dma_start(
                           out=bass.AP(x_out.tensor, tb + tq0,
                                       [[T_CORE, 128], [128 * T_CORE, NCH], [1, 512]]),
                           in_=PC[:, (J % 2) * 4096:(J % 2) * 4096 + 4096].rearrange("p (c t) -> p c t", t=512)))(),
                       deps=(o_ev,), tag=f"l1b_st{b}_{J}")
            out_final.append(st)
            p2["pc" + str(J % 2)] = st
            p2["wo"] = o_ev[-2:]
            p2["atcons"] = o_ev[-1]
            p2["s1b"] = o_ev[-4:]
            p2["s2b"] = o_ev[-1]
        prev_b = {"E_cons": out_final[-1], "banks": out_final[-1], "latb": out_final[-1],
                  "kv_cons": out_final[-1], "upcons": out_final[-1],
                  "pc0": p2.get("pc0"), "pc1": p2.get("pc1")}
    return out_final



def emit_l1_ffn(mk, x_in, x_out, prev=()):
    P = mk.P
    pb = mk.pb
    XN, PA, PB, PC, PD = mk.XN, mk.PA, mk.PB, mk.PC, mk.PD
    ln_done = emit_ln(mk, x_in, deps=prev)
    lnbar = ln_done[-1]
    E_, F_ = 15, 1024
    b1lo = mk.col("l1f_b1lo")
    b1hi = mk.col("l1f_b1hi")
    gmc = mk.col("l1f_gm")
    sblo = mk.col("l1f_sblo")
    sbhi = mk.col("l1f_sbhi")
    sgm = mk.col("l1f_sgm")
    negBIG = mk.const_col(-30.0)
    combf = PB[:, 8192:9152].bitcast(F32)
    combT = PB[:, 9152:13248]
    fc2b = PB[:, 13248:14272]
    srow = mk.ROWB[0:1, 0:1024]

    lb1 = P.dma("gpsimd", lambda e: e.dma_start(out=fc2b[0:15, :], in_=mk.dram["l1f_fc2b"]),
                deps=(prev,), tag="l1f_fc2b")
    lb2 = P.dma("sync", lambda e: e.dma_start(out=PD[0:1, 4096:5120], in_=mk.dram["r_l1f_sb2"]),
                deps=(prev,), tag="l1f_sb2")
    lb3 = P.step("vector", lambda e: e.tensor_copy(srow, PD[0:1, 4096:5120]),
                 deps=(lb2, prev), tag="l1f_sb2c")

    gwT = PA[:, 8192:8320]
    lgw = P.dma("gpsimd", lambda e: e.dma_start(out=gwT, in_=mk.dram["l1f_gate_wTc"]),
                deps=(prev,), tag="l1f_gw")
    lg_sbf = None
    gt_ev = []
    prevg = {}
    for tt in range(8 if DBG_GATE >= 0 else 0):
        def mmg(e, tt=tt):
            last = None
            for c in range(NCH):
                last = e.matmul(pb[4][0:16, 0:512], gwT[:, c * 16:(c + 1) * 16],
                                XN[:, c * T_CORE + tt * 512: c * T_CORE + tt * 512 + 512],
                                start=(c == 0), stop=(c == NCH - 1))
            return last
        lg_sb = PD[:, 1536:2048]
        s_g = P.step("tensor", mmg, deps=(lgw, lnbar, prevg.get("pb4")), tag=f"l1f_g{tt}")
        e_g = P.step("vector", lambda e: e.tensor_copy(lg_sb[0:15, :], pb[4][0:15, 0:512]),
                     deps=(s_g, prevg.get("lgsb")), tag=f"l1f_ge{tt}")
        prevg["pb4"] = e_g
        for q in range(4 if DBG_GATE >= 1 else 0):
            tc = tt * 4 + q
            lg_sb = PD[:, 1536:2048]
            t1 = P.step("tensor", (lambda q=q: lambda e: e.transpose(
                pb[5][:, 0:15], lg_sb[0:15, q * 128:(q + 1) * 128], mk.IDF[0:15, 0:15]))(),
                deps=(e_g, prevg.get("tp")), tag=f"l1f_tp{tt}_{q}")
            lg = PD[:, 1024:1040]
            t2 = P.step("vector", lambda e: e.tensor_copy(lg[:, 0:15], pb[5][:, 0:15]),
                        deps=(t1, prevg.get("lg")), tag=f"l1f_tpc{tt}_{q}")
            if DBG_DUMP and tt == 0 and q == 0:
                dmp2 = mk.dram_out("dbg_lg", [128, 16])
                dst1 = P.dma("sync", lambda e: e.dma_start(out=dmp2, in_=PD[:, 1024:1040]), deps=(t2,),
                             tag="dbglg")
                dmp3 = mk.dram_out("dbg_lgsb", [15, 512], F32)
                dst2 = P.dma("sync", lambda e: e.dma_start(out=dmp3, in_=lg_sb[0:15, :]), deps=(t2, e_g),
                             tag="dbglgsb")
                prevg["lg"] = [dst1, dst2]
                prevg["pb4"] = [dst1, dst2, e_g]
            nm = PD[:, 1040:1041]
            w1 = PD[:, 1041:1042]
            w2 = PD[:, 1042:1043]
            w3 = PD[:, 1043:1044]
            den = PD[:, 1044:1045]
            Pt = PD[:, 1056:1072]
            P2t = PD[:, 1072:1088]
            msk = PD[:, 1088:1104]
            sel = PD[:, 1104:1120]
            if DBG_GATE < 2:
                prevg["tp"] = t2
                prevg["lg"] = t2
                continue
            g1 = P.step("vector", lambda e: e.reduce_max(nm, lg[:, 0:15], AX.X, negate=True),
                        deps=(t2, prevg.get("sm")), tag=f"l1f_nm{tt}_{q}")
            g2 = P.step("scalar", lambda e: e.activation(Pt[:, 0:15], lg[:, 0:15], AF.Exp,
                                                         bias=nm[:, 0:1], scale=1.0),
                        deps=(g1, prevg.get("sm")), tag=f"l1f_ex{tt}_{q}")
            g3 = P.step("vector", lambda e: e.reduce_sum(den, Pt[:, 0:15], AX.X),
                        deps=(g2,), tag=f"l1f_sum{tt}_{q}")
            g4 = P.step("scalar", lambda e: e.activation(den, den, AF.Ln, scale=1.0),
                        deps=(g3,), tag=f"l1f_ln{tt}_{q}")
            g5 = P.step("scalar", lambda e: e.activation(den, den, AF.Exp, scale=-1.0),
                        deps=(g4,), tag=f"l1f_rs{tt}_{q}")
            g6 = P.step("vector", lambda e: e.tensor_scalar(Pt[:, 0:15], Pt[:, 0:15], den[:, 0:1],
                                                            None, ALU.mult),
                        deps=(g5,), tag=f"l1f_p{tt}_{q}")
            if DBG_GATE < 3:
                prevg["tp"] = t2
                prevg["lg"] = g2
                prevg["sm"] = [g6, g2]
                continue
            g7 = P.step("vector", lambda e: e.reduce_max(w1, Pt[:, 0:15], AX.X), deps=(g6,),
                        tag=f"l1f_m1{tt}_{q}")
            g8 = P.step("vector", lambda e: e.tensor_scalar(msk[:, 0:15], Pt[:, 0:15], w1[:, 0:1],
                                                            None, ALU.is_ge),
                        deps=(g7,), tag=f"l1f_k1{tt}_{q}")
            g9 = P.step("vector", lambda e: e.tensor_copy(sel[:, 0:15], msk[:, 0:15]),
                        deps=(g8, prevg.get("sel")), tag=f"l1f_s1{tt}_{q}")
            g10 = P.step("vector", lambda e: e.scalar_tensor_tensor(P2t[:, 0:15], msk[:, 0:15],
                                                                    negBIG[:, 0:1], Pt[:, 0:15],
                                                                    ALU.mult, ALU.add),
                         deps=(g9, g6, mk.cdep(-30.0)), tag=f"l1f_p2{tt}_{q}")
            g11 = P.step("vector", lambda e: e.reduce_max(w2, P2t[:, 0:15], AX.X), deps=(g10,),
                         tag=f"l1f_m2{tt}_{q}")
            g12 = P.step("vector", lambda e: e.tensor_scalar(msk[:, 0:15], P2t[:, 0:15], w2[:, 0:1],
                                                             None, ALU.is_ge),
                         deps=(g11,), tag=f"l1f_k2{tt}_{q}")
            g13 = P.step("vector", lambda e: e.tensor_add(sel[:, 0:15], sel[:, 0:15], msk[:, 0:15]),
                         deps=(g12,), tag=f"l1f_s2{tt}_{q}")
            g14 = P.step("vector", lambda e: e.scalar_tensor_tensor(P2t[:, 0:15], msk[:, 0:15],
                                                                    negBIG[:, 0:1], P2t[:, 0:15],
                                                                    ALU.mult, ALU.add),
                         deps=(g13, g10), tag=f"l1f_p3{tt}_{q}")
            g15 = P.step("vector", lambda e: e.reduce_max(w3, P2t[:, 0:15], AX.X), deps=(g14,),
                         tag=f"l1f_m3{tt}_{q}")
            g16 = P.step("vector", lambda e: e.tensor_scalar(msk[:, 0:15], P2t[:, 0:15], w3[:, 0:1],
                                                             None, ALU.is_ge),
                         deps=(g15,), tag=f"l1f_k3{tt}_{q}")
            g17 = P.step("vector", lambda e: e.tensor_add(sel[:, 0:15], sel[:, 0:15], msk[:, 0:15]),
                         deps=(g16,), tag=f"l1f_s3{tt}_{q}")
            g18 = P.step("vector", lambda e: e.tensor_add(den, w1, w2), deps=(g15, g7, g11, g5),
                         tag=f"l1f_d1{tt}_{q}")
            g19 = P.step("vector", lambda e: e.tensor_add(den, den, w3), deps=(g18,),
                         tag=f"l1f_d2{tt}_{q}")
            g20 = P.step("vector", lambda e: e.tensor_single_scalar(den, den, 1e-20, ALU.add),
                         deps=(g19,), tag=f"l1f_d3{tt}_{q}")
            g21 = P.step("scalar", lambda e: e.activation(den, den, AF.Ln, scale=1.0),
                         deps=(g20,), tag=f"l1f_d4{tt}_{q}")
            g22 = P.step("scalar", lambda e: e.activation(den, den, AF.Exp, scale=-1.0),
                         deps=(g21,), tag=f"l1f_d5{tt}_{q}")
            cslice = combf[:, tc * 15:(tc + 1) * 15]
            g23 = P.step("vector", (lambda cslice=cslice: lambda e: e.tensor_mul(
                cslice, Pt[:, 0:15], sel[:, 0:15]))(),
                deps=(g17, g6, prevg.get("combf" + str(tc % 2))), tag=f"l1f_cb{tt}_{q}")
            g24 = P.step("vector", (lambda cslice=cslice: lambda e: e.tensor_scalar(
                cslice, cslice, den[:, 0:1], None, ALU.mult))(),
                deps=(g23, g22), tag=f"l1f_cb2{tt}_{q}")
            cbf = PD[:, 1120:1136].bitcast(BF16)
            g25 = P.step("vector", (lambda cslice=cslice, cbf=cbf: lambda e: e.tensor_copy(
                cbf[:, 0:15], cslice))(),
                deps=(g24, prevg.get("cbf")), tag=f"l1f_cbc{tt}_{q}")
            pb5hi = pb[5][:, 64:128].bitcast(BF16)
            t3 = P.step("tensor", (lambda cbf=cbf, pb5hi=pb5hi: lambda e: e.transpose(
                pb5hi[0:15, 0:128], cbf[:, 0:15], mk.IDB[:]))(),
                deps=(g25, t1, prevg.get("tp2")), tag=f"l1f_tq{tt}_{q}")
            t4 = P.step("vector", (lambda tc=tc, pb5hi=pb5hi: lambda e: e.tensor_copy(
                combT[0:15, tc * 128:(tc + 1) * 128], pb5hi[0:15, 0:128]))(),
                deps=(t3, prevg.get("combT")), tag=f"l1f_tqe{tt}_{q}")
            prevg["sm"] = [g23, g10, g14]
            prevg["sel"] = g23
            prevg["lg"] = g2
            prevg["tp"] = t4
            prevg["tp2"] = t4
            prevg["cbf"] = t3
            prevg["combT"] = None
            prevg["combf" + str(tc % 2)] = None
        prevg["lgsb"] = t1 if DBG_GATE >= 1 else e_g
    gate_done = [t4, g24] if DBG_GATE >= 3 else ([lgw, lb1, lb3] + [v for v in prevg.values() if v is not None])

    if DBG_DUMP:
        dmp = mk.dram_out("dbg_comb", [128, 480])
        P.dma("sync", lambda e: e.dma_start(out=dmp, in_=combf), deps=(gate_done,), tag="dbgc")
    out_final = []
    prev_pass = {}
    for ps in range(4):
        t0 = ps * 1024
        ttA, ttB = t0 // 512, t0 // 512 + 1
        z_y = []
        for half in range(2):
            z_y.append(P.step("vector", (lambda half=half: lambda e: e.memset(
                PD[:, half * 4096:(half + 1) * 4096], 0.0))(),
                deps=(gate_done, prev_pass.get("yacc"), lb3, prev_pass.get("pccons")),
                tag=f"l1f_z{ps}_{half}"))
        a_done = {}
        w_free = [gate_done + [lgw] + list(z_y), gate_done + list(z_y)]
        yexp_done = list(z_y)
        prev_e = {}
        w2l = None
        y_ev = list(z_y)
        for ei in range(min(E_, DBG_NEXP) if DBG_L1F >= 2 else 0):
            wl0 = mk.load_wT("l1f_fc1T", PA[:, 0:8192], 1024, deps=(w_free[0],),
                             col0=0, rowstride=2048, base=ei * D * 2048, tag=f"_{ps}_{ei}0")
            wl1 = mk.load_wT("l1f_fc1T", PA[:, 8192:16384], 1024, deps=(w_free[1],),
                             col0=1024, rowstride=2048, base=ei * D * 2048, tag=f"_{ps}_{ei}1")
            a_ev = []
            mm_all = []
            for ti in range(2):
                tt = ttA + ti
                for fc in range(NCH):
                    bxh = pb[fc % 2]
                    bg = pb[2 + fc % 2]

                    def mmxh(e, tt=tt, fc=fc, bxh=bxh):
                        last = None
                        for c in range(NCH):
                            last = e.matmul(bxh[:, 0:512],
                                            PA[:, c * 1024 + fc * 128: c * 1024 + fc * 128 + 128],
                                            XN[:, c * T_CORE + tt * 512: c * T_CORE + tt * 512 + 512],
                                            start=(c == 0), stop=(c == NCH - 1))
                        return last

                    def mmgg(e, tt=tt, fc=fc, bg=bg):
                        last = None
                        for c in range(NCH):
                            last = e.matmul(bg[:, 0:512],
                                            PA[:, 8192 + c * 1024 + fc * 128: 8192 + c * 1024 + fc * 128 + 128],
                                            XN[:, c * T_CORE + tt * 512: c * T_CORE + tt * 512 + 512],
                                            start=(c == 0), stop=(c == NCH - 1))
                        return last
                    pv = a_ev[-2] if len(a_ev) >= 2 else prev_e.get("banks")
                    s_xh = P.step("tensor", mmxh, deps=(wl0, ln_done[tt], pv, z_y),
                                  tag=f"l1f_xh{ps}_{ei}_{ti}_{fc}")
                    s_gg = P.step("tensor", mmgg, deps=(wl1, ln_done[tt], pv, z_y),
                                  tag=f"l1f_gg{ps}_{ei}_{ti}_{fc}")
                    mm_all += [s_xh, s_gg]
                    if DBG_EXPPH < 2:
                        a_ev.append(P.step("vector", (lambda bxh=bxh, bg=bg,
                                                      dst=PB[:, ti * 4096 + fc * 512: ti * 4096 + fc * 512 + 512]:
                                                      lambda e: e.tensor_copy(dst, bg[:, 0:512]))(),
                                           deps=(s_xh, s_gg, a_ev[-2] if len(a_ev) >= 2 else prev_e.get("acons")),
                                           tag=f"l1f_acp{ps}_{ei}_{ti}_{fc}"))
                        continue
                    gsc = PC[:, (fc % 2) * 512:(fc % 2) * 512 + 512]
                    xsc = PC[:, 1024 + (fc % 2) * 512: 1536 + (fc % 2) * 512]
                    pvs = a_ev[-2] if len(a_ev) >= 2 else prev_e.get("scr")
                    e1 = P.step("scalar", (lambda bg=bg, gsc=gsc, ei=ei, fc=fc: lambda e:
                                           e.activation(gsc, bg[:, 0:512], AF.Gelu,
                                                        bias=b1hi[:, ei * 8 + fc: ei * 8 + fc + 1],
                                                        scale=1.0))(),
                                deps=(s_gg, pvs), tag=f"l1f_gel{ps}_{ei}_{ti}_{fc}")
                    e2 = P.step("vector", (lambda bxh=bxh, xsc=xsc, ei=ei, fc=fc: lambda e:
                                           e.tensor_scalar(xsc, bxh[:, 0:512],
                                                           b1lo[:, ei * 8 + fc: ei * 8 + fc + 1],
                                                           gmc[:, ei * 8 + fc: ei * 8 + fc + 1],
                                                           ALU.add, ALU.mult))(),
                                deps=(s_xh, pvs), tag=f"l1f_xsc{ps}_{ei}_{ti}_{fc}")
                    dst = PB[:, ti * 4096 + fc * 512: ti * 4096 + fc * 512 + 512]
                    e3 = P.step("gpsimd", (lambda gsc=gsc, xsc=xsc, dst=dst: lambda e:
                                           e.tensor_mul(dst, gsc, xsc))(),
                                deps=(e1, e2, prev_e.get("acons")), tag=f"l1f_a{ps}_{ei}_{ti}_{fc}")
                    a_ev.append(e3)
            w2l = mk.load_wT("l1f_fc2T", PA[:, 0:8192], 1024, deps=(mm_all[-2], mm_all[-1]),
                            base=ei * D * D, tag=f"_{ps}_{ei}2")
            if DBG_EXPPH < 3:
                prev_e = {"banks": a_ev[-2:], "scr": a_ev[-2:], "acons": a_ev[-4:],
                          "ybanks": None}
                w_free = [[w2l], mm_all[-1:] + [a_ev[-1]]]
                yexp_done.append(a_ev[-1])
                continue
            y_ev = []
            for ti in range(2):
                for tcq in range(4):
                    tc = ti * 4 + tcq
                    for dh in range(2):
                        bank = pb[4 + (len(y_ev) % 4)]

                        def mmy(e, ti=ti, tcq=tcq, dh=dh, bank=bank):
                            last = None
                            for fc in range(NCH):
                                last = e.matmul(bank[:, 0:512],
                                                PB[:, ti * 4096 + fc * 512 + tcq * 128:
                                                   ti * 4096 + fc * 512 + tcq * 128 + 128],
                                                PA[:, fc * 1024 + dh * 512: fc * 1024 + dh * 512 + 512],
                                                start=(fc == 0), stop=(fc == NCH - 1))
                            return last
                        pv = y_ev[-4] if len(y_ev) >= 4 else (mm_all[-1], prev_e.get("ybanks"))
                        s_y = P.step("tensor", mmy, deps=(w2l, a_ev, pv), tag=f"l1f_y{ps}_{ei}_{tc}_{dh}")
                        ysl = PD[:, tc * 1024 + dh * 512: tc * 1024 + dh * 512 + 512]
                        cidx = (t0 // 128 + tc) * 15 + ei
                        e_y = P.step("vector",
                                     (lambda bank=bank, ysl=ysl, cidx=cidx: lambda e:
                                      e.scalar_tensor_tensor(ysl, bank[:, 0:512],
                                                             combf[:, cidx:cidx + 1], ysl,
                                                             ALU.mult, ALU.add))(),
                                     deps=(s_y, z_y, prev_pass.get("ycons"),
                                           yexp_done[-16] if len(yexp_done) >= 16 else None),
                                     tag=f"l1f_ye{ps}_{ei}_{tc}_{dh}")
                        y_ev.append(e_y)
                        yexp_done.append(e_y)
            prev_e = {"banks": a_ev[-2:], "scr": a_ev[-2:], "acons": y_ev[-8:],
                      "ybanks": y_ev[-4:]}
            w_free = [[w2l] + y_ev[-4:], mm_all[-1:] + [a_ev[-1]]]
        sh_y = list(yexp_done[-2:])
        prev_sh = {"w0": ([w2l] if w2l else []) + y_ev[-4:], "w1": ([w2l] if w2l else []) + y_ev[-4:],
                   "acons": y_ev[-8:], "banks": y_ev[-2:], "ybanks": y_ev[-4:]}
        for ti in range(2 if DBG_L1F >= 3 else 0):
            tt = ttA + ti
            as_ev = []
            for j in range(2):
                wx = mk.load_wT("l1f_sfc1T", PA[:, 0:8192], 1024,
                                deps=(prev_sh["w0"],), col0=j * 1024, rowstride=4096,
                                tag=f"_{ps}_{ti}_{j}x")
                wg = mk.load_wT("l1f_sfc1T", PA[:, 8192:16384], 1024,
                                deps=(prev_sh["w1"],), col0=2048 + j * 1024, rowstride=4096,
                                tag=f"_{ps}_{ti}_{j}g")
                mm_lastx = mm_lastg = None
                for fcl in range(NCH):
                    fc = 8 * j + fcl
                    bxh = pb[fcl % 2]
                    bg = pb[2 + fcl % 2]

                    def mmxs(e, tt=tt, fcl=fcl, bxh=bxh):
                        last = None
                        for c in range(NCH):
                            last = e.matmul(bxh[:, 0:512],
                                            PA[:, c * 1024 + fcl * 128: c * 1024 + fcl * 128 + 128],
                                            XN[:, c * T_CORE + tt * 512: c * T_CORE + tt * 512 + 512],
                                            start=(c == 0), stop=(c == NCH - 1))
                        return last

                    def mmgs(e, tt=tt, fcl=fcl, bg=bg):
                        last = None
                        for c in range(NCH):
                            last = e.matmul(bg[:, 0:512],
                                            PA[:, 8192 + c * 1024 + fcl * 128: 8192 + c * 1024 + fcl * 128 + 128],
                                            XN[:, c * T_CORE + tt * 512: c * T_CORE + tt * 512 + 512],
                                            start=(c == 0), stop=(c == NCH - 1))
                        return last
                    pv = as_ev[-2] if len(as_ev) >= 2 else prev_sh["banks"]
                    s_xh = P.step("tensor", mmxs, deps=(wx, ln_done[tt], pv),
                                  tag=f"l1f_sxh{ps}_{ti}_{fc}")
                    s_gg = P.step("tensor", mmgs, deps=(wg, ln_done[tt], pv),
                                  tag=f"l1f_sgg{ps}_{ti}_{fc}")
                    mm_lastx, mm_lastg = s_xh, s_gg
                    gsc = PC[:, (fcl % 2) * 512:(fcl % 2) * 512 + 512]
                    xsc = PC[:, 1024 + (fcl % 2) * 512: 1536 + (fcl % 2) * 512]
                    pvs = as_ev[-2] if len(as_ev) >= 2 else prev_sh["acons"]
                    e1 = P.step("scalar", (lambda bg=bg, gsc=gsc, fc=fc: lambda e:
                                           e.activation(gsc, bg[:, 0:512], AF.Gelu,
                                                        bias=sbhi[:, fc:fc + 1], scale=1.0))(),
                                deps=(s_gg, pvs), tag=f"l1f_sgel{ps}_{ti}_{fc}")
                    e2 = P.step("vector", (lambda bxh=bxh, xsc=xsc, fc=fc: lambda e:
                                           e.tensor_scalar(xsc, bxh[:, 0:512],
                                                           sblo[:, fc:fc + 1], sgm[:, fc:fc + 1],
                                                           ALU.add, ALU.mult))(),
                                deps=(s_xh, pvs), tag=f"l1f_sxsc{ps}_{ti}_{fc}")
                    dst = PB[:, fc * 512:(fc + 1) * 512]
                    e3 = P.step("gpsimd", (lambda gsc=gsc, xsc=xsc, dst=dst: lambda e:
                                           e.tensor_mul(dst, gsc, xsc))(),
                                deps=(e1, e2, prev_sh.get("ascons")), tag=f"l1f_sa{ps}_{ti}_{fc}")
                    as_ev.append(e3)
                prev_sh["w0"] = [mm_lastx]
                prev_sh["w1"] = [mm_lastg]
                prev_sh["banks"] = as_ev[-2:]
                prev_sh["acons"] = as_ev[-2:]
            if DBG_SHPH < 2:
                sh_y.extend(as_ev[-2:])
                prev_sh["ascons"] = as_ev[-2:]
                prev_sh["w0"] = as_ev[-2:]
                prev_sh["w1"] = as_ev[-2:]
                continue
            wf0 = mk.load_wT("l1f_sfc2T", PA[:, 0:8192], 1024, deps=(prev_sh["w0"], as_ev[-2:]),
                             base=0, din=D, tag=f"_{ps}_{ti}f0")
            wf1 = mk.load_wT("l1f_sfc2T", PA[:, 8192:16384], 1024, deps=(prev_sh["w1"], as_ev[-1:]),
                             base=D * D, din=D, tag=f"_{ps}_{ti}f1")
            ys_ev = []
            for tcq in range(4):
                tc = ti * 4 + tcq
                for dh in range(2):
                    bank = pb[4 + (len(ys_ev) % 4)]

                    def mmys(e, ti=ti, tcq=tcq, tc=tc, dh=dh, bank=bank, t0=t0):
                        last = None
                        for fc in range(16):
                            wbuf_off = (fc % 8) * 1024 + dh * 512 + (0 if fc < 8 else 8192)
                            last = e.matmul(bank[:, 0:512],
                                     PB[:, fc * 512 + tcq * 128: fc * 512 + tcq * 128 + 128],
                                     PA[:, wbuf_off: wbuf_off + 512],
                                     start=(fc == 0), stop=(fc == 15 and not DBG_SHBIAS))
                        if not DBG_SHBIAS:
                            return last
                        e.matmul(bank[:, 0:512], mk.ONEB[0:1, 0:128],
                                 srow[0:1, dh * 512: dh * 512 + 512], start=False, stop=False)
                        return e.matmul(bank[:, 0:512],
                                        combT[0:15, t0 + tc * 128: t0 + tc * 128 + 128],
                                        fc2b[0:15, dh * 512: dh * 512 + 512],
                                        start=False, stop=True)
                    pv = ys_ev[-4] if len(ys_ev) >= 4 else (prev_sh.get("ybanks"),)
                    s_y = P.step("tensor", mmys, deps=(wf0, wf1, as_ev, lb1, lb3, gate_done, pv),
                                 tag=f"l1f_ys{ps}_{ti}_{tc}_{dh}")
                    ysl = PD[:, tc * 1024 + dh * 512: tc * 1024 + dh * 512 + 512]
                    e_y = P.step("vector",
                                 (lambda bank=bank, ysl=ysl: lambda e:
                                  e.tensor_add(ysl, bank[:, 0:512], ysl))(),
                                 deps=(s_y, yexp_done[-1],
                                       sh_y[-16] if len(sh_y) >= 16 else None),
                                 tag=f"l1f_yse{ps}_{ti}_{tc}_{dh}")
                    ys_ev.append(e_y)
                    sh_y.append(e_y)
            prev_sh["w0"] = [wf0] + ys_ev[-4:]
            prev_sh["w1"] = [wf1] + ys_ev[-4:]
            prev_sh["ybanks"] = ys_ev[-4:]
            prev_sh["ascons"] = ys_ev[-8:]
        tr_ev = []
        for ti in range(2):
            tt = ttA + ti
            pchalf = PC[:, ti * 4096:(ti + 1) * 4096]
            ldx = P.dma("sync",
                        (lambda tt=tt, pchalf=pchalf: lambda e: e.dma_start(
                            out=pchalf.rearrange("p (c t) -> p c t", t=512),
                            in_=bass.AP(x_in.tensor, tt * 512,
                                        [[T_CORE, 128], [128 * T_CORE, NCH], [1, 512]])))(),
                        deps=(sh_y, lnbar, prev_pass.get("pccons")), tag=f"l1f_ldx{ps}_{ti}")
            for tcq in range(4 if DBG_FIN >= 1 else 0):
                tc = ti * 4 + tcq
                for c in range(NCH):
                    bank = pb[len(tr_ev) % 4]

                    def mtr(e, tc=tc, c=c, bank=bank):
                        return e.transpose(bank[:, 0:128],
                                           PD[:, tc * 1024 + c * 128: tc * 1024 + c * 128 + 128],
                                           mk.IDF[:])
                    pv = tr_ev[-4] if len(tr_ev) >= 4 else (sh_y, prev_pass.get("trb"))
                    s_t = P.step("tensor", mtr, deps=(sh_y[-1], pv), tag=f"l1f_tr{ps}_{tc}_{c}")
                    dstx = pchalf[:, c * 512 + tcq * 128: c * 512 + tcq * 128 + 128]
                    e_t = P.step("vector", (lambda bank=bank, dstx=dstx: lambda e:
                                            e.tensor_add(dstx, bank[:, 0:128], dstx))(),
                                 deps=(s_t, ldx), tag=f"l1f_tre{ps}_{tc}_{c}")
                    tr_ev.append(e_t)
            st = P.dma("sync",
                       (lambda tt=tt, pchalf=pchalf: lambda e: e.dma_start(
                           out=bass.AP(x_out.tensor, tt * 512,
                                       [[T_CORE, 128], [128 * T_CORE, NCH], [1, 512]]),
                           in_=pchalf.rearrange("p (c t) -> p c t", t=512)))(),
                       deps=(tr_ev, ldx), tag=f"l1f_st{ps}_{ti}")
            out_final.append(st)
        prev_pass = {"yacc": tr_ev if tr_ev else z_y, "ycons": tr_ev, "pccons": out_final[-2:],
                     "trb": tr_ev[-4:] if tr_ev else None}
    return out_final



def emit_l1_attn_a(mk, x_in, x_out, prev=()):
    P = mk.P
    pb = mk.pb
    XN, PA, PB, PC, PD = mk.XN, mk.PA, mk.PB, mk.PC, mk.PD
    XNs = 8 * T_CORE
    ln_done = emit_ln(mk, x_in, deps=prev)
    lnbar = ln_done[-1]
    RATIO = 1.0 / 16.0
    EPSK = 1e-4
    bkc = mk.col("l1a_bk"); bqc = mk.col("l1a_bq"); boc = mk.col("l1a_bo")
    g2c = mk.col("l1a_g2"); be2c = mk.col("l1a_be2")
    g3c = mk.col("l1a_g3"); be3c = mk.col("l1a_be3")
    b1c = mk.col("l1a_b1")
    b2c = mk.col("l1a_b2")
    epscol = mk.const_col(LN_EPS)
    PCw = PC[:, 0:4096].bitcast(BF16)
    PCw2 = PC[:, 4096:8192].bitcast(BF16)
    PCf = PC[:, 4096:8192]
    DG = PD[:, 0:256]
    CTX = PD[:, 256:1296].bitcast(BF16)
    PRJ = PD[:, 1296:1424].bitcast(BF16)
    EHB = PD[:, 1424:1426].bitcast(BF16)
    KP = PD[:, 1440:1568].bitcast(BF16)
    QPT = PD[:, 1568:2080].bitcast(BF16)
    K2Q = PD[:, 2080:2336].bitcast(BF16)
    ATT = PD[:, 4384:6432].bitcast(BF16)
    RB1 = PD[:, 6432:6944]
    DGQ_BASE = 6944
    MX = PD[:, 7008:7024]
    NBC = PD[:, 7024:7025]
    r_a = mk.ROWS[0:1, 0:512]
    r_b = mk.ROWS[0:1, 512:1024]
    rr1 = PD[0:1, 7040:7056]
    rr2 = PD[0:1, 7064:7065]

    PRJF = PD[:, 7168:7424]
    p1 = P.dma("sync", lambda e: e.dma_start(out=PRJF[0:64, :], in_=mk.dram["l1a_projT"]),
               deps=(prev,), tag="l1a_pld")
    p1b = P.dma("sync", lambda e: e.dma_start(out=PRJF[64:128, :], in_=mk.dram["l1a_projT"]),
                deps=(prev,), tag="l1a_pld2")
    p2a = P.step("vector", lambda e: e.tensor_scalar(PRJ[0:64, :], PRJF[0:64, :],
                                                     float(64.0 ** (-0.25)), None, ALU.mult),
                 deps=(p1, prev), tag="l1a_pscale")
    p2b = P.step("vector", lambda e: e.tensor_scalar(PRJ[64:128, :], PRJF[64:128, :],
                                                     float(64.0 ** (-0.25)), None, ALU.mult),
                 deps=(p1b, prev), tag="l1a_pscale2")
    p2 = [p2a, p2b]
    e1_ = P.step("vector", lambda e: e.memset(EHB[0:64, 0:1], 1.0), deps=(prev,), tag="l1a_eh0")
    e2_ = P.step("vector", lambda e: e.memset(EHB[64:128, 0:1], 0.0), deps=(prev,), tag="l1a_eh1")
    e3_ = P.step("vector", lambda e: e.memset(EHB[0:64, 1:2], 0.0), deps=(prev,), tag="l1a_eh2")
    e4_ = P.step("vector", lambda e: e.memset(EHB[64:128, 1:2], 1.0), deps=(prev,), tag="l1a_eh3")
    ehb_done = [e1_, e2_, e3_, e4_]
    out_final = []
    prev_b = {}
    for b in range(B_PER_CORE):
        tb = b * T_SEQ
        ldr = P.dma("sync", lambda e: e.dma_start(out=PCf[0:1, 0:1024], in_=mk.dram["r_l1a_bv"]),
                    deps=(prev_b.get("allc"), prev, lnbar), tag=f"l1a_bvld{b}")
        rv = P.step("vector", lambda e: e.tensor_copy(mk.ROWB[0:1, 0:1024], PCf[0:1, 0:1024]),
                    deps=(ldr, prev_b.get("rowb")), tag=f"l1a_bvc{b}")
        wv = mk.load_wT("l1a_wvT", PCw[:, 0:8192], 1024, deps=(prev_b.get("allc"), lnbar, prev),
                        tag=f"_{b}v")
        v_ev = []
        for j in range(16):
            for h2 in range(2):
                bank = pb[(j * 2 + h2) % 4]

                def mmv(e, j=j, h2=h2, bank=bank, tb=tb):
                    for c in range(NCH):
                        e.matmul(bank[:, 0:512],
                                 XN[:, c * T_CORE + tb + j * 128: c * T_CORE + tb + j * 128 + 128],
                                 PCw[:, c * 1024 + h2 * 512: c * 1024 + h2 * 512 + 512],
                                 start=(c == 0), stop=False)
                    return e.matmul(bank[:, 0:512], mk.ONEB[0:1, 0:128],
                                    mk.ROWB[0:1, h2 * 512: h2 * 512 + 512],
                                    start=False, stop=True)
                pv = v_ev[-4] if len(v_ev) >= 4 else prev_b.get("banks")
                s_mm = P.step("tensor", mmv, deps=(wv, rv, ln_done[(tb + j * 128) // 512], pv),
                              tag=f"l1a_v{b}_{j}_{h2}")
                dst = PB[:, j * 1024 + h2 * 512: j * 1024 + h2 * 512 + 512]
                ev = P.step("vector", (lambda bank=bank, dst=dst: lambda e:
                                       e.tensor_copy(dst, bank[:, 0:512]))(),
                            deps=(s_mm,), tag=f"l1a_vev{b}_{j}_{h2}")
                v_ev.append(ev)
        v_done = v_ev[-1]
        wk = mk.load_wT("l1a_wkT", PCw2[:, 0:8192], 1024, deps=(v_ev[-8:], rv, lnbar), tag=f"_{b}k")
        k_ev = []
        for o in range(NCH):
            for nt in range(4):
                bank = pb[(o * 4 + nt) % 4]

                def mmk(e, o=o, nt=nt, bank=bank, tb=tb):
                    last = None
                    for c in range(NCH):
                        last = e.matmul(bank[:, 0:512],
                                        PCw2[:, c * 1024 + o * 128: c * 1024 + o * 128 + 128],
                                        XN[:, c * T_CORE + tb + nt * 512: c * T_CORE + tb + nt * 512 + 512],
                                        start=(c == 0), stop=(c == NCH - 1))
                    return last
                pv = k_ev[-4] if len(k_ev) >= 4 else v_ev[-4:]
                s_mm = P.step("tensor", mmk, deps=(wk, ln_done[(tb + nt * 512) // 512], pv),
                              tag=f"l1a_k{b}_{o}_{nt}")
                dst = PA[:, o * 2048 + nt * 512: o * 2048 + nt * 512 + 512]
                ev = P.step("vector", (lambda bank=bank, dst=dst, o=o: lambda e:
                                       e.tensor_scalar(dst, bank[:, 0:512], bkc[:, o:o + 1], None,
                                                       ALU.add))(),
                            deps=(s_mm,), tag=f"l1a_kev{b}_{o}_{nt}")
                k_ev.append(ev)
        k_done = k_ev[-1]
        dg_ev = []
        K2 = PCw[:, 0:512]
        prev_k2 = [v_done] + k_ev[-2:]
        for c in range(NCH):
            for nt in range(4):
                sq = P.step("scalar", (lambda c=c, nt=nt: lambda e: e.activation(
                    K2, PA[:, c * 2048 + nt * 512: c * 2048 + nt * 512 + 512], AF.Square))(),
                    deps=(k_done, prev_k2), tag=f"l1a_k2{b}_{c}_{nt}")

                def mmd(e, c=c, nt=nt):
                    last = None
                    for jj in range(4):
                        last = e.matmul(pb[4][:, 4 * jj: 4 * jj + 2],
                                        K2[:, jj * 128:(jj + 1) * 128], EHB[:, 0:2],
                                        start=True, stop=True)
                    return last
                s_d = P.step("tensor", mmd, deps=(sq, ehb_done, dg_ev[-1] if dg_ev else None),
                             tag=f"l1a_dg{b}_{c}_{nt}")
                dgv = bass.AP(PD, (nt * 4) * 16 + 2 * c, [[8192, 128], [16, 4], [1, 2]])
                e_d = P.step("vector", (lambda dgv=dgv: lambda e: e.tensor_copy(
                    dgv, pb[4][:, 0:16].rearrange("p (j x) -> p j x", x=4)[:, :, 0:2]))(),
                    deps=(s_d, prev_b.get("dgc")), tag=f"l1a_dge{b}_{c}_{nt}")
                dg_ev.append(e_d)
                prev_k2 = [s_d]
        dg_done = dg_ev[-1]
        ctx_ev = []
        ph = {"xp": dg_ev[-1], "mx": None, "ctxb": prev_b.get("ctxb")}
        for h in range(16):
            hb = (h % 2) * 64
            hc = h // 2
            mx_ev = []
            for j in range(16):
                bank = pb[j % 2]

                def mmx(e, hb=hb, hc=hc, j=j, bank=bank):
                    return e.matmul(bank[:, 0:256],
                                    PA[hb:hb + 64, hc * 2048 + j * 128: hc * 2048 + j * 128 + 128],
                                    PRJ[hb:hb + 64, 0:256], start=True, stop=True)
                pv = mx_ev[-2] if len(mx_ev) >= 2 else ph.get("xp")
                s_x = P.step("tensor", mmx, deps=(k_done, p2, pv), tag=f"l1a_xk{b}_{h}_{j}")
                m1 = P.step("vector", (lambda j=j, bank=bank: lambda e: e.reduce_max(
                    MX[:, j:j + 1], bank[:, 0:256], AX.X))(),
                    deps=(s_x, ph.get("mx")), tag=f"l1a_mxk{b}_{h}_{j}")
                mx_ev.append(m1)
            gk1 = P.step("gpsimd", lambda e: e.reduce_max(rr1, MX[:, 0:16], AX.C),
                         deps=(mx_ev, ph.get("g1")), tag=f"l1a_gk1{b}_{h}")
            gk2 = P.step("vector", lambda e: e.reduce_max(rr2, rr1, AX.X, negate=True),
                         deps=(gk1, ph.get("g2")), tag=f"l1a_gk2{b}_{h}")
            gk3 = P.step("tensor", lambda e: e.matmul(pb[0][:, 8:9], mk.ONEF[0:1, 0:128], rr2,
                                                      start=True, stop=True),
                         deps=(gk2, ph.get("g3"), mx_ev[-2:]), tag=f"l1a_gk3{b}_{h}")
            gk4 = P.step("vector", lambda e: e.tensor_copy(NBC, pb[0][:, 8:9]),
                         deps=(gk3, ph.get("g4")), tag=f"l1a_gk4{b}_{h}")
            kp_ev = []
            for j in range(16):
                bank = pb[2 + j % 2]

                def mmx2(e, hb=hb, hc=hc, j=j, bank=bank):
                    return e.matmul(bank[:, 0:256],
                                    PA[hb:hb + 64, hc * 2048 + j * 128: hc * 2048 + j * 128 + 128],
                                    PRJ[hb:hb + 64, 0:256], start=True, stop=True)
                pv = kp_ev[-2] if len(kp_ev) >= 2 else (mx_ev[-1], ph.get("kp2"))
                s_x2 = P.step("tensor", mmx2, deps=(k_done, p2, pv, gk4),
                              tag=f"l1a_xk2{b}_{h}_{j}")
                bcol = PD[:, 7026 + (j % 2): 7027 + (j % 2)]
                dgcol = bass.AP(PD, j * 16 + h, [[8192, 128], [1, 1]])
                c00625 = mk.const_col(-0.0625)
                b_c = P.step("vector", (lambda bcol=bcol, dgcol=dgcol: lambda e:
                                        e.scalar_tensor_tensor(bcol, dgcol, c00625[:, 0:1], NBC,
                                                               ALU.mult, ALU.add))(),
                             deps=(dg_done, gk4, kp_ev[-2] if len(kp_ev) >= 2 else ph.get("bc"),
                                   mk.cdep(-0.0625)),
                             tag=f"l1a_bck{b}_{h}_{j}")
                exk = P.step("scalar", (lambda bank=bank, bcol=bcol: lambda e:
                                        e.activation(KP, bank[:, 0:256], AF.Exp,
                                                     bias=bcol[:, 0:1], scale=1.0))(),
                             deps=(s_x2, b_c, ph.get("kpc")), tag=f"l1a_ek{b}_{h}_{j}")
                tsk = P.step("vector", lambda e: e.tensor_scalar(KP, KP, RATIO, RATIO * EPSK,
                                                                 ALU.mult, ALU.add),
                             deps=(exk,), tag=f"l1a_tk{b}_{h}_{j}")
                def mmc(e, h=h, j=j):
                    last = None
                    for mc in range(2):
                        e.matmul(pb[4 + mc][0:128, 0:64],
                                 KP[:, mc * 128:(mc + 1) * 128],
                                 PB[:, j * 1024 + h * 64: j * 1024 + h * 64 + 64],
                                 start=(j == 0), stop=(j == 15))
                        last = e.matmul(pb[6 + mc][0:128, 0:1],
                                        KP[:, mc * 128:(mc + 1) * 128], mk.ONECB[:, 0:1],
                                        start=(j == 0), stop=(j == 15))
                    return last
                s_c = P.step("tensor", mmc, deps=(tsk, v_done, ph.get("ctxmm")), tag=f"l1a_ctx{b}_{h}_{j}")
                kp_ev.append(s_c)
                ph["kpc"] = s_c
            ec1 = P.step("vector", (lambda h=h: lambda e: e.tensor_copy(
                CTX[:, h * 130: h * 130 + 64], pb[4][:, 0:64]))(),
                deps=(kp_ev[-1], ph.get("ctxb")), tag=f"l1a_ctxe1{b}_{h}")
            ec1b = P.step("vector", (lambda h=h: lambda e: e.tensor_copy(
                CTX[:, h * 130 + 64: h * 130 + 65], pb[6][:, 0:1]))(),
                deps=(kp_ev[-1], ec1), tag=f"l1a_ctxe1b{b}_{h}")
            ec2 = P.step("vector", (lambda h=h: lambda e: e.tensor_copy(
                CTX[:, h * 130 + 65: h * 130 + 129], pb[5][:, 0:64]))(),
                deps=(kp_ev[-1], ec1b), tag=f"l1a_ctxe2{b}_{h}")
            ec2b = P.step("vector", (lambda h=h: lambda e: e.tensor_copy(
                CTX[:, h * 130 + 129: h * 130 + 130], pb[7][:, 0:1]))(),
                deps=(kp_ev[-1], ec2), tag=f"l1a_ctxe2b{b}_{h}")
            ec2 = ec2b
            ctx_ev.append(ec2)
            ph["ctxmm"] = ec2
            ph["xp"] = kp_ev[-1]
            ph["mx"] = gk1
            ph["g1"] = gk2
            ph["g2"] = gk3
            ph["g3"] = gk4
            ph["g4"] = exk
            ph["bc"] = exk
            ph["kp2"] = kp_ev[-1]
        ctx_done = ctx_ev[-1]
        prev_b = dict(prev_b)
        prev_b["phaseA"] = ctx_done
        QT = PD[:, 2336:4384].bitcast(BF16)
        pnt = {}
        for nt in range(4):
            t0 = tb + nt * 512
            tt = t0 // 512
            wq = mk.load_wT("l1a_wqT", PCw[:, 0:8192], 1024,
                            deps=(k_ev[-4:], dg_ev[-2:], ph["xp"], ctx_done, pnt.get("pcw")),
                            tag=f"_{b}q{nt}")
            wo = mk.load_wT("l1a_woT", PB[:, 0:8192], 1024,
                            deps=(v_ev[-4:], ctx_done, dg_ev[-1:], pnt.get("pbw")),
                            tag=f"_{b}o{nt}")
            q_ev = []
            for o in range(NCH):
                bank = pb[o % 4]

                def mmq(e, o=o, t0=t0, bank=bank):
                    last = None
                    for c in range(NCH):
                        last = e.matmul(bank[:, 0:512],
                                        PCw[:, c * 1024 + o * 128: c * 1024 + o * 128 + 128],
                                        XN[:, c * T_CORE + t0: c * T_CORE + t0 + 512],
                                        start=(c == 0), stop=(c == NCH - 1))
                    return last
                pv = q_ev[o - 4] if o >= 4 else (pnt.get("qb"), ctx_done)
                s_q = P.step("tensor", mmq, deps=(wq, ln_done[tt], pv), tag=f"l1a_q{b}_{nt}_{o}")
                dst = QT[:, o * 512:(o + 1) * 512]
                ev = P.step("vector", (lambda bank=bank, dst=dst, o=o: lambda e:
                                       e.tensor_scalar(dst, bank[:, 0:512], bqc[:, o:o + 1], None,
                                                       ALU.add))(),
                            deps=(s_q, pnt.get("qtc")), tag=f"l1a_qe{b}_{nt}_{o}")
                q_ev.append(ev)
            q_done = q_ev[-1]
            dq_ev = []
            for c in range(NCH):
                sq = P.step("scalar", (lambda c=c: lambda e: e.activation(
                    K2Q, QT[:, c * 512:(c + 1) * 512], AF.Square))(),
                    deps=(q_done, dq_ev[-1] if dq_ev else pnt.get("dq")),
                    tag=f"l1a_q2{b}_{nt}_{c}")

                def mmdq(e, c=c):
                    last = None
                    for jj in range(4):
                        last = e.matmul(pb[4][:, 4 * jj: 4 * jj + 2],
                                        K2Q[:, jj * 128:(jj + 1) * 128], EHB[:, 0:2],
                                        start=True, stop=True)
                    return last
                s_d = P.step("tensor", mmdq, deps=(sq, ehb_done, dq_ev[-1] if dq_ev else pnt.get("dq2")),
                             tag=f"l1a_dq{b}_{nt}_{c}")
                dqv = bass.AP(PD, DGQ_BASE + 2 * c, [[8192, 128], [16, 4], [1, 2]])
                e_d = P.step("vector", (lambda dqv=dqv: lambda e: e.tensor_copy(
                    dqv, pb[4][:, 0:16].rearrange("p (j x) -> p j x", x=4)[:, :, 0:2]))(),
                    deps=(s_d, pnt.get("dqc")), tag=f"l1a_dqe{b}_{nt}_{c}")
                dq_ev.append(e_d)
            dq_done = dq_ev[-1]
            pnt["dq"] = dq_ev[-2:]
            pnt["dq2"] = dq_ev[-1]
            pnt["dqc"] = None
            at_ev = []
            for p_ in range(8):
                rows = []
                for hh in range(2):
                    h = 2 * p_ + hh
                    hb = (h % 2) * 64
                    hc = h // 2
                    u_parts = []
                    den_parts = []
                    for jj in range(4):
                        bank = pb[jj % 2]

                        def mmxq(e, hb=hb, hc=hc, jj=jj, bank=bank):
                            return e.matmul(bank[:, 0:256],
                                            QT[hb:hb + 64, hc * 512 + jj * 128: hc * 512 + jj * 128 + 128],
                                            PRJ[hb:hb + 64, 0:256], start=True, stop=True)
                        pv = u_parts[-1] if u_parts else (at_ev[-1] if at_ev else (dq_done, pnt.get("xq")))
                        s_x = P.step("tensor", mmxq, deps=(q_done, p2, pv), tag=f"l1a_xq{b}_{nt}_{h}_{jj}")
                        stc = PD[:, 7028:7029]
                        m1 = P.step("vector", (lambda bank=bank, stc=stc: lambda e:
                                               e.reduce_max(stc, bank[:, 0:256], AX.X, negate=True))(),
                                    deps=(s_x, pnt.get("stc")), tag=f"l1a_sq{b}_{nt}_{h}_{jj}")
                        bcol = PD[:, 7030:7031]
                        dqcol = bass.AP(PD, DGQ_BASE + jj * 16 + h, [[8192, 128], [1, 1]])
                        c00625 = mk.const_col(-0.0625)
                        b_c = P.step("vector", (lambda bcol=bcol, dqcol=dqcol, stc=stc: lambda e:
                                                e.scalar_tensor_tensor(bcol, dqcol, c00625[:, 0:1], stc,
                                                                       ALU.mult, ALU.add))(),
                                     deps=(dq_done, m1, pnt.get("bcq"), mk.cdep(-0.0625)),
                                     tag=f"l1a_bq{b}_{nt}_{h}_{jj}")
                        exq = P.step("scalar", (lambda bank=bank, bcol=bcol: lambda e:
                                                e.activation(KP, bank[:, 0:256], AF.Exp,
                                                             bias=bcol[:, 0:1], scale=1.0))(),
                                     deps=(s_x, b_c, pnt.get("kpq")), tag=f"l1a_eq{b}_{nt}_{h}_{jj}")
                        tsq = P.step("vector", lambda e: e.tensor_scalar(KP, KP, RATIO, RATIO * EPSK,
                                                                         ALU.mult, ALU.add),
                                     deps=(exq,), tag=f"l1a_tq{b}_{nt}_{h}_{jj}")
                        pbB = pb[2][:, 0:64].bitcast(BF16)
                        pbB2 = pb[2][:, 64:128].bitcast(BF16)
                        tp1 = P.step("tensor", (lambda pbB=pbB: lambda e: e.transpose(
                            pbB[:, 0:128], KP[:, 0:128], mk.IDB[:]))(),
                            deps=(tsq, pnt.get("tp1")), tag=f"l1a_tp1{b}_{nt}_{h}_{jj}")
                        tc1 = P.step("vector", (lambda pbB=pbB, jj=jj: lambda e: e.tensor_copy(
                            QPT[:, jj * 128:(jj + 1) * 128], pbB[:, 0:128]))(),
                            deps=(tp1, pnt.get("tc1")), tag=f"l1a_tc1{b}_{nt}_{h}_{jj}")
                        tp2 = P.step("tensor", (lambda pbB2=pbB2: lambda e: e.transpose(
                            pbB2[:, 0:128], KP[:, 128:256], mk.IDB[:]))(),
                            deps=(tsq, tc1, pnt.get("tp2")), tag=f"l1a_tp2{b}_{nt}_{h}_{jj}")
                        tc2 = P.step("vector", (lambda pbB2=pbB2, jj=jj: lambda e: e.tensor_copy(
                            QPT[:, 512 + jj * 128: 512 + (jj + 1) * 128], pbB2[:, 0:128]))(),
                            deps=(tp2, pnt.get("tc2")), tag=f"l1a_tc2{b}_{nt}_{h}_{jj}")
                        u_parts.append(tc2)
                    ubank = pb[5] if hh == 0 else pb[3]

                    def mmu(e, h=h, ubank=ubank):
                        e.matmul(ubank[0:64, 0:512], CTX[:, h * 130: h * 130 + 64],
                                 QPT[:, 0:512], start=True, stop=False)
                        return e.matmul(ubank[0:64, 0:512], CTX[:, h * 130 + 65: h * 130 + 129],
                                        QPT[:, 512:1024], start=False, stop=True)
                    s_u = P.step("tensor", mmu, deps=(u_parts, ctx_done, pnt.get("u" + str(hh))),
                                 tag=f"l1a_U{b}_{nt}_{h}")

                    def mmden(e, h=h):
                        e.matmul(pb[6][0:1, 0:512], CTX[:, h * 130 + 64: h * 130 + 65],
                                 QPT[:, 0:512], start=True, stop=False)
                        return e.matmul(pb[6][0:1, 0:512], CTX[:, h * 130 + 129: h * 130 + 130],
                                        QPT[:, 512:1024], start=False, stop=True)
                    s_den = P.step("tensor", mmden, deps=(u_parts, ctx_done, rows, pnt.get("denb")),
                                   tag=f"l1a_den{b}_{nt}_{h}")
                    rrow = r_a if hh == 0 else r_b
                    z1 = P.step("vector", (lambda rrow=rrow: lambda e: e.tensor_copy(
                        rrow, pb[6][0:1, 0:512]))(),
                        deps=(s_den, pnt.get("rrow")), tag=f"l1a_z1{b}_{nt}_{h}")
                    z2 = P.step("scalar", (lambda rrow=rrow: lambda e: e.activation(
                        rrow, rrow, AF.Ln, scale=1.0))(), deps=(z1,), tag=f"l1a_z2{b}_{nt}_{h}")
                    z3 = P.step("scalar", (lambda rrow=rrow: lambda e: e.activation(
                        rrow, rrow, AF.Exp, scale=-1.0))(), deps=(z2,), tag=f"l1a_z3{b}_{nt}_{h}")
                    rows.append(z3)
                    pnt["u" + str(hh)] = s_u
                    pnt["rrow"] = z2
                    pnt["denb"] = z1
                    pnt["xq"] = exq
                    pnt["kpq"] = tsq
                    pnt["tp1"] = tc1
                    pnt["tp2"] = tc2
                    pnt["tc1"] = tp2
                    pnt["tc2"] = s_u
                    pnt["stc"] = b_c
                    pnt["bcq"] = exq
                    if hh == 0:
                        u0 = s_u
                    else:
                        u1 = s_u
                def mmr(e):
                    e.matmul(pb[7][:, 0:512], mk.EH[0:1, 0:128], r_a, start=True, stop=False)
                    return e.matmul(pb[7][:, 0:512], mk.EH[0:1, 128:256], r_b, start=False, stop=True)
                s_r = P.step("tensor", mmr, deps=(rows, pnt.get("rbcmm")), tag=f"l1a_rbc{b}_{nt}_{p_}")
                c_r = P.step("vector", lambda e: e.tensor_copy(RB1, pb[7][:, 0:512]),
                             deps=(s_r, pnt.get("rbcc")), tag=f"l1a_rbce{b}_{nt}_{p_}")
                dstA = ATT[:, p_ * 512:(p_ + 1) * 512]
                ea0 = P.step("vector", (lambda dstA=dstA: lambda e: e.tensor_mul(
                    dstA[0:64, :], pb[5][0:64, 0:512], RB1[0:64, :]))(),
                    deps=(u0, c_r, pnt.get("att")), tag=f"l1a_a0{b}_{nt}_{p_}")
                ea1 = P.step("vector", (lambda dstA=dstA: lambda e: e.tensor_mul(
                    dstA[64:128, :], pb[3][0:64, 0:512], RB1[64:128, :]))(),
                    deps=(u1, c_r, ea0), tag=f"l1a_a1{b}_{nt}_{p_}")
                at_ev.append(ea1)
                pnt["rbcmm"] = [ea0, ea1]
                pnt["rbcc"] = [ea0, ea1]
                pnt["att"] = None
            y1_ev = []
            for o in range(NCH):
                bank = pb[o % 4]

                def mmo(e, o=o, bank=bank):
                    last = None
                    for c in range(NCH):
                        last = e.matmul(bank[:, 0:512],
                                        PB[:, c * 1024 + o * 128: c * 1024 + o * 128 + 128],
                                        ATT[:, c * 512:(c + 1) * 512],
                                        start=(c == 0), stop=(c == NCH - 1))
                    return last
                pv = y1_ev[o - 4] if o >= 4 else (at_ev, pnt.get("yb"))
                s_o = P.step("tensor", mmo, deps=(wo, at_ev[-1], pv), tag=f"l1a_o{b}_{nt}_{o}")
                dst = PCf[:, o * 512:(o + 1) * 512]
                e_o = P.step("vector",
                             (lambda bank=bank, dst=dst, o=o, t0=t0: lambda e:
                              e.scalar_tensor_tensor(dst, bank[:, 0:512], boc[:, o:o + 1],
                                                     XN[:, o * T_CORE + t0: o * T_CORE + t0 + 512],
                                                     ALU.add, ALU.add))(),
                             deps=(s_o, pnt.get("pcf")), tag=f"l1a_oe{b}_{nt}_{o}")
                y1_ev.append(e_o)
            x2res = emit_inner_ln(mk, f"l1a_ln2_{b}_{nt}", PCf, g2c, be2c, epscol,
                                  y1_ev, pa_off=0,
                                  out_bf=(PA, 8192), x2_hbm=(None, 0), pnt=pnt)
            pnt["qtc"] = x2res["pa_cons"]
            pnt["qb"] = y1_ev[-4:]
            pnt["yb"] = y1_ev[-4:]
            pnt["pcf"] = x2res["pcf_cons"]
            pnt["dq"] = x2res["done"]
            pnt["xq"] = x2res["done"]
            fin = emit_l1a_ffn(mk, b, nt, t0, x2res, x_in, x_out, pnt)
            out_final.append(fin["st"])
            pnt.update(fin["pnt"])
        prev_b = {"allc": out_final[-1], "banks": out_final[-1], "rowb": out_final[-1],
                  "dgc": out_final[-1], "ctxb": out_final[-1]}
    return out_final



def emit_inner_ln(mk, tag, ysrc, gcol, becol, epscol, y_done, pa_off, out_bf, x2_hbm, pnt):
    P = mk.P
    pb = mk.pb
    PA = mk.PA
    r_m = mk.ROWS[0:1, 1024:1536]
    r_t = mk.ROWS[0:1, 1536:2048]
    ybf = PA[:, 0:4096]
    ysq = PA[:, 4096:8192]
    outb, ob_off = out_bf
    hbm_ap, t0 = x2_hbm
    c1 = P.step("vector", (lambda: lambda e: e.tensor_copy(ybf, ysrc[:, 0:4096]))(),
                deps=(y_done, pnt.get(tag + "pa")), tag=tag + "_cast")
    c2 = P.step("scalar", lambda e: e.activation(ysq, ybf, AF.Square), deps=(c1,), tag=tag + "_sq")

    def mk_stats(srcb, bank):
        def f(e):
            mm = None
            for c in range(NCH):
                mm = e.matmul(bank[0:1, 0:512], mk.ONECB[:, 0:1], srcb[:, c * 512:(c + 1) * 512],
                              start=(c == 0), stop=(c == NCH - 1))
            return mm
        return f
    m1 = P.step("tensor", mk_stats(ybf, pb[6]), deps=(c1, pnt.get(tag + "b6")), tag=tag + "_s1")
    m2 = P.step("tensor", mk_stats(ysq, pb[7]), deps=(c2, pnt.get(tag + "b7")), tag=tag + "_s2")
    q1 = P.step("vector", lambda e: e.tensor_single_scalar(r_m, pb[6][0:1, 0:512], 1.0 / D, ALU.mult),
                deps=(m1, pnt.get(tag + "rows")), tag=tag + "_m")
    q2 = P.step("vector", lambda e: e.tensor_single_scalar(r_t, pb[7][0:1, 0:512], 1.0 / D, ALU.mult),
                deps=(m2, pnt.get(tag + "rows")), tag=tag + "_e2")
    q3a = P.step("vector", lambda e: e.tensor_mul(mk.ROWS[0:1, 0:512], r_m, r_m),
                 deps=(q1, pnt.get(tag + "ra")), tag=tag + "_msq")
    q3 = P.step("vector", lambda e: e.tensor_sub(r_t, r_t, mk.ROWS[0:1, 0:512]),
                deps=(q2, q3a), tag=tag + "_var")
    q5 = P.step("scalar", lambda e: e.activation(r_t, r_t, AF.Ln, bias=epscol[0:1, 0:1], scale=1.0),
                deps=(q3, mk.cdep(LN_EPS)), tag=tag + "_ln")
    q6 = P.step("scalar", lambda e: e.activation(mk.ROWS[0:1, 0:512], r_t, AF.Exp, scale=-0.5),
                deps=(q5, q3a), tag=tag + "_rstd")
    q7 = P.step("vector", lambda e: e.tensor_mul(r_t, r_m, mk.ROWS[0:1, 0:512]),
                deps=(q6, q1), tag=tag + "_mr")
    b1 = P.step("tensor", lambda e: e.matmul(pb[4][:, 0:512], mk.ONEF[0:1, 0:128],
                                             mk.ROWS[0:1, 0:512], start=True, stop=True),
                deps=(q6, pnt.get(tag + "b4")), tag=tag + "_rb")
    b2 = P.step("tensor", lambda e: e.matmul(pb[5][:, 0:512], mk.ONEF[0:1, 0:128], r_t,
                                             start=True, stop=True),
                deps=(q7, pnt.get(tag + "b5")), tag=tag + "_mrb")
    done = []
    pcf_cons = []
    pa_cons = []
    for c in range(NCH):
        ysl = ysrc[:, c * 512:(c + 1) * 512]
        u1 = P.step("vector", (lambda ysl=ysl: lambda e: e.tensor_mul(ysl, ysl, pb[4][:, 0:512]))(),
                    deps=(b1, m1, m2, done[-1] if done else None), tag=tag + f"_t1_{c}")
        u2 = P.step("vector", (lambda ysl=ysl: lambda e: e.tensor_sub(ysl, ysl, pb[5][:, 0:512]))(),
                    deps=(u1, b2), tag=tag + f"_t2_{c}")
        u3 = P.step("vector", (lambda ysl=ysl, c=c: lambda e: e.tensor_scalar(
            ysl, ysl, gcol[:, c:c + 1], becol[:, c:c + 1], ALU.mult, ALU.add))(),
            deps=(u2,), tag=tag + f"_x2_{c}")
        u4 = P.step("vector", (lambda ysl=ysl, c=c: lambda e: e.tensor_copy(
            outb[:, ob_off + c * 512: ob_off + (c + 1) * 512], ysl))(),
            deps=(u3, pnt.get(tag + "ob")), tag=tag + f"_x2b_{c}")
        done.append(u4)
        pcf_cons.append(u4)
        pa_cons.append(u4)
    std = None
    if hbm_ap is not None:
        std = P.dma("sync", (lambda: lambda e: e.dma_start(
            out=bass.AP(hbm_ap.tensor, t0, [[T_CORE, 128], [128 * T_CORE, NCH], [1, 512]]),
            in_=ysrc[:, 0:4096].rearrange("p (c t) -> p c t", t=512)))(),
            deps=(done,), tag=tag + "_st")
    pnt[tag + "b6"] = q1
    pnt[tag + "b7"] = q2
    pnt[tag + "b4"] = done[-1]
    pnt[tag + "b5"] = done[-1]
    pnt[tag + "rows"] = [b1, b2]
    pnt[tag + "ra"] = b1
    return {"done": done, "pcf_cons": pcf_cons + ([std] if std else []),
            "pa_cons": pa_cons, "st": std, "b4": done[-1]}


def emit_l1a_ffn(mk, b, nt, t0, x2res, x_in, x_out, pnt):
    P = mk.P
    pb = mk.pb
    PA, PB, PC, PD = mk.PA, mk.PB, mk.PC, mk.PD
    PCw = PC[:, 0:4096].bitcast(BF16)
    PCf = PC[:, 4096:8192]
    b1c = mk.col("l1a_b1")
    b2c = mk.col("l1a_b2")
    g3c = mk.col("l1a_g3")
    be3c = mk.col("l1a_be3")
    epscol = mk.const_col(LN_EPS)
    x2bf = (PA, 8192)
    a_ev = []
    w1lds = []
    for blk in range(4):
        w1 = mk.load_wT("l1a_w1T", PCw[:, 0:8192], 1024,
                        deps=(x2res["pcf_cons"], a_ev[-8:] if a_ev else pnt.get("w1f"),
                              pnt.get("pcw")),
                        col0=blk * 1024, rowstride=4096, tag=f"_{b}_{nt}_{blk}")
        w1lds.append(w1)
        for fcl in range(NCH):
            fc = blk * 8 + fcl
            bank = pb[fcl % 4]

            def mmh(e, fcl=fcl, bank=bank):
                last = None
                for c in range(NCH):
                    last = e.matmul(bank[:, 0:512],
                                    PCw[:, c * 1024 + fcl * 128: c * 1024 + fcl * 128 + 128],
                                    PA[:, 8192 + c * 512: 8192 + (c + 1) * 512],
                                    start=(c == 0), stop=(c == NCH - 1))
                return last
            pv = a_ev[-4] if len(a_ev) >= 4 else pnt.get("hb")
            s_h = P.step("tensor", mmh, deps=(w1, x2res["pa_cons"], pv),
                         tag=f"l1a_h{b}_{nt}_{fc}")
            dst = PB[:, fc * 512:(fc + 1) * 512]
            e_h = P.step("scalar", (lambda bank=bank, dst=dst, fc=fc: lambda e:
                                    e.activation(dst, bank[:, 0:512], AF.Relu,
                                                 bias=b1c[:, fc:fc + 1], scale=1.0))(),
                         deps=(s_h, pnt.get("ab")), tag=f"l1a_relu{b}_{nt}_{fc}")
            a_ev.append(e_h)
    w2lds = []
    mm_h2 = {o: [] for o in range(NCH)}
    for blk in range(4):
        w2 = mk.load_wT("l1a_w2T", PCw[:, 0:8192], 1024,
                        deps=(a_ev, w1lds[-1]) if blk == 0 else ([w2lds[-1]] + mm_h2[NCH - 1][-1:],),
                        col0=0, rowstride=1024, base=blk * 1024 * 1024, tag=f"_{b}_{nt}w{blk}")
        w2lds.append(w2)
        for o in range(NCH):
            bank = pb[o]

            def mmy(e, blk=blk, o=o, bank=bank):
                last = None
                for fcl in range(NCH):
                    fc = blk * 8 + fcl
                    last = e.matmul(bank[:, 0:512],
                                    PCw[:, fcl * 1024 + o * 128: fcl * 1024 + o * 128 + 128],
                                    PB[:, fc * 512:(fc + 1) * 512],
                                    start=(fc == 0), stop=(fc == 31))
                return last
            deps = (w2, a_ev[-1] if blk == 0 else mm_h2[o][-1],
                    (x2res["done"] if blk == 0 and o < 4 else None),
                    (pnt.get("y2b") if blk == 0 else None))
            s_y = P.step("tensor", mmy, deps=deps, tag=f"l1a_y2{b}_{nt}_{blk}_{o}")
            mm_h2[o].append(s_y)
    y2_ev = []
    for o in range(NCH):
        dst = PCf[:, o * 512:(o + 1) * 512]
        e_y = P.step("vector",
                     (lambda o=o, dst=dst: lambda e: e.scalar_tensor_tensor(
                         dst, pb[o][:, 0:512], b2c[:, o:o + 1], dst, ALU.add, ALU.add))(),
                     deps=(mm_h2[o][-1], x2res["pcf_cons"], y2_ev[-1] if y2_ev else None),
                     tag=f"l1a_y2e{b}_{nt}_{o}")
        y2_ev.append(e_y)
    ln3 = emit_inner_ln(mk, f"l1a_ln3_{b}_{nt}", PCf, g3c, be3c, epscol, y2_ev,
                        pa_off=0, out_bf=(PA, 12288), x2_hbm=(None, 0), pnt=pnt)
    XO = PD[:, 2336:6432]
    ldx = P.dma("sync",
                (lambda t0=t0: lambda e: e.dma_start(
                    out=XO.rearrange("p (c t) -> p c t", t=512),
                    in_=bass.AP(x_in.tensor, t0, [[T_CORE, 128], [128 * T_CORE, NCH], [1, 512]])))(),
                deps=(pnt.get("xoc"), ln3["done"][0:1]), tag=f"l1a_ldx{b}_{nt}")
    o_ev = []
    for c in range(NCH):
        e_o = P.step("vector", (lambda c=c: lambda e: e.tensor_add(
            XO[:, c * 512:(c + 1) * 512], PCf[:, c * 512:(c + 1) * 512],
            XO[:, c * 512:(c + 1) * 512]))(),
            deps=(ldx, ln3["done"][c], o_ev[-1] if o_ev else None), tag=f"l1a_xo{b}_{nt}_{c}")
        o_ev.append(e_o)
    st = P.dma("sync",
               (lambda t0=t0: lambda e: e.dma_start(
                   out=bass.AP(x_out.tensor, t0, [[T_CORE, 128], [128 * T_CORE, NCH], [1, 512]]),
                   in_=XO.rearrange("p (c t) -> p c t", t=512)))(),
               deps=(o_ev,), tag=f"l1a_st{b}_{nt}")
    newpnt = {"w1f": [st], "pcw": mm_h2[NCH - 1][-1], "pbw": mm_h2[NCH - 1][-1],
              "hb": y2_ev[-1], "ab": mm_h2[0][-1] if mm_h2[0] else None,
              "y2b": ln3["done"][-1], "xoc": st, "qtc": st, "qb": [st], "dq": [st], "dq2": st,
              "xq": st, "tp1": st, "tp2": st, "tc1": st, "tc2": st, "stc": st, "bcq": st,
              "kpq": st, "u0": st, "u1": st, "rrow": st, "denb": st, "rbcmm": st, "rbcc": st,
              "att": st, "yb": [st], "pcf": [st]}
    return {"st": st, "pnt": newpnt}



def emit_l1_attn_a(mk, x_in, x_out, prev=()):
    P = mk.P
    pb = mk.pb
    XN, PA, PB, PC, PD = mk.XN, mk.PA, mk.PB, mk.PC, mk.PD
    XNs = 8 * T_CORE
    ln_done = emit_ln(mk, x_in, deps=prev)
    lnbar = ln_done[-1]
    RATIO = 1.0 / 16.0
    EPSK = 1e-4
    bkc = mk.col("l1a_bk"); bqc = mk.col("l1a_bq"); boc = mk.col("l1a_bo")
    g2c = mk.col("l1a_g2"); be2c = mk.col("l1a_be2")
    g3c = mk.col("l1a_g3"); be3c = mk.col("l1a_be3")
    b1c = mk.col("l1a_b1")
    b2c = mk.col("l1a_b2")
    epscol = mk.const_col(LN_EPS)
    PCw = PC[:, 0:4096].bitcast(BF16)
    PCw2 = PC[:, 4096:8192].bitcast(BF16)
    PCf = PC[:, 4096:8192]
    DG = PD[:, 0:256]
    CTX = PD[:, 256:1296].bitcast(BF16)
    PRJ = PD[:, 1296:1424].bitcast(BF16)
    EHB = PD[:, 1424:1426].bitcast(BF16)
    KP = PD[:, 1440:1568].bitcast(BF16)
    QPT = PD[:, 1568:2080].bitcast(BF16)
    K2Q = PD[:, 2080:2336].bitcast(BF16)
    ATT = PD[:, 4384:6432].bitcast(BF16)
    RB1 = PD[:, 6432:6944]
    DGQ_BASE = 6944
    MX = PD[:, 7008:7024]
    NBC = PD[:, 7024:7025]
    r_a = mk.ROWS[0:1, 0:512]
    r_b = mk.ROWS[0:1, 512:1024]
    rr1 = PD[0:1, 7040:7056]
    rr2 = PD[0:1, 7064:7065]

    PRJF = PD[:, 7168:7424]
    p1 = P.dma("sync", lambda e: e.dma_start(out=PRJF[0:64, :], in_=mk.dram["l1a_projT"]),
               deps=(prev,), tag="l1a_pld")
    p1b = P.dma("sync", lambda e: e.dma_start(out=PRJF[64:128, :], in_=mk.dram["l1a_projT"]),
                deps=(prev,), tag="l1a_pld2")
    p2a = P.step("vector", lambda e: e.tensor_scalar(PRJ[0:64, :], PRJF[0:64, :],
                                                     float(64.0 ** (-0.25)), None, ALU.mult),
                 deps=(p1, prev), tag="l1a_pscale")
    p2b = P.step("vector", lambda e: e.tensor_scalar(PRJ[64:128, :], PRJF[64:128, :],
                                                     float(64.0 ** (-0.25)), None, ALU.mult),
                 deps=(p1b, prev), tag="l1a_pscale2")
    p2 = [p2a, p2b]
    e1_ = P.step("vector", lambda e: e.memset(EHB[0:64, 0:1], 1.0), deps=(prev,), tag="l1a_eh0")
    e2_ = P.step("vector", lambda e: e.memset(EHB[64:128, 0:1], 0.0), deps=(prev,), tag="l1a_eh1")
    e3_ = P.step("vector", lambda e: e.memset(EHB[0:64, 1:2], 0.0), deps=(prev,), tag="l1a_eh2")
    e4_ = P.step("vector", lambda e: e.memset(EHB[64:128, 1:2], 1.0), deps=(prev,), tag="l1a_eh3")
    ehb_done = [e1_, e2_, e3_, e4_]
    out_final = []
    prev_b = {}
    for b in range(B_PER_CORE):
        tb = b * T_SEQ
        ldr = P.dma("sync", lambda e: e.dma_start(out=PCf[0:1, 0:1024], in_=mk.dram["r_l1a_bv"]),
                    deps=(prev_b.get("allc"), prev, lnbar), tag=f"l1a_bvld{b}")
        rv = P.step("vector", lambda e: e.tensor_copy(mk.ROWB[0:1, 0:1024], PCf[0:1, 0:1024]),
                    deps=(ldr, prev_b.get("rowb")), tag=f"l1a_bvc{b}")
        wv = mk.load_wT("l1a_wvT", PCw[:, 0:8192], 1024, deps=(prev_b.get("allc"), lnbar, prev),
                        tag=f"_{b}v")
        v_ev = []
        for j in range(16):
            for h2 in range(2):
                bank = pb[(j * 2 + h2) % 4]

                def mmv(e, j=j, h2=h2, bank=bank, tb=tb):
                    for c in range(NCH):
                        e.matmul(bank[:, 0:512],
                                 XN[:, c * T_CORE + tb + j * 128: c * T_CORE + tb + j * 128 + 128],
                                 PCw[:, c * 1024 + h2 * 512: c * 1024 + h2 * 512 + 512],
                                 start=(c == 0), stop=False)
                    return e.matmul(bank[:, 0:512], mk.ONEB[0:1, 0:128],
                                    mk.ROWB[0:1, h2 * 512: h2 * 512 + 512],
                                    start=False, stop=True)
                pv = v_ev[-4] if len(v_ev) >= 4 else prev_b.get("banks")
                s_mm = P.step("tensor", mmv, deps=(wv, rv, ln_done[(tb + j * 128) // 512], pv),
                              tag=f"l1a_v{b}_{j}_{h2}")
                dst = PB[:, j * 1024 + h2 * 512: j * 1024 + h2 * 512 + 512]
                ev = P.step("vector", (lambda bank=bank, dst=dst: lambda e:
                                       e.tensor_copy(dst, bank[:, 0:512]))(),
                            deps=(s_mm,), tag=f"l1a_vev{b}_{j}_{h2}")
                v_ev.append(ev)
        v_done = v_ev[-1]
        wk = mk.load_wT("l1a_wkT", PCw2[:, 0:8192], 1024, deps=(v_ev[-8:], rv, lnbar), tag=f"_{b}k")
        k_ev = []
        for o in range(NCH):
            for nt in range(4):
                bank = pb[(o * 4 + nt) % 4]

                def mmk(e, o=o, nt=nt, bank=bank, tb=tb):
                    last = None
                    for c in range(NCH):
                        last = e.matmul(bank[:, 0:512],
                                        PCw2[:, c * 1024 + o * 128: c * 1024 + o * 128 + 128],
                                        XN[:, c * T_CORE + tb + nt * 512: c * T_CORE + tb + nt * 512 + 512],
                                        start=(c == 0), stop=(c == NCH - 1))
                    return last
                pv = k_ev[-4] if len(k_ev) >= 4 else v_ev[-4:]
                s_mm = P.step("tensor", mmk, deps=(wk, ln_done[(tb + nt * 512) // 512], pv),
                              tag=f"l1a_k{b}_{o}_{nt}")
                dst = PA[:, o * 2048 + nt * 512: o * 2048 + nt * 512 + 512]
                ev = P.step("vector", (lambda bank=bank, dst=dst, o=o: lambda e:
                                       e.tensor_scalar(dst, bank[:, 0:512], bkc[:, o:o + 1], None,
                                                       ALU.add))(),
                            deps=(s_mm,), tag=f"l1a_kev{b}_{o}_{nt}")
                k_ev.append(ev)
        k_done = k_ev[-1]
        dg_ev = []
        K2 = PCw[:, 0:512]
        prev_k2 = [v_done] + k_ev[-2:]
        for c in range(NCH):
            for nt in range(4):
                sq = P.step("scalar", (lambda c=c, nt=nt: lambda e: e.activation(
                    K2, PA[:, c * 2048 + nt * 512: c * 2048 + nt * 512 + 512], AF.Square))(),
                    deps=(k_done, prev_k2), tag=f"l1a_k2{b}_{c}_{nt}")

                def mmd(e, c=c, nt=nt):
                    last = None
                    for jj in range(4):
                        last = e.matmul(pb[4][:, 4 * jj: 4 * jj + 2],
                                        K2[:, jj * 128:(jj + 1) * 128], EHB[:, 0:2],
                                        start=True, stop=True)
                    return last
                s_d = P.step("tensor", mmd, deps=(sq, ehb_done, dg_ev[-1] if dg_ev else None),
                             tag=f"l1a_dg{b}_{c}_{nt}")
                dgv = bass.AP(PD, (nt * 4) * 16 + 2 * c, [[8192, 128], [16, 4], [1, 2]])
                e_d = P.step("vector", (lambda dgv=dgv: lambda e: e.tensor_copy(
                    dgv, pb[4][:, 0:16].rearrange("p (j x) -> p j x", x=4)[:, :, 0:2]))(),
                    deps=(s_d, prev_b.get("dgc")), tag=f"l1a_dge{b}_{c}_{nt}")
                dg_ev.append(e_d)
                prev_k2 = [s_d]
        dg_done = dg_ev[-1]
        ctx_ev = []
        ph = {"xp": dg_ev[-1], "mx": None, "ctxb": prev_b.get("ctxb")}
        for h in range(16):
            hb = (h % 2) * 64
            hc = h // 2
            mx_ev = []
            for j in range(16):
                bank = pb[j % 2]

                def mmx(e, hb=hb, hc=hc, j=j, bank=bank):
                    return e.matmul(bank[:, 0:256],
                                    PA[hb:hb + 64, hc * 2048 + j * 128: hc * 2048 + j * 128 + 128],
                                    PRJ[hb:hb + 64, 0:256], start=True, stop=True)
                pv = mx_ev[-2] if len(mx_ev) >= 2 else ph.get("xp")
                s_x = P.step("tensor", mmx, deps=(k_done, p2, pv), tag=f"l1a_xk{b}_{h}_{j}")
                m1 = P.step("vector", (lambda j=j, bank=bank: lambda e: e.reduce_max(
                    MX[:, j:j + 1], bank[:, 0:256], AX.X))(),
                    deps=(s_x, ph.get("mx")), tag=f"l1a_mxk{b}_{h}_{j}")
                mx_ev.append(m1)
            gk1 = P.step("gpsimd", lambda e: e.reduce_max(rr1, MX[:, 0:16], AX.C),
                         deps=(mx_ev, ph.get("g1")), tag=f"l1a_gk1{b}_{h}")
            gk2 = P.step("vector", lambda e: e.reduce_max(rr2, rr1, AX.X, negate=True),
                         deps=(gk1, ph.get("g2")), tag=f"l1a_gk2{b}_{h}")
            gk3 = P.step("tensor", lambda e: e.matmul(pb[0][:, 8:9], mk.ONEF[0:1, 0:128], rr2,
                                                      start=True, stop=True),
                         deps=(gk2, ph.get("g3"), mx_ev[-2:]), tag=f"l1a_gk3{b}_{h}")
            gk4 = P.step("vector", lambda e: e.tensor_copy(NBC, pb[0][:, 8:9]),
                         deps=(gk3, ph.get("g4")), tag=f"l1a_gk4{b}_{h}")
            kp_ev = []
            for j in range(16):
                bank = pb[2 + j % 2]

                def mmx2(e, hb=hb, hc=hc, j=j, bank=bank):
                    return e.matmul(bank[:, 0:256],
                                    PA[hb:hb + 64, hc * 2048 + j * 128: hc * 2048 + j * 128 + 128],
                                    PRJ[hb:hb + 64, 0:256], start=True, stop=True)
                pv = kp_ev[-2] if len(kp_ev) >= 2 else (mx_ev[-1], ph.get("kp2"))
                s_x2 = P.step("tensor", mmx2, deps=(k_done, p2, pv, gk4),
                              tag=f"l1a_xk2{b}_{h}_{j}")
                bcol = PD[:, 7026 + (j % 2): 7027 + (j % 2)]
                dgcol = bass.AP(PD, j * 16 + h, [[8192, 128], [1, 1]])
                c00625 = mk.const_col(-0.0625)
                b_c = P.step("vector", (lambda bcol=bcol, dgcol=dgcol: lambda e:
                                        e.scalar_tensor_tensor(bcol, dgcol, c00625[:, 0:1], NBC,
                                                               ALU.mult, ALU.add))(),
                             deps=(dg_done, gk4, kp_ev[-2] if len(kp_ev) >= 2 else ph.get("bc"),
                                   mk.cdep(-0.0625)),
                             tag=f"l1a_bck{b}_{h}_{j}")
                exk = P.step("scalar", (lambda bank=bank, bcol=bcol: lambda e:
                                        e.activation(KP, bank[:, 0:256], AF.Exp,
                                                     bias=bcol[:, 0:1], scale=1.0))(),
                             deps=(s_x2, b_c, ph.get("kpc")), tag=f"l1a_ek{b}_{h}_{j}")
                tsk = P.step("vector", lambda e: e.tensor_scalar(KP, KP, RATIO, RATIO * EPSK,
                                                                 ALU.mult, ALU.add),
                             deps=(exk,), tag=f"l1a_tk{b}_{h}_{j}")
                def mmc(e, h=h, j=j):
                    last = None
                    for mc in range(2):
                        e.matmul(pb[4 + mc][0:128, 0:64],
                                 KP[:, mc * 128:(mc + 1) * 128],
                                 PB[:, j * 1024 + h * 64: j * 1024 + h * 64 + 64],
                                 start=(j == 0), stop=(j == 15))
                        last = e.matmul(pb[6 + mc][0:128, 0:1],
                                        KP[:, mc * 128:(mc + 1) * 128], mk.ONECB[:, 0:1],
                                        start=(j == 0), stop=(j == 15))
                    return last
                s_c = P.step("tensor", mmc, deps=(tsk, v_done, ph.get("ctxmm")), tag=f"l1a_ctx{b}_{h}_{j}")
                kp_ev.append(s_c)
                ph["kpc"] = s_c
            ec1 = P.step("vector", (lambda h=h: lambda e: e.tensor_copy(
                CTX[:, h * 130: h * 130 + 64], pb[4][:, 0:64]))(),
                deps=(kp_ev[-1], ph.get("ctxb")), tag=f"l1a_ctxe1{b}_{h}")
            ec1b = P.step("vector", (lambda h=h: lambda e: e.tensor_copy(
                CTX[:, h * 130 + 64: h * 130 + 65], pb[6][:, 0:1]))(),
                deps=(kp_ev[-1], ec1), tag=f"l1a_ctxe1b{b}_{h}")
            ec2 = P.step("vector", (lambda h=h: lambda e: e.tensor_copy(
                CTX[:, h * 130 + 65: h * 130 + 129], pb[5][:, 0:64]))(),
                deps=(kp_ev[-1], ec1b), tag=f"l1a_ctxe2{b}_{h}")
            ec2b = P.step("vector", (lambda h=h: lambda e: e.tensor_copy(
                CTX[:, h * 130 + 129: h * 130 + 130], pb[7][:, 0:1]))(),
                deps=(kp_ev[-1], ec2), tag=f"l1a_ctxe2b{b}_{h}")
            ec2 = ec2b
            ctx_ev.append(ec2)
            ph["ctxmm"] = ec2
            ph["xp"] = kp_ev[-1]
            ph["mx"] = gk1
            ph["g1"] = gk2
            ph["g2"] = gk3
            ph["g3"] = gk4
            ph["g4"] = exk
            ph["bc"] = exk
            ph["kp2"] = kp_ev[-1]
        ctx_done = ctx_ev[-1]
        prev_b = dict(prev_b)
        prev_b["phaseA"] = ctx_done
        QT = PD[:, 2336:4384].bitcast(BF16)
        pnt = {}
        for nt in range(4):
            t0 = tb + nt * 512
            tt = t0 // 512
            wq = mk.load_wT("l1a_wqT", PCw[:, 0:8192], 1024,
                            deps=(k_ev[-4:], dg_ev[-2:], ph["xp"], ctx_done, pnt.get("pcw")),
                            tag=f"_{b}q{nt}")
            wo = mk.load_wT("l1a_woT", PB[:, 0:8192], 1024,
                            deps=(v_ev[-4:], ctx_done, dg_ev[-1:], pnt.get("pbw")),
                            tag=f"_{b}o{nt}")
            q_ev = []
            for o in range(NCH):
                bank = pb[o % 4]

                def mmq(e, o=o, t0=t0, bank=bank):
                    last = None
                    for c in range(NCH):
                        last = e.matmul(bank[:, 0:512],
                                        PCw[:, c * 1024 + o * 128: c * 1024 + o * 128 + 128],
                                        XN[:, c * T_CORE + t0: c * T_CORE + t0 + 512],
                                        start=(c == 0), stop=(c == NCH - 1))
                    return last
                pv = q_ev[o - 4] if o >= 4 else (pnt.get("qb"), ctx_done)
                s_q = P.step("tensor", mmq, deps=(wq, ln_done[tt], pv), tag=f"l1a_q{b}_{nt}_{o}")
                dst = QT[:, o * 512:(o + 1) * 512]
                ev = P.step("vector", (lambda bank=bank, dst=dst, o=o: lambda e:
                                       e.tensor_scalar(dst, bank[:, 0:512], bqc[:, o:o + 1], None,
                                                       ALU.add))(),
                            deps=(s_q, pnt.get("qtc")), tag=f"l1a_qe{b}_{nt}_{o}")
                q_ev.append(ev)
            q_done = q_ev[-1]
            dq_ev = []
            for c in range(NCH):
                sq = P.step("scalar", (lambda c=c: lambda e: e.activation(
                    K2Q, QT[:, c * 512:(c + 1) * 512], AF.Square))(),
                    deps=(q_done, dq_ev[-1] if dq_ev else pnt.get("dq")),
                    tag=f"l1a_q2{b}_{nt}_{c}")

                def mmdq(e, c=c):
                    last = None
                    for jj in range(4):
                        last = e.matmul(pb[4][:, 4 * jj: 4 * jj + 2],
                                        K2Q[:, jj * 128:(jj + 1) * 128], EHB[:, 0:2],
                                        start=True, stop=True)
                    return last
                s_d = P.step("tensor", mmdq, deps=(sq, ehb_done, dq_ev[-1] if dq_ev else pnt.get("dq2")),
                             tag=f"l1a_dq{b}_{nt}_{c}")
                dqv = bass.AP(PD, DGQ_BASE + 2 * c, [[8192, 128], [16, 4], [1, 2]])
                e_d = P.step("vector", (lambda dqv=dqv: lambda e: e.tensor_copy(
                    dqv, pb[4][:, 0:16].rearrange("p (j x) -> p j x", x=4)[:, :, 0:2]))(),
                    deps=(s_d, pnt.get("dqc")), tag=f"l1a_dqe{b}_{nt}_{c}")
                dq_ev.append(e_d)
            dq_done = dq_ev[-1]
            pnt["dq"] = dq_ev[-2:]
            pnt["dq2"] = dq_ev[-1]
            pnt["dqc"] = None
            at_ev = []
            for p_ in range(8):
                rows = []
                for hh in range(2):
                    h = 2 * p_ + hh
                    hb = (h % 2) * 64
                    hc = h // 2
                    u_parts = []
                    den_parts = []
                    for jj in range(4):
                        bank = pb[jj % 2]

                        def mmxq(e, hb=hb, hc=hc, jj=jj, bank=bank):
                            return e.matmul(bank[:, 0:256],
                                            QT[hb:hb + 64, hc * 512 + jj * 128: hc * 512 + jj * 128 + 128],
                                            PRJ[hb:hb + 64, 0:256], start=True, stop=True)
                        pv = u_parts[-1] if u_parts else (at_ev[-1] if at_ev else (dq_done, pnt.get("xq")))
                        s_x = P.step("tensor", mmxq, deps=(q_done, p2, pv), tag=f"l1a_xq{b}_{nt}_{h}_{jj}")
                        stc = PD[:, 7028:7029]
                        m1 = P.step("vector", (lambda bank=bank, stc=stc: lambda e:
                                               e.reduce_max(stc, bank[:, 0:256], AX.X, negate=True))(),
                                    deps=(s_x, pnt.get("stc")), tag=f"l1a_sq{b}_{nt}_{h}_{jj}")
                        bcol = PD[:, 7030:7031]
                        dqcol = bass.AP(PD, DGQ_BASE + jj * 16 + h, [[8192, 128], [1, 1]])
                        c00625 = mk.const_col(-0.0625)
                        b_c = P.step("vector", (lambda bcol=bcol, dqcol=dqcol, stc=stc: lambda e:
                                                e.scalar_tensor_tensor(bcol, dqcol, c00625[:, 0:1], stc,
                                                                       ALU.mult, ALU.add))(),
                                     deps=(dq_done, m1, pnt.get("bcq"), mk.cdep(-0.0625)),
                                     tag=f"l1a_bq{b}_{nt}_{h}_{jj}")
                        exq = P.step("scalar", (lambda bank=bank, bcol=bcol: lambda e:
                                                e.activation(KP, bank[:, 0:256], AF.Exp,
                                                             bias=bcol[:, 0:1], scale=1.0))(),
                                     deps=(s_x, b_c, pnt.get("kpq")), tag=f"l1a_eq{b}_{nt}_{h}_{jj}")
                        tsq = P.step("vector", lambda e: e.tensor_scalar(KP, KP, RATIO, RATIO * EPSK,
                                                                         ALU.mult, ALU.add),
                                     deps=(exq,), tag=f"l1a_tq{b}_{nt}_{h}_{jj}")
                        pbB = pb[2][:, 0:64].bitcast(BF16)
                        pbB2 = pb[2][:, 64:128].bitcast(BF16)
                        tp1 = P.step("tensor", (lambda pbB=pbB: lambda e: e.transpose(
                            pbB[:, 0:128], KP[:, 0:128], mk.IDB[:]))(),
                            deps=(tsq, pnt.get("tp1")), tag=f"l1a_tp1{b}_{nt}_{h}_{jj}")
                        tc1 = P.step("vector", (lambda pbB=pbB, jj=jj: lambda e: e.tensor_copy(
                            QPT[:, jj * 128:(jj + 1) * 128], pbB[:, 0:128]))(),
                            deps=(tp1, pnt.get("tc1")), tag=f"l1a_tc1{b}_{nt}_{h}_{jj}")
                        tp2 = P.step("tensor", (lambda pbB2=pbB2: lambda e: e.transpose(
                            pbB2[:, 0:128], KP[:, 128:256], mk.IDB[:]))(),
                            deps=(tsq, tc1, pnt.get("tp2")), tag=f"l1a_tp2{b}_{nt}_{h}_{jj}")
                        tc2 = P.step("vector", (lambda pbB2=pbB2, jj=jj: lambda e: e.tensor_copy(
                            QPT[:, 512 + jj * 128: 512 + (jj + 1) * 128], pbB2[:, 0:128]))(),
                            deps=(tp2, pnt.get("tc2")), tag=f"l1a_tc2{b}_{nt}_{h}_{jj}")
                        u_parts.append(tc2)
                    ubank = pb[5] if hh == 0 else pb[3]

                    def mmu(e, h=h, ubank=ubank):
                        e.matmul(ubank[0:64, 0:512], CTX[:, h * 130: h * 130 + 64],
                                 QPT[:, 0:512], start=True, stop=False)
                        return e.matmul(ubank[0:64, 0:512], CTX[:, h * 130 + 65: h * 130 + 129],
                                        QPT[:, 512:1024], start=False, stop=True)
                    s_u = P.step("tensor", mmu, deps=(u_parts, ctx_done, pnt.get("u" + str(hh))),
                                 tag=f"l1a_U{b}_{nt}_{h}")

                    def mmden(e, h=h):
                        e.matmul(pb[6][0:1, 0:512], CTX[:, h * 130 + 64: h * 130 + 65],
                                 QPT[:, 0:512], start=True, stop=False)
                        return e.matmul(pb[6][0:1, 0:512], CTX[:, h * 130 + 129: h * 130 + 130],
                                        QPT[:, 512:1024], start=False, stop=True)
                    s_den = P.step("tensor", mmden, deps=(u_parts, ctx_done, rows, pnt.get("denb")),
                                   tag=f"l1a_den{b}_{nt}_{h}")
                    rrow = r_a if hh == 0 else r_b
                    z1 = P.step("vector", (lambda rrow=rrow: lambda e: e.tensor_copy(
                        rrow, pb[6][0:1, 0:512]))(),
                        deps=(s_den, pnt.get("rrow")), tag=f"l1a_z1{b}_{nt}_{h}")
                    z2 = P.step("scalar", (lambda rrow=rrow: lambda e: e.activation(
                        rrow, rrow, AF.Ln, scale=1.0))(), deps=(z1,), tag=f"l1a_z2{b}_{nt}_{h}")
                    z3 = P.step("scalar", (lambda rrow=rrow: lambda e: e.activation(
                        rrow, rrow, AF.Exp, scale=-1.0))(), deps=(z2,), tag=f"l1a_z3{b}_{nt}_{h}")
                    rows.append(z3)
                    pnt["u" + str(hh)] = s_u
                    pnt["rrow"] = z2
                    pnt["denb"] = z1
                    pnt["xq"] = exq
                    pnt["kpq"] = tsq
                    pnt["tp1"] = tc1
                    pnt["tp2"] = tc2
                    pnt["tc1"] = tp2
                    pnt["tc2"] = s_u
                    pnt["stc"] = b_c
                    pnt["bcq"] = exq
                    if hh == 0:
                        u0 = s_u
                    else:
                        u1 = s_u
                def mmr(e):
                    e.matmul(pb[7][:, 0:512], mk.EH[0:1, 0:128], r_a, start=True, stop=False)
                    return e.matmul(pb[7][:, 0:512], mk.EH[0:1, 128:256], r_b, start=False, stop=True)
                s_r = P.step("tensor", mmr, deps=(rows, pnt.get("rbcmm")), tag=f"l1a_rbc{b}_{nt}_{p_}")
                c_r = P.step("vector", lambda e: e.tensor_copy(RB1, pb[7][:, 0:512]),
                             deps=(s_r, pnt.get("rbcc")), tag=f"l1a_rbce{b}_{nt}_{p_}")
                dstA = ATT[:, p_ * 512:(p_ + 1) * 512]
                ea0 = P.step("vector", (lambda dstA=dstA: lambda e: e.tensor_mul(
                    dstA[0:64, :], pb[5][0:64, 0:512], RB1[0:64, :]))(),
                    deps=(u0, c_r, pnt.get("att")), tag=f"l1a_a0{b}_{nt}_{p_}")
                ea1 = P.step("vector", (lambda dstA=dstA: lambda e: e.tensor_mul(
                    dstA[64:128, :], pb[3][0:64, 0:512], RB1[64:128, :]))(),
                    deps=(u1, c_r, ea0), tag=f"l1a_a1{b}_{nt}_{p_}")
                at_ev.append(ea1)
                pnt["rbcmm"] = [ea0, ea1]
                pnt["rbcc"] = [ea0, ea1]
                pnt["att"] = None
            y1_ev = []
            for o in range(NCH):
                bank = pb[o % 4]

                def mmo(e, o=o, bank=bank):
                    last = None
                    for c in range(NCH):
                        last = e.matmul(bank[:, 0:512],
                                        PB[:, c * 1024 + o * 128: c * 1024 + o * 128 + 128],
                                        ATT[:, c * 512:(c + 1) * 512],
                                        start=(c == 0), stop=(c == NCH - 1))
                    return last
                pv = y1_ev[o - 4] if o >= 4 else (at_ev, pnt.get("yb"))
                s_o = P.step("tensor", mmo, deps=(wo, at_ev[-1], pv), tag=f"l1a_o{b}_{nt}_{o}")
                dst = PCf[:, o * 512:(o + 1) * 512]
                e_o = P.step("vector",
                             (lambda bank=bank, dst=dst, o=o, t0=t0: lambda e:
                              e.scalar_tensor_tensor(dst, bank[:, 0:512], boc[:, o:o + 1],
                                                     XN[:, o * T_CORE + t0: o * T_CORE + t0 + 512],
                                                     ALU.add, ALU.add))(),
                             deps=(s_o, pnt.get("pcf")), tag=f"l1a_oe{b}_{nt}_{o}")
                y1_ev.append(e_o)
            x2res = emit_inner_ln(mk, f"l1a_ln2_{b}_{nt}", PCf, g2c, be2c, epscol,
                                  y1_ev, pa_off=0,
                                  out_bf=(PA, 8192), x2_hbm=(None, 0), pnt=pnt)
            pnt["qtc"] = x2res["pa_cons"]
            pnt["qb"] = y1_ev[-4:]
            pnt["yb"] = y1_ev[-4:]
            pnt["pcf"] = x2res["pcf_cons"]
            pnt["dq"] = x2res["done"]
            pnt["xq"] = x2res["done"]
            fin = emit_l1a_ffn(mk, b, nt, t0, x2res, x_in, x_out, pnt)
            out_final.append(fin["st"])
            pnt.update(fin["pnt"])
        prev_b = {"allc": out_final[-1], "banks": out_final[-1], "rowb": out_final[-1],
                  "dgc": out_final[-1], "ctxb": out_final[-1]}
    return out_final



def emit_inner_ln(mk, tag, ysrc, gcol, becol, epscol, y_done, pa_off, out_bf, x2_hbm, pnt):
    P = mk.P
    pb = mk.pb
    PA = mk.PA
    r_m = mk.ROWS[0:1, 1024:1536]
    r_t = mk.ROWS[0:1, 1536:2048]
    ybf = PA[:, 0:4096]
    ysq = PA[:, 4096:8192]
    outb, ob_off = out_bf
    hbm_ap, t0 = x2_hbm
    c1 = P.step("vector", (lambda: lambda e: e.tensor_copy(ybf, ysrc[:, 0:4096]))(),
                deps=(y_done, pnt.get(tag + "pa")), tag=tag + "_cast")
    c2 = P.step("scalar", lambda e: e.activation(ysq, ybf, AF.Square), deps=(c1,), tag=tag + "_sq")

    def mk_stats(srcb, bank):
        def f(e):
            mm = None
            for c in range(NCH):
                mm = e.matmul(bank[0:1, 0:512], mk.ONECB[:, 0:1], srcb[:, c * 512:(c + 1) * 512],
                              start=(c == 0), stop=(c == NCH - 1))
            return mm
        return f
    m1 = P.step("tensor", mk_stats(ybf, pb[6]), deps=(c1, pnt.get(tag + "b6")), tag=tag + "_s1")
    m2 = P.step("tensor", mk_stats(ysq, pb[7]), deps=(c2, pnt.get(tag + "b7")), tag=tag + "_s2")
    q1 = P.step("vector", lambda e: e.tensor_single_scalar(r_m, pb[6][0:1, 0:512], 1.0 / D, ALU.mult),
                deps=(m1, pnt.get(tag + "rows")), tag=tag + "_m")
    q2 = P.step("vector", lambda e: e.tensor_single_scalar(r_t, pb[7][0:1, 0:512], 1.0 / D, ALU.mult),
                deps=(m2, pnt.get(tag + "rows")), tag=tag + "_e2")
    q3a = P.step("vector", lambda e: e.tensor_mul(mk.ROWS[0:1, 0:512], r_m, r_m),
                 deps=(q1, pnt.get(tag + "ra")), tag=tag + "_msq")
    q3 = P.step("vector", lambda e: e.tensor_sub(r_t, r_t, mk.ROWS[0:1, 0:512]),
                deps=(q2, q3a), tag=tag + "_var")
    q5 = P.step("scalar", lambda e: e.activation(r_t, r_t, AF.Ln, bias=epscol[0:1, 0:1], scale=1.0),
                deps=(q3, mk.cdep(LN_EPS)), tag=tag + "_ln")
    q6 = P.step("scalar", lambda e: e.activation(mk.ROWS[0:1, 0:512], r_t, AF.Exp, scale=-0.5),
                deps=(q5, q3a), tag=tag + "_rstd")
    q7 = P.step("vector", lambda e: e.tensor_mul(r_t, r_m, mk.ROWS[0:1, 0:512]),
                deps=(q6, q1), tag=tag + "_mr")
    b1 = P.step("tensor", lambda e: e.matmul(pb[4][:, 0:512], mk.ONEF[0:1, 0:128],
                                             mk.ROWS[0:1, 0:512], start=True, stop=True),
                deps=(q6, pnt.get(tag + "b4")), tag=tag + "_rb")
    b2 = P.step("tensor", lambda e: e.matmul(pb[5][:, 0:512], mk.ONEF[0:1, 0:128], r_t,
                                             start=True, stop=True),
                deps=(q7, pnt.get(tag + "b5")), tag=tag + "_mrb")
    done = []
    pcf_cons = []
    pa_cons = []
    for c in range(NCH):
        ysl = ysrc[:, c * 512:(c + 1) * 512]
        u1 = P.step("vector", (lambda ysl=ysl: lambda e: e.tensor_mul(ysl, ysl, pb[4][:, 0:512]))(),
                    deps=(b1, m1, m2, done[-1] if done else None), tag=tag + f"_t1_{c}")
        u2 = P.step("vector", (lambda ysl=ysl: lambda e: e.tensor_sub(ysl, ysl, pb[5][:, 0:512]))(),
                    deps=(u1, b2), tag=tag + f"_t2_{c}")
        u3 = P.step("vector", (lambda ysl=ysl, c=c: lambda e: e.tensor_scalar(
            ysl, ysl, gcol[:, c:c + 1], becol[:, c:c + 1], ALU.mult, ALU.add))(),
            deps=(u2,), tag=tag + f"_x2_{c}")
        u4 = P.step("vector", (lambda ysl=ysl, c=c: lambda e: e.tensor_copy(
            outb[:, ob_off + c * 512: ob_off + (c + 1) * 512], ysl))(),
            deps=(u3, pnt.get(tag + "ob")), tag=tag + f"_x2b_{c}")
        done.append(u4)
        pcf_cons.append(u4)
        pa_cons.append(u4)
    std = None
    if hbm_ap is not None:
        std = P.dma("sync", (lambda: lambda e: e.dma_start(
            out=bass.AP(hbm_ap.tensor, t0, [[T_CORE, 128], [128 * T_CORE, NCH], [1, 512]]),
            in_=ysrc[:, 0:4096].rearrange("p (c t) -> p c t", t=512)))(),
            deps=(done,), tag=tag + "_st")
    pnt[tag + "b6"] = q1
    pnt[tag + "b7"] = q2
    pnt[tag + "b4"] = done[-1]
    pnt[tag + "b5"] = done[-1]
    pnt[tag + "rows"] = [b1, b2]
    pnt[tag + "ra"] = b1
    return {"done": done, "pcf_cons": pcf_cons + ([std] if std else []),
            "pa_cons": pa_cons, "st": std, "b4": done[-1]}


def emit_l1a_ffn(mk, b, nt, t0, x2res, x_in, x_out, pnt):
    P = mk.P
    pb = mk.pb
    PA, PB, PC, PD = mk.PA, mk.PB, mk.PC, mk.PD
    PCw = PC[:, 0:4096].bitcast(BF16)
    PCf = PC[:, 4096:8192]
    b1c = mk.col("l1a_b1")
    b2c = mk.col("l1a_b2")
    g3c = mk.col("l1a_g3")
    be3c = mk.col("l1a_be3")
    epscol = mk.const_col(LN_EPS)
    x2bf = (PA, 8192)
    a_ev = []
    w1lds = []
    for blk in range(4):
        w1 = mk.load_wT("l1a_w1T", PCw[:, 0:8192], 1024,
                        deps=(x2res["pcf_cons"], a_ev[-8:] if a_ev else pnt.get("w1f"),
                              pnt.get("pcw")),
                        col0=blk * 1024, rowstride=4096, tag=f"_{b}_{nt}_{blk}")
        w1lds.append(w1)
        for fcl in range(NCH):
            fc = blk * 8 + fcl
            bank = pb[fcl % 4]

            def mmh(e, fcl=fcl, bank=bank):
                last = None
                for c in range(NCH):
                    last = e.matmul(bank[:, 0:512],
                                    PCw[:, c * 1024 + fcl * 128: c * 1024 + fcl * 128 + 128],
                                    PA[:, 8192 + c * 512: 8192 + (c + 1) * 512],
                                    start=(c == 0), stop=(c == NCH - 1))
                return last
            pv = a_ev[-4] if len(a_ev) >= 4 else pnt.get("hb")
            s_h = P.step("tensor", mmh, deps=(w1, x2res["pa_cons"], pv),
                         tag=f"l1a_h{b}_{nt}_{fc}")
            dst = PB[:, fc * 512:(fc + 1) * 512]
            e_h = P.step("scalar", (lambda bank=bank, dst=dst, fc=fc: lambda e:
                                    e.activation(dst, bank[:, 0:512], AF.Relu,
                                                 bias=b1c[:, fc:fc + 1], scale=1.0))(),
                         deps=(s_h, pnt.get("ab")), tag=f"l1a_relu{b}_{nt}_{fc}")
            a_ev.append(e_h)
    w2lds = []
    mm_h2 = {o: [] for o in range(NCH)}
    for blk in range(4):
        w2 = mk.load_wT("l1a_w2T", PCw[:, 0:8192], 1024,
                        deps=(a_ev, w1lds[-1]) if blk == 0 else ([w2lds[-1]] + mm_h2[NCH - 1][-1:],),
                        col0=0, rowstride=1024, base=blk * 1024 * 1024, tag=f"_{b}_{nt}w{blk}")
        w2lds.append(w2)
        for o in range(NCH):
            bank = pb[o]

            def mmy(e, blk=blk, o=o, bank=bank):
                last = None
                for fcl in range(NCH):
                    fc = blk * 8 + fcl
                    last = e.matmul(bank[:, 0:512],
                                    PCw[:, fcl * 1024 + o * 128: fcl * 1024 + o * 128 + 128],
                                    PB[:, fc * 512:(fc + 1) * 512],
                                    start=(fc == 0), stop=(fc == 31))
                return last
            deps = (w2, a_ev[-1] if blk == 0 else mm_h2[o][-1],
                    (x2res["done"] if blk == 0 and o < 4 else None),
                    (pnt.get("y2b") if blk == 0 else None))
            s_y = P.step("tensor", mmy, deps=deps, tag=f"l1a_y2{b}_{nt}_{blk}_{o}")
            mm_h2[o].append(s_y)
    y2_ev = []
    for o in range(NCH):
        dst = PCf[:, o * 512:(o + 1) * 512]
        e_y = P.step("vector",
                     (lambda o=o, dst=dst: lambda e: e.scalar_tensor_tensor(
                         dst, pb[o][:, 0:512], b2c[:, o:o + 1], dst, ALU.add, ALU.add))(),
                     deps=(mm_h2[o][-1], x2res["pcf_cons"], y2_ev[-1] if y2_ev else None),
                     tag=f"l1a_y2e{b}_{nt}_{o}")
        y2_ev.append(e_y)
    ln3 = emit_inner_ln(mk, f"l1a_ln3_{b}_{nt}", PCf, g3c, be3c, epscol, y2_ev,
                        pa_off=0, out_bf=(PA, 12288), x2_hbm=(None, 0), pnt=pnt)
    XO = PD[:, 2336:6432]
    ldx = P.dma("sync",
                (lambda t0=t0: lambda e: e.dma_start(
                    out=XO.rearrange("p (c t) -> p c t", t=512),
                    in_=bass.AP(x_in.tensor, t0, [[T_CORE, 128], [128 * T_CORE, NCH], [1, 512]])))(),
                deps=(pnt.get("xoc"), ln3["done"][0:1]), tag=f"l1a_ldx{b}_{nt}")
    o_ev = []
    for c in range(NCH):
        e_o = P.step("vector", (lambda c=c: lambda e: e.tensor_add(
            XO[:, c * 512:(c + 1) * 512], PCf[:, c * 512:(c + 1) * 512],
            XO[:, c * 512:(c + 1) * 512]))(),
            deps=(ldx, ln3["done"][c], o_ev[-1] if o_ev else None), tag=f"l1a_xo{b}_{nt}_{c}")
        o_ev.append(e_o)
    st = P.dma("sync",
               (lambda t0=t0: lambda e: e.dma_start(
                   out=bass.AP(x_out.tensor, t0, [[T_CORE, 128], [128 * T_CORE, NCH], [1, 512]]),
                   in_=XO.rearrange("p (c t) -> p c t", t=512)))(),
               deps=(o_ev,), tag=f"l1a_st{b}_{nt}")
    newpnt = {"w1f": [st], "pcw": mm_h2[NCH - 1][-1], "pbw": mm_h2[NCH - 1][-1],
              "hb": y2_ev[-1], "ab": mm_h2[0][-1] if mm_h2[0] else None,
              "y2b": ln3["done"][-1], "xoc": st, "qtc": st, "qb": [st], "dq": [st], "dq2": st,
              "xq": st, "tp1": st, "tp2": st, "tc1": st, "tc2": st, "stc": st, "bcq": st,
              "kpq": st, "u0": st, "u1": st, "rrow": st, "denb": st, "rbcmm": st, "rbcc": st,
              "att": st, "yb": [st], "pcf": [st]}
    return {"st": st, "pnt": newpnt}



SUBLAYERS = ["l0_attn_a", "l0_ffn_1", "l0_attn_b", "l0_ffn_2",
             "l1_attn_a", "l1_ffn_1", "l1_attn_b", "l1_ffn_2"]


def declare_inputs(mk, subs):
    mk.dram_in("x_fm", [D, T_CORE])
    if "l0_attn_a" in subs:
        for n in ("l0a_wqT", "l0a_wkT", "l0a_wvT", "l0a_woT"):
            mk.dram_in(n, [D, D])
        mk.dram_in("r_l0a_bkv", [1, 2048])
        mk.load_cols("l0a_bq", 8)
        mk.load_cols("l0a_bo", 8)
        mk.derive_col("l0a_bq1", "l0a_bq",
                      lambda e, dst, src: e.tensor_single_scalar(dst, src, 1.0, ALU.add))
    if any(s.startswith("l0_ffn") for s in subs):
        mk.dram_in("l0f_w", [D, D])
        mk.load_cols("l0f_bias", 8)
    if "l1_attn_a" in subs:
        for n in ("l1a_wqT", "l1a_wkT", "l1a_wvT", "l1a_woT"):
            mk.dram_in(n, [D, D])
        mk.dram_in("l1a_projT", [64, 256])
        mk.dram_in("l1a_w1T", [D, 4096])
        mk.dram_in("l1a_w2T", [4096, D])
        mk.dram_in("r_l1a_bv", [1, D])
        for n in ("l1a_bq", "l1a_bk", "l1a_bo", "l1a_g2", "l1a_be2", "l1a_g3", "l1a_be3",
                  "l1a_b2"):
            mk.load_cols(n[4:] if False else n, 8)
        mk.load_cols("l1a_b1", 32)
    if any(s.startswith("l1_ffn") for s in subs):
        mk.dram_in("l1f_gate_wTc", [128, 128])
        mk.dram_in("l1f_fc1T", [15, D, 2048])
        mk.dram_in("l1f_fc2T", [15, D, D])
        mk.dram_in("l1f_fc2b", [15, D])
        mk.dram_in("l1f_sfc1T", [D, 4096])
        mk.dram_in("l1f_sfc2T", [2048, D])
        mk.dram_in("r_l1f_sb2", [1, D])
        mk.load_cols("l1f_b1lo", 120)
        mk.load_cols("l1f_b1hi", 120)
        mk.load_cols("l1f_gm", 120)
        mk.load_cols("l1f_sblo", 16)
        mk.load_cols("l1f_sbhi", 16)
        mk.load_cols("l1f_sgm", 16)
    if "l1_attn_b" in subs:
        mk.dram_in("l1b_wq", [D, D])
        mk.dram_in("l1b_wk_down", [D, 64])
        mk.dram_in("l1b_wv_down", [D, 64])
        mk.dram_in("l1b_wk_up", [32, 128])
        mk.dram_in("l1b_wv_up", [32, 128])
        mk.dram_in("l1b_wo", [D, D])
    if "l0_attn_b" in subs:
        mk.dram_in("l0b_wqkvT", [2, D, 3 * D])
        mk.dram_in("l0b_woutT", [2, D, D])
        mk.dram_in("r_l0b_bv", [2, D])
        mk.load_cols("l0b_bqk", 32)
        mk.load_cols("l0b_bout", 16)
        def _bsum(e, dst, srcap):
            return e.scalar_tensor_tensor(dst, srcap[:, 0:8], mk.const_col(1.0 / 3.0)[:, 0:1],
                                          None, ALU.add, ALU.mult)
        o = mk._col_off
        mk._col_off += 8
        dst = mk.COLS[:, o:o + 8]
        b01, bst = mk.cols["l0b_bout"]
        s1 = mk.P.step("vector", lambda e: e.tensor_add(dst, b01[:, 0:8], b01[:, 8:16]),
                       deps=(bst,), tag="l0b_bsum_a")
        s2 = mk.P.step("vector", lambda e: e.tensor_scalar(dst, dst, 1.0 / 3.0, None, ALU.mult),
                       deps=(s1,), tag="l0b_bsum_b")
        mk.cols["l0b_bsum3"] = (dst, s2)
        mk.init_deps.append(s2)


def build(subs, n_launch_out=True):
    nc = bass.Bass("TRN2", target_bir_lowering=False)
    stack = contextlib.ExitStack()
    with stack:
        mk = MK(nc, stack)
        mk.init_consts()
        declare_inputs(mk, subs)
        x_out_final = mk.dram_out("out_fm", [D, T_CORE])
        xtmp = mk.dram_tmp("x_tmp", [D, T_CORE]) if len(subs) > 1 else None
        cur_in = mk.dram["x_fm"]
        finals = []
        for i, s in enumerate(subs):
            is_last = (i == len(subs) - 1)
            cur_out = x_out_final if is_last else xtmp
            prev = tuple(finals)
            if s == "l0_attn_a":
                finals = emit_l0_attn_a(mk, cur_in, cur_out, prev)
            elif s.startswith("l0_ffn"):
                first = not getattr(mk, "_l0f_done", False)
                finals = emit_l0_ffn(mk, cur_in, cur_out, prev, first_call=first)
                mk._l0f_done = True
            elif s == "l0_attn_b":
                finals = emit_l0_attn_b(mk, cur_in, cur_out, prev, branches=BRANCHES_L0B)
            elif s == "l1_attn_b":
                finals = emit_l1_attn_b(mk, cur_in, cur_out, prev)
            elif s.startswith("l1_ffn"):
                finals = emit_l1_ffn(mk, cur_in, cur_out, prev)
            elif s == "l1_attn_a":
                finals = emit_l1_attn_a(mk, cur_in, cur_out, prev)
            else:
                raise NotImplementedError(s)
            cur_in = cur_out
        mk.P.step("vector", lambda e: e.memset(mk.CONST[:, 15:16], 0.0), deps=(finals,),
                  tag="final_wait")
        mk.P.finalize(stack)
    return nc


def stage_core_inputs(inputs, core, subs):
    x = inputs["L0_attn_a_hidden_states"]
    d = {}
    xc = np.ascontiguousarray(
        x[core * B_PER_CORE:(core + 1) * B_PER_CORE].reshape(T_CORE, D).T).astype(np.float32)
    d["x_fm"] = xc
    if "l0_attn_a" in subs:
        d["l0a_wqT"] = np.ascontiguousarray(inputs["L0_attn_a_Wq"].T)
        d["l0a_wkT"] = np.ascontiguousarray(inputs["L0_attn_a_Wk"].T)
        d["l0a_wvT"] = np.ascontiguousarray(inputs["L0_attn_a_Wv"].T)
        d["l0a_woT"] = np.ascontiguousarray(inputs["L0_attn_a_Wo"].T)
        d["r_l0a_bkv"] = np.concatenate([inputs["L0_attn_a_bk"], inputs["L0_attn_a_bv"]])[None, :]
        d["c_l0a_bq"] = np.ascontiguousarray(inputs["L0_attn_a_bq"].reshape(8, 128).T)
        d["c_l0a_bo"] = np.ascontiguousarray(inputs["L0_attn_a_bo"].reshape(8, 128).T)
    if any(s.startswith("l0_ffn") for s in subs):
        d["l0f_w"] = inputs["L0_ffn_weight"]
        d["c_l0f_bias"] = np.ascontiguousarray(inputs["L0_ffn_bias"].reshape(8, 128).T)
    if "l1_attn_b" in subs:
        for nm in ("wq", "wk_down", "wv_down", "wk_up", "wv_up", "wo"):
            d["l1b_" + nm] = np.ascontiguousarray(inputs["L1_attn_b_" + nm])
    if "l1_attn_a" in subs:
        d["l1a_wqT"] = np.ascontiguousarray(inputs["L1_attn_a_Wq"].T)
        d["l1a_wkT"] = np.ascontiguousarray(inputs["L1_attn_a_Wk"].T)
        d["l1a_wvT"] = np.ascontiguousarray(inputs["L1_attn_a_Wv"].T)
        d["l1a_woT"] = np.ascontiguousarray(inputs["L1_attn_a_Wo"].T)
        d["l1a_projT"] = np.ascontiguousarray(inputs["L1_attn_a_proj"].T)
        d["l1a_w1T"] = np.ascontiguousarray(inputs["L1_attn_a_W1"].T)
        d["l1a_w2T"] = np.ascontiguousarray(inputs["L1_attn_a_W2"].T)
        d["r_l1a_bv"] = np.ascontiguousarray(inputs["L1_attn_a_bv"][None, :])
        cc8 = lambda v: np.ascontiguousarray(v.reshape(-1, 128).T)
        d["c_l1a_bq"] = cc8(inputs["L1_attn_a_bq"])
        d["c_l1a_bk"] = cc8(inputs["L1_attn_a_bk"])
        d["c_l1a_bo"] = cc8(inputs["L1_attn_a_bo"])
        d["c_l1a_g2"] = cc8(inputs["L1_attn_a_g2"])
        d["c_l1a_be2"] = cc8(inputs["L1_attn_a_be2"])
        d["c_l1a_g3"] = cc8(inputs["L1_attn_a_g3"])
        d["c_l1a_be3"] = cc8(inputs["L1_attn_a_be3"])
        d["c_l1a_b2"] = cc8(inputs["L1_attn_a_b2"])
        d["c_l1a_b1"] = cc8(inputs["L1_attn_a_b1"])
    if any(s.startswith("l1_ffn") for s in subs):
        gw = inputs["L1_ffn_gate_w"].T.reshape(8, 128, 15).transpose(1, 0, 2)
        gwp = np.zeros((128, 8, 16), np.float32)
        gwp[:, :, :15] = gw
        d["l1f_gate_wTc"] = np.ascontiguousarray(gwp.reshape(128, 128))
        d["l1f_fc1T"] = np.ascontiguousarray(inputs["L1_ffn_fc1_w"].transpose(0, 2, 1))
        d["l1f_fc2T"] = np.ascontiguousarray(inputs["L1_ffn_fc2_w"].transpose(0, 2, 1))
        d["l1f_fc2b"] = np.ascontiguousarray(inputs["L1_ffn_fc2_b"])
        d["l1f_sfc1T"] = np.ascontiguousarray(inputs["L1_ffn_s_fc1_w"].T)
        d["l1f_sfc2T"] = np.ascontiguousarray(inputs["L1_ffn_s_fc2_w"].T)
        d["r_l1f_sb2"] = np.ascontiguousarray(inputs["L1_ffn_s_fc2_b"][None, :])
        cc = lambda v: np.ascontiguousarray(v.reshape(-1, 128).T)
        d["c_l1f_b1lo"] = np.ascontiguousarray(np.concatenate(
            [cc(inputs["L1_ffn_fc1_b"][e, :1024]) for e in range(15)], axis=1))
        d["c_l1f_b1hi"] = np.ascontiguousarray(np.concatenate(
            [cc(inputs["L1_ffn_fc1_b"][e, 1024:]) for e in range(15)], axis=1))
        d["c_l1f_gm"] = np.ascontiguousarray(np.concatenate(
            [cc(inputs["L1_ffn_geglu_mult"][e]) for e in range(15)], axis=1))
        d["c_l1f_sblo"] = cc(inputs["L1_ffn_s_fc1_b"][:2048])
        d["c_l1f_sbhi"] = cc(inputs["L1_ffn_s_fc1_b"][2048:])
        d["c_l1f_sgm"] = cc(inputs["L1_ffn_s_geglu_mult"])
    if "l0_attn_b" in subs:
        d["l0b_wqkvT"] = np.ascontiguousarray(inputs["L0_attn_b_Wqkv"][:2].transpose(0, 2, 1))
        d["l0b_woutT"] = np.ascontiguousarray(inputs["L0_attn_b_Wout"][:2].transpose(0, 2, 1))
        d["r_l0b_bv"] = np.ascontiguousarray(inputs["L0_attn_b_bqkv"][:2, 2048:3072])
        bqk = np.stack([inputs["L0_attn_b_bqkv"][br, i * 1024:(i + 1) * 1024].reshape(8, 128).T
                        for br in range(2) for i in range(2)], 0)
        d["c_l0b_bqk"] = np.ascontiguousarray(np.concatenate(list(bqk), axis=1))
        d["c_l0b_bout"] = np.ascontiguousarray(np.concatenate(
            [inputs["L0_attn_b_bout"][br].reshape(8, 128).T for br in range(2)], axis=1))
    return {k: np.ascontiguousarray(v, dtype=v.dtype) for k, v in d.items()}


def run_subs(inputs, subs, x_override=None):
    nc = build(subs)
    in_maps = []
    for c in range(NCORE):
        src = dict(inputs)
        if x_override is not None:
            src["L0_attn_a_hidden_states"] = x_override
        in_maps.append(stage_core_inputs(src, c, subs))
    res = run_bass_kernel_spmd(nc, in_maps, core_ids=list(range(NCORE)))
    outs = []
    for c in range(NCORE):
        o = res.results[c]["out_fm"]
        outs.append(o.T.reshape(B_PER_CORE, T_SEQ, D))
    return np.concatenate(outs, axis=0)


def kernel(**inputs):
    return run_subs(inputs, SUBLAYERS)


if __name__ == "__main__":
    pass


# version 04
# speedup vs baseline: 1.0357x; 1.0357x over previous
import numpy as np
import contextlib
import concourse.bass as bass
import concourse.mybir as mybir
from concourse.bass_utils import run_bass_kernel_spmd

F32 = mybir.dt.float32
BF16 = mybir.dt.bfloat16
I32 = mybir.dt.int32
AF = mybir.ActivationFunctionType
ALU = mybir.AluOpType
AX = mybir.AxisListType

D = 1024
NCORE = 8
B_PER_CORE = 2
T_SEQ = 2048
T_CORE = B_PER_CORE * T_SEQ
NCH = D // 128
LN_EPS = 1e-5
BRANCHES_L0B = (0, 1)
DBG_L1F = 3
DBG_GATE = 3
DBG_FIN = 1
DBG_NEXP = 15
DBG_DUMP = 0
DBG_SHBIAS = 1
DBG_SHPH = 2
DBG_EXPPH = 3

ENGINES = ("tensor", "vector", "scalar", "gpsimd", "sync")


class Step:
    __slots__ = ("engine", "emit", "deps", "is_dma", "idx", "inc_needed",
                 "sem_val_after", "tag")

    def __init__(self, engine, emit, deps, is_dma, idx, tag=""):
        self.engine = engine
        self.emit = emit
        self.deps = deps
        self.is_dma = is_dma
        self.idx = idx
        self.inc_needed = False
        self.sem_val_after = None
        self.tag = tag


def _flat(deps):
    out = []
    for d in deps:
        if d is None:
            continue
        if isinstance(d, (list, tuple)):
            out.extend(_flat(d))
        else:
            out.append(d)
    return out


class Prog:
    def __init__(self, nc):
        self.nc = nc
        self.steps = []

    def step(self, engine, emit, deps=(), tag=""):
        s = Step(engine, emit, tuple(_flat(deps)), False, len(self.steps), tag)
        self.steps.append(s)
        return s

    def dma(self, queue, emit, deps=(), tag=""):
        s = Step(queue, emit, tuple(_flat(deps)), True, len(self.steps), tag)
        self.steps.append(s)
        return s

    def finalize(self, stack):
        nc = self.nc
        for s in self.steps:
            if s.is_dma:
                s.inc_needed = True
            for d in s.deps:
                d.inc_needed = True
        chan_of = {}
        for s in self.steps:
            chan_of[s.idx] = ("dma_" + s.engine) if s.is_dma else ("eng_" + s.engine)
        chans = sorted(set(chan_of.values()))
        sems = {c: stack.enter_context(nc.semaphore("sem_" + c)) for c in chans}
        counts = {c: 0 for c in chans}
        for s in self.steps:
            c = chan_of[s.idx]
            if s.inc_needed:
                counts[c] += 16 if s.is_dma else 1
                s.sem_val_after = counts[c]
        per_engine = {e: [] for e in ENGINES}
        for s in self.steps:
            per_engine[s.engine].append(s)
        block = stack.enter_context(nc.Block())

        prev_dma_target = {}

        def mk_body(steps):
            def body(eng):
                for s in steps:
                    needs = {}
                    for d in s.deps:
                        c = chan_of[d.idx]
                        needs[c] = max(needs.get(c, 0), d.sem_val_after)
                    if s.is_dma:
                        c = chan_of[s.idx]
                        t = prev_dma_target.get(c)
                        if t is not None:
                            needs[c] = max(needs.get(c, 0), t)
                    for c, v in sorted(needs.items()):
                        eng.wait_ge(sems[c], v)
                    last = s.emit(eng)
                    if s.inc_needed:
                        assert last is not None, f"step '{s.tag}' must return last instruction"
                        last.then_inc(sems[chan_of[s.idx]], 16 if s.is_dma else 1)
                        if s.is_dma:
                            prev_dma_target[chan_of[s.idx]] = s.sem_val_after
            return body

        for e in ENGINES:
            if per_engine[e]:
                getattr(block, e)(mk_body(per_engine[e]))
        return sems


class MK:
    def __init__(self, nc, stack):
        self.nc = nc
        self.stack = stack
        self.P = Prog(nc)
        self.dram = {}
        sb = lambda n, sh, dt: stack.enter_context(nc.sbuf_tensor(n, list(sh), dt))
        self.XN = sb("XN", [128, 8 * T_CORE], BF16)
        self.PA = sb("PA", [128, 16384], BF16)
        self.PB = sb("PB", [128, 16384], BF16)
        self.PC = sb("PC", [128, 8192], F32)
        self.PD = sb("PD", [128, 8192], F32)
        self.ROWS = sb("ROWS", [1, 2048], F32)
        self.ROWB = sb("ROWB", [1, 1024], BF16)
        self.COLS = sb("COLS", [128, 640], F32)
        self.CONST = sb("CONST", [128, 16], F32)
        self.ONEB = sb("ONEB", [1, 512], BF16)
        self.ONEF = sb("ONEF", [1, 128], F32)
        self.ONECB = sb("ONECB", [128, 1], BF16)
        self.EH = sb("EH", [1, 256], F32)
        self.IDB = sb("IDB", [128, 128], BF16)
        self.IDF = sb("IDF", [128, 128], F32)
        self.pb = [stack.enter_context(nc.psum_tensor(f"pb{i}", [128, 512], F32))
                   for i in range(8)]
        self._col_off = 0
        self.cols = {}
        self._const_off = 0
        self.consts = {}
        self.init_deps = []

    def dram_in(self, name, shape, dtype=F32):
        ap = self.nc.dram_tensor(name, list(shape), dtype, kind="ExternalInput").ap()
        self.dram[name] = ap
        return ap

    def dram_out(self, name, shape, dtype=F32):
        ap = self.nc.dram_tensor(name, list(shape), dtype, kind="ExternalOutput").ap()
        self.dram[name] = ap
        return ap

    def dram_tmp(self, name, shape, dtype=F32):
        ap = self.nc.dram_tensor(name, list(shape), dtype).ap()
        self.dram[name] = ap
        return ap

    def const_col(self, value):
        if value not in self.consts:
            i = self._const_off
            self._const_off += 1
            assert i < 15
            ap = self.CONST[:, i:i + 1]
            st = self.P.step("vector", (lambda ap=ap, v=value: lambda e: e.memset(ap, v))(),
                             tag=f"const{value}")
            self.init_deps.append(st)
            self.consts[value] = (ap, st)
        return self.consts[value][0]

    def col(self, name):
        return self.cols[name][0]

    def cdep(self, value):
        return self.consts[value][1]

    def load_cols(self, name, n):
        src = self.dram_in("c_" + name, [128, n])
        o = self._col_off
        self._col_off += n
        assert self._col_off <= 640, "COLS overflow"
        dst = self.COLS[:, o:o + n]
        st = self.P.dma("sync", (lambda dst=dst, src=src: lambda e: e.dma_start(out=dst, in_=src))(),
                        tag=f"cols_{name}")
        self.cols[name] = (dst, st)
        self.init_deps.append(st)
        return dst

    def derive_col(self, name, src_name, fn):
        srcap, srcstep = self.cols[src_name]
        n = srcap.shape[-1]
        o = self._col_off
        self._col_off += n
        assert self._col_off <= 640
        dst = self.COLS[:, o:o + n]
        st = self.P.step("vector", (lambda dst=dst, srcap=srcap: lambda e: fn(e, dst, srcap))(),
                         deps=(srcstep,), tag=f"dcol_{name}")
        self.cols[name] = (dst, st)
        self.init_deps.append(st)
        return dst

    def init_consts(self):
        P = self.P
        s = []
        s.append(P.step("vector", lambda e: e.memset(self.ONEB[:], 1.0), tag="i_oneb"))
        s.append(P.step("vector", lambda e: e.memset(self.ONEF[:], 1.0), tag="i_onef"))
        s.append(P.step("vector", lambda e: e.memset(self.ONECB[:], 1.0), tag="i_onecb"))
        s.append(P.step("vector", lambda e: e.memset(self.EH[0:1, 0:64], 1.0), tag="i_eh0"))
        s.append(P.step("vector", lambda e: e.memset(self.EH[0:1, 64:192], 0.0), tag="i_eh1"))
        s.append(P.step("vector", lambda e: e.memset(self.EH[0:1, 192:256], 1.0), tag="i_eh2"))
        io = self.PC[:, 0:128].bitcast(I32)
        s0 = P.step("gpsimd", lambda e: e.iota(io, [[1, 128]], base=0, channel_multiplier=-1),
                    tag="i_iota")
        s.append(P.step("vector", lambda e: e.tensor_single_scalar(self.IDB[:], io, 0, ALU.is_equal),
                        deps=(s0,), tag="i_idb"))
        s.append(P.step("vector", lambda e: e.tensor_single_scalar(self.IDF[:], io, 0, ALU.is_equal),
                        deps=(s0,), tag="i_idf"))
        self.init_deps.extend(s)
        return s

    def load_wT(self, dram_name, dst, dout, deps=(), din=D, col0=0, rowstride=None,
                base=0, tag=""):
        src = self.dram[dram_name]
        if rowstride is None:
            rowstride = dout
        nchunk = din // 128
        view = dst.rearrange("p (c n) -> p c n", c=nchunk)
        return self.P.dma("gpsimd",
                          (lambda view=view, src=src: lambda e: e.dma_start(
                               out=view,
                               in_=bass.AP(src.tensor, base + col0,
                                           [[rowstride, 128], [128 * rowstride, nchunk], [1, dout]])))(),
                          deps=deps, tag=f"w_{dram_name}{tag}")



def emit_ln(mk, x_in, deps=()):
    P = mk.P
    pb = mk.pb
    r_m = mk.ROWS[0:1, 0:512]
    r_t = mk.ROWS[0:1, 512:1024]
    r_r = mk.ROWS[0:1, 1024:1536]
    r_mr = mk.ROWS[0:1, 1536:2048]
    epscol = mk.const_col(LN_EPS)
    done = []
    prev_x = {0: [deps, mk.init_deps], 1: [deps, mk.init_deps]}
    prev_pa = mk.init_deps
    prev_rows = None
    prev_b = {4: None, 5: None, 6: None, 7: None}
    for tt in range(8):
        xt = mk.PC[:, (tt % 2) * 4096:(tt % 2) * 4096 + 4096]
        ld = P.dma("sync",
                   (lambda xt=xt, tt=tt: lambda e: e.dma_start(
                       out=xt.rearrange("p (c t) -> p c t", t=512),
                       in_=bass.AP(x_in.tensor, tt * 512, [[T_CORE, 128], [128 * T_CORE, NCH], [1, 512]])))(),
                   deps=(prev_x[tt % 2],), tag=f"ln_ld{tt}")
        xbf = mk.PA[:, 0:4096]
        x2b = mk.PA[:, 4096:8192]
        c1 = P.step("vector", (lambda xt=xt, xbf=xbf: lambda e: e.tensor_copy(xbf, xt))(),
                    deps=(ld, prev_pa), tag=f"ln_cast{tt}")
        c2 = P.step("scalar", (lambda xbf=xbf, x2b=x2b: lambda e: e.activation(x2b, xbf, AF.Square))(),
                    deps=(c1,), tag=f"ln_sq{tt}")

        def mk_stats(src, bank):
            def f(e):
                mm = None
                for c in range(NCH):
                    mm = e.matmul(bank[0:1, 0:512], mk.ONECB[:, 0:1],
                                  src[:, c * 512:(c + 1) * 512],
                                  start=(c == 0), stop=(c == NCH - 1))
                return mm
            return f
        m1 = P.step("tensor", mk_stats(xbf, pb[6]), deps=(c1, prev_b[6]), tag=f"ln_s1_{tt}")
        m2 = P.step("tensor", mk_stats(x2b, pb[7]), deps=(c2, prev_b[7]), tag=f"ln_s2_{tt}")
        q1 = P.step("vector", lambda e: e.tensor_single_scalar(r_m, pb[6][0:1, 0:512], 1.0 / D, ALU.mult),
                    deps=(m1, prev_rows), tag=f"ln_m{tt}")
        q2 = P.step("vector", lambda e: e.tensor_single_scalar(r_t, pb[7][0:1, 0:512], 1.0 / D, ALU.mult),
                    deps=(m2, prev_rows), tag=f"ln_e2{tt}")
        q3a = P.step("vector", lambda e: e.tensor_mul(r_r, r_m, r_m),
                     deps=(q1, prev_rows), tag=f"ln_msq{tt}")
        q3 = P.step("vector", lambda e: e.tensor_sub(r_t, r_t, r_r),
                    deps=(q2, q3a), tag=f"ln_var{tt}")
        q5 = P.step("scalar", lambda e: e.activation(r_t, r_t, AF.Ln, bias=epscol[0:1, 0:1], scale=1.0),
                    deps=(q3, mk.cdep(LN_EPS)), tag=f"ln_ln{tt}")
        q6 = P.step("scalar", lambda e: e.activation(r_r, r_t, AF.Exp, scale=-0.5),
                    deps=(q5,), tag=f"ln_rstd{tt}")
        q7 = P.step("vector", lambda e: e.tensor_mul(r_mr, r_m, r_r), deps=(q6, q1), tag=f"ln_mr{tt}")
        b1 = P.step("tensor", lambda e: e.matmul(pb[4][:, 0:512], mk.ONEF[0:1, 0:128], r_r,
                                                 start=True, stop=True),
                    deps=(q6, prev_b[4]), tag=f"ln_rb{tt}")
        b2 = P.step("tensor", lambda e: e.matmul(pb[5][:, 0:512], mk.ONEF[0:1, 0:128], r_mr,
                                                 start=True, stop=True),
                    deps=(q7, prev_b[5]), tag=f"ln_mrb{tt}")
        prev_rows = [b1, b2]
        xn_steps = []
        x_cons = []
        for c in range(NCH):
            xc = xt[:, c * 512:(c + 1) * 512]
            t1 = mk.PD[:, (c % 2) * 512:(c % 2) * 512 + 512]
            dep_prev_t1 = xn_steps[c - 2] if c >= 2 else None
            u1 = P.step("vector",
                        (lambda xc=xc, t1=t1: lambda e: e.tensor_mul(t1, xc, pb[4][:, 0:512]))(),
                        deps=(ld, b1, dep_prev_t1), tag=f"ln_t1_{tt}_{c}")
            xn_sl = mk.XN[:, c * T_CORE + tt * 512: c * T_CORE + tt * 512 + 512]
            u2 = P.step("vector",
                        (lambda t1=t1, xn_sl=xn_sl: lambda e: e.tensor_sub(xn_sl, t1, pb[5][:, 0:512]))(),
                        deps=(u1, b2), tag=f"ln_xn_{tt}_{c}")
            xn_steps.append(u2)
            x_cons.append(u1)
        prev_x[tt % 2] = x_cons
        prev_pa = [m1, m2]
        prev_b[6] = q1
        prev_b[7] = q2
        prev_b[4] = xn_steps[-1]
        prev_b[5] = xn_steps[-1]
        done.append(xn_steps)
    return done



def emit_l0_attn_a(mk, x_in, x_out, prev=()):
    P = mk.P
    pb = mk.pb
    XN, PA, PB, PC, PD = mk.XN, mk.PA, mk.PB, mk.PC, mk.PD
    WB = PC[:, 0:4096].bitcast(BF16)
    ln_done = emit_ln(mk, x_in, deps=prev)
    lnbar = ln_done[-1]
    bq = mk.col("l0a_bq"); bq1 = mk.col("l0a_bq1"); bo = mk.col("l0a_bo")

    rrow = mk.dram["r_l0a_bkv"]
    rowf = PD[0:1, 6656:7680]
    out_final = []
    prev_batch_done = None

    wb_free = None
    for b in range(B_PER_CORE):
        tb = b * T_SEQ
        ev_all = {}
        rowb_cons = prev_batch_done
        for wname, dstpool, kind in (("l0a_wkT", PA, "k"), ("l0a_wvT", PB, "v")):
            src_off = 0 if kind == "k" else 1024
            ldr = P.dma("sync", (lambda src_off=src_off: lambda e: e.dma_start(
                out=rowf[0:1, 0:1024], in_=rrow[0:1, src_off:src_off + 1024]))(),
                deps=(rowb_cons, prev_batch_done), tag=f"l0a_ldr_{kind}{b}")
            rcast = P.step("vector", lambda e: e.tensor_copy(mk.ROWB[0:1, 0:1024], rowf[0:1, 0:1024]),
                           deps=(ldr, rowb_cons), tag=f"l0a_rc_{kind}{b}")
            wld = mk.load_wT(wname, WB[:, :], 1024, deps=(wb_free, lnbar, prev_batch_done))
            ev_steps = []
            for j in range(16):
                for h in range(2):
                    bank = pb[(j * 2 + h) % 4]
                    tsl = tb + j * 128

                    def mm(e, tsl=tsl, h=h, bank=bank):
                        for c in range(NCH):
                            e.matmul(bank[:, 0:512],
                                     XN[:, c * T_CORE + tsl: c * T_CORE + tsl + 128],
                                     WB[:, c * 1024 + h * 512: c * 1024 + h * 512 + 512],
                                     start=(c == 0), stop=False)
                        return e.matmul(bank[:, 0:512], mk.ONEB[0:1, 0:128],
                                        mk.ROWB[0:1, h * 512: h * 512 + 512],
                                        start=False, stop=True)
                    prev_bank = ev_steps[-4] if len(ev_steps) >= 4 else None
                    s_mm = P.step("tensor", mm,
                                  deps=(wld, rcast, ln_done[tsl // 512], prev_bank),
                                  tag=f"l0a_p{kind}{b}_{j}_{h}")
                    dst = dstpool[:, j * 1024 + h * 512: j * 1024 + h * 512 + 512]
                    if kind == "k":
                        t_zm = PD[:, 2048 + (h % 2) * 512: 2560 + (h % 2) * 512]
                        t_e = PD[:, 3072 + (h % 2) * 512: 3584 + (h % 2) * 512]
                        t_z1 = PD[:, 4096 + (h % 2) * 512: 4608 + (h % 2) * 512]
                        prev_scr = ev_steps[-2] if len(ev_steps) >= 2 else None
                        e1 = P.step("vector", (lambda bank=bank, t_zm=t_zm: lambda e:
                                               e.tensor_single_scalar(t_zm, bank[:, 0:512], 0.0, ALU.min))(),
                                    deps=(s_mm, prev_scr), tag=f"l0a_kzm{b}_{j}_{h}")
                        e2 = P.step("scalar", (lambda t_zm=t_zm, t_e=t_e: lambda e:
                                               e.activation(t_e, t_zm, AF.Exp, scale=1.0))(),
                                    deps=(e1, prev_scr), tag=f"l0a_kexp{b}_{j}_{h}")
                        e3 = P.step("vector", (lambda bank=bank, t_z1=t_z1: lambda e:
                                               e.tensor_single_scalar(t_z1, bank[:, 0:512], 1.0, ALU.add))(),
                                    deps=(s_mm, prev_scr), tag=f"l0a_kz1{b}_{j}_{h}")
                        e4 = P.step("vector", (lambda t_e=t_e, t_z1=t_z1, dst=dst: lambda e:
                                               e.tensor_max(dst, t_e, t_z1))(),
                                    deps=(e2, e3), tag=f"l0a_kf{b}_{j}_{h}")
                        ev_steps.append(e4)
                    else:
                        e1 = P.step("vector", (lambda bank=bank, dst=dst: lambda e:
                                               e.tensor_copy(dst, bank[:, 0:512]))(),
                                    deps=(s_mm,), tag=f"l0a_vev{b}_{j}_{h}")
                        ev_steps.append(e1)
            ev_all[kind] = ev_steps
            rowb_cons = ev_steps[-1]
            wb_free = ev_steps[-1]

        k_done = ev_all["k"][-1]
        v_done = ev_all["v"][-1]

        KV = PD[:, 5120:5376].bitcast(BF16)
        ksum = PD[:, 5376:5384].bitcast(BF16)
        kv_ev = []
        for h in range(16):
            bank = pb[4 + (h % 2)]

            def mmkv(e, h=h, bank=bank):
                mmop = None
                for j in range(16):
                    mmop = e.matmul(bank[0:64, 0:64],
                                    PA[:, j * 1024 + h * 64: j * 1024 + h * 64 + 64],
                                    PB[:, j * 1024 + h * 64: j * 1024 + h * 64 + 64],
                                    start=(j == 0), stop=(j == 15))
                return mmop
            prev_kv = kv_ev[h - 2] if h >= 2 else None
            s_kv = P.step("tensor", mmkv, deps=(k_done, v_done, prev_kv), tag=f"l0a_kv{b}_{h}")
            dst = KV[(h % 2) * 64:(h % 2) * 64 + 64, (h // 2) * 64:(h // 2) * 64 + 64]
            e_kv = P.step("vector", (lambda bank=bank, dst=dst: lambda e:
                                     e.tensor_copy(dst, bank[0:64, 0:64]))(),
                          deps=(s_kv,), tag=f"l0a_kvev{b}_{h}")
            kv_ev.append(e_kv)

        def mmksum(e):
            mmop = None
            for c in range(NCH):
                for j in range(16):
                    mmop = e.matmul(pb[6][:, c:c + 1],
                                    PA[:, j * 1024 + c * 128: j * 1024 + c * 128 + 128],
                                    mk.ONECB[:, 0:1],
                                    start=(j == 0), stop=(j == 15))
            return mmop
        s_ks = P.step("tensor", mmksum, deps=(k_done,), tag=f"l0a_ksum{b}")
        e_ks = P.step("vector", lambda e: e.tensor_copy(ksum[:, 0:8], pb[6][:, 0:8]),
                      deps=(s_ks,), tag=f"l0a_ksumev{b}")

        wq_ld = mk.load_wT("l0a_wqT", PB[:, 0:8192], 1024, deps=(kv_ev[-1], e_ks))
        wo_ld = mk.load_wT("l0a_woT", PB[:, 8192:16384], 1024, deps=(kv_ev[-1], e_ks))
        q_tile = PA[:, 0:4096]
        a_tile = PA[:, 4096:8192]
        prev_nt = {}
        for nt in range(4):
            t0 = tb + nt * 512
            ttile = t0 // 512
            ldx = P.dma("sync",
                        (lambda t0=t0: lambda e: e.dma_start(
                            out=PC[:, 0:4096].rearrange("p (c t) -> p c t", t=512),
                            in_=bass.AP(x_in.tensor, t0, [[T_CORE, 128], [128 * T_CORE, NCH], [1, 512]])))(),
                        deps=(prev_nt.get("x_cons"), lnbar, k_done, v_done),
                        tag=f"l0a_ldx{b}_{nt}")
            q_ev = []
            for o in range(NCH):
                bank = pb[o % 4]

                def mmq(e, o=o, t0=t0, bank=bank):
                    mmop = None
                    for c in range(NCH):
                        mmop = e.matmul(bank[:, 0:512],
                                        PB[:, c * 1024 + o * 128: c * 1024 + o * 128 + 128],
                                        XN[:, c * T_CORE + t0: c * T_CORE + t0 + 512],
                                        start=(c == 0), stop=(c == NCH - 1))
                    return mmop
                prev_bank = q_ev[o - 4] if o >= 4 else prev_nt.get("qbank")
                s_q = P.step("tensor", mmq, deps=(wq_ld, e_ks, prev_bank),
                             tag=f"l0a_q{b}_{nt}_{o}")
                t_zm = PD[:, 4096:4608]
                t_e = PD[:, 6656:7168]
                t_z1 = PD[:, 7168:7680]
                prev_scr = q_ev[o - 1] if o >= 1 else prev_nt.get("qscr")
                e1 = P.step("vector", (lambda bank=bank, t_zm=t_zm, o=o: lambda e:
                                       e.tensor_scalar(t_zm, bank[:, 0:512], bq[:, o:o + 1], 0.0,
                                                       ALU.add, ALU.min))(),
                            deps=(s_q, prev_scr), tag=f"l0a_qzm{b}_{nt}_{o}")
                e2 = P.step("scalar", (lambda t_zm=t_zm, t_e=t_e: lambda e:
                                       e.activation(t_e, t_zm, AF.Exp, scale=1.0))(),
                            deps=(e1, prev_scr), tag=f"l0a_qexp{b}_{nt}_{o}")
                e3 = P.step("vector", (lambda bank=bank, t_z1=t_z1, o=o: lambda e:
                                       e.tensor_single_scalar(t_z1, bank[:, 0:512], bq1[:, o:o + 1],
                                                              ALU.add))(),
                            deps=(s_q, prev_scr), tag=f"l0a_qz1{b}_{nt}_{o}")
                dst = q_tile[:, o * 512:(o + 1) * 512]
                e4 = P.step("vector", (lambda t_e=t_e, t_z1=t_z1, dst=dst: lambda e:
                                       e.tensor_max(dst, t_e, t_z1))(),
                            deps=(e2, e3), tag=f"l0a_qf{b}_{nt}_{o}")
                q_ev.append(e4)
            at_ev = []
            for p in range(8):
                h0, h1 = 2 * p, 2 * p + 1
                z_rows = []
                for hh, h in enumerate((h0, h1)):
                    def mmden(e, h=h):
                        return e.matmul(pb[6][0:1, 0:512],
                                        ksum[(h % 2) * 64:(h % 2) * 64 + 64, (h // 2):(h // 2) + 1],
                                        q_tile[(h % 2) * 64:(h % 2) * 64 + 64,
                                               (h // 2) * 512:(h // 2) * 512 + 512],
                                        start=True, stop=True)
                    dep_prev_den = z_rows[-1] if z_rows else (at_ev[-1] if at_ev else prev_nt.get("den"))
                    s_d = P.step("tensor", mmden, deps=(e_ks, q_ev[h // 2], dep_prev_den),
                                 tag=f"l0a_den{b}_{nt}_{h}")
                    r_d = mk.ROWS[0:1, hh * 512: hh * 512 + 512]
                    d1 = P.step("vector", (lambda r_d=r_d: lambda e:
                                           e.tensor_single_scalar(r_d, pb[6][0:1, 0:512], 1e-6, ALU.add))(),
                                deps=(s_d, prev_nt.get("zrows"), lnbar), tag=f"l0a_zadd{b}_{nt}_{h}")
                    d2 = P.step("scalar", (lambda r_d=r_d: lambda e:
                                           e.activation(r_d, r_d, AF.Ln, scale=1.0))(),
                                deps=(d1,), tag=f"l0a_zln{b}_{nt}_{h}")
                    d3 = P.step("scalar", (lambda r_d=r_d: lambda e:
                                           e.activation(r_d, r_d, AF.Exp, scale=-1.0))(),
                                deps=(d2,), tag=f"l0a_zexp{b}_{nt}_{h}")
                    z_rows.append(d3)
                def mmU0(e, h0=h0):
                    return e.matmul(pb[5][0:64, 0:512],
                                    KV[0:64, (h0 // 2) * 64:(h0 // 2) * 64 + 64],
                                    q_tile[0:64, (h0 // 2) * 512:(h0 // 2) * 512 + 512],
                                    start=True, stop=True)

                def mmU1(e, h1=h1):
                    return e.matmul(pb[4][0:64, 0:512],
                                    KV[64:128, (h1 // 2) * 64:(h1 // 2) * 64 + 64],
                                    q_tile[64:128, (h1 // 2) * 512:(h1 // 2) * 512 + 512],
                                    start=True, stop=True)
                u_dep = at_ev[-1] if at_ev else prev_nt.get("u")
                s_u0 = P.step("tensor", mmU0, deps=(kv_ev[-1], q_ev[p], u_dep), tag=f"l0a_U0{b}_{nt}_{p}")
                s_u1 = P.step("tensor", mmU1, deps=(kv_ev[-1], q_ev[p], u_dep), tag=f"l0a_U1{b}_{nt}_{p}")
                def mmrbc(e):
                    e.matmul(pb[7][:, 0:512], mk.EH[0:1, 0:128], mk.ROWS[0:1, 0:512],
                             start=True, stop=False)
                    return e.matmul(pb[7][:, 0:512], mk.EH[0:1, 128:256], mk.ROWS[0:1, 512:1024],
                                    start=False, stop=True)
                s_r = P.step("tensor", mmrbc,
                             deps=(z_rows, at_ev[-1] if at_ev else prev_nt.get("rbc")),
                             tag=f"l0a_rbc{b}_{nt}_{p}")
                rbc_sb = PD[:, 5632 + (p % 2) * 512: 6144 + (p % 2) * 512]
                c_r = P.step("vector", (lambda rbc_sb=rbc_sb: lambda e:
                                        e.tensor_copy(rbc_sb, pb[7][:, 0:512]))(),
                             deps=(s_r, at_ev[-2] if len(at_ev) >= 2 else prev_nt.get("rbc2")),
                             tag=f"l0a_rbcev{b}_{nt}_{p}")
                dst = a_tile[:, p * 512:(p + 1) * 512]
                ea0 = P.step("vector", (lambda rbc_sb=rbc_sb, dst=dst: lambda e:
                                        e.tensor_mul(dst[0:64, :], pb[5][0:64, 0:512], rbc_sb[0:64, :]))(),
                             deps=(s_u0, c_r), tag=f"l0a_at0{b}_{nt}_{p}")
                ea1 = P.step("vector", (lambda rbc_sb=rbc_sb, dst=dst: lambda e:
                                        e.tensor_mul(dst[64:128, :], pb[4][0:64, 0:512], rbc_sb[64:128, :]))(),
                             deps=(s_u1, c_r, ea0), tag=f"l0a_at1{b}_{nt}_{p}")
                at_ev.append(ea1)
            o_ev = []
            for o in range(NCH):
                bank = pb[o % 4]

                def mmo(e, o=o, bank=bank):
                    mmop = None
                    for c in range(NCH):
                        mmop = e.matmul(bank[:, 0:512],
                                        PB[:, 8192 + c * 1024 + o * 128: 8192 + c * 1024 + o * 128 + 128],
                                        a_tile[:, c * 512:(c + 1) * 512],
                                        start=(c == 0), stop=(c == NCH - 1))
                    return mmop
                prev_bank = o_ev[o - 4] if o >= 4 else q_ev[-1]
                s_o = P.step("tensor", mmo, deps=(wo_ld, at_ev[-1], prev_bank),
                             tag=f"l0a_o{b}_{nt}_{o}")
                dst = PD[:, o * 512:(o + 1) * 512]
                e_o = P.step("vector",
                             (lambda bank=bank, dst=dst, o=o: lambda e:
                              e.scalar_tensor_tensor(dst, bank[:, 0:512], bo[:, o:o + 1],
                                                     PC[:, o * 512:(o + 1) * 512],
                                                     ALU.add, ALU.add))(),
                             deps=(s_o, ldx, prev_nt.get("pd_out")), tag=f"l0a_oev{b}_{nt}_{o}")
                o_ev.append(e_o)
            st = P.dma("sync",
                       (lambda t0=t0: lambda e: e.dma_start(
                           out=bass.AP(x_out.tensor, t0, [[T_CORE, 128], [128 * T_CORE, NCH], [1, 512]]),
                           in_=PD[:, 0:4096].rearrange("p (c t) -> p c t", t=512)))(),
                       deps=(o_ev,), tag=f"l0a_st{b}_{nt}")
            out_final.append(st)
            prev_nt = {"x_cons": o_ev[-1], "qbank": o_ev[-4:], "qscr": q_ev[-1],
                       "den": at_ev[-1], "u": at_ev[-1], "rbc": at_ev[-1],
                       "rbc2": at_ev[-2] if len(at_ev) >= 2 else None, "pd_out": st,
                       "zrows": at_ev[-1]}
        prev_batch_done = out_final[-1]
        wb_free = prev_batch_done
    return out_final



def quant_scratch(mk):
    PD = mk.PD
    PAf = mk.PA[:, :].bitcast(F32)
    scr = {}
    names = ["vt", "y", "y2", "s1", "s2", "m", "im", "u", "ri", "rf", "d", "p"]
    for i, nm in enumerate(names):
        scr[nm] = PD[:, 1024 + i * 512: 1024 + (i + 1) * 512]
        scr[nm + "_off"] = 1024 + i * 512
    for i, nm in enumerate(["red", "sc", "lnsc", "rs"]):
        scr[nm] = PD[:, 7168 + i * 16: 7168 + (i + 1) * 16]
        scr[nm + "_off"] = 7168 + i * 16
    for i, nm in enumerate(["rs_fm", "s_fm", "cs", "bc"]):
        scr[nm] = PAf[:, 4096 + i * 512: 4096 + (i + 1) * 512]
        scr[nm + "_off"] = 4096 + i * 512
    return scr


def _quant_core(mk, tag, scr, xin, rs_fm, s_fm, out_bf, deps_y):
    P = mk.P
    c_m2 = mk.const_col(-2.0)
    c_q = mk.const_col(0.25)
    s10 = P.step("vector", (lambda: lambda e: e.tensor_mul(scr["y"], xin, rs_fm))(),
                 deps=deps_y, tag=tag + "_y")
    s11 = P.step("scalar", lambda e: e.activation(scr["y2"], scr["y"], AF.Square),
                 deps=(s10,), tag=tag + "_y2")
    s12 = P.step("gpsimd", lambda e: e.tensor_single_scalar(scr["s1"], scr["y2"], 4.0, ALU.is_ge),
                 deps=(s11,), tag=tag + "_s1")
    s13 = P.step("gpsimd", lambda e: e.tensor_single_scalar(scr["s2"], scr["y2"], 16.0, ALU.is_ge),
                 deps=(s11,), tag=tag + "_s2")
    s14 = P.step("gpsimd", lambda e: e.tensor_scalar(scr["m"], scr["s1"], -4.0, 8.0, ALU.mult, ALU.add),
                 deps=(s12,), tag=tag + "_m8")
    s15 = P.step("vector", lambda e: e.scalar_tensor_tensor(scr["m"], scr["s2"], c_m2[:, 0:1], scr["m"],
                                                            ALU.mult, ALU.add),
                 deps=(s13, s14, mk.cdep(-2.0)), tag=tag + "_m")
    s16 = P.step("gpsimd", lambda e: e.tensor_scalar(scr["im"], scr["s1"], 0.125, 0.125, ALU.mult, ALU.add),
                 deps=(s12,), tag=tag + "_im8")
    s17 = P.step("vector", lambda e: e.scalar_tensor_tensor(scr["im"], scr["s2"], c_q[:, 0:1], scr["im"],
                                                            ALU.mult, ALU.add),
                 deps=(s13, s16, mk.cdep(0.25)), tag=tag + "_im")
    s18 = P.step("gpsimd", lambda e: e.tensor_mul(scr["u"], scr["y"], scr["m"]),
                 deps=(s10, s15), tag=tag + "_u")
    ri = scr["ri"].bitcast(I32)[:, 0:512]
    s19 = P.step("vector", (lambda ri=ri: lambda e: e.tensor_single_scalar(ri, scr["u"], 0.0, ALU.add))(),
                 deps=(s18,), tag=tag + "_conv")
    s20 = P.step("vector", (lambda ri=ri: lambda e: e.tensor_scalar(scr["rf"], ri, 1.0, None, ALU.mult))(),
                 deps=(s19,), tag=tag + "_back")
    s21 = P.step("vector", lambda e: e.tensor_sub(scr["d"], scr["u"], scr["rf"]),
                 deps=(s20, s18), tag=tag + "_d")
    s22 = P.step("vector", lambda e: e.tensor_single_scalar(scr["p"], scr["d"], 0.5, ALU.is_ge),
                 deps=(s21,), tag=tag + "_p1")
    s23 = P.step("vector", lambda e: e.tensor_add(scr["rf"], scr["rf"], scr["p"]),
                 deps=(s22,), tag=tag + "_cor1")
    s24 = P.step("vector", lambda e: e.tensor_single_scalar(scr["p"], scr["d"], -0.5, ALU.is_le),
                 deps=(s23, s21), tag=tag + "_p2")
    s25 = P.step("vector", lambda e: e.tensor_sub(scr["rf"], scr["rf"], scr["p"]),
                 deps=(s24,), tag=tag + "_cor2")
    s26 = P.step("gpsimd", (lambda s_fm=s_fm: lambda e: e.tensor_mul(scr["cs"], scr["im"], s_fm))(),
                 deps=(s17, deps_y), tag=tag + "_cs")
    s27 = P.step("vector", (lambda out_bf=out_bf: lambda e: e.tensor_mul(out_bf, scr["rf"], scr["cs"]))(),
                 deps=(s25, s26), tag=tag + "_xd")
    return s27


def _scales_steps(mk, tag, scr, red_dep):
    P = mk.P
    s3 = P.step("vector", lambda e: e.tensor_scalar(scr["sc"], scr["red"], 1.0 / 7.5, 1e-30,
                                                    ALU.mult, ALU.max),
                deps=(red_dep,), tag=tag + "_sc")
    s4 = P.step("scalar", lambda e: e.activation(scr["lnsc"], scr["sc"], AF.Ln, scale=1.0),
                deps=(s3,), tag=tag + "_ln")
    s5 = P.step("scalar", lambda e: e.activation(scr["rs"], scr["lnsc"], AF.Exp, scale=-1.0),
                deps=(s4,), tag=tag + "_exp")
    return s3, s5


def _bcast32(mk, tag, scr, small_off, dst, deps):
    P = mk.P
    srcap = bass.AP(mk.PD, small_off, [[8192, 128], [1, 16], [0, 32]])
    return P.step("vector", (lambda dst=dst, srcap=srcap: lambda e: e.tensor_copy(
        dst.rearrange("p (b k) -> p b k", k=32), srcap))(), deps=deps, tag=tag)


def emit_quant_chain(mk, tag, xn32, scr, out_bf, deps):
    P = mk.P
    s1 = P.step("vector", (lambda xn32=xn32: lambda e: e.transpose(scr["vt"], xn32))(),
                deps=deps, tag=tag + "_vt")
    s2 = P.step("vector", lambda e: e.tensor_reduce(scr["red"], scr["vt"].rearrange("p (b k) -> p b k", k=32),
                                                    AX.X, ALU.max, apply_absolute_value=True),
                deps=(s1,), tag=tag + "_red")
    s3, s5 = _scales_steps(mk, tag, scr, s2)
    b1 = _bcast32(mk, tag + "_bc1", scr, scr["rs_off"], scr["bc"], (s5,))
    t1 = P.step("vector", lambda e: e.transpose(scr["rs_fm"], scr["bc"]), deps=(b1,), tag=tag + "_vt2")
    b2 = _bcast32(mk, tag + "_bc2", scr, scr["sc_off"], scr["bc"], (t1, s3))
    t2 = P.step("vector", lambda e: e.transpose(scr["s_fm"], scr["bc"]), deps=(b2,), tag=tag + "_vt3")
    return _quant_core(mk, tag, scr, xn32, scr["rs_fm"], scr["s_fm"], out_bf,
                       deps_y=(t1, t2, deps))


def emit_quant_free(mk, tag, xin, scr, out_bf, deps):
    P = mk.P
    s2_ = P.step("vector", (lambda xin=xin: lambda e: e.tensor_reduce(
        scr["red"], xin.rearrange("p (b k) -> p b k", k=32),
        AX.X, ALU.max, apply_absolute_value=True))(),
        deps=deps, tag=tag + "_red")
    s3, s5 = _scales_steps(mk, tag, scr, s2_)
    b1 = _bcast32(mk, tag + "_bc1", scr, scr["rs_off"], scr["rs_fm"], (s5,))
    b2 = _bcast32(mk, tag + "_bc2", scr, scr["sc_off"], scr["s_fm"], (s3,))
    return _quant_core(mk, tag, scr, xin, scr["rs_fm"], scr["s_fm"], out_bf,
                       deps_y=(b1, b2, deps))


def emit_ln_quant(mk, x_in, deps=()):
    P = mk.P
    pb = mk.pb
    r_m = mk.ROWS[0:1, 0:512]
    r_t = mk.ROWS[0:1, 512:1024]
    r_r = mk.ROWS[0:1, 1024:1536]
    r_mr = mk.ROWS[0:1, 1536:2048]
    epscol = mk.const_col(LN_EPS)
    scr = quant_scratch(mk)
    done = []
    prev_x = {0: [deps, mk.init_deps], 1: [deps, mk.init_deps]}
    prev_pa = mk.init_deps
    prev_rows = None
    prev_b = {4: None, 5: None, 6: None, 7: None}
    prev_q = None
    for tt in range(8):
        xt = mk.PC[:, (tt % 2) * 4096:(tt % 2) * 4096 + 4096]
        ld = P.dma("sync",
                   (lambda xt=xt, tt=tt: lambda e: e.dma_start(
                       out=xt.rearrange("p (c t) -> p c t", t=512),
                       in_=bass.AP(x_in.tensor, tt * 512, [[T_CORE, 128], [128 * T_CORE, NCH], [1, 512]])))(),
                   deps=(prev_x[tt % 2],), tag=f"lq_ld{tt}")
        xbf = mk.PA[:, 0:4096]
        x2b = mk.PA[:, 4096:8192]
        c1 = P.step("vector", (lambda xt=xt, xbf=xbf: lambda e: e.tensor_copy(xbf, xt))(),
                    deps=(ld, prev_pa), tag=f"lq_cast{tt}")
        c2 = P.step("scalar", (lambda xbf=xbf, x2b=x2b: lambda e: e.activation(x2b, xbf, AF.Square))(),
                    deps=(c1,), tag=f"lq_sq{tt}")

        def mk_stats(src, bank):
            def f(e):
                mm = None
                for c in range(NCH):
                    mm = e.matmul(bank[0:1, 0:512], mk.ONECB[:, 0:1],
                                  src[:, c * 512:(c + 1) * 512],
                                  start=(c == 0), stop=(c == NCH - 1))
                return mm
            return f
        m1 = P.step("tensor", mk_stats(xbf, pb[6]), deps=(c1, prev_b[6]), tag=f"lq_s1_{tt}")
        m2 = P.step("tensor", mk_stats(x2b, pb[7]), deps=(c2, prev_b[7]), tag=f"lq_s2_{tt}")
        q1 = P.step("vector", lambda e: e.tensor_single_scalar(r_m, pb[6][0:1, 0:512], 1.0 / D, ALU.mult),
                    deps=(m1, prev_rows), tag=f"lq_m{tt}")
        q2 = P.step("vector", lambda e: e.tensor_single_scalar(r_t, pb[7][0:1, 0:512], 1.0 / D, ALU.mult),
                    deps=(m2, prev_rows), tag=f"lq_e2{tt}")
        q3a = P.step("vector", lambda e: e.tensor_mul(r_r, r_m, r_m),
                     deps=(q1, prev_rows), tag=f"lq_msq{tt}")
        q3 = P.step("vector", lambda e: e.tensor_sub(r_t, r_t, r_r),
                    deps=(q2, q3a), tag=f"lq_var{tt}")
        q5 = P.step("scalar", lambda e: e.activation(r_t, r_t, AF.Ln, bias=epscol[0:1, 0:1], scale=1.0),
                    deps=(q3, mk.cdep(LN_EPS)), tag=f"lq_ln{tt}")
        q6 = P.step("scalar", lambda e: e.activation(r_r, r_t, AF.Exp, scale=-0.5),
                    deps=(q5,), tag=f"lq_rstd{tt}")
        q7 = P.step("vector", lambda e: e.tensor_mul(r_mr, r_m, r_r), deps=(q6, q1), tag=f"lq_mr{tt}")
        b1 = P.step("tensor", lambda e: e.matmul(pb[4][:, 0:512], mk.ONEF[0:1, 0:128], r_r,
                                                 start=True, stop=True),
                    deps=(q6, prev_b[4]), tag=f"lq_rb{tt}")
        b2 = P.step("tensor", lambda e: e.matmul(pb[5][:, 0:512], mk.ONEF[0:1, 0:128], r_mr,
                                                 start=True, stop=True),
                    deps=(q7, prev_b[5]), tag=f"lq_mrb{tt}")
        prev_rows = [b1, b2]
        xd_steps = []
        x_cons = []
        for c in range(NCH):
            xc = xt[:, c * 512:(c + 1) * 512]
            t1 = mk.PD[:, 0:512]
            xn32 = mk.PD[:, 512:1024]
            u1 = P.step("vector",
                        (lambda xc=xc, t1=t1: lambda e: e.tensor_mul(t1, xc, pb[4][:, 0:512]))(),
                        deps=(ld, b1, prev_q), tag=f"lq_t1_{tt}_{c}")
            u2 = P.step("vector",
                        (lambda t1=t1, xn32=xn32: lambda e: e.tensor_sub(xn32, t1, pb[5][:, 0:512]))(),
                        deps=(u1, b2), tag=f"lq_xn_{tt}_{c}")
            xn_sl = mk.XN[:, c * T_CORE + tt * 512: c * T_CORE + tt * 512 + 512]
            qd = emit_quant_chain(mk, f"lq_{tt}_{c}", xn32, scr, xn_sl, deps=(u2,))
            prev_q = qd
            xd_steps.append(qd)
            x_cons.append(u1)
        prev_x[tt % 2] = x_cons
        prev_pa = [m1, m2]
        prev_b[6] = q1
        prev_b[7] = q2
        prev_b[4] = xd_steps[-1]
        prev_b[5] = xd_steps[-1]
        done.append(xd_steps)
    return done


def emit_l0_ffn(mk, x_in, x_out, prev=(), first_call=True):
    P = mk.P
    pb = mk.pb
    XN, PA, PB, PC, PD = mk.XN, mk.PA, mk.PB, mk.PC, mk.PD
    bias = mk.col("l0f_bias")
    wdT = PB[:, 0:8192]
    wdT_hbm = mk.dram.get("wdT_bf")
    if wdT_hbm is None:
        wdT_hbm = mk.dram_tmp("wdT_bf", [128, 8192], BF16)

    if first_call:
        PBf = PB[:, :].bitcast(F32)
        wsrc = mk.dram["l0f_w"]
        wq_scr = quant_scratch(mk)
        wtile = PBf[:, 5120:6144]
        wd_bf = PB[:, 12288:13312]
        pbB = pb[3][:, 0:64].bitcast(BF16)
        wd_done = []
        prev_c = prev
        for r in range(NCH):
            ldw = P.dma("sync", (lambda r=r: lambda e: e.dma_start(
                out=wtile, in_=wsrc[r * 128:(r + 1) * 128, :]))(),
                deps=(prev_c, mk.init_deps), tag=f"l0f_wld{r}")
            halves = []
            for hh in range(2):
                xin = wtile[:, hh * 512:(hh + 1) * 512]
                outb = wd_bf[:, hh * 512:(hh + 1) * 512]
                qs = emit_quant_free(mk, f"l0f_w{r}_{hh}", xin, wq_scr, outb,
                                     deps=(ldw, halves[-1] if halves else prev_c))
                halves.append(qs)
            tr_steps = []
            for c in range(NCH):
                def mtr(e, r=r, c=c):
                    return e.transpose(pbB[:, 0:128], wd_bf[:, c * 128:(c + 1) * 128], mk.IDB[:])
                t1 = P.step("tensor", mtr, deps=(halves, tr_steps[-1] if tr_steps else None,
                                                 mk.init_deps),
                            tag=f"l0f_tr{r}_{c}")
                def etr(e, r=r, c=c):
                    return e.tensor_copy(wdT[:, c * 1024 + r * 128: c * 1024 + r * 128 + 128],
                                         pbB[:, 0:128])
                t2 = P.step("vector", etr, deps=(t1,), tag=f"l0f_trev{r}_{c}")
                tr_steps.append(t2)
            wd_done.extend(tr_steps)
            prev_c = tr_steps[-1]
        wst = P.dma("sync", lambda e: e.dma_start(out=wdT_hbm, in_=wdT), deps=(wd_done,),
                    tag="l0f_wst")
        wd_ready = wd_done + [wst]
    else:
        wld2 = P.dma("sync", lambda e: e.dma_start(out=wdT, in_=wdT_hbm), deps=(prev,),
                     tag="l0f_wld2")
        wd_ready = [wld2]

    ln_done = emit_ln_quant(mk, x_in, deps=(prev, wd_ready))
    lnbar = ln_done[-1]

    out_final = []
    prev_nt = {}
    for tt in range(8):
        t0 = tt * 512
        ldx = P.dma("sync",
                    (lambda t0=t0: lambda e: e.dma_start(
                        out=PC[:, 0:4096].rearrange("p (c t) -> p c t", t=512),
                        in_=bass.AP(x_in.tensor, t0, [[T_CORE, 128], [128 * T_CORE, NCH], [1, 512]])))(),
                    deps=(prev_nt.get("x_cons"), lnbar), tag=f"l0f_ldx{tt}")
        o_ev = []
        for o in range(NCH):
            bank = pb[o % 4]

            def mmo(e, o=o, t0=t0, bank=bank):
                mmop = None
                for c in range(NCH):
                    mmop = e.matmul(bank[:, 0:512],
                                    wdT[:, c * 1024 + o * 128: c * 1024 + o * 128 + 128],
                                    XN[:, c * T_CORE + t0: c * T_CORE + t0 + 512],
                                    start=(c == 0), stop=(c == NCH - 1))
                return mmop
            prev_bank = o_ev[o - 4] if o >= 4 else prev_nt.get("banks")
            s_o = P.step("tensor", mmo, deps=(wd_ready, ln_done[tt], prev_bank),
                         tag=f"l0f_mm{tt}_{o}")
            dst = PD[:, o * 512:(o + 1) * 512]
            e_o = P.step("vector",
                         (lambda bank=bank, dst=dst, o=o: lambda e:
                          e.scalar_tensor_tensor(dst, bank[:, 0:512], bias[:, o:o + 1],
                                                 PC[:, o * 512:(o + 1) * 512],
                                                 ALU.add, ALU.add))(),
                         deps=(s_o, ldx, prev_nt.get("pd_out"), lnbar), tag=f"l0f_oev{tt}_{o}")
            o_ev.append(e_o)
        st = P.dma("sync",
                   (lambda t0=t0: lambda e: e.dma_start(
                       out=bass.AP(x_out.tensor, t0, [[T_CORE, 128], [128 * T_CORE, NCH], [1, 512]]),
                       in_=PD[:, 0:4096].rearrange("p (c t) -> p c t", t=512)))(),
                   deps=(o_ev,), tag=f"l0f_st{tt}")
        out_final.append(st)
        prev_nt = {"x_cons": o_ev[-1], "banks": o_ev[-4:], "pd_out": st}
    return out_final



def emit_l0_attn_b(mk, x_in, x_out, prev=(), branches=(0, 1)):
    P = mk.P
    pb = mk.pb
    XN, PA, PB, PC, PD = mk.XN, mk.PA, mk.PB, mk.PC, mk.PD
    XNs = 8 * T_CORE
    ln_done = emit_ln(mk, x_in, deps=prev)
    lnbar = ln_done[-1]
    bqk = mk.col("l0b_bqk")
    bsum3 = mk.col("l0b_bsum3")
    third = mk.const_col(1.0 / 3.0)
    vrow_src = mk.dram["r_l0b_bv"]
    out_final = []
    prev_half = {}
    for b in range(B_PER_CORE):
        for hf in range(2):
            H0 = b * T_SEQ + hf * 1024
            ldx = P.dma("sync",
                        (lambda H0=H0: lambda e: e.dma_start(
                            out=PC[:, :].rearrange("p (c t) -> p c t", t=1024),
                            in_=bass.AP(x_in.tensor, H0, [[T_CORE, 128], [128 * T_CORE, NCH], [1, 1024]])))(),
                        deps=(prev_half.get("x_cons"), lnbar), tag=f"l0b_ldx{b}{hf}")
            st = None
            for br in branches:
                if br == 0:
                    tsel = [(H0, 1), (H0 + 512, 1)]
                else:
                    tsel = [(H0, 2), (H0 + 1, 2)]
                last_kind = prev_half.get("wcons") if br == 0 else p1_done
                proj_ev = {}
                for wi, kind in enumerate(("q", "k", "v")):
                    wld = mk.load_wT("l0b_wqkvT", PB[:, 8192:16384], 1024,
                                     deps=(last_kind, lnbar), col0=wi * 1024, rowstride=3072,
                                     base=br * D * 3072, tag=f"_{b}{hf}{br}{kind}")
                    if kind == "v":
                        ldr = P.dma("sync", (lambda br=br: lambda e: e.dma_start(
                            out=PD[0:1, 5120:6144], in_=vrow_src[br:br + 1, :]))(),
                            deps=(prev_half.get("rowb"), proj_ev.get("k")), tag=f"l0b_vr{b}{hf}{br}")
                        rcast = P.step("vector", lambda e: e.tensor_copy(mk.ROWB[0:1, 0:1024],
                                                                        PD[0:1, 5120:6144]),
                                       deps=(ldr, prev_half.get("rowb")), tag=f"l0b_vrc{b}{hf}{br}")
                    ev = []
                    if kind in ("q", "k"):
                        dstbuf = PA
                        dst0 = 0 if kind == "q" else 8192
                        bcol = bqk[:, br * 16 + (0 if kind == "q" else 8):
                                   br * 16 + (0 if kind == "q" else 8) + 8]
                        for o in range(NCH):
                            for s2 in range(2):
                                off, stride = tsel[s2]
                                bank = pb[(o * 2 + s2) % 4]

                                def mmp(e, o=o, off=off, stride=stride, bank=bank):
                                    mmop = None
                                    for c in range(NCH):
                                        rhs = bass.AP(XN, c * T_CORE + off, [[XNs, 128], [stride, 512]])
                                        mmop = e.matmul(bank[:, 0:512],
                                                        PB[:, 8192 + c * 1024 + o * 128:
                                                           8192 + c * 1024 + o * 128 + 128],
                                                        rhs, start=(c == 0), stop=(c == NCH - 1))
                                    return mmop
                                pv = ev[-4] if len(ev) >= 4 else last_kind
                                s_mm = P.step("tensor", mmp, deps=(wld, lnbar, pv),
                                              tag=f"l0b_{kind}{b}{hf}{br}_{o}_{s2}")
                                dst = dstbuf[:, dst0 + o * 1024 + s2 * 512: dst0 + o * 1024 + s2 * 512 + 512]
                                if kind == "q":
                                    e1 = P.step("vector",
                                                (lambda bank=bank, dst=dst, bcol=bcol, o=o: lambda e:
                                                 e.tensor_scalar(dst, bank[:, 0:512], bcol[:, o:o + 1],
                                                                 0.125, ALU.add, ALU.mult))(),
                                                deps=(s_mm,), tag=f"l0b_{kind}ev{b}{hf}{br}_{o}_{s2}")
                                else:
                                    e1 = P.step("vector",
                                                (lambda bank=bank, dst=dst, bcol=bcol, o=o: lambda e:
                                                 e.tensor_scalar(dst, bank[:, 0:512], bcol[:, o:o + 1], None,
                                                                 ALU.add))(),
                                                deps=(s_mm,), tag=f"l0b_{kind}ev{b}{hf}{br}_{o}_{s2}")
                                ev.append(e1)
                    else:
                        for j in range(8):
                            for h2 in range(2):
                                s2, jj = divmod(j, 4)
                                off, stride = tsel[s2]
                                bank = pb[(j * 2 + h2) % 4]

                                def mmv(e, off=off, stride=stride, jj=jj, h2=h2, bank=bank):
                                    for c in range(NCH):
                                        lhs = bass.AP(XN, c * T_CORE + off + stride * jj * 128,
                                                      [[XNs, 128], [stride, 128]])
                                        e.matmul(bank[:, 0:512], lhs,
                                                 PB[:, 8192 + c * 1024 + h2 * 512:
                                                    8192 + c * 1024 + h2 * 512 + 512],
                                                 start=(c == 0), stop=False)
                                    return e.matmul(bank[:, 0:512], mk.ONEB[0:1, 0:128],
                                                    mk.ROWB[0:1, h2 * 512: h2 * 512 + 512],
                                                    start=False, stop=True)
                                pv = ev[-4] if len(ev) >= 4 else last_kind
                                s_mm = P.step("tensor", mmv, deps=(wld, rcast, lnbar, pv),
                                              tag=f"l0b_v{b}{hf}{br}_{j}_{h2}")
                                dst = PB[:, j * 1024 + h2 * 512: j * 1024 + h2 * 512 + 512]
                                e1 = P.step("vector", (lambda bank=bank, dst=dst: lambda e:
                                                       e.tensor_copy(dst, bank[:, 0:512]))(),
                                            deps=(s_mm,), tag=f"l0b_vev{b}{hf}{br}_{j}_{h2}")
                                ev.append(e1)
                    proj_ev[kind] = ev
                    last_kind = ev[-1]
                q_done = proj_ev["q"][-1]
                k_done = proj_ev["k"][-1]
                v_done = proj_ev["v"][-1]
                E = PD[:, 0:1024].bitcast(BF16)
                mx = PD[:, 1024:1028]
                negB = PD[:, 1032:1033]
                rbc_sb = [PD[:, 1544:2056], PD[:, 2056:2568]]
                r_neg = PD[0:1, 4096:4100]
                r_nb = PD[0:1, 4104:4105]
                r_s = [mk.ROWS[0:1, 0:512], mk.ROWS[0:1, 512:1024]]
                at_done = []
                prev_p2 = {"E": v_done, "srow": None}
                for s2 in range(2):
                    for p in range(8):
                        pair_rows = []
                        for hh in range(2):
                            h = 2 * p + hh
                            hb = (h % 2) * 64
                            hc = h // 2
                            def mmS(e, hb=hb, hc=hc, s2=s2):
                                last = None
                                for i in range(4):
                                    last = e.matmul(pb[i][0:128, 0:512],
                                                    PA[hb:hb + 64, 8192 + hc * 1024 + s2 * 512 + i * 128:
                                                       8192 + hc * 1024 + s2 * 512 + i * 128 + 128],
                                                    PA[hb:hb + 64, hc * 1024 + s2 * 512:
                                                       hc * 1024 + s2 * 512 + 512],
                                                    start=True, stop=True)
                                return last
                            s_S = P.step("tensor", mmS, deps=(q_done, k_done, prev_p2["E"]),
                                         tag=f"l0b_S{b}{hf}{br}_{s2}_{h}")
                            mxs = []
                            for i in range(4):
                                mxs.append(P.step("vector",
                                                  (lambda i=i: lambda e: e.reduce_max(
                                                      mx[:, i:i + 1], pb[i][:, 0:512], AX.X))(),
                                                  deps=(s_S, prev_p2.get("mx")),
                                                  tag=f"l0b_mx{b}{hf}{br}_{s2}_{h}_{i}"))
                            g1 = P.step("gpsimd", lambda e: e.reduce_max(r_neg, mx, AX.C),
                                        deps=(mxs, prev_p2.get("rneg")), tag=f"l0b_gr{b}{hf}{br}_{s2}_{h}")
                            g2 = P.step("vector", lambda e: e.reduce_max(r_nb, r_neg, AX.X, negate=True),
                                        deps=(g1, prev_p2.get("nbread")), tag=f"l0b_nb{b}{hf}{br}_{s2}_{h}")
                            g3 = P.step("tensor", lambda e: e.matmul(pb[6][:, 8:9], mk.ONEF[0:1, 0:128],
                                                                     r_nb, start=True, stop=True),
                                        deps=(g2, prev_p2.get("nbmm")), tag=f"l0b_nbb{b}{hf}{br}_{s2}_{h}")
                            g4 = P.step("vector", lambda e: e.tensor_copy(negB, pb[6][:, 8:9]),
                                        deps=(g3, prev_p2.get("negB")), tag=f"l0b_nbc{b}{hf}{br}_{s2}_{h}")
                            exps = []
                            for i in range(4):
                                exps.append(P.step("scalar",
                                                   (lambda i=i: lambda e: e.activation(
                                                       E[:, i * 512:(i + 1) * 512], pb[i][:, 0:512],
                                                       AF.Exp, bias=negB[:, 0:1], scale=1.0))(),
                                                   deps=(g4, s_S, prev_p2.get("Econs"),),
                                                   tag=f"l0b_E{b}{hf}{br}_{s2}_{h}_{i}"))
                            def mms(e):
                                last = None
                                for i in range(4):
                                    last = e.matmul(pb[6][0:1, 0:512], mk.ONECB[:, 0:1],
                                                    E[:, i * 512:(i + 1) * 512],
                                                    start=(i == 0), stop=(i == 3))
                                return last
                            s_sum = P.step("tensor", mms, deps=(exps, prev_p2.get("srow_mm")),
                                           tag=f"l0b_sum{b}{hf}{br}_{s2}_{h}")
                            rr = r_s[hh]
                            z1 = P.step("vector", (lambda rr=rr: lambda e: e.tensor_copy(rr, pb[6][0:1, 0:512]))(),
                                        deps=(s_sum, prev_p2.get("rrow")), tag=f"l0b_sc{b}{hf}{br}_{s2}_{h}")
                            z2 = P.step("scalar", (lambda rr=rr: lambda e: e.activation(rr, rr, AF.Ln, scale=1.0))(),
                                        deps=(z1,), tag=f"l0b_zl{b}{hf}{br}_{s2}_{h}")
                            z3 = P.step("scalar", (lambda rr=rr: lambda e: e.activation(rr, rr, AF.Exp, scale=-1.0))(),
                                        deps=(z2,), tag=f"l0b_ze{b}{hf}{br}_{s2}_{h}")
                            def mmu(e, hb=hb, hc=hc, s2=s2, hh=hh):
                                bank = pb[5] if hh == 0 else pb[4]
                                last = None
                                for i in range(4):
                                    last = e.matmul(bank[0:64, 0:512],
                                                    PB[:, (s2 * 4 + i) * 1024 + hc * 128 + hb:
                                                       (s2 * 4 + i) * 1024 + hc * 128 + hb + 64],
                                                    E[:, i * 512:(i + 1) * 512],
                                                    start=(i == 0), stop=(i == 3))
                                return last
                            s_u = P.step("tensor", mmu, deps=(exps, v_done, prev_p2.get("u" + str(hh))),
                                         tag=f"l0b_U{b}{hf}{br}_{s2}_{h}")
                            pair_rows.append((z3, s_u))
                            prev_p2["E"] = s_sum
                            prev_p2["Econs"] = [s_sum, s_u]
                            prev_p2["mx"] = g1
                            prev_p2["rneg"] = g2
                            prev_p2["nbmm"] = z1
                            prev_p2["nbread"] = g3
                            prev_p2["negB"] = exps
                            prev_p2["srow_mm"] = z1
                            prev_p2["rrow"] = None
                        def mmr(e):
                            e.matmul(pb[7][:, 0:512], mk.EH[0:1, 0:128], r_s[0],
                                     start=True, stop=False)
                            return e.matmul(pb[7][:, 0:512], mk.EH[0:1, 128:256], r_s[1],
                                            start=False, stop=True)
                        s_r = P.step("tensor", mmr, deps=(pair_rows[0][0], pair_rows[1][0],
                                                          prev_p2.get("rbcmm")),
                                     tag=f"l0b_rbc{b}{hf}{br}_{s2}_{p}")
                        c_r = P.step("vector", (lambda p=p: lambda e: e.tensor_copy(
                            rbc_sb[p % 2], pb[7][:, 0:512]))(),
                            deps=(s_r, prev_p2.get("rbccons" + str(p % 2))),
                            tag=f"l0b_rbcev{b}{hf}{br}_{s2}_{p}")
                        prev_p2["rbcmm"] = c_r
                        prev_p2["rrow"] = s_r
                        dsts = PB[:, 8192 + s2 * 4096 + p * 512: 8192 + s2 * 4096 + p * 512 + 512]
                        ea0 = P.step("vector", (lambda dsts=dsts, p=p: lambda e: e.tensor_mul(
                            dsts[0:64, :], pb[5][0:64, 0:512], rbc_sb[p % 2][0:64, :]))(),
                            deps=(pair_rows[0][1], c_r), tag=f"l0b_a0{b}{hf}{br}_{s2}_{p}")
                        ea1 = P.step("vector", (lambda dsts=dsts, p=p: lambda e: e.tensor_mul(
                            dsts[64:128, :], pb[4][0:64, 0:512], rbc_sb[p % 2][64:128, :]))(),
                            deps=(pair_rows[1][1], c_r, ea0), tag=f"l0b_a1{b}{hf}{br}_{s2}_{p}")
                        prev_p2["u0"] = ea0
                        prev_p2["u1"] = ea1
                        prev_p2["rbccons" + str(p % 2)] = [ea0, ea1]
                        at_done.append(ea1)
                wldo = mk.load_wT("l0b_woutT", PB[:, 0:8192], 1024,
                                  deps=(at_done, prev_p2.get("u0"), prev_p2.get("u1")),
                                  base=br * D * D, tag=f"_{b}{hf}{br}o")
                o_evs = []
                for o in range(NCH):
                    for s2 in range(2):
                        bank = pb[(o * 2 + s2) % 4]

                        def mmo(e, o=o, s2=s2, bank=bank):
                            mmop = None
                            for c in range(NCH):
                                mmop = e.matmul(bank[:, 0:512],
                                                PB[:, c * 1024 + o * 128: c * 1024 + o * 128 + 128],
                                                PB[:, 8192 + s2 * 4096 + c * 512:
                                                   8192 + s2 * 4096 + c * 512 + 512],
                                                start=(c == 0), stop=(c == NCH - 1))
                            return mmop
                        pv = o_evs[-4] if len(o_evs) >= 4 else at_done[-1]
                        s_o = P.step("tensor", mmo, deps=(wldo, at_done[-1], pv),
                                     tag=f"l0b_o{b}{hf}{br}_{o}_{s2}")
                        if br == 0:
                            pcsl = PC[:, o * 1024 + s2 * 512: o * 1024 + s2 * 512 + 512]
                        else:
                            pcsl = bass.AP(PC, o * 1024 + s2, [[8192, 128], [2, 512]])
                        e_o = P.step("vector",
                                     (lambda bank=bank, pcsl=pcsl, o=o: lambda e:
                                      e.scalar_tensor_tensor(pcsl, bank[:, 0:512], third[:, 0:1],
                                                             pcsl, ALU.mult, ALU.add))(),
                                     deps=(s_o, ldx, prev_half.get("pc_cons"), mk.cdep(1.0 / 3.0)),
                                     tag=f"l0b_ov{b}{hf}{br}_{o}_{s2}")
                        if br == 0:
                            e_b = P.step("vector",
                                         (lambda pcsl=pcsl, o=o: lambda e:
                                          e.tensor_scalar(pcsl, pcsl, bsum3[:, o:o + 1], None, ALU.add))(),
                                         deps=(e_o,), tag=f"l0b_ob{b}{hf}{br}_{o}_{s2}")
                            o_evs.append(e_b)
                        else:
                            o_evs.append(e_o)
                p1_done = o_evs[-1]
                if br == branches[-1]:
                    st = P.dma("sync",
                               (lambda H0=H0: lambda e: e.dma_start(
                                   out=bass.AP(x_out.tensor, H0, [[T_CORE, 128], [128 * T_CORE, NCH], [1, 1024]]),
                                   in_=PC[:, :].rearrange("p (c t) -> p c t", t=1024)))(),
                               deps=(o_evs,), tag=f"l0b_st{b}{hf}")
                    out_final.append(st)
            prev_half = {"x_cons": st, "wcons": o_evs[-1], "banks": o_evs[-4:],
                         "rowb": p1_done, "pc_cons": st}
    return out_final



def emit_l1_attn_b(mk, x_in, x_out, prev=()):
    P = mk.P
    pb = mk.pb
    XN, PA, PB, PC, PD = mk.XN, mk.PA, mk.PB, mk.PC, mk.PD
    XNs = 8 * T_CORE
    ln_done = emit_ln(mk, x_in, deps=prev)
    lnbar = ln_done[-1]
    SCALE = 1.0 / np.sqrt(128.0)
    out_final = []

    MB = PD[:, 4096:5120].bitcast(BF16)
    mio = PD[:, 3072:3584].bitcast(I32)
    mask_steps = []
    for mi in range(4):
        g1 = P.step("gpsimd", (lambda mi=mi: lambda e: e.iota(mio[:, 0:512], [[1, 512]],
                                                              base=-mi * 128,
                                                              channel_multiplier=-1))(),
                    deps=(prev, mask_steps), tag=f"l1b_mio{mi}")
        g2 = P.step("vector", (lambda mi=mi: lambda e: e.tensor_single_scalar(
            MB[:, mi * 512:(mi + 1) * 512], mio[:, 0:512], 0, ALU.is_ge))(),
            deps=(g1,), tag=f"l1b_msk{mi}")
        mask_steps.append(g2)

    prev_b = {}
    for b in range(B_PER_CORE):
        tb = b * T_SEQ
        wqld = mk.load_wT("l1b_wq", PB[:, 8192:16384], 1024,
                          deps=(prev_b.get("E_cons"), lnbar, prev, mask_steps),
                          tag=f"_{b}")
        q_ev = []
        for h in range(8):
            for jt in range(4):
                bank = pb[(h * 4 + jt) % 4]

                def mmq(e, h=h, jt=jt, bank=bank, tb=tb):
                    mmop = None
                    for c in range(NCH):
                        mmop = e.matmul(bank[:, 0:512],
                                        PB[:, 8192 + c * 1024 + h * 128: 8192 + c * 1024 + h * 128 + 128],
                                        XN[:, c * T_CORE + tb + jt * 512: c * T_CORE + tb + jt * 512 + 512],
                                        start=(c == 0), stop=(c == NCH - 1))
                    return mmop
                pv = q_ev[-4] if len(q_ev) >= 4 else prev_b.get("banks")
                s_mm = P.step("tensor", mmq, deps=(wqld, ln_done[(tb + jt * 512) // 512], pv),
                              tag=f"l1b_q{b}_{h}_{jt}")
                dst = PA[:, h * 2048 + jt * 512: h * 2048 + jt * 512 + 512]
                e1 = P.step("vector", (lambda bank=bank, dst=dst: lambda e:
                                       e.tensor_scalar(dst, bank[:, 0:512], SCALE, None, ALU.mult))(),
                            deps=(s_mm,), tag=f"l1b_qe{b}_{h}_{jt}")
                q_ev.append(e1)
        klat_g = [PD[:, 5120:6144].bitcast(BF16), PD[:, 6144:7168].bitcast(BF16)]
        vlat_g = [PD[:, 2048:3072].bitcast(BF16), PD[:, 3072:4096].bitcast(BF16)]
        wkd = mk.load_wT("l1b_wk_down", PB[:, 8192:8704], 64, deps=(q_ev,), tag=f"_{b}")
        wvd = mk.load_wT("l1b_wv_down", PB[:, 8704:9216], 64, deps=(q_ev,), tag=f"_{b}")
        lat_ev = []
        for (wofs, dstl, wld) in ((8192, klat_g, wkd), (8704, vlat_g, wvd)):
            for g in range(2):
                for jt in range(4):
                    bank = pb[4 + (len(lat_ev) % 2)]

                    def mml(e, wofs=wofs, g=g, jt=jt, bank=bank, tb=tb):
                        mmop = None
                        for c in range(NCH):
                            mmop = e.matmul(bank[0:32, 0:512],
                                            PB[:, wofs + c * 64 + g * 32: wofs + c * 64 + g * 32 + 32],
                                            XN[:, c * T_CORE + tb + jt * 512: c * T_CORE + tb + jt * 512 + 512],
                                            start=(c == 0), stop=(c == NCH - 1))
                        return mmop
                    pv = lat_ev[-2] if len(lat_ev) >= 2 else prev_b.get("latb")
                    s_mm = P.step("tensor", mml, deps=(wld, ln_done[(tb + jt * 512) // 512], pv),
                                  tag=f"l1b_lat{b}_{wofs}_{g}_{jt}")
                    dstb = dstl[g]
                    e1 = P.step("vector", (lambda dstb=dstb, jt=jt, bank=bank: lambda e:
                                           e.tensor_copy(dstb[0:32, jt * 512:(jt + 1) * 512],
                                                         bank[0:32, 0:512]))(),
                                deps=(s_mm,), tag=f"l1b_late{b}_{wofs}_{g}_{jt}")
                    lat_ev.append(e1)
        wkup = PD[:, 7168:7232].bitcast(BF16)
        wvup = PD[:, 7232:7296].bitcast(BF16)
        lku = P.dma("gpsimd", lambda e: e.dma_start(out=wkup[0:32, :], in_=mk.dram["l1b_wk_up"]),
                    deps=(prev_b.get("upcons"), prev), tag=f"l1b_wku{b}")
        lvu = P.dma("gpsimd", lambda e: e.dma_start(out=wvup[0:32, :], in_=mk.dram["l1b_wv_up"]),
                    deps=(prev_b.get("upcons"), prev), tag=f"l1b_wvu{b}")
        up_ev = []
        for g in range(2):
            for jt in range(4):
                bank = pb[(len(up_ev)) % 4]

                def mmk(e, g=g, jt=jt, bank=bank):
                    return e.matmul(bank[:, 0:512], wkup[0:32, :],
                                    klat_g[g][0:32, jt * 512:(jt + 1) * 512],
                                    start=True, stop=True)
                pv = up_ev[-4] if len(up_ev) >= 4 else lat_ev[-1]
                s_mm = P.step("tensor", mmk, deps=(lku, lat_ev, pv), tag=f"l1b_kup{b}_{g}_{jt}")
                e1 = P.step("vector", (lambda g=g, jt=jt, bank=bank: lambda e:
                                       e.tensor_copy(PB[:, g * 2048 + jt * 512: g * 2048 + jt * 512 + 512],
                                                     bank[:, 0:512]))(),
                            deps=(s_mm, prev_b.get("kv_cons")), tag=f"l1b_kupe{b}_{g}_{jt}")
                up_ev.append(e1)
        for g in range(2):
            for j in range(16):
                bank = pb[(len(up_ev)) % 4]

                def mmv(e, g=g, j=j, bank=bank):
                    return e.matmul(bank[:, 0:128],
                                    vlat_g[g][0:32, j * 128:(j + 1) * 128],
                                    wvup[0:32, :], start=True, stop=True)
                pv = up_ev[-4] if len(up_ev) >= 4 else lat_ev[-1]
                s_mm = P.step("tensor", mmv, deps=(lvu, lat_ev, pv), tag=f"l1b_vup{b}_{g}_{j}")
                e1 = P.step("vector", (lambda g=g, j=j, bank=bank: lambda e:
                                       e.tensor_copy(PB[:, 4096 + j * 256 + g * 128:
                                                        4096 + j * 256 + g * 128 + 128],
                                                     bank[:, 0:128]))(),
                            deps=(s_mm, prev_b.get("kv_cons")), tag=f"l1b_vupe{b}_{g}_{j}")
                up_ev.append(e1)
        kv_done = up_ev[-1]

        E = PB[:, 8192:16384]
        mx = PD[:, 0:16]
        negB = PD[:, 16:17]
        r_ng = PD[0:1, 7296:7312]
        r_nb = PD[0:1, 7320:7321]
        r_s = mk.ROWS[0:1, 0:512]
        AT = PD[:, 2048:4096].bitcast(BF16)
        wo_bufs = [PD[:, 5120:5632].bitcast(BF16), PD[:, 5632:6144].bitcast(BF16)]
        p2 = {"qcons": kv_done, "wo": lat_ev}
        for J in range(4):
            nI = 4 * J + 4
            tq0 = J * 512
            ldx = P.dma("sync",
                        (lambda tb=tb, tq0=tq0, J=J: lambda e: e.dma_start(
                            out=PC[:, (J % 2) * 4096:(J % 2) * 4096 + 4096].rearrange("p (c t) -> p c t", t=512),
                            in_=bass.AP(x_in.tensor, tb + tq0,
                                        [[T_CORE, 128], [128 * T_CORE, NCH], [1, 512]])))(),
                        deps=(prev_b.get("pc" + str(J % 2)), lnbar, p2.get("pc" + str(J % 2))),
                        tag=f"l1b_ldx{b}_{J}")
            at_ev = []
            for h in range(8):
                g = h // 4
                def mmS1(e, g=g, h=h, tq0=tq0, nI=nI):
                    last = None
                    for I in range(nI):
                        last = e.matmul(pb[I % 2][:, 0:512],
                                        PB[:, g * 2048 + I * 128: g * 2048 + I * 128 + 128],
                                        PA[:, h * 2048 + tq0: h * 2048 + tq0 + 512],
                                        start=True, stop=True)
                    return last
                sS1 = []
                mx_ev = []
                for I0 in range(0, nI, 2):
                    def mmS1p(e, g=g, h=h, tq0=tq0, I0=I0, nI=nI):
                        last = None
                        for I in range(I0, min(I0 + 2, nI)):
                            last = e.matmul(pb[I % 2][:, 0:512],
                                            PB[:, g * 2048 + I * 128: g * 2048 + I * 128 + 128],
                                            PA[:, h * 2048 + tq0: h * 2048 + tq0 + 512],
                                            start=True, stop=True)
                        return last
                    dep_pb = mx_ev[-2:] if len(mx_ev) >= 2 else (p2.get("s1b"),)
                    s1 = P.step("tensor", mmS1p, deps=(kv_done, q_ev, dep_pb),
                                tag=f"l1b_S1{b}_{J}_{h}_{I0}")
                    for I in range(I0, min(I0 + 2, nI)):
                        mI = P.step("vector", (lambda I=I: lambda e: e.reduce_max(
                            mx[:, I:I + 1], pb[I % 2][:, 0:512], AX.X))(),
                            deps=(s1, p2.get("mx")), tag=f"l1b_mx{b}_{J}_{h}_{I}")
                        mx_ev.append(mI)
                g1 = P.step("gpsimd", (lambda nI=nI: lambda e: e.reduce_max(
                    r_ng[0:1, 0:nI], mx[:, 0:nI], AX.C))(),
                    deps=(mx_ev, p2.get("rng")), tag=f"l1b_g1{b}_{J}_{h}")
                g2 = P.step("vector", (lambda nI=nI: lambda e: e.reduce_max(
                    r_nb, r_ng[0:1, 0:nI], AX.X, negate=True))(),
                    deps=(g1, p2.get("nbread")), tag=f"l1b_g2{b}_{J}_{h}")
                g3 = P.step("tensor", lambda e: e.matmul(pb[6][:, 8:9], mk.ONEF[0:1, 0:128],
                                                         r_nb, start=True, stop=True),
                            deps=(g2, p2.get("nbmm")), tag=f"l1b_g3{b}_{J}_{h}")
                g4 = P.step("vector", lambda e: e.tensor_copy(negB, pb[6][:, 8:9]),
                            deps=(g3, p2.get("negBc")), tag=f"l1b_g4{b}_{J}_{h}")
                e_ev = []
                for I in range(nI):
                    bank = pb[I % 4]

                    def mmS2(e, g=g, h=h, tq0=tq0, I=I, bank=bank):
                        return e.matmul(bank[:, 0:512],
                                        PB[:, g * 2048 + I * 128: g * 2048 + I * 128 + 128],
                                        PA[:, h * 2048 + tq0: h * 2048 + tq0 + 512],
                                        start=True, stop=True)
                    dep_pb = e_ev[I - 4] if I >= 4 else (mx_ev, p2.get("s2b"))
                    s2 = P.step("tensor", mmS2, deps=(kv_done, q_ev, dep_pb, g4),
                                tag=f"l1b_S2{b}_{J}_{h}_{I}")
                    dstE = E[:, I * 512:(I + 1) * 512]
                    ex = P.step("scalar", (lambda bank=bank, dstE=dstE: lambda e:
                                           e.activation(dstE, bank[:, 0:512], AF.Exp,
                                                        bias=negB[:, 0:1], scale=1.0))(),
                                deps=(s2, g4, p2.get("Econs" + str(I))), tag=f"l1b_E{b}_{J}_{h}_{I}")
                    if I >= 4 * J:
                        mi = I - 4 * J
                        ex = P.step("vector", (lambda dstE=dstE, mi=mi: lambda e:
                                               e.tensor_mul(dstE, dstE, MB[:, mi * 512:(mi + 1) * 512]))(),
                                    deps=(ex,), tag=f"l1b_Em{b}_{J}_{h}_{I}")
                    e_ev.append(ex)
                    p2["Econs" + str(I)] = None
                def mms(e, nI=nI):
                    last = None
                    for I in range(nI):
                        last = e.matmul(pb[6][0:1, 0:512], mk.ONECB[:, 0:1],
                                        E[:, I * 512:(I + 1) * 512],
                                        start=(I == 0), stop=(I == nI - 1))
                    return last
                s_sum = P.step("tensor", mms, deps=(e_ev, p2.get("srmm")), tag=f"l1b_sum{b}_{J}_{h}")
                z1 = P.step("vector", lambda e: e.tensor_copy(r_s, pb[6][0:1, 0:512]),
                            deps=(s_sum, p2.get("rrow")), tag=f"l1b_z1{b}_{J}_{h}")
                z2 = P.step("scalar", lambda e: e.activation(r_s, r_s, AF.Ln, scale=1.0),
                            deps=(z1,), tag=f"l1b_z2{b}_{J}_{h}")
                z3 = P.step("scalar", lambda e: e.activation(r_s, r_s, AF.Exp, scale=-1.0),
                            deps=(z2,), tag=f"l1b_z3{b}_{J}_{h}")
                def mmu(e, g=g, nI=nI):
                    last = None
                    for I in range(nI):
                        last = e.matmul(pb[5][:, 0:512],
                                        PB[:, 4096 + I * 256 + g * 128: 4096 + I * 256 + g * 128 + 128],
                                        E[:, I * 512:(I + 1) * 512],
                                        start=(I == 0), stop=(I == nI - 1))
                    return last
                s_u = P.step("tensor", mmu, deps=(e_ev, kv_done, p2.get("ucons")),
                             tag=f"l1b_U{b}_{J}_{h}")
                s_r = P.step("tensor", lambda e: e.matmul(pb[7][:, 0:512], mk.ONEF[0:1, 0:128],
                                                          r_s, start=True, stop=True),
                             deps=(z3, p2.get("rbcmm")), tag=f"l1b_rbc{b}_{J}_{h}")
                rbc_sb = PD[:, 1024 + (h % 2) * 512: 1536 + (h % 2) * 512]
                c_r = P.step("vector", (lambda rbc_sb=rbc_sb: lambda e:
                                        e.tensor_copy(rbc_sb, pb[7][:, 0:512]))(),
                             deps=(s_r, p2.get("rbccons" + str(h % 2))),
                             tag=f"l1b_rbce{b}_{J}_{h}")
                dstA = AT[:, h * 512:(h + 1) * 512]
                ea = P.step("vector", (lambda dstA=dstA, rbc_sb=rbc_sb: lambda e:
                                       e.tensor_mul(dstA, pb[5][:, 0:512], rbc_sb))(),
                            deps=(s_u, c_r, p2.get("atcons"), kv_done), tag=f"l1b_at{b}_{J}_{h}")
                at_ev.append(ea)
                p2["mx"] = g1
                p2["rng"] = g2
                p2["nbread"] = g3
                p2["nbmm"] = z1
                p2["negBc"] = e_ev
                p2["s1b"] = e_ev[-4:]
                p2["s2b"] = e_ev[-1]
                p2["srmm"] = z1
                p2["rrow"] = s_r
                p2["ucons"] = ea
                p2["rbcmm"] = c_r
                p2["rbccons" + str(h % 2)] = ea
                p2["atcons"] = None
                for I in range(nI):
                    p2["Econs" + str(I)] = [s_sum, s_u]
            o_ev = []
            for o in range(NCH):
                wob = wo_bufs[o % 2]
                wld = mk.load_wT("l1b_wo", wob, 128,
                                 deps=((o_ev[o - 2],) if o >= 2 else (kv_done, p2.get("wo"))),
                                 col0=o * 128, rowstride=1024, tag=f"_{b}_{J}_{o}")
                bank = pb[o % 4]

                def mmo(e, wob=wob, bank=bank):
                    mmop = None
                    for c in range(NCH):
                        mmop = e.matmul(bank[:, 0:512],
                                        wob[:, c * 128:(c + 1) * 128],
                                        AT[:, c * 512:(c + 1) * 512],
                                        start=(c == 0), stop=(c == NCH - 1))
                    return mmop
                pv = o_ev[o - 4] if o >= 4 else (at_ev, p2.get("s2b"))
                s_o = P.step("tensor", mmo, deps=(wld, at_ev, pv), tag=f"l1b_o{b}_{J}_{o}")
                pcsl = PC[:, (J % 2) * 4096 + o * 512: (J % 2) * 4096 + o * 512 + 512]
                e_o = P.step("vector",
                             (lambda bank=bank, pcsl=pcsl: lambda e:
                              e.tensor_add(pcsl, bank[:, 0:512], pcsl))(),
                             deps=(s_o, ldx), tag=f"l1b_oe{b}_{J}_{o}")
                o_ev.append(e_o)
            st = P.dma("sync",
                       (lambda tb=tb, tq0=tq0, J=J: lambda e: e.dma_start(
                           out=bass.AP(x_out.tensor, tb + tq0,
                                       [[T_CORE, 128], [128 * T_CORE, NCH], [1, 512]]),
                           in_=PC[:, (J % 2) * 4096:(J % 2) * 4096 + 4096].rearrange("p (c t) -> p c t", t=512)))(),
                       deps=(o_ev,), tag=f"l1b_st{b}_{J}")
            out_final.append(st)
            p2["pc" + str(J % 2)] = st
            p2["wo"] = o_ev[-2:]
            p2["atcons"] = o_ev[-1]
            p2["s1b"] = o_ev[-4:]
            p2["s2b"] = o_ev[-1]
        prev_b = {"E_cons": out_final[-1], "banks": out_final[-1], "latb": out_final[-1],
                  "kv_cons": out_final[-1], "upcons": out_final[-1],
                  "pc0": p2.get("pc0"), "pc1": p2.get("pc1")}
    return out_final



def emit_l1_ffn(mk, x_in, x_out, prev=()):
    P = mk.P
    pb = mk.pb
    XN, PA, PB, PC, PD = mk.XN, mk.PA, mk.PB, mk.PC, mk.PD
    ln_done = emit_ln(mk, x_in, deps=prev)
    lnbar = ln_done[-1]
    E_, F_ = 15, 1024
    b1lo = mk.col("l1f_b1lo")
    b1hi = mk.col("l1f_b1hi")
    gmc = mk.col("l1f_gm")
    sblo = mk.col("l1f_sblo")
    sbhi = mk.col("l1f_sbhi")
    sgm = mk.col("l1f_sgm")
    negBIG = mk.const_col(-30.0)
    combf = PB[:, 8192:9152].bitcast(F32)
    combT = PB[:, 9152:13248]
    fc2b = PB[:, 13248:14272]
    srow = mk.ROWB[0:1, 0:1024]

    lb1 = P.dma("gpsimd", lambda e: e.dma_start(out=fc2b[0:15, :], in_=mk.dram["l1f_fc2b"]),
                deps=(prev,), tag="l1f_fc2b")
    lb2 = P.dma("sync", lambda e: e.dma_start(out=PD[0:1, 4096:5120], in_=mk.dram["r_l1f_sb2"]),
                deps=(prev,), tag="l1f_sb2")
    lb3 = P.step("vector", lambda e: e.tensor_copy(srow, PD[0:1, 4096:5120]),
                 deps=(lb2, prev), tag="l1f_sb2c")

    gwT = PA[:, 8192:8320]
    lgw = P.dma("gpsimd", lambda e: e.dma_start(out=gwT, in_=mk.dram["l1f_gate_wTc"]),
                deps=(prev,), tag="l1f_gw")
    lg_sbf = None
    gt_ev = []
    prevg = {}
    for tt in range(8 if DBG_GATE >= 0 else 0):
        def mmg(e, tt=tt):
            last = None
            for c in range(NCH):
                last = e.matmul(pb[4][0:16, 0:512], gwT[:, c * 16:(c + 1) * 16],
                                XN[:, c * T_CORE + tt * 512: c * T_CORE + tt * 512 + 512],
                                start=(c == 0), stop=(c == NCH - 1))
            return last
        lg_sb = PD[:, 1536:2048]
        s_g = P.step("tensor", mmg, deps=(lgw, lnbar, prevg.get("pb4")), tag=f"l1f_g{tt}")
        e_g = P.step("vector", lambda e: e.tensor_copy(lg_sb[0:15, :], pb[4][0:15, 0:512]),
                     deps=(s_g, prevg.get("lgsb")), tag=f"l1f_ge{tt}")
        prevg["pb4"] = e_g
        for q in range(4 if DBG_GATE >= 1 else 0):
            tc = tt * 4 + q
            lg_sb = PD[:, 1536:2048]
            t1 = P.step("tensor", (lambda q=q: lambda e: e.transpose(
                pb[5][:, 0:15], lg_sb[0:15, q * 128:(q + 1) * 128], mk.IDF[0:15, 0:15]))(),
                deps=(e_g, prevg.get("tp")), tag=f"l1f_tp{tt}_{q}")
            lg = PD[:, 1024:1040]
            t2 = P.step("vector", lambda e: e.tensor_copy(lg[:, 0:15], pb[5][:, 0:15]),
                        deps=(t1, prevg.get("lg")), tag=f"l1f_tpc{tt}_{q}")
            if DBG_DUMP and tt == 0 and q == 0:
                dmp2 = mk.dram_out("dbg_lg", [128, 16])
                dst1 = P.dma("sync", lambda e: e.dma_start(out=dmp2, in_=PD[:, 1024:1040]), deps=(t2,),
                             tag="dbglg")
                dmp3 = mk.dram_out("dbg_lgsb", [15, 512], F32)
                dst2 = P.dma("sync", lambda e: e.dma_start(out=dmp3, in_=lg_sb[0:15, :]), deps=(t2, e_g),
                             tag="dbglgsb")
                prevg["lg"] = [dst1, dst2]
                prevg["pb4"] = [dst1, dst2, e_g]
            nm = PD[:, 1040:1041]
            w1 = PD[:, 1041:1042]
            w2 = PD[:, 1042:1043]
            w3 = PD[:, 1043:1044]
            den = PD[:, 1044:1045]
            Pt = PD[:, 1056:1072]
            P2t = PD[:, 1072:1088]
            msk = PD[:, 1088:1104]
            sel = PD[:, 1104:1120]
            if DBG_GATE < 2:
                prevg["tp"] = t2
                prevg["lg"] = t2
                continue
            g1 = P.step("vector", lambda e: e.reduce_max(nm, lg[:, 0:15], AX.X, negate=True),
                        deps=(t2, prevg.get("sm")), tag=f"l1f_nm{tt}_{q}")
            g2 = P.step("scalar", lambda e: e.activation(Pt[:, 0:15], lg[:, 0:15], AF.Exp,
                                                         bias=nm[:, 0:1], scale=1.0),
                        deps=(g1, prevg.get("sm")), tag=f"l1f_ex{tt}_{q}")
            g3 = P.step("vector", lambda e: e.reduce_sum(den, Pt[:, 0:15], AX.X),
                        deps=(g2,), tag=f"l1f_sum{tt}_{q}")
            g4 = P.step("scalar", lambda e: e.activation(den, den, AF.Ln, scale=1.0),
                        deps=(g3,), tag=f"l1f_ln{tt}_{q}")
            g5 = P.step("scalar", lambda e: e.activation(den, den, AF.Exp, scale=-1.0),
                        deps=(g4,), tag=f"l1f_rs{tt}_{q}")
            g6 = P.step("vector", lambda e: e.tensor_scalar(Pt[:, 0:15], Pt[:, 0:15], den[:, 0:1],
                                                            None, ALU.mult),
                        deps=(g5,), tag=f"l1f_p{tt}_{q}")
            if DBG_GATE < 3:
                prevg["tp"] = t2
                prevg["lg"] = g2
                prevg["sm"] = [g6, g2]
                continue
            g7 = P.step("vector", lambda e: e.reduce_max(w1, Pt[:, 0:15], AX.X), deps=(g6,),
                        tag=f"l1f_m1{tt}_{q}")
            g8 = P.step("vector", lambda e: e.tensor_scalar(msk[:, 0:15], Pt[:, 0:15], w1[:, 0:1],
                                                            None, ALU.is_ge),
                        deps=(g7,), tag=f"l1f_k1{tt}_{q}")
            g9 = P.step("vector", lambda e: e.tensor_copy(sel[:, 0:15], msk[:, 0:15]),
                        deps=(g8, prevg.get("sel")), tag=f"l1f_s1{tt}_{q}")
            g10 = P.step("vector", lambda e: e.scalar_tensor_tensor(P2t[:, 0:15], msk[:, 0:15],
                                                                    negBIG[:, 0:1], Pt[:, 0:15],
                                                                    ALU.mult, ALU.add),
                         deps=(g9, g6, mk.cdep(-30.0)), tag=f"l1f_p2{tt}_{q}")
            g11 = P.step("vector", lambda e: e.reduce_max(w2, P2t[:, 0:15], AX.X), deps=(g10,),
                         tag=f"l1f_m2{tt}_{q}")
            g12 = P.step("vector", lambda e: e.tensor_scalar(msk[:, 0:15], P2t[:, 0:15], w2[:, 0:1],
                                                             None, ALU.is_ge),
                         deps=(g11,), tag=f"l1f_k2{tt}_{q}")
            g13 = P.step("vector", lambda e: e.tensor_add(sel[:, 0:15], sel[:, 0:15], msk[:, 0:15]),
                         deps=(g12,), tag=f"l1f_s2{tt}_{q}")
            g14 = P.step("vector", lambda e: e.scalar_tensor_tensor(P2t[:, 0:15], msk[:, 0:15],
                                                                    negBIG[:, 0:1], P2t[:, 0:15],
                                                                    ALU.mult, ALU.add),
                         deps=(g13, g10), tag=f"l1f_p3{tt}_{q}")
            g15 = P.step("vector", lambda e: e.reduce_max(w3, P2t[:, 0:15], AX.X), deps=(g14,),
                         tag=f"l1f_m3{tt}_{q}")
            g16 = P.step("vector", lambda e: e.tensor_scalar(msk[:, 0:15], P2t[:, 0:15], w3[:, 0:1],
                                                             None, ALU.is_ge),
                         deps=(g15,), tag=f"l1f_k3{tt}_{q}")
            g17 = P.step("vector", lambda e: e.tensor_add(sel[:, 0:15], sel[:, 0:15], msk[:, 0:15]),
                         deps=(g16,), tag=f"l1f_s3{tt}_{q}")
            g18 = P.step("vector", lambda e: e.tensor_add(den, w1, w2), deps=(g15, g7, g11, g5),
                         tag=f"l1f_d1{tt}_{q}")
            g19 = P.step("vector", lambda e: e.tensor_add(den, den, w3), deps=(g18,),
                         tag=f"l1f_d2{tt}_{q}")
            g20 = P.step("vector", lambda e: e.tensor_single_scalar(den, den, 1e-20, ALU.add),
                         deps=(g19,), tag=f"l1f_d3{tt}_{q}")
            g21 = P.step("scalar", lambda e: e.activation(den, den, AF.Ln, scale=1.0),
                         deps=(g20,), tag=f"l1f_d4{tt}_{q}")
            g22 = P.step("scalar", lambda e: e.activation(den, den, AF.Exp, scale=-1.0),
                         deps=(g21,), tag=f"l1f_d5{tt}_{q}")
            cslice = combf[:, tc * 15:(tc + 1) * 15]
            g23 = P.step("vector", (lambda cslice=cslice: lambda e: e.tensor_mul(
                cslice, Pt[:, 0:15], sel[:, 0:15]))(),
                deps=(g17, g6, prevg.get("combf" + str(tc % 2))), tag=f"l1f_cb{tt}_{q}")
            g24 = P.step("vector", (lambda cslice=cslice: lambda e: e.tensor_scalar(
                cslice, cslice, den[:, 0:1], None, ALU.mult))(),
                deps=(g23, g22), tag=f"l1f_cb2{tt}_{q}")
            cbf = PD[:, 1120:1136].bitcast(BF16)
            g25 = P.step("vector", (lambda cslice=cslice, cbf=cbf: lambda e: e.tensor_copy(
                cbf[:, 0:15], cslice))(),
                deps=(g24, prevg.get("cbf")), tag=f"l1f_cbc{tt}_{q}")
            pb5hi = pb[5][:, 64:128].bitcast(BF16)
            t3 = P.step("tensor", (lambda cbf=cbf, pb5hi=pb5hi: lambda e: e.transpose(
                pb5hi[0:15, 0:128], cbf[:, 0:15], mk.IDB[:]))(),
                deps=(g25, t1, prevg.get("tp2")), tag=f"l1f_tq{tt}_{q}")
            t4 = P.step("vector", (lambda tc=tc, pb5hi=pb5hi: lambda e: e.tensor_copy(
                combT[0:15, tc * 128:(tc + 1) * 128], pb5hi[0:15, 0:128]))(),
                deps=(t3, prevg.get("combT")), tag=f"l1f_tqe{tt}_{q}")
            prevg["sm"] = [g23, g10, g14]
            prevg["sel"] = g23
            prevg["lg"] = g2
            prevg["tp"] = t4
            prevg["tp2"] = t4
            prevg["cbf"] = t3
            prevg["combT"] = None
            prevg["combf" + str(tc % 2)] = None
        prevg["lgsb"] = t1 if DBG_GATE >= 1 else e_g
    gate_done = [t4, g24] if DBG_GATE >= 3 else ([lgw, lb1, lb3] + [v for v in prevg.values() if v is not None])

    if DBG_DUMP:
        dmp = mk.dram_out("dbg_comb", [128, 480])
        P.dma("sync", lambda e: e.dma_start(out=dmp, in_=combf), deps=(gate_done,), tag="dbgc")
    out_final = []
    prev_pass = {}
    for ps in range(4):
        t0 = ps * 1024
        ttA, ttB = t0 // 512, t0 // 512 + 1
        z_y = []
        for half in range(2):
            z_y.append(P.step("vector", (lambda half=half: lambda e: e.memset(
                PD[:, half * 4096:(half + 1) * 4096], 0.0))(),
                deps=(gate_done, prev_pass.get("yacc"), lb3, prev_pass.get("pccons")),
                tag=f"l1f_z{ps}_{half}"))
        a_done = {}
        w_free = [gate_done + [lgw] + list(z_y), gate_done + list(z_y)]
        yexp_done = list(z_y)
        prev_e = {}
        w2l = None
        y_ev = list(z_y)
        for ei in range(min(E_, DBG_NEXP) if DBG_L1F >= 2 else 0):
            wl0 = mk.load_wT("l1f_fc1T", PA[:, 0:8192], 1024, deps=(w_free[0],),
                             col0=0, rowstride=2048, base=ei * D * 2048, tag=f"_{ps}_{ei}0")
            wl1 = mk.load_wT("l1f_fc1T", PA[:, 8192:16384], 1024, deps=(w_free[1],),
                             col0=1024, rowstride=2048, base=ei * D * 2048, tag=f"_{ps}_{ei}1")
            a_ev = []
            mm_all = []
            for ti in range(2):
                tt = ttA + ti
                for fc in range(NCH):
                    bxh = pb[fc % 2]
                    bg = pb[2 + fc % 2]

                    def mmxh(e, tt=tt, fc=fc, bxh=bxh):
                        last = None
                        for c in range(NCH):
                            last = e.matmul(bxh[:, 0:512],
                                            PA[:, c * 1024 + fc * 128: c * 1024 + fc * 128 + 128],
                                            XN[:, c * T_CORE + tt * 512: c * T_CORE + tt * 512 + 512],
                                            start=(c == 0), stop=(c == NCH - 1))
                        return last

                    def mmgg(e, tt=tt, fc=fc, bg=bg):
                        last = None
                        for c in range(NCH):
                            last = e.matmul(bg[:, 0:512],
                                            PA[:, 8192 + c * 1024 + fc * 128: 8192 + c * 1024 + fc * 128 + 128],
                                            XN[:, c * T_CORE + tt * 512: c * T_CORE + tt * 512 + 512],
                                            start=(c == 0), stop=(c == NCH - 1))
                        return last
                    pv = a_ev[-2] if len(a_ev) >= 2 else prev_e.get("banks")
                    s_xh = P.step("tensor", mmxh, deps=(wl0, ln_done[tt], pv, z_y),
                                  tag=f"l1f_xh{ps}_{ei}_{ti}_{fc}")
                    s_gg = P.step("tensor", mmgg, deps=(wl1, ln_done[tt], pv, z_y),
                                  tag=f"l1f_gg{ps}_{ei}_{ti}_{fc}")
                    mm_all += [s_xh, s_gg]
                    if DBG_EXPPH < 2:
                        a_ev.append(P.step("vector", (lambda bxh=bxh, bg=bg,
                                                      dst=PB[:, ti * 4096 + fc * 512: ti * 4096 + fc * 512 + 512]:
                                                      lambda e: e.tensor_copy(dst, bg[:, 0:512]))(),
                                           deps=(s_xh, s_gg, a_ev[-2] if len(a_ev) >= 2 else prev_e.get("acons")),
                                           tag=f"l1f_acp{ps}_{ei}_{ti}_{fc}"))
                        continue
                    gsc = PC[:, (fc % 2) * 512:(fc % 2) * 512 + 512]
                    xsc = PC[:, 1024 + (fc % 2) * 512: 1536 + (fc % 2) * 512]
                    pvs = a_ev[-2] if len(a_ev) >= 2 else prev_e.get("scr")
                    e1 = P.step("scalar", (lambda bg=bg, gsc=gsc, ei=ei, fc=fc: lambda e:
                                           e.activation(gsc, bg[:, 0:512], AF.Gelu,
                                                        bias=b1hi[:, ei * 8 + fc: ei * 8 + fc + 1],
                                                        scale=1.0))(),
                                deps=(s_gg, pvs), tag=f"l1f_gel{ps}_{ei}_{ti}_{fc}")
                    e2 = P.step("vector", (lambda bxh=bxh, xsc=xsc, ei=ei, fc=fc: lambda e:
                                           e.tensor_scalar(xsc, bxh[:, 0:512],
                                                           b1lo[:, ei * 8 + fc: ei * 8 + fc + 1],
                                                           gmc[:, ei * 8 + fc: ei * 8 + fc + 1],
                                                           ALU.add, ALU.mult))(),
                                deps=(s_xh, pvs), tag=f"l1f_xsc{ps}_{ei}_{ti}_{fc}")
                    dst = PB[:, ti * 4096 + fc * 512: ti * 4096 + fc * 512 + 512]
                    e3 = P.step("gpsimd", (lambda gsc=gsc, xsc=xsc, dst=dst: lambda e:
                                           e.tensor_mul(dst, gsc, xsc))(),
                                deps=(e1, e2, prev_e.get("acons")), tag=f"l1f_a{ps}_{ei}_{ti}_{fc}")
                    a_ev.append(e3)
            w2l = mk.load_wT("l1f_fc2T", PA[:, 0:8192], 1024, deps=(mm_all[-2], mm_all[-1]),
                            base=ei * D * D, tag=f"_{ps}_{ei}2")
            if DBG_EXPPH < 3:
                prev_e = {"banks": a_ev[-2:], "scr": a_ev[-2:], "acons": a_ev[-4:],
                          "ybanks": None}
                w_free = [[w2l], mm_all[-1:] + [a_ev[-1]]]
                yexp_done.append(a_ev[-1])
                continue
            y_ev = []
            for ti in range(2):
                for tcq in range(4):
                    tc = ti * 4 + tcq
                    for dh in range(2):
                        bank = pb[4 + (len(y_ev) % 4)]

                        def mmy(e, ti=ti, tcq=tcq, dh=dh, bank=bank):
                            last = None
                            for fc in range(NCH):
                                last = e.matmul(bank[:, 0:512],
                                                PB[:, ti * 4096 + fc * 512 + tcq * 128:
                                                   ti * 4096 + fc * 512 + tcq * 128 + 128],
                                                PA[:, fc * 1024 + dh * 512: fc * 1024 + dh * 512 + 512],
                                                start=(fc == 0), stop=(fc == NCH - 1))
                            return last
                        pv = y_ev[-4] if len(y_ev) >= 4 else (mm_all[-1], prev_e.get("ybanks"))
                        s_y = P.step("tensor", mmy, deps=(w2l, a_ev, pv), tag=f"l1f_y{ps}_{ei}_{tc}_{dh}")
                        ysl = PD[:, tc * 1024 + dh * 512: tc * 1024 + dh * 512 + 512]
                        cidx = (t0 // 128 + tc) * 15 + ei
                        e_y = P.step("vector",
                                     (lambda bank=bank, ysl=ysl, cidx=cidx: lambda e:
                                      e.scalar_tensor_tensor(ysl, bank[:, 0:512],
                                                             combf[:, cidx:cidx + 1], ysl,
                                                             ALU.mult, ALU.add))(),
                                     deps=(s_y, z_y, prev_pass.get("ycons"),
                                           yexp_done[-16] if len(yexp_done) >= 16 else None),
                                     tag=f"l1f_ye{ps}_{ei}_{tc}_{dh}")
                        y_ev.append(e_y)
                        yexp_done.append(e_y)
            prev_e = {"banks": a_ev[-2:], "scr": a_ev[-2:], "acons": y_ev[-8:],
                      "ybanks": y_ev[-4:]}
            w_free = [[w2l] + y_ev[-4:], mm_all[-1:] + [a_ev[-1]]]
        sh_y = list(yexp_done[-2:])
        prev_sh = {"w0": ([w2l] if w2l else []) + y_ev[-4:], "w1": ([w2l] if w2l else []) + y_ev[-4:],
                   "acons": y_ev[-8:], "banks": y_ev[-2:], "ybanks": y_ev[-4:]}
        for ti in range(2 if DBG_L1F >= 3 else 0):
            tt = ttA + ti
            as_ev = []
            for j in range(2):
                wx = mk.load_wT("l1f_sfc1T", PA[:, 0:8192], 1024,
                                deps=(prev_sh["w0"],), col0=j * 1024, rowstride=4096,
                                tag=f"_{ps}_{ti}_{j}x")
                wg = mk.load_wT("l1f_sfc1T", PA[:, 8192:16384], 1024,
                                deps=(prev_sh["w1"],), col0=2048 + j * 1024, rowstride=4096,
                                tag=f"_{ps}_{ti}_{j}g")
                mm_lastx = mm_lastg = None
                for fcl in range(NCH):
                    fc = 8 * j + fcl
                    bxh = pb[fcl % 2]
                    bg = pb[2 + fcl % 2]

                    def mmxs(e, tt=tt, fcl=fcl, bxh=bxh):
                        last = None
                        for c in range(NCH):
                            last = e.matmul(bxh[:, 0:512],
                                            PA[:, c * 1024 + fcl * 128: c * 1024 + fcl * 128 + 128],
                                            XN[:, c * T_CORE + tt * 512: c * T_CORE + tt * 512 + 512],
                                            start=(c == 0), stop=(c == NCH - 1))
                        return last

                    def mmgs(e, tt=tt, fcl=fcl, bg=bg):
                        last = None
                        for c in range(NCH):
                            last = e.matmul(bg[:, 0:512],
                                            PA[:, 8192 + c * 1024 + fcl * 128: 8192 + c * 1024 + fcl * 128 + 128],
                                            XN[:, c * T_CORE + tt * 512: c * T_CORE + tt * 512 + 512],
                                            start=(c == 0), stop=(c == NCH - 1))
                        return last
                    pv = as_ev[-2] if len(as_ev) >= 2 else prev_sh["banks"]
                    s_xh = P.step("tensor", mmxs, deps=(wx, ln_done[tt], pv),
                                  tag=f"l1f_sxh{ps}_{ti}_{fc}")
                    s_gg = P.step("tensor", mmgs, deps=(wg, ln_done[tt], pv),
                                  tag=f"l1f_sgg{ps}_{ti}_{fc}")
                    mm_lastx, mm_lastg = s_xh, s_gg
                    gsc = PC[:, (fcl % 2) * 512:(fcl % 2) * 512 + 512]
                    xsc = PC[:, 1024 + (fcl % 2) * 512: 1536 + (fcl % 2) * 512]
                    pvs = as_ev[-2] if len(as_ev) >= 2 else prev_sh["acons"]
                    e1 = P.step("scalar", (lambda bg=bg, gsc=gsc, fc=fc: lambda e:
                                           e.activation(gsc, bg[:, 0:512], AF.Gelu,
                                                        bias=sbhi[:, fc:fc + 1], scale=1.0))(),
                                deps=(s_gg, pvs), tag=f"l1f_sgel{ps}_{ti}_{fc}")
                    e2 = P.step("vector", (lambda bxh=bxh, xsc=xsc, fc=fc: lambda e:
                                           e.tensor_scalar(xsc, bxh[:, 0:512],
                                                           sblo[:, fc:fc + 1], sgm[:, fc:fc + 1],
                                                           ALU.add, ALU.mult))(),
                                deps=(s_xh, pvs), tag=f"l1f_sxsc{ps}_{ti}_{fc}")
                    dst = PB[:, fc * 512:(fc + 1) * 512]
                    e3 = P.step("gpsimd", (lambda gsc=gsc, xsc=xsc, dst=dst: lambda e:
                                           e.tensor_mul(dst, gsc, xsc))(),
                                deps=(e1, e2, prev_sh.get("ascons")), tag=f"l1f_sa{ps}_{ti}_{fc}")
                    as_ev.append(e3)
                prev_sh["w0"] = [mm_lastx]
                prev_sh["w1"] = [mm_lastg]
                prev_sh["banks"] = as_ev[-2:]
                prev_sh["acons"] = as_ev[-2:]
            if DBG_SHPH < 2:
                sh_y.extend(as_ev[-2:])
                prev_sh["ascons"] = as_ev[-2:]
                prev_sh["w0"] = as_ev[-2:]
                prev_sh["w1"] = as_ev[-2:]
                continue
            wf0 = mk.load_wT("l1f_sfc2T", PA[:, 0:8192], 1024, deps=(prev_sh["w0"], as_ev[-2:]),
                             base=0, din=D, tag=f"_{ps}_{ti}f0")
            wf1 = mk.load_wT("l1f_sfc2T", PA[:, 8192:16384], 1024, deps=(prev_sh["w1"], as_ev[-1:]),
                             base=D * D, din=D, tag=f"_{ps}_{ti}f1")
            ys_ev = []
            for tcq in range(4):
                tc = ti * 4 + tcq
                for dh in range(2):
                    bank = pb[4 + (len(ys_ev) % 4)]

                    def mmys(e, ti=ti, tcq=tcq, tc=tc, dh=dh, bank=bank, t0=t0):
                        last = None
                        for fc in range(16):
                            wbuf_off = (fc % 8) * 1024 + dh * 512 + (0 if fc < 8 else 8192)
                            last = e.matmul(bank[:, 0:512],
                                     PB[:, fc * 512 + tcq * 128: fc * 512 + tcq * 128 + 128],
                                     PA[:, wbuf_off: wbuf_off + 512],
                                     start=(fc == 0), stop=(fc == 15 and not DBG_SHBIAS))
                        if not DBG_SHBIAS:
                            return last
                        e.matmul(bank[:, 0:512], mk.ONEB[0:1, 0:128],
                                 srow[0:1, dh * 512: dh * 512 + 512], start=False, stop=False)
                        return e.matmul(bank[:, 0:512],
                                        combT[0:15, t0 + tc * 128: t0 + tc * 128 + 128],
                                        fc2b[0:15, dh * 512: dh * 512 + 512],
                                        start=False, stop=True)
                    pv = ys_ev[-4] if len(ys_ev) >= 4 else (prev_sh.get("ybanks"),)
                    s_y = P.step("tensor", mmys, deps=(wf0, wf1, as_ev, lb1, lb3, gate_done, pv),
                                 tag=f"l1f_ys{ps}_{ti}_{tc}_{dh}")
                    ysl = PD[:, tc * 1024 + dh * 512: tc * 1024 + dh * 512 + 512]
                    e_y = P.step("vector",
                                 (lambda bank=bank, ysl=ysl: lambda e:
                                  e.tensor_add(ysl, bank[:, 0:512], ysl))(),
                                 deps=(s_y, yexp_done[-1],
                                       sh_y[-16] if len(sh_y) >= 16 else None),
                                 tag=f"l1f_yse{ps}_{ti}_{tc}_{dh}")
                    ys_ev.append(e_y)
                    sh_y.append(e_y)
            prev_sh["w0"] = [wf0] + ys_ev[-4:]
            prev_sh["w1"] = [wf1] + ys_ev[-4:]
            prev_sh["ybanks"] = ys_ev[-4:]
            prev_sh["ascons"] = ys_ev[-8:]
        tr_ev = []
        for ti in range(2):
            tt = ttA + ti
            pchalf = PC[:, ti * 4096:(ti + 1) * 4096]
            ldx = P.dma("sync",
                        (lambda tt=tt, pchalf=pchalf: lambda e: e.dma_start(
                            out=pchalf.rearrange("p (c t) -> p c t", t=512),
                            in_=bass.AP(x_in.tensor, tt * 512,
                                        [[T_CORE, 128], [128 * T_CORE, NCH], [1, 512]])))(),
                        deps=(sh_y, lnbar, prev_pass.get("pccons")), tag=f"l1f_ldx{ps}_{ti}")
            for tcq in range(4 if DBG_FIN >= 1 else 0):
                tc = ti * 4 + tcq
                for c in range(NCH):
                    bank = pb[len(tr_ev) % 4]

                    def mtr(e, tc=tc, c=c, bank=bank):
                        return e.transpose(bank[:, 0:128],
                                           PD[:, tc * 1024 + c * 128: tc * 1024 + c * 128 + 128],
                                           mk.IDF[:])
                    pv = tr_ev[-4] if len(tr_ev) >= 4 else (sh_y, prev_pass.get("trb"))
                    s_t = P.step("tensor", mtr, deps=(sh_y[-1], pv), tag=f"l1f_tr{ps}_{tc}_{c}")
                    dstx = pchalf[:, c * 512 + tcq * 128: c * 512 + tcq * 128 + 128]
                    e_t = P.step("vector", (lambda bank=bank, dstx=dstx: lambda e:
                                            e.tensor_add(dstx, bank[:, 0:128], dstx))(),
                                 deps=(s_t, ldx), tag=f"l1f_tre{ps}_{tc}_{c}")
                    tr_ev.append(e_t)
            st = P.dma("sync",
                       (lambda tt=tt, pchalf=pchalf: lambda e: e.dma_start(
                           out=bass.AP(x_out.tensor, tt * 512,
                                       [[T_CORE, 128], [128 * T_CORE, NCH], [1, 512]]),
                           in_=pchalf.rearrange("p (c t) -> p c t", t=512)))(),
                       deps=(tr_ev, ldx), tag=f"l1f_st{ps}_{ti}")
            out_final.append(st)
        prev_pass = {"yacc": tr_ev if tr_ev else z_y, "ycons": tr_ev, "pccons": out_final[-2:],
                     "trb": tr_ev[-4:] if tr_ev else None}
    return out_final



def emit_l1_attn_a(mk, x_in, x_out, prev=()):
    P = mk.P
    pb = mk.pb
    XN, PA, PB, PC, PD = mk.XN, mk.PA, mk.PB, mk.PC, mk.PD
    XNs = 8 * T_CORE
    ln_done = emit_ln(mk, x_in, deps=prev)
    lnbar = ln_done[-1]
    RATIO = 1.0 / 16.0
    EPSK = 1e-4
    bkc = mk.col("l1a_bk"); bqc = mk.col("l1a_bq"); boc = mk.col("l1a_bo")
    g2c = mk.col("l1a_g2"); be2c = mk.col("l1a_be2")
    g3c = mk.col("l1a_g3"); be3c = mk.col("l1a_be3")
    b1c = mk.col("l1a_b1")
    b2c = mk.col("l1a_b2")
    epscol = mk.const_col(LN_EPS)
    PCw = PC[:, 0:4096].bitcast(BF16)
    PCw2 = PC[:, 4096:8192].bitcast(BF16)
    PCf = PC[:, 4096:8192]
    DG = PD[:, 0:256]
    CTX = PD[:, 256:1296].bitcast(BF16)
    PRJ = PD[:, 1296:1424].bitcast(BF16)
    EHB = PD[:, 1424:1426].bitcast(BF16)
    KP = PD[:, 1440:1568].bitcast(BF16)
    QPT = PD[:, 1568:2080].bitcast(BF16)
    K2Q = PD[:, 2080:2336].bitcast(BF16)
    ATT = PD[:, 4384:6432].bitcast(BF16)
    RB1 = PD[:, 6432:6944]
    DGQ_BASE = 6944
    MX = PD[:, 7008:7024]
    NBC = PD[:, 7024:7025]
    r_a = mk.ROWS[0:1, 0:512]
    r_b = mk.ROWS[0:1, 512:1024]
    rr1 = PD[0:1, 7040:7056]
    rr2 = PD[0:1, 7064:7065]

    PRJF = PD[:, 7168:7424]
    p1 = P.dma("sync", lambda e: e.dma_start(out=PRJF[0:64, :], in_=mk.dram["l1a_projT"]),
               deps=(prev,), tag="l1a_pld")
    p1b = P.dma("sync", lambda e: e.dma_start(out=PRJF[64:128, :], in_=mk.dram["l1a_projT"]),
                deps=(prev,), tag="l1a_pld2")
    p2a = P.step("vector", lambda e: e.tensor_scalar(PRJ[0:64, :], PRJF[0:64, :],
                                                     float(64.0 ** (-0.25)), None, ALU.mult),
                 deps=(p1, prev), tag="l1a_pscale")
    p2b = P.step("vector", lambda e: e.tensor_scalar(PRJ[64:128, :], PRJF[64:128, :],
                                                     float(64.0 ** (-0.25)), None, ALU.mult),
                 deps=(p1b, prev), tag="l1a_pscale2")
    p2 = [p2a, p2b]
    e1_ = P.step("vector", lambda e: e.memset(EHB[0:64, 0:1], 1.0), deps=(prev,), tag="l1a_eh0")
    e2_ = P.step("vector", lambda e: e.memset(EHB[64:128, 0:1], 0.0), deps=(prev,), tag="l1a_eh1")
    e3_ = P.step("vector", lambda e: e.memset(EHB[0:64, 1:2], 0.0), deps=(prev,), tag="l1a_eh2")
    e4_ = P.step("vector", lambda e: e.memset(EHB[64:128, 1:2], 1.0), deps=(prev,), tag="l1a_eh3")
    ehb_done = [e1_, e2_, e3_, e4_]
    out_final = []
    prev_b = {}
    for b in range(B_PER_CORE):
        tb = b * T_SEQ
        ldr = P.dma("sync", lambda e: e.dma_start(out=PCf[0:1, 0:1024], in_=mk.dram["r_l1a_bv"]),
                    deps=(prev_b.get("allc"), prev, lnbar), tag=f"l1a_bvld{b}")
        rv = P.step("vector", lambda e: e.tensor_copy(mk.ROWB[0:1, 0:1024], PCf[0:1, 0:1024]),
                    deps=(ldr, prev_b.get("rowb")), tag=f"l1a_bvc{b}")
        wv = mk.load_wT("l1a_wvT", PCw[:, 0:8192], 1024, deps=(prev_b.get("allc"), lnbar, prev),
                        tag=f"_{b}v")
        v_ev = []
        for j in range(16):
            for h2 in range(2):
                bank = pb[(j * 2 + h2) % 4]

                def mmv(e, j=j, h2=h2, bank=bank, tb=tb):
                    for c in range(NCH):
                        e.matmul(bank[:, 0:512],
                                 XN[:, c * T_CORE + tb + j * 128: c * T_CORE + tb + j * 128 + 128],
                                 PCw[:, c * 1024 + h2 * 512: c * 1024 + h2 * 512 + 512],
                                 start=(c == 0), stop=False)
                    return e.matmul(bank[:, 0:512], mk.ONEB[0:1, 0:128],
                                    mk.ROWB[0:1, h2 * 512: h2 * 512 + 512],
                                    start=False, stop=True)
                pv = v_ev[-4] if len(v_ev) >= 4 else prev_b.get("banks")
                s_mm = P.step("tensor", mmv, deps=(wv, rv, ln_done[(tb + j * 128) // 512], pv),
                              tag=f"l1a_v{b}_{j}_{h2}")
                dst = PB[:, j * 1024 + h2 * 512: j * 1024 + h2 * 512 + 512]
                ev = P.step("vector", (lambda bank=bank, dst=dst: lambda e:
                                       e.tensor_copy(dst, bank[:, 0:512]))(),
                            deps=(s_mm,), tag=f"l1a_vev{b}_{j}_{h2}")
                v_ev.append(ev)
        v_done = v_ev[-1]
        wk = mk.load_wT("l1a_wkT", PCw2[:, 0:8192], 1024, deps=(v_ev[-8:], rv, lnbar), tag=f"_{b}k")
        k_ev = []
        for o in range(NCH):
            for nt in range(4):
                bank = pb[(o * 4 + nt) % 4]

                def mmk(e, o=o, nt=nt, bank=bank, tb=tb):
                    last = None
                    for c in range(NCH):
                        last = e.matmul(bank[:, 0:512],
                                        PCw2[:, c * 1024 + o * 128: c * 1024 + o * 128 + 128],
                                        XN[:, c * T_CORE + tb + nt * 512: c * T_CORE + tb + nt * 512 + 512],
                                        start=(c == 0), stop=(c == NCH - 1))
                    return last
                pv = k_ev[-4] if len(k_ev) >= 4 else v_ev[-4:]
                s_mm = P.step("tensor", mmk, deps=(wk, ln_done[(tb + nt * 512) // 512], pv),
                              tag=f"l1a_k{b}_{o}_{nt}")
                dst = PA[:, o * 2048 + nt * 512: o * 2048 + nt * 512 + 512]
                ev = P.step("vector", (lambda bank=bank, dst=dst, o=o: lambda e:
                                       e.tensor_scalar(dst, bank[:, 0:512], bkc[:, o:o + 1], None,
                                                       ALU.add))(),
                            deps=(s_mm,), tag=f"l1a_kev{b}_{o}_{nt}")
                k_ev.append(ev)
        k_done = k_ev[-1]
        dg_ev = []
        K2 = PCw[:, 0:512]
        prev_k2 = [v_done] + k_ev[-2:]
        for c in range(NCH):
            for nt in range(4):
                sq = P.step("scalar", (lambda c=c, nt=nt: lambda e: e.activation(
                    K2, PA[:, c * 2048 + nt * 512: c * 2048 + nt * 512 + 512], AF.Square))(),
                    deps=(k_done, prev_k2), tag=f"l1a_k2{b}_{c}_{nt}")

                def mmd(e, c=c, nt=nt):
                    last = None
                    for jj in range(4):
                        last = e.matmul(pb[4][:, 4 * jj: 4 * jj + 2],
                                        K2[:, jj * 128:(jj + 1) * 128], EHB[:, 0:2],
                                        start=True, stop=True)
                    return last
                s_d = P.step("tensor", mmd, deps=(sq, ehb_done, dg_ev[-1] if dg_ev else None),
                             tag=f"l1a_dg{b}_{c}_{nt}")
                dgv = bass.AP(PD, (nt * 4) * 16 + 2 * c, [[8192, 128], [16, 4], [1, 2]])
                e_d = P.step("vector", (lambda dgv=dgv: lambda e: e.tensor_copy(
                    dgv, pb[4][:, 0:16].rearrange("p (j x) -> p j x", x=4)[:, :, 0:2]))(),
                    deps=(s_d, prev_b.get("dgc")), tag=f"l1a_dge{b}_{c}_{nt}")
                dg_ev.append(e_d)
                prev_k2 = [s_d]
        dg_done = dg_ev[-1]
        ctx_ev = []
        ph = {"xp": dg_ev[-1], "mx": None, "ctxb": prev_b.get("ctxb")}
        for h in range(16):
            hb = (h % 2) * 64
            hc = h // 2
            mx_ev = []
            for j in range(16):
                bank = pb[j % 2]

                def mmx(e, hb=hb, hc=hc, j=j, bank=bank):
                    return e.matmul(bank[:, 0:256],
                                    PA[hb:hb + 64, hc * 2048 + j * 128: hc * 2048 + j * 128 + 128],
                                    PRJ[hb:hb + 64, 0:256], start=True, stop=True)
                pv = mx_ev[-2] if len(mx_ev) >= 2 else ph.get("xp")
                s_x = P.step("tensor", mmx, deps=(k_done, p2, pv), tag=f"l1a_xk{b}_{h}_{j}")
                m1 = P.step("vector", (lambda j=j, bank=bank: lambda e: e.reduce_max(
                    MX[:, j:j + 1], bank[:, 0:256], AX.X))(),
                    deps=(s_x, ph.get("mx")), tag=f"l1a_mxk{b}_{h}_{j}")
                mx_ev.append(m1)
            gk1 = P.step("gpsimd", lambda e: e.reduce_max(rr1, MX[:, 0:16], AX.C),
                         deps=(mx_ev, ph.get("g1")), tag=f"l1a_gk1{b}_{h}")
            gk2 = P.step("vector", lambda e: e.reduce_max(rr2, rr1, AX.X, negate=True),
                         deps=(gk1, ph.get("g2")), tag=f"l1a_gk2{b}_{h}")
            gk3 = P.step("tensor", lambda e: e.matmul(pb[0][:, 8:9], mk.ONEF[0:1, 0:128], rr2,
                                                      start=True, stop=True),
                         deps=(gk2, ph.get("g3"), mx_ev[-2:]), tag=f"l1a_gk3{b}_{h}")
            gk4 = P.step("vector", lambda e: e.tensor_copy(NBC, pb[0][:, 8:9]),
                         deps=(gk3, ph.get("g4")), tag=f"l1a_gk4{b}_{h}")
            kp_ev = []
            for j in range(16):
                bank = pb[2 + j % 2]

                def mmx2(e, hb=hb, hc=hc, j=j, bank=bank):
                    return e.matmul(bank[:, 0:256],
                                    PA[hb:hb + 64, hc * 2048 + j * 128: hc * 2048 + j * 128 + 128],
                                    PRJ[hb:hb + 64, 0:256], start=True, stop=True)
                pv = kp_ev[-2] if len(kp_ev) >= 2 else (mx_ev[-1], ph.get("kp2"))
                s_x2 = P.step("tensor", mmx2, deps=(k_done, p2, pv, gk4),
                              tag=f"l1a_xk2{b}_{h}_{j}")
                bcol = PD[:, 7026 + (j % 2): 7027 + (j % 2)]
                dgcol = bass.AP(PD, j * 16 + h, [[8192, 128], [1, 1]])
                c00625 = mk.const_col(-0.0625)
                b_c = P.step("vector", (lambda bcol=bcol, dgcol=dgcol: lambda e:
                                        e.scalar_tensor_tensor(bcol, dgcol, c00625[:, 0:1], NBC,
                                                               ALU.mult, ALU.add))(),
                             deps=(dg_done, gk4, kp_ev[-2] if len(kp_ev) >= 2 else ph.get("bc"),
                                   mk.cdep(-0.0625)),
                             tag=f"l1a_bck{b}_{h}_{j}")
                exk = P.step("scalar", (lambda bank=bank, bcol=bcol: lambda e:
                                        e.activation(KP, bank[:, 0:256], AF.Exp,
                                                     bias=bcol[:, 0:1], scale=1.0))(),
                             deps=(s_x2, b_c, ph.get("kpc")), tag=f"l1a_ek{b}_{h}_{j}")
                tsk = P.step("vector", lambda e: e.tensor_scalar(KP, KP, RATIO, RATIO * EPSK,
                                                                 ALU.mult, ALU.add),
                             deps=(exk,), tag=f"l1a_tk{b}_{h}_{j}")
                def mmc(e, h=h, j=j):
                    last = None
                    for mc in range(2):
                        e.matmul(pb[4 + mc][0:128, 0:64],
                                 KP[:, mc * 128:(mc + 1) * 128],
                                 PB[:, j * 1024 + h * 64: j * 1024 + h * 64 + 64],
                                 start=(j == 0), stop=(j == 15))
                        last = e.matmul(pb[6 + mc][0:128, 0:1],
                                        KP[:, mc * 128:(mc + 1) * 128], mk.ONECB[:, 0:1],
                                        start=(j == 0), stop=(j == 15))
                    return last
                s_c = P.step("tensor", mmc, deps=(tsk, v_done, ph.get("ctxmm")), tag=f"l1a_ctx{b}_{h}_{j}")
                kp_ev.append(s_c)
                ph["kpc"] = s_c
            ec1 = P.step("vector", (lambda h=h: lambda e: e.tensor_copy(
                CTX[:, h * 130: h * 130 + 64], pb[4][:, 0:64]))(),
                deps=(kp_ev[-1], ph.get("ctxb")), tag=f"l1a_ctxe1{b}_{h}")
            ec1b = P.step("vector", (lambda h=h: lambda e: e.tensor_copy(
                CTX[:, h * 130 + 64: h * 130 + 65], pb[6][:, 0:1]))(),
                deps=(kp_ev[-1], ec1), tag=f"l1a_ctxe1b{b}_{h}")
            ec2 = P.step("vector", (lambda h=h: lambda e: e.tensor_copy(
                CTX[:, h * 130 + 65: h * 130 + 129], pb[5][:, 0:64]))(),
                deps=(kp_ev[-1], ec1b), tag=f"l1a_ctxe2{b}_{h}")
            ec2b = P.step("vector", (lambda h=h: lambda e: e.tensor_copy(
                CTX[:, h * 130 + 129: h * 130 + 130], pb[7][:, 0:1]))(),
                deps=(kp_ev[-1], ec2), tag=f"l1a_ctxe2b{b}_{h}")
            ec2 = ec2b
            ctx_ev.append(ec2)
            ph["ctxmm"] = ec2
            ph["xp"] = kp_ev[-1]
            ph["mx"] = gk1
            ph["g1"] = gk2
            ph["g2"] = gk3
            ph["g3"] = gk4
            ph["g4"] = exk
            ph["bc"] = exk
            ph["kp2"] = kp_ev[-1]
        ctx_done = ctx_ev[-1]
        prev_b = dict(prev_b)
        prev_b["phaseA"] = ctx_done
        QT = PD[:, 2336:4384].bitcast(BF16)
        pnt = {}
        for nt in range(4):
            t0 = tb + nt * 512
            tt = t0 // 512
            wq = mk.load_wT("l1a_wqT", PCw[:, 0:8192], 1024,
                            deps=(k_ev[-4:], dg_ev[-2:], ph["xp"], ctx_done, pnt.get("pcw")),
                            tag=f"_{b}q{nt}")
            wo = mk.load_wT("l1a_woT", PB[:, 0:8192], 1024,
                            deps=(v_ev[-4:], ctx_done, dg_ev[-1:], pnt.get("pbw")),
                            tag=f"_{b}o{nt}")
            q_ev = []
            for o in range(NCH):
                bank = pb[o % 4]

                def mmq(e, o=o, t0=t0, bank=bank):
                    last = None
                    for c in range(NCH):
                        last = e.matmul(bank[:, 0:512],
                                        PCw[:, c * 1024 + o * 128: c * 1024 + o * 128 + 128],
                                        XN[:, c * T_CORE + t0: c * T_CORE + t0 + 512],
                                        start=(c == 0), stop=(c == NCH - 1))
                    return last
                pv = q_ev[o - 4] if o >= 4 else (pnt.get("qb"), ctx_done)
                s_q = P.step("tensor", mmq, deps=(wq, ln_done[tt], pv), tag=f"l1a_q{b}_{nt}_{o}")
                dst = QT[:, o * 512:(o + 1) * 512]
                ev = P.step("vector", (lambda bank=bank, dst=dst, o=o: lambda e:
                                       e.tensor_scalar(dst, bank[:, 0:512], bqc[:, o:o + 1], None,
                                                       ALU.add))(),
                            deps=(s_q, pnt.get("qtc")), tag=f"l1a_qe{b}_{nt}_{o}")
                q_ev.append(ev)
            q_done = q_ev[-1]
            dq_ev = []
            for c in range(NCH):
                sq = P.step("scalar", (lambda c=c: lambda e: e.activation(
                    K2Q, QT[:, c * 512:(c + 1) * 512], AF.Square))(),
                    deps=(q_done, dq_ev[-1] if dq_ev else pnt.get("dq")),
                    tag=f"l1a_q2{b}_{nt}_{c}")

                def mmdq(e, c=c):
                    last = None
                    for jj in range(4):
                        last = e.matmul(pb[4][:, 4 * jj: 4 * jj + 2],
                                        K2Q[:, jj * 128:(jj + 1) * 128], EHB[:, 0:2],
                                        start=True, stop=True)
                    return last
                s_d = P.step("tensor", mmdq, deps=(sq, ehb_done, dq_ev[-1] if dq_ev else pnt.get("dq2")),
                             tag=f"l1a_dq{b}_{nt}_{c}")
                dqv = bass.AP(PD, DGQ_BASE + 2 * c, [[8192, 128], [16, 4], [1, 2]])
                e_d = P.step("vector", (lambda dqv=dqv: lambda e: e.tensor_copy(
                    dqv, pb[4][:, 0:16].rearrange("p (j x) -> p j x", x=4)[:, :, 0:2]))(),
                    deps=(s_d, pnt.get("dqc")), tag=f"l1a_dqe{b}_{nt}_{c}")
                dq_ev.append(e_d)
            dq_done = dq_ev[-1]
            pnt["dq"] = dq_ev[-2:]
            pnt["dq2"] = dq_ev[-1]
            pnt["dqc"] = None
            at_ev = []
            for p_ in range(8):
                rows = []
                for hh in range(2):
                    h = 2 * p_ + hh
                    hb = (h % 2) * 64
                    hc = h // 2
                    u_parts = []
                    den_parts = []
                    for jj in range(4):
                        bank = pb[jj % 2]

                        def mmxq(e, hb=hb, hc=hc, jj=jj, bank=bank):
                            return e.matmul(bank[:, 0:256],
                                            QT[hb:hb + 64, hc * 512 + jj * 128: hc * 512 + jj * 128 + 128],
                                            PRJ[hb:hb + 64, 0:256], start=True, stop=True)
                        pv = u_parts[-1] if u_parts else (at_ev[-1] if at_ev else (dq_done, pnt.get("xq")))
                        s_x = P.step("tensor", mmxq, deps=(q_done, p2, pv), tag=f"l1a_xq{b}_{nt}_{h}_{jj}")
                        stc = PD[:, 7028:7029]
                        m1 = P.step("vector", (lambda bank=bank, stc=stc: lambda e:
                                               e.reduce_max(stc, bank[:, 0:256], AX.X, negate=True))(),
                                    deps=(s_x, pnt.get("stc")), tag=f"l1a_sq{b}_{nt}_{h}_{jj}")
                        bcol = PD[:, 7030:7031]
                        dqcol = bass.AP(PD, DGQ_BASE + jj * 16 + h, [[8192, 128], [1, 1]])
                        c00625 = mk.const_col(-0.0625)
                        b_c = P.step("vector", (lambda bcol=bcol, dqcol=dqcol, stc=stc: lambda e:
                                                e.scalar_tensor_tensor(bcol, dqcol, c00625[:, 0:1], stc,
                                                                       ALU.mult, ALU.add))(),
                                     deps=(dq_done, m1, pnt.get("bcq"), mk.cdep(-0.0625)),
                                     tag=f"l1a_bq{b}_{nt}_{h}_{jj}")
                        exq = P.step("scalar", (lambda bank=bank, bcol=bcol: lambda e:
                                                e.activation(KP, bank[:, 0:256], AF.Exp,
                                                             bias=bcol[:, 0:1], scale=1.0))(),
                                     deps=(s_x, b_c, pnt.get("kpq")), tag=f"l1a_eq{b}_{nt}_{h}_{jj}")
                        tsq = P.step("vector", lambda e: e.tensor_scalar(KP, KP, RATIO, RATIO * EPSK,
                                                                         ALU.mult, ALU.add),
                                     deps=(exq,), tag=f"l1a_tq{b}_{nt}_{h}_{jj}")
                        pbB = pb[2][:, 0:64].bitcast(BF16)
                        pbB2 = pb[2][:, 64:128].bitcast(BF16)
                        tp1 = P.step("tensor", (lambda pbB=pbB: lambda e: e.transpose(
                            pbB[:, 0:128], KP[:, 0:128], mk.IDB[:]))(),
                            deps=(tsq, pnt.get("tp1")), tag=f"l1a_tp1{b}_{nt}_{h}_{jj}")
                        tc1 = P.step("vector", (lambda pbB=pbB, jj=jj: lambda e: e.tensor_copy(
                            QPT[:, jj * 128:(jj + 1) * 128], pbB[:, 0:128]))(),
                            deps=(tp1, pnt.get("tc1")), tag=f"l1a_tc1{b}_{nt}_{h}_{jj}")
                        tp2 = P.step("tensor", (lambda pbB2=pbB2: lambda e: e.transpose(
                            pbB2[:, 0:128], KP[:, 128:256], mk.IDB[:]))(),
                            deps=(tsq, tc1, pnt.get("tp2")), tag=f"l1a_tp2{b}_{nt}_{h}_{jj}")
                        tc2 = P.step("vector", (lambda pbB2=pbB2, jj=jj: lambda e: e.tensor_copy(
                            QPT[:, 512 + jj * 128: 512 + (jj + 1) * 128], pbB2[:, 0:128]))(),
                            deps=(tp2, pnt.get("tc2")), tag=f"l1a_tc2{b}_{nt}_{h}_{jj}")
                        u_parts.append(tc2)
                    ubank = pb[5] if hh == 0 else pb[3]

                    def mmu(e, h=h, ubank=ubank):
                        e.matmul(ubank[0:64, 0:512], CTX[:, h * 130: h * 130 + 64],
                                 QPT[:, 0:512], start=True, stop=False)
                        return e.matmul(ubank[0:64, 0:512], CTX[:, h * 130 + 65: h * 130 + 129],
                                        QPT[:, 512:1024], start=False, stop=True)
                    s_u = P.step("tensor", mmu, deps=(u_parts, ctx_done, pnt.get("u" + str(hh))),
                                 tag=f"l1a_U{b}_{nt}_{h}")

                    def mmden(e, h=h):
                        e.matmul(pb[6][0:1, 0:512], CTX[:, h * 130 + 64: h * 130 + 65],
                                 QPT[:, 0:512], start=True, stop=False)
                        return e.matmul(pb[6][0:1, 0:512], CTX[:, h * 130 + 129: h * 130 + 130],
                                        QPT[:, 512:1024], start=False, stop=True)
                    s_den = P.step("tensor", mmden, deps=(u_parts, ctx_done, rows, pnt.get("denb")),
                                   tag=f"l1a_den{b}_{nt}_{h}")
                    rrow = r_a if hh == 0 else r_b
                    z1 = P.step("vector", (lambda rrow=rrow: lambda e: e.tensor_copy(
                        rrow, pb[6][0:1, 0:512]))(),
                        deps=(s_den, pnt.get("rrow")), tag=f"l1a_z1{b}_{nt}_{h}")
                    z2 = P.step("scalar", (lambda rrow=rrow: lambda e: e.activation(
                        rrow, rrow, AF.Ln, scale=1.0))(), deps=(z1,), tag=f"l1a_z2{b}_{nt}_{h}")
                    z3 = P.step("scalar", (lambda rrow=rrow: lambda e: e.activation(
                        rrow, rrow, AF.Exp, scale=-1.0))(), deps=(z2,), tag=f"l1a_z3{b}_{nt}_{h}")
                    rows.append(z3)
                    pnt["u" + str(hh)] = s_u
                    pnt["rrow"] = z2
                    pnt["denb"] = z1
                    pnt["xq"] = exq
                    pnt["kpq"] = tsq
                    pnt["tp1"] = tc1
                    pnt["tp2"] = tc2
                    pnt["tc1"] = tp2
                    pnt["tc2"] = s_u
                    pnt["stc"] = b_c
                    pnt["bcq"] = exq
                    if hh == 0:
                        u0 = s_u
                    else:
                        u1 = s_u
                def mmr(e):
                    e.matmul(pb[7][:, 0:512], mk.EH[0:1, 0:128], r_a, start=True, stop=False)
                    return e.matmul(pb[7][:, 0:512], mk.EH[0:1, 128:256], r_b, start=False, stop=True)
                s_r = P.step("tensor", mmr, deps=(rows, pnt.get("rbcmm")), tag=f"l1a_rbc{b}_{nt}_{p_}")
                c_r = P.step("vector", lambda e: e.tensor_copy(RB1, pb[7][:, 0:512]),
                             deps=(s_r, pnt.get("rbcc")), tag=f"l1a_rbce{b}_{nt}_{p_}")
                dstA = ATT[:, p_ * 512:(p_ + 1) * 512]
                ea0 = P.step("vector", (lambda dstA=dstA: lambda e: e.tensor_mul(
                    dstA[0:64, :], pb[5][0:64, 0:512], RB1[0:64, :]))(),
                    deps=(u0, c_r, pnt.get("att")), tag=f"l1a_a0{b}_{nt}_{p_}")
                ea1 = P.step("vector", (lambda dstA=dstA: lambda e: e.tensor_mul(
                    dstA[64:128, :], pb[3][0:64, 0:512], RB1[64:128, :]))(),
                    deps=(u1, c_r, ea0), tag=f"l1a_a1{b}_{nt}_{p_}")
                at_ev.append(ea1)
                pnt["rbcmm"] = [ea0, ea1]
                pnt["rbcc"] = [ea0, ea1]
                pnt["att"] = None
            y1_ev = []
            for o in range(NCH):
                bank = pb[o % 4]

                def mmo(e, o=o, bank=bank):
                    last = None
                    for c in range(NCH):
                        last = e.matmul(bank[:, 0:512],
                                        PB[:, c * 1024 + o * 128: c * 1024 + o * 128 + 128],
                                        ATT[:, c * 512:(c + 1) * 512],
                                        start=(c == 0), stop=(c == NCH - 1))
                    return last
                pv = y1_ev[o - 4] if o >= 4 else (at_ev, pnt.get("yb"))
                s_o = P.step("tensor", mmo, deps=(wo, at_ev[-1], pv), tag=f"l1a_o{b}_{nt}_{o}")
                dst = PCf[:, o * 512:(o + 1) * 512]
                e_o = P.step("vector",
                             (lambda bank=bank, dst=dst, o=o, t0=t0: lambda e:
                              e.scalar_tensor_tensor(dst, bank[:, 0:512], boc[:, o:o + 1],
                                                     XN[:, o * T_CORE + t0: o * T_CORE + t0 + 512],
                                                     ALU.add, ALU.add))(),
                             deps=(s_o, pnt.get("pcf")), tag=f"l1a_oe{b}_{nt}_{o}")
                y1_ev.append(e_o)
            x2res = emit_inner_ln(mk, f"l1a_ln2_{b}_{nt}", PCf, g2c, be2c, epscol,
                                  y1_ev, pa_off=0,
                                  out_bf=(PA, 8192), x2_hbm=(None, 0), pnt=pnt)
            pnt["qtc"] = x2res["pa_cons"]
            pnt["qb"] = y1_ev[-4:]
            pnt["yb"] = y1_ev[-4:]
            pnt["pcf"] = x2res["pcf_cons"]
            pnt["dq"] = x2res["done"]
            pnt["xq"] = x2res["done"]
            fin = emit_l1a_ffn(mk, b, nt, t0, x2res, x_in, x_out, pnt)
            out_final.append(fin["st"])
            pnt.update(fin["pnt"])
        prev_b = {"allc": out_final[-1], "banks": out_final[-1], "rowb": out_final[-1],
                  "dgc": out_final[-1], "ctxb": out_final[-1]}
    return out_final



def emit_inner_ln(mk, tag, ysrc, gcol, becol, epscol, y_done, pa_off, out_bf, x2_hbm, pnt):
    P = mk.P
    pb = mk.pb
    PA = mk.PA
    r_m = mk.ROWS[0:1, 1024:1536]
    r_t = mk.ROWS[0:1, 1536:2048]
    ybf = PA[:, 0:4096]
    ysq = PA[:, 4096:8192]
    outb, ob_off = out_bf
    hbm_ap, t0 = x2_hbm
    c1 = P.step("vector", (lambda: lambda e: e.tensor_copy(ybf, ysrc[:, 0:4096]))(),
                deps=(y_done, pnt.get(tag + "pa")), tag=tag + "_cast")
    c2 = P.step("scalar", lambda e: e.activation(ysq, ybf, AF.Square), deps=(c1,), tag=tag + "_sq")

    def mk_stats(srcb, bank):
        def f(e):
            mm = None
            for c in range(NCH):
                mm = e.matmul(bank[0:1, 0:512], mk.ONECB[:, 0:1], srcb[:, c * 512:(c + 1) * 512],
                              start=(c == 0), stop=(c == NCH - 1))
            return mm
        return f
    m1 = P.step("tensor", mk_stats(ybf, pb[6]), deps=(c1, pnt.get(tag + "b6")), tag=tag + "_s1")
    m2 = P.step("tensor", mk_stats(ysq, pb[7]), deps=(c2, pnt.get(tag + "b7")), tag=tag + "_s2")
    q1 = P.step("vector", lambda e: e.tensor_single_scalar(r_m, pb[6][0:1, 0:512], 1.0 / D, ALU.mult),
                deps=(m1, pnt.get(tag + "rows")), tag=tag + "_m")
    q2 = P.step("vector", lambda e: e.tensor_single_scalar(r_t, pb[7][0:1, 0:512], 1.0 / D, ALU.mult),
                deps=(m2, pnt.get(tag + "rows")), tag=tag + "_e2")
    q3a = P.step("vector", lambda e: e.tensor_mul(mk.ROWS[0:1, 0:512], r_m, r_m),
                 deps=(q1, pnt.get(tag + "ra")), tag=tag + "_msq")
    q3 = P.step("vector", lambda e: e.tensor_sub(r_t, r_t, mk.ROWS[0:1, 0:512]),
                deps=(q2, q3a), tag=tag + "_var")
    q5 = P.step("scalar", lambda e: e.activation(r_t, r_t, AF.Ln, bias=epscol[0:1, 0:1], scale=1.0),
                deps=(q3, mk.cdep(LN_EPS)), tag=tag + "_ln")
    q6 = P.step("scalar", lambda e: e.activation(mk.ROWS[0:1, 0:512], r_t, AF.Exp, scale=-0.5),
                deps=(q5, q3a), tag=tag + "_rstd")
    q7 = P.step("vector", lambda e: e.tensor_mul(r_t, r_m, mk.ROWS[0:1, 0:512]),
                deps=(q6, q1), tag=tag + "_mr")
    b1 = P.step("tensor", lambda e: e.matmul(pb[4][:, 0:512], mk.ONEF[0:1, 0:128],
                                             mk.ROWS[0:1, 0:512], start=True, stop=True),
                deps=(q6, pnt.get(tag + "b4")), tag=tag + "_rb")
    b2 = P.step("tensor", lambda e: e.matmul(pb[5][:, 0:512], mk.ONEF[0:1, 0:128], r_t,
                                             start=True, stop=True),
                deps=(q7, pnt.get(tag + "b5")), tag=tag + "_mrb")
    done = []
    pcf_cons = []
    pa_cons = []
    for c in range(NCH):
        ysl = ysrc[:, c * 512:(c + 1) * 512]
        u1 = P.step("vector", (lambda ysl=ysl: lambda e: e.tensor_mul(ysl, ysl, pb[4][:, 0:512]))(),
                    deps=(b1, m1, m2, done[-1] if done else None), tag=tag + f"_t1_{c}")
        u2 = P.step("vector", (lambda ysl=ysl: lambda e: e.tensor_sub(ysl, ysl, pb[5][:, 0:512]))(),
                    deps=(u1, b2), tag=tag + f"_t2_{c}")
        u3 = P.step("vector", (lambda ysl=ysl, c=c: lambda e: e.tensor_scalar(
            ysl, ysl, gcol[:, c:c + 1], becol[:, c:c + 1], ALU.mult, ALU.add))(),
            deps=(u2,), tag=tag + f"_x2_{c}")
        u4 = P.step("vector", (lambda ysl=ysl, c=c: lambda e: e.tensor_copy(
            outb[:, ob_off + c * 512: ob_off + (c + 1) * 512], ysl))(),
            deps=(u3, pnt.get(tag + "ob")), tag=tag + f"_x2b_{c}")
        done.append(u4)
        pcf_cons.append(u4)
        pa_cons.append(u4)
    std = None
    if hbm_ap is not None:
        std = P.dma("sync", (lambda: lambda e: e.dma_start(
            out=bass.AP(hbm_ap.tensor, t0, [[T_CORE, 128], [128 * T_CORE, NCH], [1, 512]]),
            in_=ysrc[:, 0:4096].rearrange("p (c t) -> p c t", t=512)))(),
            deps=(done,), tag=tag + "_st")
    pnt[tag + "b6"] = q1
    pnt[tag + "b7"] = q2
    pnt[tag + "b4"] = done[-1]
    pnt[tag + "b5"] = done[-1]
    pnt[tag + "rows"] = [b1, b2]
    pnt[tag + "ra"] = b1
    return {"done": done, "pcf_cons": pcf_cons + ([std] if std else []),
            "pa_cons": pa_cons, "st": std, "b4": done[-1]}


def emit_l1a_ffn(mk, b, nt, t0, x2res, x_in, x_out, pnt):
    P = mk.P
    pb = mk.pb
    PA, PB, PC, PD = mk.PA, mk.PB, mk.PC, mk.PD
    PCw = PC[:, 0:4096].bitcast(BF16)
    PCf = PC[:, 4096:8192]
    b1c = mk.col("l1a_b1")
    b2c = mk.col("l1a_b2")
    g3c = mk.col("l1a_g3")
    be3c = mk.col("l1a_be3")
    epscol = mk.const_col(LN_EPS)
    x2bf = (PA, 8192)
    a_ev = []
    w1lds = []
    for blk in range(4):
        w1 = mk.load_wT("l1a_w1T", PCw[:, 0:8192], 1024,
                        deps=(x2res["pcf_cons"], a_ev[-8:] if a_ev else pnt.get("w1f"),
                              pnt.get("pcw")),
                        col0=blk * 1024, rowstride=4096, tag=f"_{b}_{nt}_{blk}")
        w1lds.append(w1)
        for fcl in range(NCH):
            fc = blk * 8 + fcl
            bank = pb[fcl % 4]

            def mmh(e, fcl=fcl, bank=bank):
                last = None
                for c in range(NCH):
                    last = e.matmul(bank[:, 0:512],
                                    PCw[:, c * 1024 + fcl * 128: c * 1024 + fcl * 128 + 128],
                                    PA[:, 8192 + c * 512: 8192 + (c + 1) * 512],
                                    start=(c == 0), stop=(c == NCH - 1))
                return last
            pv = a_ev[-4] if len(a_ev) >= 4 else pnt.get("hb")
            s_h = P.step("tensor", mmh, deps=(w1, x2res["pa_cons"], pv),
                         tag=f"l1a_h{b}_{nt}_{fc}")
            dst = PB[:, fc * 512:(fc + 1) * 512]
            e_h = P.step("scalar", (lambda bank=bank, dst=dst, fc=fc: lambda e:
                                    e.activation(dst, bank[:, 0:512], AF.Relu,
                                                 bias=b1c[:, fc:fc + 1], scale=1.0))(),
                         deps=(s_h, pnt.get("ab")), tag=f"l1a_relu{b}_{nt}_{fc}")
            a_ev.append(e_h)
    w2lds = []
    mm_h2 = {o: [] for o in range(NCH)}
    for blk in range(4):
        w2 = mk.load_wT("l1a_w2T", PCw[:, 0:8192], 1024,
                        deps=(a_ev, w1lds[-1]) if blk == 0 else ([w2lds[-1]] + mm_h2[NCH - 1][-1:],),
                        col0=0, rowstride=1024, base=blk * 1024 * 1024, tag=f"_{b}_{nt}w{blk}")
        w2lds.append(w2)
        for o in range(NCH):
            bank = pb[o]

            def mmy(e, blk=blk, o=o, bank=bank):
                last = None
                for fcl in range(NCH):
                    fc = blk * 8 + fcl
                    last = e.matmul(bank[:, 0:512],
                                    PCw[:, fcl * 1024 + o * 128: fcl * 1024 + o * 128 + 128],
                                    PB[:, fc * 512:(fc + 1) * 512],
                                    start=(fc == 0), stop=(fc == 31))
                return last
            deps = (w2, a_ev[-1] if blk == 0 else mm_h2[o][-1],
                    (x2res["done"] if blk == 0 and o < 4 else None),
                    (pnt.get("y2b") if blk == 0 else None))
            s_y = P.step("tensor", mmy, deps=deps, tag=f"l1a_y2{b}_{nt}_{blk}_{o}")
            mm_h2[o].append(s_y)
    y2_ev = []
    for o in range(NCH):
        dst = PCf[:, o * 512:(o + 1) * 512]
        e_y = P.step("vector",
                     (lambda o=o, dst=dst: lambda e: e.scalar_tensor_tensor(
                         dst, pb[o][:, 0:512], b2c[:, o:o + 1], dst, ALU.add, ALU.add))(),
                     deps=(mm_h2[o][-1], x2res["pcf_cons"], y2_ev[-1] if y2_ev else None),
                     tag=f"l1a_y2e{b}_{nt}_{o}")
        y2_ev.append(e_y)
    ln3 = emit_inner_ln(mk, f"l1a_ln3_{b}_{nt}", PCf, g3c, be3c, epscol, y2_ev,
                        pa_off=0, out_bf=(PA, 12288), x2_hbm=(None, 0), pnt=pnt)
    XO = PD[:, 2336:6432]
    ldx = P.dma("sync",
                (lambda t0=t0: lambda e: e.dma_start(
                    out=XO.rearrange("p (c t) -> p c t", t=512),
                    in_=bass.AP(x_in.tensor, t0, [[T_CORE, 128], [128 * T_CORE, NCH], [1, 512]])))(),
                deps=(pnt.get("xoc"), ln3["done"][0:1]), tag=f"l1a_ldx{b}_{nt}")
    o_ev = []
    for c in range(NCH):
        e_o = P.step("vector", (lambda c=c: lambda e: e.tensor_add(
            XO[:, c * 512:(c + 1) * 512], PCf[:, c * 512:(c + 1) * 512],
            XO[:, c * 512:(c + 1) * 512]))(),
            deps=(ldx, ln3["done"][c], o_ev[-1] if o_ev else None), tag=f"l1a_xo{b}_{nt}_{c}")
        o_ev.append(e_o)
    st = P.dma("sync",
               (lambda t0=t0: lambda e: e.dma_start(
                   out=bass.AP(x_out.tensor, t0, [[T_CORE, 128], [128 * T_CORE, NCH], [1, 512]]),
                   in_=XO.rearrange("p (c t) -> p c t", t=512)))(),
               deps=(o_ev,), tag=f"l1a_st{b}_{nt}")
    newpnt = {"w1f": [st], "pcw": mm_h2[NCH - 1][-1], "pbw": mm_h2[NCH - 1][-1],
              "hb": y2_ev[-1], "ab": mm_h2[0][-1] if mm_h2[0] else None,
              "y2b": ln3["done"][-1], "xoc": st, "qtc": st, "qb": [st], "dq": [st], "dq2": st,
              "xq": st, "tp1": st, "tp2": st, "tc1": st, "tc2": st, "stc": st, "bcq": st,
              "kpq": st, "u0": st, "u1": st, "rrow": st, "denb": st, "rbcmm": st, "rbcc": st,
              "att": st, "yb": [st], "pcf": [st]}
    return {"st": st, "pnt": newpnt}



def emit_l1_attn_a(mk, x_in, x_out, prev=()):
    P = mk.P
    pb = mk.pb
    XN, PA, PB, PC, PD = mk.XN, mk.PA, mk.PB, mk.PC, mk.PD
    XNs = 8 * T_CORE
    ln_done = emit_ln(mk, x_in, deps=prev)
    lnbar = ln_done[-1]
    RATIO = 1.0 / 16.0
    EPSK = 1e-4
    bkc = mk.col("l1a_bk"); bqc = mk.col("l1a_bq"); boc = mk.col("l1a_bo")
    g2c = mk.col("l1a_g2"); be2c = mk.col("l1a_be2")
    g3c = mk.col("l1a_g3"); be3c = mk.col("l1a_be3")
    b1c = mk.col("l1a_b1")
    b2c = mk.col("l1a_b2")
    epscol = mk.const_col(LN_EPS)
    PCw = PC[:, 0:4096].bitcast(BF16)
    PCw2 = PC[:, 4096:8192].bitcast(BF16)
    PCf = PC[:, 4096:8192]
    DG = PD[:, 0:256]
    CTX = PD[:, 256:1296].bitcast(BF16)
    PRJ = PD[:, 1296:1424].bitcast(BF16)
    EHB = PD[:, 1424:1426].bitcast(BF16)
    KP = PD[:, 1440:1568].bitcast(BF16)
    QPT = PD[:, 1568:2080].bitcast(BF16)
    K2Q = PD[:, 2080:2336].bitcast(BF16)
    ATT = PD[:, 4384:6432].bitcast(BF16)
    RB1 = PD[:, 6432:6944]
    DGQ_BASE = 6944
    MX = PD[:, 7008:7024]
    NBC = PD[:, 7024:7025]
    r_a = mk.ROWS[0:1, 0:512]
    r_b = mk.ROWS[0:1, 512:1024]
    rr1 = PD[0:1, 7040:7056]
    rr2 = PD[0:1, 7064:7065]

    PRJF = PD[:, 7168:7424]
    p1 = P.dma("sync", lambda e: e.dma_start(out=PRJF[0:64, :], in_=mk.dram["l1a_projT"]),
               deps=(prev,), tag="l1a_pld")
    p1b = P.dma("sync", lambda e: e.dma_start(out=PRJF[64:128, :], in_=mk.dram["l1a_projT"]),
                deps=(prev,), tag="l1a_pld2")
    p2a = P.step("vector", lambda e: e.tensor_scalar(PRJ[0:64, :], PRJF[0:64, :],
                                                     float(64.0 ** (-0.25)), None, ALU.mult),
                 deps=(p1, prev), tag="l1a_pscale")
    p2b = P.step("vector", lambda e: e.tensor_scalar(PRJ[64:128, :], PRJF[64:128, :],
                                                     float(64.0 ** (-0.25)), None, ALU.mult),
                 deps=(p1b, prev), tag="l1a_pscale2")
    p2 = [p2a, p2b]
    e1_ = P.step("vector", lambda e: e.memset(EHB[0:64, 0:1], 1.0), deps=(prev,), tag="l1a_eh0")
    e2_ = P.step("vector", lambda e: e.memset(EHB[64:128, 0:1], 0.0), deps=(prev,), tag="l1a_eh1")
    e3_ = P.step("vector", lambda e: e.memset(EHB[0:64, 1:2], 0.0), deps=(prev,), tag="l1a_eh2")
    e4_ = P.step("vector", lambda e: e.memset(EHB[64:128, 1:2], 1.0), deps=(prev,), tag="l1a_eh3")
    ehb_done = [e1_, e2_, e3_, e4_]
    out_final = []
    prev_b = {}
    for b in range(B_PER_CORE):
        tb = b * T_SEQ
        ldr = P.dma("sync", lambda e: e.dma_start(out=PCf[0:1, 0:1024], in_=mk.dram["r_l1a_bv"]),
                    deps=(prev_b.get("allc"), prev, lnbar), tag=f"l1a_bvld{b}")
        rv = P.step("vector", lambda e: e.tensor_copy(mk.ROWB[0:1, 0:1024], PCf[0:1, 0:1024]),
                    deps=(ldr, prev_b.get("rowb")), tag=f"l1a_bvc{b}")
        wv = mk.load_wT("l1a_wvT", PCw[:, 0:8192], 1024, deps=(prev_b.get("allc"), lnbar, prev),
                        tag=f"_{b}v")
        v_ev = []
        for j in range(16):
            for h2 in range(2):
                bank = pb[(j * 2 + h2) % 4]

                def mmv(e, j=j, h2=h2, bank=bank, tb=tb):
                    for c in range(NCH):
                        e.matmul(bank[:, 0:512],
                                 XN[:, c * T_CORE + tb + j * 128: c * T_CORE + tb + j * 128 + 128],
                                 PCw[:, c * 1024 + h2 * 512: c * 1024 + h2 * 512 + 512],
                                 start=(c == 0), stop=False)
                    return e.matmul(bank[:, 0:512], mk.ONEB[0:1, 0:128],
                                    mk.ROWB[0:1, h2 * 512: h2 * 512 + 512],
                                    start=False, stop=True)
                pv = v_ev[-4] if len(v_ev) >= 4 else prev_b.get("banks")
                s_mm = P.step("tensor", mmv, deps=(wv, rv, ln_done[(tb + j * 128) // 512], pv),
                              tag=f"l1a_v{b}_{j}_{h2}")
                dst = PB[:, j * 1024 + h2 * 512: j * 1024 + h2 * 512 + 512]
                ev = P.step("vector", (lambda bank=bank, dst=dst: lambda e:
                                       e.tensor_copy(dst, bank[:, 0:512]))(),
                            deps=(s_mm,), tag=f"l1a_vev{b}_{j}_{h2}")
                v_ev.append(ev)
        v_done = v_ev[-1]
        wk = mk.load_wT("l1a_wkT", PCw2[:, 0:8192], 1024, deps=(v_ev[-8:], rv, lnbar), tag=f"_{b}k")
        k_ev = []
        for o in range(NCH):
            for nt in range(4):
                bank = pb[(o * 4 + nt) % 4]

                def mmk(e, o=o, nt=nt, bank=bank, tb=tb):
                    last = None
                    for c in range(NCH):
                        last = e.matmul(bank[:, 0:512],
                                        PCw2[:, c * 1024 + o * 128: c * 1024 + o * 128 + 128],
                                        XN[:, c * T_CORE + tb + nt * 512: c * T_CORE + tb + nt * 512 + 512],
                                        start=(c == 0), stop=(c == NCH - 1))
                    return last
                pv = k_ev[-4] if len(k_ev) >= 4 else v_ev[-4:]
                s_mm = P.step("tensor", mmk, deps=(wk, ln_done[(tb + nt * 512) // 512], pv),
                              tag=f"l1a_k{b}_{o}_{nt}")
                dst = PA[:, o * 2048 + nt * 512: o * 2048 + nt * 512 + 512]
                ev = P.step("vector", (lambda bank=bank, dst=dst, o=o: lambda e:
                                       e.tensor_scalar(dst, bank[:, 0:512], bkc[:, o:o + 1], None,
                                                       ALU.add))(),
                            deps=(s_mm,), tag=f"l1a_kev{b}_{o}_{nt}")
                k_ev.append(ev)
        k_done = k_ev[-1]
        dg_ev = []
        K2 = PCw[:, 0:512]
        prev_k2 = [v_done] + k_ev[-2:]
        for c in range(NCH):
            for nt in range(4):
                sq = P.step("scalar", (lambda c=c, nt=nt: lambda e: e.activation(
                    K2, PA[:, c * 2048 + nt * 512: c * 2048 + nt * 512 + 512], AF.Square))(),
                    deps=(k_done, prev_k2), tag=f"l1a_k2{b}_{c}_{nt}")

                def mmd(e, c=c, nt=nt):
                    last = None
                    for jj in range(4):
                        last = e.matmul(pb[4][:, 4 * jj: 4 * jj + 2],
                                        K2[:, jj * 128:(jj + 1) * 128], EHB[:, 0:2],
                                        start=True, stop=True)
                    return last
                s_d = P.step("tensor", mmd, deps=(sq, ehb_done, dg_ev[-1] if dg_ev else None),
                             tag=f"l1a_dg{b}_{c}_{nt}")
                dgv = bass.AP(PD, (nt * 4) * 16 + 2 * c, [[8192, 128], [16, 4], [1, 2]])
                e_d = P.step("vector", (lambda dgv=dgv: lambda e: e.tensor_copy(
                    dgv, pb[4][:, 0:16].rearrange("p (j x) -> p j x", x=4)[:, :, 0:2]))(),
                    deps=(s_d, prev_b.get("dgc")), tag=f"l1a_dge{b}_{c}_{nt}")
                dg_ev.append(e_d)
                prev_k2 = [s_d]
        dg_done = dg_ev[-1]
        ctx_ev = []
        ph = {"xp": dg_ev[-1], "mx": None, "ctxb": prev_b.get("ctxb")}
        for h in range(16):
            hb = (h % 2) * 64
            hc = h // 2
            mx_ev = []
            for j in range(16):
                bank = pb[j % 2]

                def mmx(e, hb=hb, hc=hc, j=j, bank=bank):
                    return e.matmul(bank[:, 0:256],
                                    PA[hb:hb + 64, hc * 2048 + j * 128: hc * 2048 + j * 128 + 128],
                                    PRJ[hb:hb + 64, 0:256], start=True, stop=True)
                pv = mx_ev[-2] if len(mx_ev) >= 2 else ph.get("xp")
                s_x = P.step("tensor", mmx, deps=(k_done, p2, pv), tag=f"l1a_xk{b}_{h}_{j}")
                m1 = P.step("vector", (lambda j=j, bank=bank: lambda e: e.reduce_max(
                    MX[:, j:j + 1], bank[:, 0:256], AX.X))(),
                    deps=(s_x, ph.get("mx")), tag=f"l1a_mxk{b}_{h}_{j}")
                mx_ev.append(m1)
            gk1 = P.step("gpsimd", lambda e: e.reduce_max(rr1, MX[:, 0:16], AX.C),
                         deps=(mx_ev, ph.get("g1")), tag=f"l1a_gk1{b}_{h}")
            gk2 = P.step("vector", lambda e: e.reduce_max(rr2, rr1, AX.X, negate=True),
                         deps=(gk1, ph.get("g2")), tag=f"l1a_gk2{b}_{h}")
            gk3 = P.step("tensor", lambda e: e.matmul(pb[0][:, 8:9], mk.ONEF[0:1, 0:128], rr2,
                                                      start=True, stop=True),
                         deps=(gk2, ph.get("g3"), mx_ev[-2:]), tag=f"l1a_gk3{b}_{h}")
            gk4 = P.step("vector", lambda e: e.tensor_copy(NBC, pb[0][:, 8:9]),
                         deps=(gk3, ph.get("g4")), tag=f"l1a_gk4{b}_{h}")
            kp_ev = []
            for j in range(16):
                bank = pb[2 + j % 2]

                def mmx2(e, hb=hb, hc=hc, j=j, bank=bank):
                    return e.matmul(bank[:, 0:256],
                                    PA[hb:hb + 64, hc * 2048 + j * 128: hc * 2048 + j * 128 + 128],
                                    PRJ[hb:hb + 64, 0:256], start=True, stop=True)
                pv = kp_ev[-2] if len(kp_ev) >= 2 else (mx_ev[-1], ph.get("kp2"))
                s_x2 = P.step("tensor", mmx2, deps=(k_done, p2, pv, gk4),
                              tag=f"l1a_xk2{b}_{h}_{j}")
                bcol = PD[:, 7026 + (j % 2): 7027 + (j % 2)]
                dgcol = bass.AP(PD, j * 16 + h, [[8192, 128], [1, 1]])
                c00625 = mk.const_col(-0.0625)
                b_c = P.step("vector", (lambda bcol=bcol, dgcol=dgcol: lambda e:
                                        e.scalar_tensor_tensor(bcol, dgcol, c00625[:, 0:1], NBC,
                                                               ALU.mult, ALU.add))(),
                             deps=(dg_done, gk4, kp_ev[-2] if len(kp_ev) >= 2 else ph.get("bc"),
                                   mk.cdep(-0.0625)),
                             tag=f"l1a_bck{b}_{h}_{j}")
                exk = P.step("scalar", (lambda bank=bank, bcol=bcol: lambda e:
                                        e.activation(KP, bank[:, 0:256], AF.Exp,
                                                     bias=bcol[:, 0:1], scale=1.0))(),
                             deps=(s_x2, b_c, ph.get("kpc")), tag=f"l1a_ek{b}_{h}_{j}")
                tsk = P.step("vector", lambda e: e.tensor_scalar(KP, KP, RATIO, RATIO * EPSK,
                                                                 ALU.mult, ALU.add),
                             deps=(exk,), tag=f"l1a_tk{b}_{h}_{j}")
                def mmc(e, h=h, j=j):
                    last = None
                    for mc in range(2):
                        e.matmul(pb[4 + mc][0:128, 0:64],
                                 KP[:, mc * 128:(mc + 1) * 128],
                                 PB[:, j * 1024 + h * 64: j * 1024 + h * 64 + 64],
                                 start=(j == 0), stop=(j == 15))
                        last = e.matmul(pb[6 + mc][0:128, 0:1],
                                        KP[:, mc * 128:(mc + 1) * 128], mk.ONECB[:, 0:1],
                                        start=(j == 0), stop=(j == 15))
                    return last
                s_c = P.step("tensor", mmc, deps=(tsk, v_done, ph.get("ctxmm")), tag=f"l1a_ctx{b}_{h}_{j}")
                kp_ev.append(s_c)
                ph["kpc"] = s_c
            ec1 = P.step("vector", (lambda h=h: lambda e: e.tensor_copy(
                CTX[:, h * 130: h * 130 + 64], pb[4][:, 0:64]))(),
                deps=(kp_ev[-1], ph.get("ctxb")), tag=f"l1a_ctxe1{b}_{h}")
            ec1b = P.step("vector", (lambda h=h: lambda e: e.tensor_copy(
                CTX[:, h * 130 + 64: h * 130 + 65], pb[6][:, 0:1]))(),
                deps=(kp_ev[-1], ec1), tag=f"l1a_ctxe1b{b}_{h}")
            ec2 = P.step("vector", (lambda h=h: lambda e: e.tensor_copy(
                CTX[:, h * 130 + 65: h * 130 + 129], pb[5][:, 0:64]))(),
                deps=(kp_ev[-1], ec1b), tag=f"l1a_ctxe2{b}_{h}")
            ec2b = P.step("vector", (lambda h=h: lambda e: e.tensor_copy(
                CTX[:, h * 130 + 129: h * 130 + 130], pb[7][:, 0:1]))(),
                deps=(kp_ev[-1], ec2), tag=f"l1a_ctxe2b{b}_{h}")
            ec2 = ec2b
            ctx_ev.append(ec2)
            ph["ctxmm"] = ec2
            ph["xp"] = kp_ev[-1]
            ph["mx"] = gk1
            ph["g1"] = gk2
            ph["g2"] = gk3
            ph["g3"] = gk4
            ph["g4"] = exk
            ph["bc"] = exk
            ph["kp2"] = kp_ev[-1]
        ctx_done = ctx_ev[-1]
        prev_b = dict(prev_b)
        prev_b["phaseA"] = ctx_done
        QT = PD[:, 2336:4384].bitcast(BF16)
        pnt = {}
        for nt in range(4):
            t0 = tb + nt * 512
            tt = t0 // 512
            wq = mk.load_wT("l1a_wqT", PCw[:, 0:8192], 1024,
                            deps=(k_ev[-4:], dg_ev[-2:], ph["xp"], ctx_done, pnt.get("pcw")),
                            tag=f"_{b}q{nt}")
            wo = mk.load_wT("l1a_woT", PB[:, 0:8192], 1024,
                            deps=(v_ev[-4:], ctx_done, dg_ev[-1:], pnt.get("pbw")),
                            tag=f"_{b}o{nt}")
            q_ev = []
            for o in range(NCH):
                bank = pb[o % 4]

                def mmq(e, o=o, t0=t0, bank=bank):
                    last = None
                    for c in range(NCH):
                        last = e.matmul(bank[:, 0:512],
                                        PCw[:, c * 1024 + o * 128: c * 1024 + o * 128 + 128],
                                        XN[:, c * T_CORE + t0: c * T_CORE + t0 + 512],
                                        start=(c == 0), stop=(c == NCH - 1))
                    return last
                pv = q_ev[o - 4] if o >= 4 else (pnt.get("qb"), ctx_done)
                s_q = P.step("tensor", mmq, deps=(wq, ln_done[tt], pv), tag=f"l1a_q{b}_{nt}_{o}")
                dst = QT[:, o * 512:(o + 1) * 512]
                ev = P.step("vector", (lambda bank=bank, dst=dst, o=o: lambda e:
                                       e.tensor_scalar(dst, bank[:, 0:512], bqc[:, o:o + 1], None,
                                                       ALU.add))(),
                            deps=(s_q, pnt.get("qtc")), tag=f"l1a_qe{b}_{nt}_{o}")
                q_ev.append(ev)
            q_done = q_ev[-1]
            dq_ev = []
            for c in range(NCH):
                sq = P.step("scalar", (lambda c=c: lambda e: e.activation(
                    K2Q, QT[:, c * 512:(c + 1) * 512], AF.Square))(),
                    deps=(q_done, dq_ev[-1] if dq_ev else pnt.get("dq")),
                    tag=f"l1a_q2{b}_{nt}_{c}")

                def mmdq(e, c=c):
                    last = None
                    for jj in range(4):
                        last = e.matmul(pb[4][:, 4 * jj: 4 * jj + 2],
                                        K2Q[:, jj * 128:(jj + 1) * 128], EHB[:, 0:2],
                                        start=True, stop=True)
                    return last
                s_d = P.step("tensor", mmdq, deps=(sq, ehb_done, dq_ev[-1] if dq_ev else pnt.get("dq2")),
                             tag=f"l1a_dq{b}_{nt}_{c}")
                dqv = bass.AP(PD, DGQ_BASE + 2 * c, [[8192, 128], [16, 4], [1, 2]])
                e_d = P.step("vector", (lambda dqv=dqv: lambda e: e.tensor_copy(
                    dqv, pb[4][:, 0:16].rearrange("p (j x) -> p j x", x=4)[:, :, 0:2]))(),
                    deps=(s_d, pnt.get("dqc")), tag=f"l1a_dqe{b}_{nt}_{c}")
                dq_ev.append(e_d)
            dq_done = dq_ev[-1]
            pnt["dq"] = dq_ev[-2:]
            pnt["dq2"] = dq_ev[-1]
            pnt["dqc"] = None
            at_ev = []
            for p_ in range(8):
                rows = []
                for hh in range(2):
                    h = 2 * p_ + hh
                    hb = (h % 2) * 64
                    hc = h // 2
                    u_parts = []
                    den_parts = []
                    for jj in range(4):
                        bank = pb[jj % 2]

                        def mmxq(e, hb=hb, hc=hc, jj=jj, bank=bank):
                            return e.matmul(bank[:, 0:256],
                                            QT[hb:hb + 64, hc * 512 + jj * 128: hc * 512 + jj * 128 + 128],
                                            PRJ[hb:hb + 64, 0:256], start=True, stop=True)
                        pv = u_parts[-1] if u_parts else (at_ev[-1] if at_ev else (dq_done, pnt.get("xq")))
                        s_x = P.step("tensor", mmxq, deps=(q_done, p2, pv), tag=f"l1a_xq{b}_{nt}_{h}_{jj}")
                        stc = PD[:, 7028:7029]
                        m1 = P.step("vector", (lambda bank=bank, stc=stc: lambda e:
                                               e.reduce_max(stc, bank[:, 0:256], AX.X, negate=True))(),
                                    deps=(s_x, pnt.get("stc")), tag=f"l1a_sq{b}_{nt}_{h}_{jj}")
                        bcol = PD[:, 7030:7031]
                        dqcol = bass.AP(PD, DGQ_BASE + jj * 16 + h, [[8192, 128], [1, 1]])
                        c00625 = mk.const_col(-0.0625)
                        b_c = P.step("vector", (lambda bcol=bcol, dqcol=dqcol, stc=stc: lambda e:
                                                e.scalar_tensor_tensor(bcol, dqcol, c00625[:, 0:1], stc,
                                                                       ALU.mult, ALU.add))(),
                                     deps=(dq_done, m1, pnt.get("bcq"), mk.cdep(-0.0625)),
                                     tag=f"l1a_bq{b}_{nt}_{h}_{jj}")
                        exq = P.step("scalar", (lambda bank=bank, bcol=bcol: lambda e:
                                                e.activation(KP, bank[:, 0:256], AF.Exp,
                                                             bias=bcol[:, 0:1], scale=1.0))(),
                                     deps=(s_x, b_c, pnt.get("kpq")), tag=f"l1a_eq{b}_{nt}_{h}_{jj}")
                        tsq = P.step("vector", lambda e: e.tensor_scalar(KP, KP, RATIO, RATIO * EPSK,
                                                                         ALU.mult, ALU.add),
                                     deps=(exq,), tag=f"l1a_tq{b}_{nt}_{h}_{jj}")
                        pbB = pb[2][:, 0:64].bitcast(BF16)
                        pbB2 = pb[2][:, 64:128].bitcast(BF16)
                        tp1 = P.step("tensor", (lambda pbB=pbB: lambda e: e.transpose(
                            pbB[:, 0:128], KP[:, 0:128], mk.IDB[:]))(),
                            deps=(tsq, pnt.get("tp1")), tag=f"l1a_tp1{b}_{nt}_{h}_{jj}")
                        tc1 = P.step("vector", (lambda pbB=pbB, jj=jj: lambda e: e.tensor_copy(
                            QPT[:, jj * 128:(jj + 1) * 128], pbB[:, 0:128]))(),
                            deps=(tp1, pnt.get("tc1")), tag=f"l1a_tc1{b}_{nt}_{h}_{jj}")
                        tp2 = P.step("tensor", (lambda pbB2=pbB2: lambda e: e.transpose(
                            pbB2[:, 0:128], KP[:, 128:256], mk.IDB[:]))(),
                            deps=(tsq, tc1, pnt.get("tp2")), tag=f"l1a_tp2{b}_{nt}_{h}_{jj}")
                        tc2 = P.step("vector", (lambda pbB2=pbB2, jj=jj: lambda e: e.tensor_copy(
                            QPT[:, 512 + jj * 128: 512 + (jj + 1) * 128], pbB2[:, 0:128]))(),
                            deps=(tp2, pnt.get("tc2")), tag=f"l1a_tc2{b}_{nt}_{h}_{jj}")
                        u_parts.append(tc2)
                    ubank = pb[5] if hh == 0 else pb[3]

                    def mmu(e, h=h, ubank=ubank):
                        e.matmul(ubank[0:64, 0:512], CTX[:, h * 130: h * 130 + 64],
                                 QPT[:, 0:512], start=True, stop=False)
                        return e.matmul(ubank[0:64, 0:512], CTX[:, h * 130 + 65: h * 130 + 129],
                                        QPT[:, 512:1024], start=False, stop=True)
                    s_u = P.step("tensor", mmu, deps=(u_parts, ctx_done, pnt.get("u" + str(hh))),
                                 tag=f"l1a_U{b}_{nt}_{h}")

                    def mmden(e, h=h):
                        e.matmul(pb[6][0:1, 0:512], CTX[:, h * 130 + 64: h * 130 + 65],
                                 QPT[:, 0:512], start=True, stop=False)
                        return e.matmul(pb[6][0:1, 0:512], CTX[:, h * 130 + 129: h * 130 + 130],
                                        QPT[:, 512:1024], start=False, stop=True)
                    s_den = P.step("tensor", mmden, deps=(u_parts, ctx_done, rows, pnt.get("denb")),
                                   tag=f"l1a_den{b}_{nt}_{h}")
                    rrow = r_a if hh == 0 else r_b
                    z1 = P.step("vector", (lambda rrow=rrow: lambda e: e.tensor_copy(
                        rrow, pb[6][0:1, 0:512]))(),
                        deps=(s_den, pnt.get("rrow")), tag=f"l1a_z1{b}_{nt}_{h}")
                    z2 = P.step("scalar", (lambda rrow=rrow: lambda e: e.activation(
                        rrow, rrow, AF.Ln, scale=1.0))(), deps=(z1,), tag=f"l1a_z2{b}_{nt}_{h}")
                    z3 = P.step("scalar", (lambda rrow=rrow: lambda e: e.activation(
                        rrow, rrow, AF.Exp, scale=-1.0))(), deps=(z2,), tag=f"l1a_z3{b}_{nt}_{h}")
                    rows.append(z3)
                    pnt["u" + str(hh)] = s_u
                    pnt["rrow"] = z2
                    pnt["denb"] = z1
                    pnt["xq"] = exq
                    pnt["kpq"] = tsq
                    pnt["tp1"] = tc1
                    pnt["tp2"] = tc2
                    pnt["tc1"] = tp2
                    pnt["tc2"] = s_u
                    pnt["stc"] = b_c
                    pnt["bcq"] = exq
                    if hh == 0:
                        u0 = s_u
                    else:
                        u1 = s_u
                def mmr(e):
                    e.matmul(pb[7][:, 0:512], mk.EH[0:1, 0:128], r_a, start=True, stop=False)
                    return e.matmul(pb[7][:, 0:512], mk.EH[0:1, 128:256], r_b, start=False, stop=True)
                s_r = P.step("tensor", mmr, deps=(rows, pnt.get("rbcmm")), tag=f"l1a_rbc{b}_{nt}_{p_}")
                c_r = P.step("vector", lambda e: e.tensor_copy(RB1, pb[7][:, 0:512]),
                             deps=(s_r, pnt.get("rbcc")), tag=f"l1a_rbce{b}_{nt}_{p_}")
                dstA = ATT[:, p_ * 512:(p_ + 1) * 512]
                ea0 = P.step("vector", (lambda dstA=dstA: lambda e: e.tensor_mul(
                    dstA[0:64, :], pb[5][0:64, 0:512], RB1[0:64, :]))(),
                    deps=(u0, c_r, pnt.get("att")), tag=f"l1a_a0{b}_{nt}_{p_}")
                ea1 = P.step("vector", (lambda dstA=dstA: lambda e: e.tensor_mul(
                    dstA[64:128, :], pb[3][0:64, 0:512], RB1[64:128, :]))(),
                    deps=(u1, c_r, ea0), tag=f"l1a_a1{b}_{nt}_{p_}")
                at_ev.append(ea1)
                pnt["rbcmm"] = [ea0, ea1]
                pnt["rbcc"] = [ea0, ea1]
                pnt["att"] = None
            y1_ev = []
            for o in range(NCH):
                bank = pb[o % 4]

                def mmo(e, o=o, bank=bank):
                    last = None
                    for c in range(NCH):
                        last = e.matmul(bank[:, 0:512],
                                        PB[:, c * 1024 + o * 128: c * 1024 + o * 128 + 128],
                                        ATT[:, c * 512:(c + 1) * 512],
                                        start=(c == 0), stop=(c == NCH - 1))
                    return last
                pv = y1_ev[o - 4] if o >= 4 else (at_ev, pnt.get("yb"))
                s_o = P.step("tensor", mmo, deps=(wo, at_ev[-1], pv), tag=f"l1a_o{b}_{nt}_{o}")
                dst = PCf[:, o * 512:(o + 1) * 512]
                e_o = P.step("vector",
                             (lambda bank=bank, dst=dst, o=o, t0=t0: lambda e:
                              e.scalar_tensor_tensor(dst, bank[:, 0:512], boc[:, o:o + 1],
                                                     XN[:, o * T_CORE + t0: o * T_CORE + t0 + 512],
                                                     ALU.add, ALU.add))(),
                             deps=(s_o, pnt.get("pcf")), tag=f"l1a_oe{b}_{nt}_{o}")
                y1_ev.append(e_o)
            x2res = emit_inner_ln(mk, f"l1a_ln2_{b}_{nt}", PCf, g2c, be2c, epscol,
                                  y1_ev, pa_off=0,
                                  out_bf=(PA, 8192), x2_hbm=(None, 0), pnt=pnt)
            pnt["qtc"] = x2res["pa_cons"]
            pnt["qb"] = y1_ev[-4:]
            pnt["yb"] = y1_ev[-4:]
            pnt["pcf"] = x2res["pcf_cons"]
            pnt["dq"] = x2res["done"]
            pnt["xq"] = x2res["done"]
            fin = emit_l1a_ffn(mk, b, nt, t0, x2res, x_in, x_out, pnt)
            out_final.append(fin["st"])
            pnt.update(fin["pnt"])
        prev_b = {"allc": out_final[-1], "banks": out_final[-1], "rowb": out_final[-1],
                  "dgc": out_final[-1], "ctxb": out_final[-1]}
    return out_final



def emit_inner_ln(mk, tag, ysrc, gcol, becol, epscol, y_done, pa_off, out_bf, x2_hbm, pnt):
    P = mk.P
    pb = mk.pb
    PA = mk.PA
    r_m = mk.ROWS[0:1, 1024:1536]
    r_t = mk.ROWS[0:1, 1536:2048]
    ybf = PA[:, 0:4096]
    ysq = PA[:, 4096:8192]
    outb, ob_off = out_bf
    hbm_ap, t0 = x2_hbm
    c1 = P.step("vector", (lambda: lambda e: e.tensor_copy(ybf, ysrc[:, 0:4096]))(),
                deps=(y_done, pnt.get(tag + "pa")), tag=tag + "_cast")
    c2 = P.step("scalar", lambda e: e.activation(ysq, ybf, AF.Square), deps=(c1,), tag=tag + "_sq")

    def mk_stats(srcb, bank):
        def f(e):
            mm = None
            for c in range(NCH):
                mm = e.matmul(bank[0:1, 0:512], mk.ONECB[:, 0:1], srcb[:, c * 512:(c + 1) * 512],
                              start=(c == 0), stop=(c == NCH - 1))
            return mm
        return f
    m1 = P.step("tensor", mk_stats(ybf, pb[6]), deps=(c1, pnt.get(tag + "b6")), tag=tag + "_s1")
    m2 = P.step("tensor", mk_stats(ysq, pb[7]), deps=(c2, pnt.get(tag + "b7")), tag=tag + "_s2")
    q1 = P.step("vector", lambda e: e.tensor_single_scalar(r_m, pb[6][0:1, 0:512], 1.0 / D, ALU.mult),
                deps=(m1, pnt.get(tag + "rows")), tag=tag + "_m")
    q2 = P.step("vector", lambda e: e.tensor_single_scalar(r_t, pb[7][0:1, 0:512], 1.0 / D, ALU.mult),
                deps=(m2, pnt.get(tag + "rows")), tag=tag + "_e2")
    q3a = P.step("vector", lambda e: e.tensor_mul(mk.ROWS[0:1, 0:512], r_m, r_m),
                 deps=(q1, pnt.get(tag + "ra")), tag=tag + "_msq")
    q3 = P.step("vector", lambda e: e.tensor_sub(r_t, r_t, mk.ROWS[0:1, 0:512]),
                deps=(q2, q3a), tag=tag + "_var")
    q5 = P.step("scalar", lambda e: e.activation(r_t, r_t, AF.Ln, bias=epscol[0:1, 0:1], scale=1.0),
                deps=(q3, mk.cdep(LN_EPS)), tag=tag + "_ln")
    q6 = P.step("scalar", lambda e: e.activation(mk.ROWS[0:1, 0:512], r_t, AF.Exp, scale=-0.5),
                deps=(q5, q3a), tag=tag + "_rstd")
    q7 = P.step("vector", lambda e: e.tensor_mul(r_t, r_m, mk.ROWS[0:1, 0:512]),
                deps=(q6, q1), tag=tag + "_mr")
    b1 = P.step("tensor", lambda e: e.matmul(pb[4][:, 0:512], mk.ONEF[0:1, 0:128],
                                             mk.ROWS[0:1, 0:512], start=True, stop=True),
                deps=(q6, pnt.get(tag + "b4")), tag=tag + "_rb")
    b2 = P.step("tensor", lambda e: e.matmul(pb[5][:, 0:512], mk.ONEF[0:1, 0:128], r_t,
                                             start=True, stop=True),
                deps=(q7, pnt.get(tag + "b5")), tag=tag + "_mrb")
    done = []
    pcf_cons = []
    pa_cons = []
    for c in range(NCH):
        ysl = ysrc[:, c * 512:(c + 1) * 512]
        u1 = P.step("vector", (lambda ysl=ysl: lambda e: e.tensor_mul(ysl, ysl, pb[4][:, 0:512]))(),
                    deps=(b1, m1, m2, done[-1] if done else None), tag=tag + f"_t1_{c}")
        u2 = P.step("vector", (lambda ysl=ysl: lambda e: e.tensor_sub(ysl, ysl, pb[5][:, 0:512]))(),
                    deps=(u1, b2), tag=tag + f"_t2_{c}")
        u3 = P.step("vector", (lambda ysl=ysl, c=c: lambda e: e.tensor_scalar(
            ysl, ysl, gcol[:, c:c + 1], becol[:, c:c + 1], ALU.mult, ALU.add))(),
            deps=(u2,), tag=tag + f"_x2_{c}")
        u4 = P.step("vector", (lambda ysl=ysl, c=c: lambda e: e.tensor_copy(
            outb[:, ob_off + c * 512: ob_off + (c + 1) * 512], ysl))(),
            deps=(u3, pnt.get(tag + "ob")), tag=tag + f"_x2b_{c}")
        done.append(u4)
        pcf_cons.append(u4)
        pa_cons.append(u4)
    std = None
    if hbm_ap is not None:
        std = P.dma("sync", (lambda: lambda e: e.dma_start(
            out=bass.AP(hbm_ap.tensor, t0, [[T_CORE, 128], [128 * T_CORE, NCH], [1, 512]]),
            in_=ysrc[:, 0:4096].rearrange("p (c t) -> p c t", t=512)))(),
            deps=(done,), tag=tag + "_st")
    pnt[tag + "b6"] = q1
    pnt[tag + "b7"] = q2
    pnt[tag + "b4"] = done[-1]
    pnt[tag + "b5"] = done[-1]
    pnt[tag + "rows"] = [b1, b2]
    pnt[tag + "ra"] = b1
    return {"done": done, "pcf_cons": pcf_cons + ([std] if std else []),
            "pa_cons": pa_cons, "st": std, "b4": done[-1]}


def emit_l1a_ffn(mk, b, nt, t0, x2res, x_in, x_out, pnt):
    P = mk.P
    pb = mk.pb
    PA, PB, PC, PD = mk.PA, mk.PB, mk.PC, mk.PD
    PCw = PC[:, 0:4096].bitcast(BF16)
    PCf = PC[:, 4096:8192]
    b1c = mk.col("l1a_b1")
    b2c = mk.col("l1a_b2")
    g3c = mk.col("l1a_g3")
    be3c = mk.col("l1a_be3")
    epscol = mk.const_col(LN_EPS)
    x2bf = (PA, 8192)
    a_ev = []
    w1lds = []
    for blk in range(4):
        w1 = mk.load_wT("l1a_w1T", PCw[:, 0:8192], 1024,
                        deps=(x2res["pcf_cons"], a_ev[-8:] if a_ev else pnt.get("w1f"),
                              pnt.get("pcw")),
                        col0=blk * 1024, rowstride=4096, tag=f"_{b}_{nt}_{blk}")
        w1lds.append(w1)
        for fcl in range(NCH):
            fc = blk * 8 + fcl
            bank = pb[fcl % 4]

            def mmh(e, fcl=fcl, bank=bank):
                last = None
                for c in range(NCH):
                    last = e.matmul(bank[:, 0:512],
                                    PCw[:, c * 1024 + fcl * 128: c * 1024 + fcl * 128 + 128],
                                    PA[:, 8192 + c * 512: 8192 + (c + 1) * 512],
                                    start=(c == 0), stop=(c == NCH - 1))
                return last
            pv = a_ev[-4] if len(a_ev) >= 4 else pnt.get("hb")
            s_h = P.step("tensor", mmh, deps=(w1, x2res["pa_cons"], pv),
                         tag=f"l1a_h{b}_{nt}_{fc}")
            dst = PB[:, fc * 512:(fc + 1) * 512]
            e_h = P.step("scalar", (lambda bank=bank, dst=dst, fc=fc: lambda e:
                                    e.activation(dst, bank[:, 0:512], AF.Relu,
                                                 bias=b1c[:, fc:fc + 1], scale=1.0))(),
                         deps=(s_h, pnt.get("ab")), tag=f"l1a_relu{b}_{nt}_{fc}")
            a_ev.append(e_h)
    w2lds = []
    mm_h2 = {o: [] for o in range(NCH)}
    for blk in range(4):
        w2 = mk.load_wT("l1a_w2T", PCw[:, 0:8192], 1024,
                        deps=(a_ev, w1lds[-1]) if blk == 0 else ([w2lds[-1]] + mm_h2[NCH - 1][-1:],),
                        col0=0, rowstride=1024, base=blk * 1024 * 1024, tag=f"_{b}_{nt}w{blk}")
        w2lds.append(w2)
        for o in range(NCH):
            bank = pb[o]

            def mmy(e, blk=blk, o=o, bank=bank):
                last = None
                for fcl in range(NCH):
                    fc = blk * 8 + fcl
                    last = e.matmul(bank[:, 0:512],
                                    PCw[:, fcl * 1024 + o * 128: fcl * 1024 + o * 128 + 128],
                                    PB[:, fc * 512:(fc + 1) * 512],
                                    start=(fc == 0), stop=(fc == 31))
                return last
            deps = (w2, a_ev[-1] if blk == 0 else mm_h2[o][-1],
                    (x2res["done"] if blk == 0 and o < 4 else None),
                    (pnt.get("y2b") if blk == 0 else None))
            s_y = P.step("tensor", mmy, deps=deps, tag=f"l1a_y2{b}_{nt}_{blk}_{o}")
            mm_h2[o].append(s_y)
    y2_ev = []
    for o in range(NCH):
        dst = PCf[:, o * 512:(o + 1) * 512]
        e_y = P.step("vector",
                     (lambda o=o, dst=dst: lambda e: e.scalar_tensor_tensor(
                         dst, pb[o][:, 0:512], b2c[:, o:o + 1], dst, ALU.add, ALU.add))(),
                     deps=(mm_h2[o][-1], x2res["pcf_cons"], y2_ev[-1] if y2_ev else None),
                     tag=f"l1a_y2e{b}_{nt}_{o}")
        y2_ev.append(e_y)
    ln3 = emit_inner_ln(mk, f"l1a_ln3_{b}_{nt}", PCf, g3c, be3c, epscol, y2_ev,
                        pa_off=0, out_bf=(PA, 12288), x2_hbm=(None, 0), pnt=pnt)
    XO = PD[:, 2336:6432]
    ldx = P.dma("sync",
                (lambda t0=t0: lambda e: e.dma_start(
                    out=XO.rearrange("p (c t) -> p c t", t=512),
                    in_=bass.AP(x_in.tensor, t0, [[T_CORE, 128], [128 * T_CORE, NCH], [1, 512]])))(),
                deps=(pnt.get("xoc"), ln3["done"][0:1]), tag=f"l1a_ldx{b}_{nt}")
    o_ev = []
    for c in range(NCH):
        e_o = P.step("vector", (lambda c=c: lambda e: e.tensor_add(
            XO[:, c * 512:(c + 1) * 512], PCf[:, c * 512:(c + 1) * 512],
            XO[:, c * 512:(c + 1) * 512]))(),
            deps=(ldx, ln3["done"][c], o_ev[-1] if o_ev else None), tag=f"l1a_xo{b}_{nt}_{c}")
        o_ev.append(e_o)
    st = P.dma("sync",
               (lambda t0=t0: lambda e: e.dma_start(
                   out=bass.AP(x_out.tensor, t0, [[T_CORE, 128], [128 * T_CORE, NCH], [1, 512]]),
                   in_=XO.rearrange("p (c t) -> p c t", t=512)))(),
               deps=(o_ev,), tag=f"l1a_st{b}_{nt}")
    newpnt = {"w1f": [st], "pcw": mm_h2[NCH - 1][-1], "pbw": mm_h2[NCH - 1][-1],
              "hb": y2_ev[-1], "ab": mm_h2[0][-1] if mm_h2[0] else None,
              "y2b": ln3["done"][-1], "xoc": st, "qtc": st, "qb": [st], "dq": [st], "dq2": st,
              "xq": st, "tp1": st, "tp2": st, "tc1": st, "tc2": st, "stc": st, "bcq": st,
              "kpq": st, "u0": st, "u1": st, "rrow": st, "denb": st, "rbcmm": st, "rbcc": st,
              "att": st, "yb": [st], "pcf": [st]}
    return {"st": st, "pnt": newpnt}



SUBLAYERS = ["l0_attn_a", "l0_ffn_1", "l0_attn_b", "l0_ffn_2",
             "l1_attn_a", "l1_ffn_1", "l1_attn_b", "l1_ffn_2"]


def declare_inputs(mk, subs):
    mk.dram_in("x_fm", [D, T_CORE])
    if "l0_attn_a" in subs:
        for n in ("l0a_wqT", "l0a_wkT", "l0a_wvT", "l0a_woT"):
            mk.dram_in(n, [D, D])
        mk.dram_in("r_l0a_bkv", [1, 2048])
        mk.load_cols("l0a_bq", 8)
        mk.load_cols("l0a_bo", 8)
        mk.derive_col("l0a_bq1", "l0a_bq",
                      lambda e, dst, src: e.tensor_single_scalar(dst, src, 1.0, ALU.add))
    if any(s.startswith("l0_ffn") for s in subs):
        mk.dram_in("l0f_w", [D, D])
        mk.load_cols("l0f_bias", 8)
    if "l1_attn_a" in subs:
        for n in ("l1a_wqT", "l1a_wkT", "l1a_wvT", "l1a_woT"):
            mk.dram_in(n, [D, D])
        mk.dram_in("l1a_projT", [64, 256])
        mk.dram_in("l1a_w1T", [D, 4096])
        mk.dram_in("l1a_w2T", [4096, D])
        mk.dram_in("r_l1a_bv", [1, D])
        for n in ("l1a_bq", "l1a_bk", "l1a_bo", "l1a_g2", "l1a_be2", "l1a_g3", "l1a_be3",
                  "l1a_b2"):
            mk.load_cols(n[4:] if False else n, 8)
        mk.load_cols("l1a_b1", 32)
    if any(s.startswith("l1_ffn") for s in subs):
        mk.dram_in("l1f_gate_wTc", [128, 128])
        mk.dram_in("l1f_fc1T", [15, D, 2048])
        mk.dram_in("l1f_fc2T", [15, D, D])
        mk.dram_in("l1f_fc2b", [15, D])
        mk.dram_in("l1f_sfc1T", [D, 4096])
        mk.dram_in("l1f_sfc2T", [2048, D])
        mk.dram_in("r_l1f_sb2", [1, D])
        mk.load_cols("l1f_b1lo", 120)
        mk.load_cols("l1f_b1hi", 120)
        mk.load_cols("l1f_gm", 120)
        mk.load_cols("l1f_sblo", 16)
        mk.load_cols("l1f_sbhi", 16)
        mk.load_cols("l1f_sgm", 16)
    if "l1_attn_b" in subs:
        mk.dram_in("l1b_wq", [D, D])
        mk.dram_in("l1b_wk_down", [D, 64])
        mk.dram_in("l1b_wv_down", [D, 64])
        mk.dram_in("l1b_wk_up", [32, 128])
        mk.dram_in("l1b_wv_up", [32, 128])
        mk.dram_in("l1b_wo", [D, D])
    if "l0_attn_b" in subs:
        mk.dram_in("l0b_wqkvT", [2, D, 3 * D])
        mk.dram_in("l0b_woutT", [2, D, D])
        mk.dram_in("r_l0b_bv", [2, D])
        mk.load_cols("l0b_bqk", 32)
        mk.load_cols("l0b_bout", 16)
        def _bsum(e, dst, srcap):
            return e.scalar_tensor_tensor(dst, srcap[:, 0:8], mk.const_col(1.0 / 3.0)[:, 0:1],
                                          None, ALU.add, ALU.mult)
        o = mk._col_off
        mk._col_off += 8
        dst = mk.COLS[:, o:o + 8]
        b01, bst = mk.cols["l0b_bout"]
        s1 = mk.P.step("vector", lambda e: e.tensor_add(dst, b01[:, 0:8], b01[:, 8:16]),
                       deps=(bst,), tag="l0b_bsum_a")
        s2 = mk.P.step("vector", lambda e: e.tensor_scalar(dst, dst, 1.0 / 3.0, None, ALU.mult),
                       deps=(s1,), tag="l0b_bsum_b")
        mk.cols["l0b_bsum3"] = (dst, s2)
        mk.init_deps.append(s2)


def build(subs, n_launch_out=True):
    nc = bass.Bass("TRN2", target_bir_lowering=False)
    stack = contextlib.ExitStack()
    with stack:
        mk = MK(nc, stack)
        mk.init_consts()
        declare_inputs(mk, subs)
        x_out_final = mk.dram_out("out_fm", [D, T_CORE])
        xtmp = mk.dram_tmp("x_tmp", [D, T_CORE]) if len(subs) > 1 else None
        cur_in = mk.dram["x_fm"]
        finals = []
        for i, s in enumerate(subs):
            is_last = (i == len(subs) - 1)
            cur_out = x_out_final if is_last else xtmp
            prev = tuple(finals)
            if s == "l0_attn_a":
                finals = emit_l0_attn_a(mk, cur_in, cur_out, prev)
            elif s.startswith("l0_ffn"):
                first = not getattr(mk, "_l0f_done", False)
                finals = emit_l0_ffn(mk, cur_in, cur_out, prev, first_call=first)
                mk._l0f_done = True
            elif s == "l0_attn_b":
                finals = emit_l0_attn_b(mk, cur_in, cur_out, prev, branches=BRANCHES_L0B)
            elif s == "l1_attn_b":
                finals = emit_l1_attn_b(mk, cur_in, cur_out, prev)
            elif s.startswith("l1_ffn"):
                finals = emit_l1_ffn(mk, cur_in, cur_out, prev)
            elif s == "l1_attn_a":
                finals = emit_l1_attn_a(mk, cur_in, cur_out, prev)
            else:
                raise NotImplementedError(s)
            cur_in = cur_out
        mk.P.step("vector", lambda e: e.memset(mk.CONST[:, 15:16], 0.0), deps=(finals,),
                  tag="final_wait")
        mk.P.finalize(stack)
    return nc


def stage_core_inputs(inputs, core, subs):
    x = inputs["L0_attn_a_hidden_states"]
    d = {}
    xc = np.ascontiguousarray(
        x[core * B_PER_CORE:(core + 1) * B_PER_CORE].reshape(T_CORE, D).T).astype(np.float32)
    d["x_fm"] = xc
    if "l0_attn_a" in subs:
        d["l0a_wqT"] = np.ascontiguousarray(inputs["L0_attn_a_Wq"].T)
        d["l0a_wkT"] = np.ascontiguousarray(inputs["L0_attn_a_Wk"].T)
        d["l0a_wvT"] = np.ascontiguousarray(inputs["L0_attn_a_Wv"].T)
        d["l0a_woT"] = np.ascontiguousarray(inputs["L0_attn_a_Wo"].T)
        d["r_l0a_bkv"] = np.concatenate([inputs["L0_attn_a_bk"], inputs["L0_attn_a_bv"]])[None, :]
        d["c_l0a_bq"] = np.ascontiguousarray(inputs["L0_attn_a_bq"].reshape(8, 128).T)
        d["c_l0a_bo"] = np.ascontiguousarray(inputs["L0_attn_a_bo"].reshape(8, 128).T)
    if any(s.startswith("l0_ffn") for s in subs):
        d["l0f_w"] = inputs["L0_ffn_weight"]
        d["c_l0f_bias"] = np.ascontiguousarray(inputs["L0_ffn_bias"].reshape(8, 128).T)
    if "l1_attn_b" in subs:
        for nm in ("wq", "wk_down", "wv_down", "wk_up", "wv_up", "wo"):
            d["l1b_" + nm] = np.ascontiguousarray(inputs["L1_attn_b_" + nm])
    if "l1_attn_a" in subs:
        d["l1a_wqT"] = np.ascontiguousarray(inputs["L1_attn_a_Wq"].T)
        d["l1a_wkT"] = np.ascontiguousarray(inputs["L1_attn_a_Wk"].T)
        d["l1a_wvT"] = np.ascontiguousarray(inputs["L1_attn_a_Wv"].T)
        d["l1a_woT"] = np.ascontiguousarray(inputs["L1_attn_a_Wo"].T)
        d["l1a_projT"] = np.ascontiguousarray(inputs["L1_attn_a_proj"].T)
        d["l1a_w1T"] = np.ascontiguousarray(inputs["L1_attn_a_W1"].T)
        d["l1a_w2T"] = np.ascontiguousarray(inputs["L1_attn_a_W2"].T)
        d["r_l1a_bv"] = np.ascontiguousarray(inputs["L1_attn_a_bv"][None, :])
        cc8 = lambda v: np.ascontiguousarray(v.reshape(-1, 128).T)
        d["c_l1a_bq"] = cc8(inputs["L1_attn_a_bq"])
        d["c_l1a_bk"] = cc8(inputs["L1_attn_a_bk"])
        d["c_l1a_bo"] = cc8(inputs["L1_attn_a_bo"])
        d["c_l1a_g2"] = cc8(inputs["L1_attn_a_g2"])
        d["c_l1a_be2"] = cc8(inputs["L1_attn_a_be2"])
        d["c_l1a_g3"] = cc8(inputs["L1_attn_a_g3"])
        d["c_l1a_be3"] = cc8(inputs["L1_attn_a_be3"])
        d["c_l1a_b2"] = cc8(inputs["L1_attn_a_b2"])
        d["c_l1a_b1"] = cc8(inputs["L1_attn_a_b1"])
    if any(s.startswith("l1_ffn") for s in subs):
        gw = inputs["L1_ffn_gate_w"].T.reshape(8, 128, 15).transpose(1, 0, 2)
        gwp = np.zeros((128, 8, 16), np.float32)
        gwp[:, :, :15] = gw
        d["l1f_gate_wTc"] = np.ascontiguousarray(gwp.reshape(128, 128))
        d["l1f_fc1T"] = np.ascontiguousarray(inputs["L1_ffn_fc1_w"].transpose(0, 2, 1))
        d["l1f_fc2T"] = np.ascontiguousarray(inputs["L1_ffn_fc2_w"].transpose(0, 2, 1))
        d["l1f_fc2b"] = np.ascontiguousarray(inputs["L1_ffn_fc2_b"])
        d["l1f_sfc1T"] = np.ascontiguousarray(inputs["L1_ffn_s_fc1_w"].T)
        d["l1f_sfc2T"] = np.ascontiguousarray(inputs["L1_ffn_s_fc2_w"].T)
        d["r_l1f_sb2"] = np.ascontiguousarray(inputs["L1_ffn_s_fc2_b"][None, :])
        cc = lambda v: np.ascontiguousarray(v.reshape(-1, 128).T)
        d["c_l1f_b1lo"] = np.ascontiguousarray(np.concatenate(
            [cc(inputs["L1_ffn_fc1_b"][e, :1024]) for e in range(15)], axis=1))
        d["c_l1f_b1hi"] = np.ascontiguousarray(np.concatenate(
            [cc(inputs["L1_ffn_fc1_b"][e, 1024:]) for e in range(15)], axis=1))
        d["c_l1f_gm"] = np.ascontiguousarray(np.concatenate(
            [cc(inputs["L1_ffn_geglu_mult"][e]) for e in range(15)], axis=1))
        d["c_l1f_sblo"] = cc(inputs["L1_ffn_s_fc1_b"][:2048])
        d["c_l1f_sbhi"] = cc(inputs["L1_ffn_s_fc1_b"][2048:])
        d["c_l1f_sgm"] = cc(inputs["L1_ffn_s_geglu_mult"])
    if "l0_attn_b" in subs:
        d["l0b_wqkvT"] = np.ascontiguousarray(inputs["L0_attn_b_Wqkv"][:2].transpose(0, 2, 1))
        d["l0b_woutT"] = np.ascontiguousarray(inputs["L0_attn_b_Wout"][:2].transpose(0, 2, 1))
        d["r_l0b_bv"] = np.ascontiguousarray(inputs["L0_attn_b_bqkv"][:2, 2048:3072])
        bqk = np.stack([inputs["L0_attn_b_bqkv"][br, i * 1024:(i + 1) * 1024].reshape(8, 128).T
                        for br in range(2) for i in range(2)], 0)
        d["c_l0b_bqk"] = np.ascontiguousarray(np.concatenate(list(bqk), axis=1))
        d["c_l0b_bout"] = np.ascontiguousarray(np.concatenate(
            [inputs["L0_attn_b_bout"][br].reshape(8, 128).T for br in range(2)], axis=1))
    return {k: np.ascontiguousarray(v, dtype=v.dtype) for k, v in d.items()}


def run_subs(inputs, subs, x_override=None):
    nc = build(subs)
    in_maps = []
    for c in range(NCORE):
        src = dict(inputs)
        if x_override is not None:
            src["L0_attn_a_hidden_states"] = x_override
        in_maps.append(stage_core_inputs(src, c, subs))
    res = run_bass_kernel_spmd(nc, in_maps, core_ids=list(range(NCORE)))
    outs = []
    for c in range(NCORE):
        o = res.results[c]["out_fm"]
        outs.append(o.T.reshape(B_PER_CORE, T_SEQ, D))
    return np.concatenate(outs, axis=0)


def kernel(**inputs):
    return run_subs(inputs, SUBLAYERS)


if __name__ == "__main__":
    pass
